# version 05
# baseline (speedup 1.0000x reference)
import math
import numpy as np
import ml_dtypes
import concourse.bass as bass
import concourse.mybir as mybir
from concourse.bass_utils import run_bass_kernel_spmd
from contextlib import ExitStack

F32 = mybir.dt.float32
BF16 = mybir.dt.bfloat16
AF = mybir.ActivationFunctionType
ALU = mybir.AluOpType

NEG = -1.0e30
KEY_SCALE_MAX = math.log(2 ** 16 - 1)
CONSTS = (1e-5, 1e-10, 10.0, KEY_SCALE_MAX, 1.0, 0.0, -1.0, 0.125, -40.0, -20.0)


class Sem:

    def __init__(self, nc, ctx, name, dma=False):
        self.hA = ctx.enter_context(nc.semaphore(name + "a"))
        self.hB = ctx.enter_context(nc.semaphore(name + "b"))
        self.va = 0
        self.vb = 0
        self.ic = 0
        self.dma = dma

    @property
    def n(self):
        return (self.va, self.vb)

    def inc(self, instr, k=1):
        if self.ic % 2 == 0:
            instr.then_inc(self.hA, k)
            self.va += k
        else:
            instr.then_inc(self.hB, k)
            self.vb += k
        self.ic += 1
        return (self.va, self.vb)

    def tok_after(self, snap, j, k=1):
        va, vb, ic = snap
        for i in range(j):
            if (ic + i) % 2 == 0:
                va += k
            else:
                vb += k
        return (va, vb)

    @property
    def snap(self):
        return (self.va, self.vb, self.ic)

    def wait(self, eng, tok=None):
        va, vb = (self.va, self.vb) if (tok is None or self.dma) else tok
        if va:
            eng.wait_ge(self.hA, va)
        if vb:
            eng.wait_ge(self.hB, vb)


class Arena:
    def __init__(self, tensor, width):
        self.t = tensor
        self.w = width
        self.stack = [0]

    def alloc(self, cols):
        o = self.stack[-1]
        assert o + cols <= self.w, f"arena overflow {o}+{cols}>{self.w}"
        self.stack[-1] = o + cols
        return o

    def push(self):
        self.stack.append(self.stack[-1])

    def pop(self):
        self.stack.pop()


def hope_tables(T, base=10000, d_model=1024):
    dim = d_model // 2
    pos = np.arange(dim, dtype=np.float64)
    freqs_np = 1.0 / base ** (pos / dim)
    active = int(np.sum(freqs_np * 2 * math.pi * T >= 1.0))
    active = min(active, dim - d_model // 4)
    freqs = freqs_np[:active].astype(np.float32)
    d = np.arange(0, T, dtype=np.float32)
    theta = (d[:, None] * freqs[None, :]).astype(np.float32)
    pv = (np.cos(theta, dtype=np.float32) + np.sin(theta, dtype=np.float32)).astype(np.float32)
    nh = active // 32
    per_head = pv.astype(np.float64).reshape(T, nh, 32).sum(axis=2)
    hi = per_head.astype(np.float32)
    lo = (per_head - hi.astype(np.float64)).astype(np.float32)
    return hi.T.copy(), lo.T.copy()


def shift2d(row):
    L = row.shape[0]
    padded = np.concatenate([np.zeros(127, row.dtype), row])
    s = np.lib.stride_tricks.sliding_window_view(padded, L)
    return np.ascontiguousarray(s[::-1])


def build_tables(T):
    LPAD = T + 128
    L = LPAD + T
    d = np.arange(-LPAD, T)
    mask_a = np.where(d >= 1, 0.0, NEG).astype(np.float32)
    mask_f = np.where(d >= 0, 0.0, NEG).astype(np.float32)
    hi8, lo8 = hope_tables(T)
    hope_hi = np.zeros((8, L), dtype=np.float32)
    hope_lo = np.zeros((8, L), dtype=np.float32)
    hope_hi[:, :LPAD] = NEG
    hope_hi[:, LPAD:] = hi8
    hope_lo[:, LPAD:] = lo8
    ma2 = shift2d(mask_a)
    mf2 = shift2d(mask_f)
    hh2 = np.concatenate([shift2d(hope_hi[h]) for h in range(8)], axis=0)
    hl2 = np.concatenate([shift2d(hope_lo[h]) for h in range(8)], axis=0)
    return ma2, mf2, hh2, hl2


def mk_constants():
    c = np.zeros((128, 192), dtype=np.float32)
    c[:, 0] = 1.0
    c[:, 1] = np.arange(1, 129, dtype=np.float32)
    for h in range(8):
        c[h, 16 + 8 * h:16 + 8 * h + 8] = 1.0
    for h in range(8):
        c[64, 80 + 8 * h + h] = 1.0
    return c


def mk_selectors():
    s = np.zeros((64, 32 * 128), dtype=np.float32)
    for e in range(8):
        for j in range(4):
            m = e * 4 + j
            s[(2 * j) * 8 + e, 128 * m:128 * m + 64] = 1.0
            s[(2 * j + 1) * 8 + e, 128 * m + 64:128 * m + 128] = 1.0
    return s


class KB:
    def __init__(self, T, subs):
        self.T = T
        self.subs = subs
        self.NT = T // 512
        self.NC = T // 128
        self.TP1 = T + 1
        self.LPAD = T + 128
        self.TBL = self.LPAD + T
        self.MEGW = T + 384
        self.nc = bass.Bass("TRN2", target_bir_lowering=False)

    def build(self):
        nc = self.nc
        T = self.T

        def din(name, shape, dtype=F32):
            return nc.dram_tensor(name, shape, dtype, kind="ExternalInput").ap()

        self.x_in = din("x_feat", [1024, T])
        self.wkT = din("wkT", [1024, 1024])
        self.wvT = din("wvT", [1024, 1024])
        self.wcT = din("wcT", [128, 8 * 1024])
        self.aprm = din("aprm", [1, 64])
        self.wqkvT = din("wqkvT", [1024, 3072])
        self.bqkv = din("bqkv", [1, 3072])
        self.wpT = din("wpT", [128, 8 * 1024])
        self.bproj = din("bproj", [1, 1024])
        self.wqf = din("wqf", [1024, 512])
        self.wkf = din("wkf", [1024, 512])
        self.wgf = din("wgf", [1024, 64])
        self.wve = din("wve", [8, 1024, 512])
        self.woe = din("woe", [32, 128, 1024])
        self.tbl_ma = din("tbl_ma", [128, self.TBL])
        self.tbl_mf = din("tbl_mf", [128, self.TBL])
        self.tbl_hh = din("tbl_hh", [8 * 128, self.TBL])
        self.tbl_hl = din("tbl_hl", [8 * 128, self.TBL])
        self.cstb_in = din("cstb_in", [128, 2048])
        self.tblu = din("tblu", [128, 256])
        self.mrows = din("mrows", [2, 4096])
        self.cstf_in = din("cstf_in", [128, 512])
        self.selc_in = din("selc_in", [64, 32 * 128])

        self.y_out = nc.dram_tensor("y", [1024, T], F32, kind="ExternalOutput").ap()
        self.xres = nc.dram_tensor("xres", [1024, T], F32, kind="Internal").ap()
        self.knhb = nc.dram_tensor("knhb", [128, 8 * self.TP1], BF16, kind="Internal").ap()
        self.qhb = nc.dram_tensor("qhb", [1024, T], BF16, kind="Internal").ap()
        self.aghb = nc.dram_tensor("aghb", [2, T], F32, kind="Internal").ap()

        ctx = ExitStack()
        with ctx:
            self._alloc(ctx)
            self._emit()
        return nc

    def _alloc(self, ctx):
        nc = self.nc
        self.pt = ctx.enter_context(nc.sbuf_tensor("pt", [128, 4 * 512], BF16))
        self.rows = ctx.enter_context(nc.sbuf_tensor("rows", [128, 2048], F32))
        self.cstf = ctx.enter_context(nc.sbuf_tensor("cstf", [128, 512], F32))
        self.cstb = ctx.enter_context(nc.sbuf_tensor("cstb", [128, 2048], BF16))
        self.w2 = ctx.enter_context(nc.sbuf_tensor("w2", [2, 4096], BF16))
        self.g2 = ctx.enter_context(nc.sbuf_tensor("g2", [2, 512], BF16))
        self.AB_t = ctx.enter_context(nc.sbuf_tensor("AB", [128, 76000], BF16))
        self.AF_t = ctx.enter_context(nc.sbuf_tensor("AFa", [128, 8200], F32))
        self.AB = Arena(self.AB_t, 76000)
        self.AFA = Arena(self.AF_t, 8200)

        self.pL = ctx.enter_context(nc.psum_tensor("pL", [128, 2048], F32))
        self.pA = ctx.enter_context(nc.psum_tensor("pA", [128, 1024], F32))
        self.pB = ctx.enter_context(nc.psum_tensor("pB", [128, 512], F32))
        self.pC = ctx.enter_context(nc.psum_tensor("pC", [128, 512], F32))

        self.sDW = Sem(nc, ctx, "sDW", dma=True)
        self.sDA = Sem(nc, ctx, "sDA", dma=True)
        self.sDX = Sem(nc, ctx, "sDX", dma=True)
        self.sDS = Sem(nc, ctx, "sDS", dma=True)
        self.sT = Sem(nc, ctx, "sT")
        self.sS = Sem(nc, ctx, "sS")
        self.sV = Sem(nc, ctx, "sV")
        self.sG = Sem(nc, ctx, "sG")
        self.sT2 = Sem(nc, ctx, "sT2")
        self.sS2 = Sem(nc, ctx, "sS2")
        self.sY = Sem(nc, ctx, "sY")

        self.block = ctx.enter_context(nc.Block())
        for i, val in enumerate(CONSTS):
            nc.const_aps.register(F32, val, self.cstf[:, 220 + i:221 + i])

    def t(self, f): self.block.tensor(f)
    def s(self, f): self.block.scalar(f)
    def v(self, f): self.block.vector(f)
    def g(self, f): self.block.gpsimd(f)
    def y(self, f): self.block.sync(f)

    def ld(self, eng, dst, src, sem=None):
        sem = sem or self.sDW
        sem.inc(eng.dma_start(out=dst, in_=src), 16)

    def fence(self):
        sT, sS, sV, sG, sY = self.sT, self.sS, self.sV, self.sG, self.sY
        sDS, sDX, sDW = self.sDS, self.sDX, self.sDW
        def ft(t):
            sT.wait(t)
            sT.inc(t.nop("fence", nofuse=True))
        def fs(s):
            sS.wait(s)
            sS.inc(s.nop("fence", nofuse=True))
        def fv(v):
            sV.wait(v)
            sV.inc(v.nop("fence", nofuse=True))
        def fg(g):
            sDW.wait(g)
            sG.wait(g)
            sG.inc(g.nop("fence", nofuse=True))
        def fy(y):
            sDS.wait(y)
            sDX.wait(y)
            sY.wait(y)
            sY.inc(y.nop("fence", nofuse=True))
        self.t(ft); self.s(fs); self.v(fv); self.g(fg); self.y(fy)
        def wt(t): sS.wait(t); sV.wait(t); sG.wait(t); sY.wait(t)
        def ws(s): sT.wait(s); sV.wait(s); sG.wait(s); sY.wait(s)
        def wv(v): sT.wait(v); sS.wait(v); sG.wait(v); sY.wait(v)
        def wg(g): sT.wait(g); sS.wait(g); sV.wait(g); sY.wait(g)
        def wy(y): sT.wait(y); sS.wait(y); sV.wait(y); sG.wait(y)
        self.t(wt); self.s(ws); self.v(wv); self.g(wg); self.y(wy)

    def _emit(self):
        C = self
        sDW, sDX, sDS = C.sDW, C.sDX, C.sDS
        sV = C.sV

        def f(g):
            C.ld(g, C.cstf[:, :], C.cstf_in[:, :])
            C.ld(g, C.cstb[:, :], C.cstb_in[:, :])
            C.ld(g, C.xres[:, :], C.x_in[:, :])
        C.g(f)
        def f(g):
            C.ld(g, C.g2[1:2, 0:512], C.mrows[0:1, 0:512])
        C.g(f)
        C.fence()

        nsub = len(C.subs)
        for i, sub in enumerate(C.subs):
            last = (i == nsub - 1)
            if sub == "attn_a":
                C.sub_attn_a(last)
            elif sub in ("ffn1", "ffn2"):
                C.sub_ffn(last)
            elif sub == "attn_b":
                C.sub_attn_b(last)
            C.fence()

        def f(y_):
            sDS.wait(y_)
        C.y(f)
        def f(g):
            sDS.wait(g)
        C.g(f)


def emit_attention_slot(C, cfg, slot):
    sT, sS, sV, sT2, sS2 = C.sT, C.sS, C.sV, C.sT2, C.sS2
    idn = C.cstf[:, CF_ID:CF_ID + 128]
    st = cfg.setdefault("_state", {"epi": {0: None, 1: None}, "hctr": 0})
    cnt = cfg["counts"](slot)
    order = cfg.get("head_order") or list(range(cfg["H"]))
    for h in order:
        if cfg.get("pre_head"):
            cfg["pre_head"](slot, h)
        hp = st["hctr"] % 2
        st["hctr"] += 1
        av = C.pA[:, hp * 512:(hp + 1) * 512]
        qap = cfg["q_ap"](slot, h)
        snT2 = sT2.snap
        snS2 = sS2.snap
        snT = sT.snap
        prev_exp_all = sS2.n
        prev_av_all = sT2.n
        bprev = st["epi"][hp]
        ph = cfg["phantom"](slot) if cfg.get("phantom") else None

        def ft(t, slot=slot, h=h, cnt=cnt, av=av, qap=qap,
               snS2=snS2, prev_exp_all=prev_exp_all, bprev=bprev, ph=ph):
            sS2.wait(t, prev_exp_all)
            if bprev is not None:
                bprev[0].wait(t, bprev[1])
            for kt in range(cnt):
                r = kt % 4
                if kt >= 1:
                    sS2.wait(t, sS2.tok_after(snS2, kt))
                    m = t.matmul(av[0:65, :], cfg["vn_ap"](kt - 1, h),
                                 C.pt[:, ((kt - 1) % 4) * 512:((kt - 1) % 4) * 512 + 512],
                                 start=(kt - 1 == 0), stop=False)
                    sT2.inc(m)
                bias = cfg["bias_ap"](slot, kt, h)
                m = t.matmul(C.pL[:, r * 512:(r + 1) * 512],
                             cfg["k_ap"](slot, kt, h), qap,
                             start=True, stop=(bias is None))
                if bias is not None:
                    m = t.matmul(C.pL[:, r * 512:(r + 1) * 512], idn, bias,
                                 start=False, stop=True)
                sT.inc(m)
            sS2.wait(t, sS2.tok_after(snS2, cnt))
            m = t.matmul(av[0:65, :], cfg["vn_ap"](cnt - 1, h),
                         C.pt[:, ((cnt - 1) % 4) * 512:((cnt - 1) % 4) * 512 + 512],
                         start=(cnt - 1 == 0), stop=(ph is None))
            if ph is not None:
                m = t.matmul(av[0:65, :], ph[0], ph[1], start=False, stop=True)
            sT2.inc(m)
        C.t(ft)
        final_av_tok = sT2.n

        def fs(s, cnt=cnt, snT=snT, snT2=snT2, prev_av_all=prev_av_all):
            for kt in range(cnt):
                r = kt % 4
                if kt < 4:
                    sT2.wait(s, prev_av_all)
                else:
                    sT2.wait(s, sT2.tok_after(snT2, kt - 3))
                sT.wait(s, sT.tok_after(snT, kt + 1))
                a = s.activation(C.pt[:, r * 512:(r + 1) * 512],
                                 C.pL[:, r * 512:(r + 1) * 512], AF.Exp,
                                 bias=cfg.get("eshift", 0.0))
                sS2.inc(a)
        C.s(fs)

        st["epi"][hp] = cfg["epilogue"](slot, h, av, final_av_tok)


def emit_mega_mask(C, mega_ap, tbl, qshift, W=None):
    W = W or C.MEGW
    off = C.LPAD + qshift - 384
    def f(g):
        C.sT.wait(g)
        C.sDW.inc(g.dma_start(out=mega_ap[:, 0:W], in_=tbl[0:128, off:off + W]), 16)
        C.sDW.wait(g)
        C.sG.inc(g.nop("rl", nofuse=True))
    C.g(f)
    return C.sG.n


def mega_c0(C, slot, kt):
    return slot * 512 - 128 * kt + 384


KB.emit_attention_slot = emit_attention_slot
KB.emit_mega_mask = emit_mega_mask
KB.mega_c0 = mega_c0


CB_ONES = 0
CB_P1 = 1
CB_E8 = 16
CB_OH = 80
CB_ONESB = 144
CB_P1ROW = 208
CB_ID = 336
CB_DEC = 464
CB_TRI = 592
CB_BDT = 720
CB_BD = 1744
CB_ONESR = 1880
CF_ONES = 0
CF_E8 = 16
CF_ONESB = 80
CF_CONST = 220
CF_ID = 256
CF_BD = 384


def mk_cstb():
    c = np.zeros((128, 2048), dtype=np.float32)
    c[:, CB_ONES] = 1.0
    c[:, CB_P1] = np.arange(1, 129, dtype=np.float32)
    for h in range(8):
        c[h, CB_E8 + 8 * h: CB_E8 + 8 * h + 8] = 1.0
    for h in range(8):
        c[64, CB_OH + 8 * h + h] = 1.0
    c[:, CB_ONESB:CB_ONESB + 64] = 1.0
    c[0, CB_P1ROW:CB_P1ROW + 128] = np.arange(1, 129, dtype=np.float32)
    c[:, CB_ID:CB_ID + 128] = np.eye(128, dtype=np.float32)
    p = np.arange(128)[:, None]
    j = np.arange(128)[None, :]
    c[:, CB_DEC:CB_DEC + 128] = np.maximum(p - j, 0).astype(np.float32)
    c[:, CB_TRI:CB_TRI + 128] = (p >= j).astype(np.float32)
    for dc in range(8):
        blk = np.zeros((128, 16), dtype=np.float32)
        for pp in range(128):
            blk[pp, 2 * dc + pp // 64] = 1.0
        c[:, CB_BD + 16 * dc: CB_BD + 16 * dc + 16] = blk
        bt = np.zeros((128, 128), dtype=np.float32)
        for pp in range(128):
            bt[2 * dc + pp // 64, pp] = 1.0
        c[:, CB_BDT + 128 * dc: CB_BDT + 128 * dc + 128] = bt
    c[:, CB_ONESR:CB_ONESR + 128] = 1.0
    return c


def mk_cstf():
    c = np.zeros((128, 512), dtype=np.float32)
    c[:, CF_ONES] = 1.0
    for h in range(8):
        c[h, CF_E8 + 8 * h: CF_E8 + 8 * h + 8] = 1.0
    c[:, CF_ONESB:CF_ONESB + 128] = 1.0
    c[:, CF_ID:CF_ID + 128] = np.eye(128, dtype=np.float32)
    for i, val in enumerate(CONSTS):
        c[:, CF_CONST + i] = val
    for dc in range(8):
        for pp in range(128):
            c[pp, CF_BD + 16 * dc + 2 * dc + pp // 64] = 1.0
    return c


def ln_full(C, tt, xt, xb, store=True):
    sDX, sDS, sT, sS, sV = C.sDX, C.sDS, C.sT, C.sS, C.sV
    R = C.rows
    t0 = tt * 512
    def f(y):
        sV.wait(y)
        sS.wait(y)
        for dc in range(8):
            C.ld(y, xt[:, dc * 512:(dc + 1) * 512],
                 C.xres[dc * 128:(dc + 1) * 128, t0:t0 + 512], sDX)
        sDX.wait(y)
        C.sY.inc(y.nop("rl", nofuse=True))
    C.y(f)
    tkx = C.sY.n
    def f(v):
        C.sY.wait(v, tkx)
        sT.wait(v)
        c = None
        for dc in range(8):
            c = v.tensor_copy(xb[:, dc * 512:(dc + 1) * 512], xt[:, dc * 512:(dc + 1) * 512])
        sV.inc(c)
    C.v(f)
    tkb = sV.n
    C.AFA.push()
    o = C.AFA.alloc(1024)
    sq_sl = [C.AF_t[:, o:o + 512], C.AF_t[:, o + 512:o + 1024]]
    snT = sT.snap
    def f(s):
        sV.wait(s, tkb)
        sT.wait(s, (snT[0], snT[1]))
        for dc in range(8):
            if dc >= 2:
                sT.wait(s, sT.tok_after(snT, dc - 1))
            a = s.activation(sq_sl[dc % 2][:, :], xt[:, dc * 512:(dc + 1) * 512], AF.Square)
            sS.inc(a)
    snS = sS.snap
    C.s(f)
    def f(t):
        for dc in range(8):
            sS.wait(t, sS.tok_after(snS, dc + 1))
            t.matmul(C.pB[0:1, 0:512], C.cstb[:, CB_ONES:CB_ONES + 1],
                     xb[:, dc * 512:(dc + 1) * 512], start=(dc == 0), stop=(dc == 7))
            m2 = t.matmul(C.pL[0:1, 0:512], C.cstf[:, CF_ONES:CF_ONES + 1],
                          sq_sl[dc % 2][:, :], start=(dc == 0), stop=(dc == 7))
            sT.inc(m2)
    C.t(f)
    tksum = sT.n
    def f(s):
        sT.wait(s, tksum)
        C.sY.wait(s)
        s.activation(R[0:1, 0:512], C.pB[0:1, 0:512], AF.Copy, scale=1.0 / 1024.0)
        a = s.activation(R[0:1, 512:1024], C.pL[0:1, 0:512], AF.Copy, scale=1.0 / 1024.0)
        tk = sS.inc(a)
        sS.wait(s, tk)
        a = s.activation(R[0:1, 1024:1536], R[0:1, 0:512], AF.Square)
        sS.inc(a)
    C.s(f)
    tk1 = sS.n
    def f(v):
        sS.wait(v, tk1)
        c = v.tensor_sub(R[0:1, 512:1024], R[0:1, 512:1024], R[0:1, 1024:1536])
        sV.inc(c)
    C.v(f)
    tk2 = sV.n
    def f(s):
        sV.wait(s, tk2)
        a = s.activation(R[0:1, 1024:1536], R[0:1, 512:1024], AF.Ln, bias=1e-5)
        tk = sS.inc(a)
        sS.wait(s, tk)
        a = s.activation(R[0:1, 1536:2048], R[0:1, 1024:1536], AF.Exp, scale=-0.5)
        sS.inc(a)
    C.s(f)
    tk3 = sS.n
    def f(v):
        sS.wait(v, tk3)
        v.tensor_mul(C.g2[0:1, 0:512], R[0:1, 0:512], R[0:1, 1536:2048])
        c = v.tensor_mul(R[0:1, 1024:1536], R[0:1, 0:512], R[0:1, 1536:2048])
        sV.inc(c)
    C.v(f)
    tk4 = sV.n
    if store:
        def f(y, tt=tt):
            sV.wait(y, tk4)
            C.ld(y, C.aghb[0:1, tt * 512:(tt + 1) * 512], R[0:1, 1536:2048], sDS)
            C.ld(y, C.aghb[1:2, tt * 512:(tt + 1) * 512], R[0:1, 1024:1536], sDS)
            sDS.wait(y)
            C.sY.inc(y.nop("rl", nofuse=True))
        C.y(f)
    def f(t):
        sV.wait(t, tk4)
        m = t.matmul(C.pB[0:128, 0:512], C.cstf[0:1, CF_ONESB:CF_ONESB + 128], R[0:1, 1536:2048],
                     start=True, stop=True)
        sT.inc(m)
    C.t(f)
    tk5 = sT.n
    def f(v):
        sT.wait(v, tk5)
        c = None
        for dc in range(8):
            c = v.tensor_mul(xb[:, dc * 512:(dc + 1) * 512],
                             xt[:, dc * 512:(dc + 1) * 512], C.pB[0:128, 0:512])
        sV.inc(c)
    C.v(f)
    C.AFA.pop()
    return sV.n


def ln_lite(C, tt, xt, xb, pre_t=None):
    sDX, sT, sV = C.sDX, C.sT, C.sV
    R = C.rows
    t0 = tt * 512
    def f(y):
        sV.wait(y)
        C.sS.wait(y)
        sT.wait(y)
        C.ld(y, R[0:1, 1536:2048], C.aghb[0:1, t0:t0 + 512], sDX)
        C.ld(y, R[0:1, 0:512], C.aghb[1:2, t0:t0 + 512], sDX)
        for dc in range(8):
            C.ld(y, xt[:, dc * 512:(dc + 1) * 512],
                 C.xres[dc * 128:(dc + 1) * 128, t0:t0 + 512], sDX)
        sDX.wait(y)
        C.sY.inc(y.nop("rl", nofuse=True))
    C.y(f)
    tkx = C.sY.n
    def f(t):
        C.sY.wait(t, tkx)
        if pre_t is not None:
            pre_t(t)
        m = t.matmul(C.pB[0:128, 0:512], C.cstf[0:1, CF_ONESB:CF_ONESB + 128],
                     R[0:1, 1536:2048], start=True, stop=True)
        sT.inc(m)
    C.t(f)
    tk5 = sT.n
    def f(v):
        sT.wait(v, tk5)
        c = None
        for dc in range(8):
            c = v.tensor_mul(xb[:, dc * 512:(dc + 1) * 512],
                             xt[:, dc * 512:(dc + 1) * 512], C.pB[0:128, 0:512])
        c2 = v.tensor_copy(C.g2[0:1, 0:512], R[0:1, 0:512])
        sV.inc(c2)
    C.v(f)
    return sV.n


def w1_bias(C, wsb, dout, w2off, bias_src=None, wsb_cols0=0, wait_tok=None):
    sT, sS, sG, sDW = C.sT, C.sS, C.sG, C.sDW
    def f(g):
        if bias_src is not None:
            C.ld(g, C.w2[1:2, w2off:w2off + dout], bias_src)
        else:
            C.ld(g, C.w2[1:2, w2off:w2off + dout], C.mrows[1:2, 0:dout])
        sDW.wait(g)
        sG.inc(g.nop("rl", nofuse=True))
    C.g(f)
    wtok = sG.n
    for oc in range((dout + 511) // 512):
        o0 = oc * 512
        ow = min(512, dout - o0)
        def f(t, o0=o0, ow=ow, snv=C.sV.n, sns=C.sS.n):
            sG.wait(t, wtok)
            C.sV.wait(t, snv)
            C.sS.wait(t, sns)
            m = None
            for ic in range(8):
                m = t.matmul(C.pB[0:1, 0:ow], C.cstb[:, CB_ONES:CB_ONES + 1],
                             wsb[:, wsb_cols0 + ic * dout + o0: wsb_cols0 + ic * dout + o0 + ow],
                             start=(ic == 0), stop=(ic == 7))
            sT.inc(m)
        C.t(f)
        tk = sT.n
        def f(s, o0=o0, ow=ow, tk=tk):
            sT.wait(s, tk)
            a = s.activation(C.w2[0:1, w2off + o0: w2off + o0 + ow], C.pB[0:1, 0:ow],
                             AF.Copy, scale=-1.0)
            sS.inc(a)
        C.s(f)
        tk2 = sS.n
        def f(t, tk2=tk2):
            sS.wait(t, tk2)
        C.t(f)
    return sS.n, sG.n, sDW.n


def proj_feat_oc(C, xb, wslice_fn, oc, w2off, pre=None):
    sT = C.sT
    def f(t):
        if pre is not None:
            pre(t)
        t.matmul(C.pC[0:128, 0:512], C.w2[:, w2off + oc * 128: w2off + (oc + 1) * 128],
                 C.g2[:, 0:512], start=True, stop=False)
        m = None
        for ic in range(8):
            m = t.matmul(C.pC[0:128, 0:512], wslice_fn(ic),
                         xb[:, ic * 512:(ic + 1) * 512], start=False, stop=(ic == 7))
        sT.inc(m)
    C.t(f)
    return sT.n


def proj_tok_tc(C, xb, wslice_fn, tc, ow, w2off, o0=0, pre=None):
    sT = C.sT
    def f(t):
        if pre is not None:
            pre(t)
        t.matmul(C.pC[0:128, 0:ow], C.g2[0:2, tc * 128:(tc + 1) * 128],
                 C.w2[:, w2off + o0: w2off + o0 + ow], start=True, stop=False)
        m = None
        for ic in range(8):
            m = t.matmul(C.pC[0:128, 0:ow],
                         xb[:, ic * 512 + tc * 128: ic * 512 + tc * 128 + 128],
                         wslice_fn(ic), start=False, stop=(ic == 7))
        sT.inc(m)
    C.t(f)
    return sT.n


KB.ln_full = ln_full
KB.ln_lite = ln_lite
KB.w1_bias = w1_bias
KB.proj_feat_oc = proj_feat_oc
KB.proj_tok_tc = proj_tok_tc

def load_wsec(C, dst_cols, src2d, dout, pitch, rows0=0, nic=8):
    t = C.AB_t
    def f(g):
        C.sT.wait(g)
        src = bass.AP(src2d.tensor, src2d.offset + rows0,
                      [[pitch, 128], [128 * pitch, nic], [1, dout]])
        C.sDW.inc(g.dma_start(out=t[:, dst_cols:dst_cols + nic * dout].rearrange("p (i o) -> p i o", i=nic),
                    in_=src), 16)
        C.sDW.wait(g)
        C.sG.inc(g.nop("rl", nofuse=True))
    C.g(f)
    return C.sG.n


def sub_attn_b(C, last):
    T, NT, NC, TP1 = C.T, C.NT, C.NC, C.TP1
    sv = 65 * 16
    AB, AFA = C.AB, C.AFA
    sT, sS, sV, sG, sDW, sDX, sDS = C.sT, C.sS, C.sV, C.sG, C.sDW, C.sDX, C.sDS

    AB.push()
    kn_o = AB.alloc(8 * TP1)
    kn = C.AB_t[:, kn_o:kn_o + 8 * TP1]

    def f(g):
        a = g.memset(kn[:, :].rearrange("p (c t) -> p c t", c=8)[:, :, 0:1], 0.0)
        sG.inc(a)
    C.g(f)
    C.fence()

    for sec, o0, w2off in (("q", 0, 0), ("k", 1024, 1024)):
        AB.push()
        AFA.push()
        xt = C.AF_t[:, AFA.alloc(4096):AFA.stack[-1]]
        xb_o = AB.alloc(4096)
        xb = C.AB_t[:, xb_o:xb_o + 4096]
        ws_o = AB.alloc(8192)
        ws = C.AB_t[:, ws_o:ws_o + 8192]
        wtok = load_wsec(C, ws_o, C.wqkvT, 1024, 3072, rows0=o0)
        C.w1_bias(ws, 1024, w2off, C.bqkv[0:1, o0:o0 + 1024], wait_tok=wtok)
        stg_o = AB.alloc(1024)
        stg = C.AB_t[:, stg_o:stg_o + 1024]
        stok = [(0, 0), (0, 0)]
        for tt in range(NT):
            if sec == "q":
                lt = C.ln_full(tt, xt, xb, store=True)
            else:
                lt = C.ln_lite(tt, xt, xb)
            prev_cp = [sS.n]
            for oc in range(8):
                def pre(t, pc=prev_cp[0], lt=lt):
                    sS.wait(t, pc)
                    sV.wait(t, lt)
                tok = C.proj_feat_oc(xb, lambda ic, oc=oc: ws[:, ic * 1024 + oc * 128: ic * 1024 + (oc + 1) * 128],
                                     oc, w2off, pre=pre)
                if sec == "q":
                    sl = oc % 2
                    def fse(s, oc=oc, sl=sl, tok=tok):
                        sT.wait(s, tok)
                        C.sY.wait(s, stok[sl])
                        a = s.activation(stg[:, sl * 512:(sl + 1) * 512], C.pC[0:128, 0:512], AF.Copy, scale=0.125)
                        sS.inc(a)
                    C.s(fse)
                    prev_cp[0] = sS.n
                    cp = sS.n
                    def fy(y, oc=oc, sl=sl, tt=tt, cp=cp):
                        sS.wait(y, cp)
                        C.ld(y, C.qhb[oc * 128:(oc + 1) * 128, tt * 512:(tt + 1) * 512],
                             stg[:, sl * 512:(sl + 1) * 512], sDS)
                        sDS.wait(y)
                        C.sY.inc(y.nop("rl", nofuse=True))
                    C.y(fy)
                    stok[sl] = C.sY.n
                else:
                    def fse(s, oc=oc, tt=tt, tok=tok):
                        sT.wait(s, tok)
                        a = s.activation(kn[:, oc * TP1 + tt * 512 + 1: oc * TP1 + tt * 512 + 513],
                                         C.pC[0:128, 0:512], AF.Copy)
                        sS.inc(a)
                    C.s(fse)
                    prev_cp[0] = sS.n
        AFA.pop()
        AB.pop()
        C.fence()

    vn_o = AB.alloc(NC * sv)
    vn = C.AB_t[:, vn_o:vn_o + NC * sv]
    def f(g):
        b = g.memset(vn[:, :].rearrange("p (k h c) -> p k h c", k=NC, h=16)[:, :, :, 64:65], 1.0)
        sG.inc(b)
    C.g(f)
    AB.push()
    AFA.push()
    xt = C.AF_t[:, AFA.alloc(4096):AFA.stack[-1]]
    xb_o = AB.alloc(4096)
    xb = C.AB_t[:, xb_o:xb_o + 4096]
    ws_o = AB.alloc(4096)
    ws = C.AB_t[:, ws_o:ws_o + 4096]
    o0 = 2048
    w2off = 2048
    for tt in range(NT):
        lt = C.ln_lite(tt, xt, xb)
        for sl in range(2):
            def fg(g, sl=sl):
                sT.wait(g)
                src = bass.AP(C.wqkvT.tensor, C.wqkvT.offset + o0 + sl * 512,
                              [[3072, 128], [128 * 3072, 8], [1, 512]])
                sDW.inc(g.dma_start(out=C.AB_t[:, ws_o:ws_o + 4096].rearrange("p (i o) -> p i o", i=8),
                            in_=src), 16)
                sDW.wait(g)
                sG.inc(g.nop("rl", nofuse=True))
            C.g(fg)
            wtok = sG.n
            if tt == 0:
                C.w1_bias(ws, 512, w2off + sl * 512,
                          C.bqkv[0:1, o0 + sl * 512:o0 + sl * 512 + 512], wait_tok=wtok)
            prev_ev = [sV.n]
            for tc in range(4):
                def pre(t, pv=prev_ev[0], wtok=wtok, lt=lt):
                    sG.wait(t, wtok)
                    sV.wait(t, tuple(map(max, pv, lt)))
                tok = C.proj_tok_tc(xb, lambda ic: ws[:, ic * 512:(ic + 1) * 512],
                                    tc, 512, w2off, o0=sl * 512, pre=pre)
                kt = tt * 4 + tc
                def fv(v, kt=kt, sl=sl, tok=tok, pe=prev_ev[0]):
                    sT.wait(v, tok)
                    sV.wait(v, pe)
                    dst = vn[:, kt * sv + 65 * 8 * sl: kt * sv + 65 * 8 * sl + 520]
                    dst = dst.rearrange("p (h c) -> p h c", c=65)[:, :, 0:64]
                    c = v.tensor_copy(dst, C.pC[0:128, 0:512].rearrange("p (h c) -> p h c", c=64))
                    sV.inc(c)
                C.v(fv)
                prev_ev[0] = sV.n
    AFA.pop()
    AB.pop()
    C.fence()

    AB.push()
    AFA.push()
    qs_o = AB.alloc(4096)
    qs = C.AB_t[:, qs_o:qs_o + 4096]
    ob_o = AB.alloc(4096)
    obuf = C.AB_t[:, ob_o:ob_o + 4096]
    odd_o = AB.alloc(512)
    odds = C.AB_t[:, odd_o:odd_o + 512]
    wp_o = AB.alloc(1024)
    wp = C.AB_t[:, wp_o:wp_o + 1024]
    meg_o = AFA.alloc(C.MEGW)
    mega = C.AF_t[:, meg_o:meg_o + C.MEGW]
    inv_o = AFA.alloc(512)
    invs = C.AF_t[:, inv_o:inv_o + 512]
    xo_o = AFA.alloc(1024)
    stf_o = AFA.alloc(1024)

    def f(g):
        C.ld(g, C.w2[0:1, 3072:3072 + 1024], C.mrows[1:2, 0:1024])
        C.ld(g, C.w2[1:2, 3072:3072 + 1024], C.bproj[0:1, :])
    C.g(f)
    C.fence()

    head_order = list(range(8, 16)) + list(range(0, 8))
    state = {"cur": None}

    def pre_head(slot, h):
        want = "mask" if h >= 8 else ("hope", h)
        if state["cur"] == want:
            return
        state["cur"] = want
        W = min(C.MEGW, 512 * (slot + 1) + 384)
        if want == "mask":
            tk = C.emit_mega_mask(mega, C.tbl_mf, qshift=0, W=W)
        else:
            off = C.LPAD - 384
            def fg(g, h=h, W=W):
                sT.wait(g)
                sDW.inc(g.dma_start(out=mega[:, 0:W],
                                    in_=C.tbl_hh[h * 128:(h + 1) * 128, off:off + W]), 16)
                sDW.wait(g)
                half = (W + 1) // 2
                for c0 in (0, half):
                    cw = min(half, W - c0)
                    sDW.inc(g.dma_start(out=mega[:, c0:c0 + cw],
                                        in_=C.tbl_hl[h * 128:(h + 1) * 128, off + c0:off + c0 + cw],
                                        accum_op=ALU.add), 16)
                    sDW.wait(g)
                sG.inc(g.nop("rl", nofuse=True))
            C.g(fg)
            tk = sG.n
        def ft(t, tk=tk):
            sG.wait(t, tk)
        C.t(ft)

    def q_load(slot):
        def f(y):
            sT.wait(y)
            for dc in range(8):
                C.ld(y, qs[:, dc * 512:(dc + 1) * 512],
                     C.qhb[dc * 128:(dc + 1) * 128, slot * 512:(slot + 1) * 512], sDX)
            sDX.wait(y)
            C.sY.inc(y.nop("rl", nofuse=True))
        C.y(f)
        tk = C.sY.n
        def ft(t, tk=tk):
            C.sY.wait(t, tk)
        C.t(ft)

    epst = {"tk2": (0, 0), "tk3": (0, 0), "tk4": (0, 0), "odd": (0, 0)}

    def epilogue(slot, h, av, tok_av):
        blk = h // 2
        def fs(s, tok_av=tok_av, p2=epst["tk2"]):
            C.sT2.wait(s, tok_av)
            sT.wait(s, p2)
            a = s.activation(C.rows[64:65, 0:512], av[64:65, :], AF.Ln)
            sS.inc(a)
        C.s(fs)
        tk1 = sS.n
        def ft(t, tk1=tk1, p3=epst["tk3"]):
            sS.wait(t, tk1)
            sS.wait(t, p3)
            m = t.matmul(C.pB[0:64, 0:512], C.cstf[64:65, CF_ONESB:CF_ONESB + 64],
                         C.rows[64:65, 0:512], start=True, stop=True)
            sT.inc(m)
        C.t(ft)
        tk2 = sT.n
        epst["tk2"] = tk2
        def fs2(s, tk2=tk2, p4=epst["tk4"]):
            sT.wait(s, tk2)
            sV.wait(s, p4)
            a = s.activation(invs[0:64, :], C.pB[0:64, 0:512], AF.Exp, scale=-1.0)
            sS.inc(a)
        C.s(fs2)
        tk3 = sS.n
        epst["tk3"] = tk3
        def fv(v, h=h, blk=blk, tk3=tk3, po=epst["odd"]):
            sS.wait(v, tk3)
            if h % 2 == 0:
                dst = obuf[0:64, blk * 512:(blk + 1) * 512]
            else:
                C.sY.wait(v, po)
                dst = odds[0:64, 0:512]
            c = v.tensor_mul(dst, av[0:64, :], invs[0:64, :])
            sV.inc(c)
        C.v(fv)
        tk4 = sV.n
        epst["tk4"] = tk4
        if h % 2 == 1:
            def fy(y, blk=blk, tk4=tk4):
                sV.wait(y, tk4)
                C.ld(y, obuf[64:128, blk * 512:(blk + 1) * 512], odds[0:64, 0:512], sDS)
                sDS.wait(y)
                C.sY.inc(y.nop("rl", nofuse=True))
            C.y(fy)
            epst["odd"] = C.sY.n
        return (sV, sV.n)

    def k_ap(slot, kt, h):
        dc = h // 2
        return kn[64 * (h % 2):64 * (h % 2) + 64, dc * TP1 + 1 + kt * 128: dc * TP1 + 1 + kt * 128 + 128]

    def q_ap(slot, h):
        dc = h // 2
        return qs[64 * (h % 2):64 * (h % 2) + 64, dc * 512:(dc + 1) * 512]

    def vn_ap(kt, h):
        base = kt * sv + 65 * h
        return vn[:, base:base + 65]

    def bias_ap(slot, kt, h):
        c0 = C.mega_c0(slot, kt)
        if h < 8:
            return mega[:, c0:c0 + 512]
        cnt = 4 * (slot + 1)
        if kt >= cnt - 4:
            return mega[:, c0:c0 + 512]
        return None

    cfg = dict(H=16, counts=lambda s: 4 * (s + 1), k_ap=k_ap, q_ap=q_ap,
               vn_ap=vn_ap, bias_ap=bias_ap, phantom=None, eshift=-40.0,
               epilogue=epilogue, pre_head=pre_head, head_order=head_order)

    for slot in range(NT):
        state["cur"] = None
        q_load(slot)
        C.emit_attention_slot(cfg, slot)
        C.out_proj_residual(slot, obuf, wp, wp_o, C.wpT, 3072,
                            C.AF_t[:, xo_o:xo_o + 1024], C.AF_t[:, stf_o:stf_o + 1024], last)
        C.fence()

    AFA.pop()
    AB.pop()
    AB.pop()


def out_proj_residual(C, slot, obuf, wp, wp_o, wsrc, w2off, xo, stf, last):
    sT, sS, sV, sDW, sDX, sDS = C.sT, C.sS, C.sV, C.sDW, C.sDX, C.sDS
    t0 = slot * 512
    prev = {"ev": sV.n, "st": [(0, 0), (0, 0)], "mm": (0, 0), "xv": [(0, 0), (0, 0)]}
    for oc in range(8):
        sl = oc % 2
        def fg(g, oc=oc):
            sT.wait(g, prev["mm"])
            src = bass.AP(wsrc.tensor, wsrc.offset + oc * 128,
                          [[8192, 128], [1024, 8], [1, 128]])
            sDW.inc(g.dma_start(out=wp[:, 0:1024].rearrange("p (j o) -> p j o", j=8),
                        in_=src), 16)
            sDW.wait(g)
            C.sG.inc(g.nop("rl", nofuse=True))
        C.g(fg)
        wtok = C.sG.n
        def fy(y, oc=oc, sl=sl):
            sV.wait(y, prev["xv"][sl])
            C.ld(y, xo[:, sl * 512:sl * 512 + 512], C.xres[oc * 128:(oc + 1) * 128, t0:t0 + 512], sDX)
            sDX.wait(y)
            C.sY.inc(y.nop("rl", nofuse=True))
        C.y(fy)
        xtok = C.sY.n
        def ft(t, oc=oc, wtok=wtok):
            C.sG.wait(t, wtok)
            sV.wait(t, prev["ev"])
            if oc == 0:
                C.sY.wait(t)
            t.matmul(C.pC[0:128, 0:512], C.w2[:, w2off + oc * 128: w2off + (oc + 1) * 128],
                     C.g2[:, 0:512], start=True, stop=False)
            m = None
            for j in range(8):
                m = t.matmul(C.pC[0:128, 0:512], wp[:, j * 128:(j + 1) * 128],
                             obuf[:, j * 512:(j + 1) * 512], start=False, stop=(j == 7))
            sT.inc(m)
        C.t(ft)
        mtok = sT.n
        prev["mm"] = mtok
        def fv(v, sl=sl, mtok=mtok, xtok=xtok, pst=prev["st"][sl]):
            sT.wait(v, mtok)
            C.sY.wait(v, xtok)
            C.sY.wait(v, pst)
            c = v.tensor_add(stf[:, sl * 512:sl * 512 + 512], xo[:, sl * 512:sl * 512 + 512],
                             C.pC[0:128, 0:512])
            sV.inc(c)
        C.v(fv)
        prev["ev"] = sV.n
        prev["xv"][sl] = sV.n
        vtok = sV.n
        def fy2(y, oc=oc, sl=sl, vtok=vtok):
            sV.wait(y, vtok)
            dst = (C.y_out if last else C.xres)
            C.ld(y, dst[oc * 128:(oc + 1) * 128, t0:t0 + 512], stf[:, sl * 512:sl * 512 + 512], sDS)
            sDS.wait(y)
            C.sY.inc(y.nop("rl", nofuse=True))
        C.y(fy2)
        prev["st"][sl] = C.sY.n


KB.sub_attn_b = sub_attn_b
KB.out_proj_residual = out_proj_residual
KB.load_wsec = load_wsec

def prep_inputs(ins, T=4096):
    f32 = np.float32
    d = {}
    d["wkT"] = np.ascontiguousarray(ins["L0_attn_a_W_k"].T.astype(f32))
    d["wvT"] = np.ascontiguousarray(ins["L0_attn_a_W_v"].T.astype(f32))
    wc = ins["L0_attn_a_W_c"].astype(f32)
    wc_re = np.zeros((128, 8 * 1024), dtype=f32)
    for j in range(8):
        for p in range(128):
            wc_re[p, j * 1024:(j + 1) * 1024] = wc[:, 64 * (2 * j + p // 64) + (p % 64)]
    d["wcT"] = wc_re
    d["aprm"] = np.concatenate([
        ins["L0_attn_a_leaky_key_beta"].ravel(), ins["L0_attn_a_key_scale"].ravel(),
        ins["L0_attn_a_v_coef"].ravel(), ins["L0_attn_a_val_scale"].ravel()]
    ).astype(f32).reshape(1, 64)
    d["wqkvT"] = np.ascontiguousarray(ins["L0_attn_b_w_qkv"].T.astype(f32))
    d["bqkv"] = ins["L0_attn_b_b_qkv"].astype(f32).reshape(1, 3072)
    wp = ins["L0_attn_b_w_proj"].astype(f32)
    wp_re = np.zeros((128, 8 * 1024), dtype=f32)
    for j in range(8):
        for p in range(128):
            wp_re[p, j * 1024:(j + 1) * 1024] = wp[:, 64 * (2 * j + p // 64) + (p % 64)]
    d["wpT"] = wp_re
    d["bproj"] = ins["L0_attn_b_b_proj"].astype(f32).reshape(1, 1024)
    d["wqf"] = np.ascontiguousarray(ins["L0_ffn_w_q"].astype(f32))
    d["wkf"] = np.ascontiguousarray(ins["L0_ffn_w_k"].astype(f32))
    d["wgf"] = np.ascontiguousarray(ins["L0_ffn_w_gate"].astype(f32))
    ve = ins["L0_ffn_value_experts"].astype(f32)
    d["wve"] = np.ascontiguousarray(np.transpose(ve, (1, 2, 0, 3)).reshape(8, 1024, 512))
    oe = ins["L0_ffn_output_experts"].astype(f32)
    woe = np.zeros((32, 128, 1024), dtype=f32)
    for e in range(8):
        for j in range(4):
            m = e * 4 + j
            woe[m, 0:64, :] = oe[e, 2 * j]
            woe[m, 64:128, :] = oe[e, 2 * j + 1]
    d["woe"] = woe
    ma, mf, hh, hl = build_tables(T)
    d["tbl_ma"] = ma
    d["tbl_mf"] = mf
    d["tbl_hh"] = hh
    d["tbl_hl"] = hl
    d["cstb_in"] = mk_cstb()
    p = np.arange(128)[:, None]
    j = np.arange(128)[None, :]
    tu = np.zeros((128, 256), dtype=f32)
    tu[:, 0:128] = np.maximum(j - p, 0).astype(f32)
    tu[:, 128:256] = (j >= p).astype(f32)
    d["tblu"] = tu
    mr = np.zeros((2, 4096), dtype=np.float32); mr[0, :] = 1.0
    d["mrows"] = mr
    d["cstf_in"] = mk_cstf()
    d["selc_in"] = mk_selectors()
    return d


_SUBS_ALL = ("attn_a", "ffn1", "attn_b", "ffn2")


def run_device(ins, T=4096, subs=_SUBS_ALL, n_b=None):
    x = np.asarray(ins["L0_attn_a_x"], dtype=np.float32)
    B = x.shape[0]
    n_b = n_b or B
    shared = prep_inputs(ins, T)
    kb = KB(T, subs)
    nc = kb.build()
    in_maps = []
    for c in range(8):
        b = c % n_b
        m = dict(shared)
        m["x_feat"] = np.ascontiguousarray(x[b].T)
        in_maps.append(m)
    res = run_bass_kernel_spmd(nc, in_maps, core_ids=list(range(8)))
    outs = []
    for b in range(B):
        outs.append(np.ascontiguousarray(res.results[b]["y"].T))
    return np.stack(outs, axis=0)


def kernel(**inputs):
    return run_device(inputs, T=4096, subs=_SUBS_ALL)

def sub_ffn(C, last):
    T, NT, NC, TP1 = C.T, C.NT, C.NC, C.TP1
    sv = 65 * 8
    AB, AFA = C.AB, C.AFA
    sT, sS, sV, sG, sDW, sDX, sDS = C.sT, C.sS, C.sV, C.sG, C.sDW, C.sDX, C.sDS

    AB.push()
    knf_o = AB.alloc(4 * TP1)
    knf = C.AB_t[:, knf_o:knf_o + 4 * TP1]
    vnf_o = AB.alloc(NC * sv)
    vnf = C.AB_t[:, vnf_o:vnf_o + NC * sv]
    gmT_o = AB.alloc(NT * 512)
    def f(g):
        g.memset(knf[:, :].rearrange("p (c t) -> p c t", c=4)[:, :, 0:1], 0.0)
        a = g.memset(vnf[:, :].rearrange("p (k h c) -> p k h c", k=NC, h=8)[:, :, :, 64:65], 1.0)
        sG.inc(a)
    C.g(f)
    C.fence()

    AB.push()
    AFA.push()
    xt = C.AF_t[:, AFA.alloc(4096):AFA.stack[-1]]
    gt_o = AFA.alloc(640)
    mt_o = AFA.alloc(512)
    tmp_o = AFA.alloc(512)
    xb_o = AB.alloc(4096)
    xb = C.AB_t[:, xb_o:xb_o + 4096]
    wq_o = AB.alloc(4096)
    wk_o = AB.alloc(4096)
    wg_o = AB.alloc(512)
    wv_o = AB.alloc(4096)
    stg_o = AB.alloc(1024)
    stg = C.AB_t[:, stg_o:stg_o + 1024]

    load_wsec(C, wq_o, C.wqf, 512, 512)
    load_wsec(C, wk_o, C.wkf, 512, 512)
    load_wsec(C, wg_o, C.wgf, 64, 64)
    C.w1_bias(C.AB_t[:, wq_o:wq_o + 4096], 512, 0, None)
    C.w1_bias(C.AB_t[:, wk_o:wk_o + 4096], 512, 512, None)
    C.w1_bias(C.AB_t[:, wg_o:wg_o + 512], 64, 1024, None)

    for tt in range(NT):
        lt = C.ln_full(tt, xt, xb, store=False)
        prev_cp = [sS.n]
        for oc in range(4):
            def pre(t, pc=prev_cp[0], lt=lt):
                sS.wait(t, pc)
                sV.wait(t, lt)
            tok = C.proj_feat_oc(xb, lambda ic, oc=oc: C.AB_t[:, wq_o + ic * 512 + oc * 128: wq_o + ic * 512 + (oc + 1) * 128],
                                 oc, 0, pre=pre)
            sl = oc % 2
            def fse(s, sl=sl, tok=tok):
                sT.wait(s, tok)
                C.sY.wait(s)
                a = s.activation(stg[:, sl * 512:(sl + 1) * 512], C.pC[0:128, 0:512], AF.Copy, scale=0.125)
                sS.inc(a)
            C.s(fse)
            prev_cp[0] = sS.n
            cp = sS.n
            def fy(y, oc=oc, sl=sl, tt=tt, cp=cp):
                sS.wait(y, cp)
                C.ld(y, C.qhb[oc * 128:(oc + 1) * 128, tt * 512:(tt + 1) * 512],
                     stg[:, sl * 512:(sl + 1) * 512], sDS)
                sDS.wait(y)
                C.sY.inc(y.nop("rl", nofuse=True))
            C.y(fy)
        for oc in range(4):
            def pre(t, pc=prev_cp[0]):
                sS.wait(t, pc)
            tok = C.proj_feat_oc(xb, lambda ic, oc=oc: C.AB_t[:, wk_o + ic * 512 + oc * 128: wk_o + ic * 512 + (oc + 1) * 128],
                                 oc, 512, pre=pre)
            def fse(s, oc=oc, tt=tt, tok=tok):
                sT.wait(s, tok)
                a = s.activation(knf[:, oc * TP1 + tt * 512 + 1: oc * TP1 + tt * 512 + 513],
                                 C.pC[0:128, 0:512], AF.Copy)
                sS.inc(a)
            C.s(fse)
            prev_cp[0] = sS.n
        gm_toks = {}
        for tc in range(4):
            def pre(t, pc=prev_cp[0]):
                sS.wait(t, pc)
            tok = C.proj_tok_tc(xb, lambda ic: C.AB_t[:, wg_o + ic * 64: wg_o + (ic + 1) * 64],
                                tc, 64, 1024, pre=pre)
            gbase = gt_o + tc * 160
            def fse(s, tok=tok, gbase=gbase):
                sT.wait(s, tok)
                a = s.activation(C.AF_t[:, gbase:gbase + 64], C.pC[0:128, 0:64], AF.Sigmoid)
                sS.inc(a)
            C.s(fse)
            prev_cp[0] = sS.n
            stok = sS.n
            def fv(v, gbase=gbase, stok=stok):
                sS.wait(v, stok)
                sV.wait(v)
                g3 = C.AF_t[:, gbase:gbase + 64].rearrange("p (h e) -> p h e", h=8)
                a = v.tensor_reduce(C.AF_t[:, gbase + 128:gbase + 136], g3,
                                    mybir.AxisListType.X, ALU.max)
                tk = sV.inc(a)
                sV.wait(v, tk)
                mb = bass.AP(C.AF_t.ap().tensor, C.AF_t.ap().offset + gbase + 128,
                             [C.AF_t.ap().ap[0], [1, 8], [0, 8]])
                a = v.tensor_tensor(C.AF_t[:, gbase + 64:gbase + 128],
                                    C.AF_t[:, gbase:gbase + 64], mb, ALU.is_equal)
                tk = sV.inc(a)
                sV.wait(v, tk)
                c = v.tensor_mul(C.AF_t[:, gbase + 64:gbase + 128],
                                 C.AF_t[:, gbase + 64:gbase + 128], C.AF_t[:, gbase:gbase + 64])
                gm_toks[gbase] = sV.inc(c)
            C.v(fv)
        def fgf(t, pc=prev_cp[0], lt=lt):
            sS.wait(t, pc)
            sV.wait(t, lt)
            t.matmul(C.pC[0:64, 0:512], C.w2[:, 1024:1088], C.g2[:, 0:512], start=True, stop=False)
            m = None
            for ic in range(8):
                m = t.matmul(C.pC[0:64, 0:512], C.AB_t[:, wg_o + ic * 64: wg_o + (ic + 1) * 64],
                             xb[:, ic * 512:(ic + 1) * 512], start=False, stop=(ic == 7))
            sT.inc(m)
        C.t(fgf)
        tkgf = sT.n
        def fse(s, tkgf=tkgf):
            sT.wait(s, tkgf)
            a = s.activation(C.AF_t[0:64, tmp_o:tmp_o + 512], C.pC[0:64, 0:512], AF.Sigmoid)
            sS.inc(a)
        C.s(fse)
        tk_gf = sS.n
        prev_cp[0] = sS.n
        def ftr(t, tk_gf=tk_gf, gm_toks=gm_toks):
            for tc in range(4):
                sV.wait(t, gm_toks[gt_o + tc * 160])
            sS.wait(t, tk_gf)
            m = None
            for tc in range(4):
                m = t.transpose(C.pC[0:8, 128 * tc:128 * tc + 128],
                                C.AF_t[:, gt_o + tc * 160 + 128: gt_o + tc * 160 + 136],
                                C.cstf[:, CF_ID:CF_ID + 128])
            sT.inc(m)
        C.t(ftr)
        tktr = sT.n
        def fmt(v, tktr=tktr):
            sT.wait(v, tktr)
            c = v.tensor_copy(C.AF_t[0:8, mt_o:mt_o + 512], C.pC[0:8, 0:512])
            sV.inc(c)
        C.v(fmt)
        tkmt = sV.n
        def fmb(t, tkmt=tkmt):
            sV.wait(t, tkmt)
            m = t.matmul(C.pC[0:64, 0:512], C.cstf[0:8, CF_E8:CF_E8 + 64],
                         C.AF_t[0:8, mt_o:mt_o + 512], start=True, stop=True)
            sT.inc(m)
        C.t(fmb)
        tkmb = sT.n
        def fgm(v, tkmb=tkmb, tt=tt):
            sT.wait(v, tkmb)
            a = v.tensor_tensor(C.AF_t[0:64, mt_o:mt_o + 512],
                                C.AF_t[0:64, tmp_o:tmp_o + 512], C.pC[0:64, 0:512], ALU.is_equal)
            tk = sV.inc(a)
            sV.wait(v, tk)
            c = v.tensor_mul(C.AB_t[0:64, gmT_o + tt * 512: gmT_o + (tt + 1) * 512],
                             C.AF_t[0:64, mt_o:mt_o + 512], C.AF_t[0:64, tmp_o:tmp_o + 512])
            sV.inc(c)
        C.v(fgm)
        for hh in range(8):
            def fg(g, hh=hh):
                sT.wait(g)
                src = bass.AP(C.wve.tensor, C.wve.offset + hh * 1024 * 512,
                              [[512, 128], [128 * 512, 8], [1, 512]])
                sDW.inc(g.dma_start(out=C.AB_t[:, wv_o:wv_o + 4096].rearrange("p (i o) -> p i o", i=8),
                        in_=src), 16)
                sDW.wait(g)
                sG.inc(g.nop("rl", nofuse=True))
            C.g(fg)
            C.w1_bias(C.AB_t[:, wv_o:wv_o + 4096], 512, 2048, None)
            for tc in range(4):
                def pre(t, pc=prev_cp[0]):
                    sS.wait(t, pc)
                    sV.wait(t)
                tok = C.proj_tok_tc(xb, lambda ic: C.AB_t[:, wv_o + ic * 512:wv_o + (ic + 1) * 512],
                                    tc, 512, 2048, pre=pre)
                kt = tt * 4 + tc
                gbase = gt_o + tc * 160
                def fcv(v, hh=hh, kt=kt, tok=tok, gbase=gbase):
                    sT.wait(v, tok)
                    sV.wait(v)
                    t3 = C.AF_t[:, tmp_o:tmp_o + 512]
                    gmb = bass.AP(C.AF_t.ap().tensor, C.AF_t.ap().offset + gbase + 64 + 8 * hh,
                                  [C.AF_t.ap().ap[0], [1, 8], [0, 64]])
                    a = v.tensor_tensor(t3, C.pC[0:128, 0:512], gmb, ALU.mult)
                    tk = sV.inc(a)
                    sV.wait(v, tk)
                    a = v.tensor_add(t3[:, 0:256], t3[:, 0:256], t3[:, 256:512])
                    tk = sV.inc(a)
                    sV.wait(v, tk)
                    a = v.tensor_add(t3[:, 0:128], t3[:, 0:128], t3[:, 128:256])
                    tk = sV.inc(a)
                    sV.wait(v, tk)
                    c = v.tensor_add(vnf[:, kt * sv + 65 * hh: kt * sv + 65 * hh + 64],
                                     t3[:, 0:64], t3[:, 64:128])
                    sV.inc(c)
                C.v(fcv)
    AFA.pop()
    AB.pop()
    C.fence()

    AB.push()
    AFA.push()
    qs_o = AB.alloc(2048)
    qs = C.AB_t[:, qs_o:qs_o + 2048]
    ob_o = AB.alloc(2048)
    obuf = C.AB_t[:, ob_o:ob_o + 2048]
    odd_o = AB.alloc(512)
    odds = C.AB_t[:, odd_o:odd_o + 512]
    selc_o = AB.alloc(4096)
    selc = C.AB_t[:, selc_o:selc_o + 4096]
    ub_o = AB.alloc(512)
    ub = C.AB_t[:, ub_o:ub_o + 512]
    oe_o = AB.alloc(32 * 512)
    woe_o = AB.alloc(4096)
    meg_o = AFA.alloc(C.MEGW)
    mega = C.AF_t[:, meg_o:meg_o + C.MEGW]
    srow_o = AFA.alloc(512)
    srow = C.AF_t[:, srow_o:srow_o + 512]
    xo_o = AFA.alloc(1024)
    stf_o = AFA.alloc(1024)

    def f(g):
        C.ld(g, C.AB_t[0:64, selc_o:selc_o + 4096], C.selc_in[:, :])
    C.g(f)
    C.emit_mega_mask(mega, C.tbl_mf, qshift=0)
    C.fence()

    epst = {"gmm": (0, 0)}

    def epilogue(slot, h, av, tok_av):
        blk = h // 2
        def fs(s, h=h, blk=blk, tok_av=tok_av, pg=epst["gmm"]):
            C.sT2.wait(s, tok_av)
            sT.wait(s, pg)
            if h % 2 == 0:
                dst = obuf[0:64, blk * 512:(blk + 1) * 512]
            else:
                C.sY.wait(s)
                dst = odds[0:64, 0:512]
            s.activation(dst, av[0:64, :], AF.Copy)
            a = s.activation(srow[64:65, 0:512], av[64:65, :], AF.Copy)
            sS.inc(a)
        C.s(fs)
        tk1 = sS.n
        def ft(t, h=h, tk1=tk1):
            sS.wait(t, tk1)
            m = t.matmul(C.pB[0:8, 0:512], C.cstf[64:65, CF_ID + 64 - h:CF_ID + 72 - h],
                         srow[64:65, 0:512], start=(h == 0), stop=(h == 7))
            sT.inc(m)
        C.t(ft)
        epst["gmm"] = sT.n
        if h % 2 == 1:
            def fy(y, blk=blk, tk1=tk1):
                sS.wait(y, tk1)
                C.ld(y, obuf[64:128, blk * 512:(blk + 1) * 512], odds[0:64, 0:512], sDS)
                sDS.wait(y)
                C.sY.inc(y.nop("rl", nofuse=True))
            C.y(fy)
        return (sS, tk1)

    def q_load(slot):
        def f(y):
            sT.wait(y)
            for dc in range(4):
                C.ld(y, qs[:, dc * 512:(dc + 1) * 512],
                     C.qhb[dc * 128:(dc + 1) * 128, slot * 512:(slot + 1) * 512], sDX)
            sDX.wait(y)
            C.sY.inc(y.nop("rl", nofuse=True))
        C.y(f)
        tkq = C.sY.n
        def ft(t, tkq=tkq):
            C.sY.wait(t, tkq)
        C.t(ft)

    def k_ap(slot, kt, h):
        dc = h // 2
        return knf[64 * (h % 2):64 * (h % 2) + 64, dc * TP1 + 1 + kt * 128: dc * TP1 + 1 + kt * 128 + 128]

    def q_ap(slot, h):
        dc = h // 2
        return qs[64 * (h % 2):64 * (h % 2) + 64, dc * 512:(dc + 1) * 512]

    def vn_ap(kt, h):
        base = kt * sv + 65 * h
        return vnf[:, base:base + 65]

    def bias_ap(slot, kt, h):
        cnt = 4 * (slot + 1)
        if kt >= cnt - 4:
            c0 = C.mega_c0(slot, kt)
            return mega[:, c0:c0 + 512]
        return None

    cfg = dict(H=8, counts=lambda s: 4 * (s + 1), k_ap=k_ap, q_ap=q_ap,
               vn_ap=vn_ap, bias_ap=bias_ap, phantom=None, eshift=0.0,
               epilogue=epilogue)

    for slot in range(NT):
        q_load(slot)
        C.emit_attention_slot(cfg, slot)
        def fr8(s):
            sT.wait(s)
            a = s.activation(C.rows[0:8, 0:512], C.pB[0:8, 0:512], AF.Ln)
            tk = sS.inc(a)
            sS.wait(s, tk)
            a = s.activation(C.rows[0:8, 512:1024], C.rows[0:8, 0:512], AF.Exp, scale=-1.0)
            sS.inc(a)
        C.s(fr8)
        tkr8 = sS.n
        def frb(t, tkr8=tkr8):
            sS.wait(t, tkr8)
            sV.wait(t)
            m = t.matmul(C.pC[0:64, 0:512], C.cstf[0:8, CF_E8:CF_E8 + 64],
                         C.rows[0:8, 512:1024], start=True, stop=True)
            sT.inc(m)
        C.t(frb)
        tkrb = sT.n
        def fu(v, slot=slot, tkrb=tkrb):
            sT.wait(v, tkrb)
            c = v.tensor_mul(ub[0:64, 0:512], C.AB_t[0:64, gmT_o + slot * 512: gmT_o + (slot + 1) * 512],
                             C.pC[0:64, 0:512])
            sV.inc(c)
        C.v(fu)
        tku = sV.n
        prev_oe = [tku]
        for mm_i in range(32):
            def fub(t, mm_i=mm_i, po=prev_oe[0]):
                sV.wait(t, po)
                m = t.matmul(C.pC[0:128, 0:512], C.AB_t[0:64, selc_o + 128 * mm_i: selc_o + 128 * mm_i + 128],
                             ub[0:64, 0:512], start=True, stop=True)
                sT.inc(m)
            C.t(fub)
            tub = sT.n
            jj = mm_i % 4
            def foe(v, mm_i=mm_i, jj=jj, tub=tub):
                sT.wait(v, tub)
                if mm_i == 0:
                    C.sY.wait(v)
                c = v.tensor_mul(C.AB_t[:, oe_o + mm_i * 512: oe_o + (mm_i + 1) * 512],
                                 obuf[:, jj * 512:(jj + 1) * 512], C.pC[0:128, 0:512])
                sV.inc(c)
            C.v(foe)
            prev_oe[0] = sV.n
        tkoe = sV.n
        t0 = slot * 512
        prevv = {"mm": (0, 0), "ev": tkoe, "st": [(0, 0), (0, 0)], "xv": [(0, 0), (0, 0)]}
        for oc in range(8):
            sl = oc % 2
            def fgw(g, oc=oc):
                sT.wait(g, prevv["mm"])
                src = bass.AP(C.woe.tensor, C.woe.offset + oc * 128,
                              [[1024, 128], [128 * 1024, 32], [1, 128]])
                sDW.inc(g.dma_start(out=C.AB_t[:, woe_o:woe_o + 4096].rearrange("p (m o) -> p m o", m=32),
                        in_=src), 16)
                sDW.wait(g)
                sG.inc(g.nop("rl", nofuse=True))
            C.g(fgw)
            wtok = sG.n
            def fy(y, oc=oc, sl=sl):
                sV.wait(y, prevv["xv"][sl])
                C.ld(y, C.AF_t[:, xo_o + sl * 512: xo_o + sl * 512 + 512],
                     C.xres[oc * 128:(oc + 1) * 128, t0:t0 + 512], sDX)
                sDX.wait(y)
                C.sY.inc(y.nop("rl", nofuse=True))
            C.y(fy)
            xtok = C.sY.n
            def ft(t, oc=oc, wtok=wtok):
                sG.wait(t, wtok)
                sV.wait(t, prevv["ev"])
                m = None
                for mm_i in range(32):
                    m = t.matmul(C.pC[0:128, 0:512], C.AB_t[:, woe_o + mm_i * 128: woe_o + (mm_i + 1) * 128],
                                 C.AB_t[:, oe_o + mm_i * 512: oe_o + (mm_i + 1) * 512],
                                 start=(mm_i == 0), stop=(mm_i == 31))
                sT.inc(m)
            C.t(ft)
            prevv["mm"] = sT.n
            mtok = sT.n
            def fv(v, sl=sl, mtok=mtok, pst=prevv["st"][sl], xtok=xtok):
                sT.wait(v, mtok)
                C.sY.wait(v, xtok)
                C.sY.wait(v, pst)
                c = v.tensor_add(C.AF_t[:, stf_o + sl * 512: stf_o + sl * 512 + 512],
                                 C.AF_t[:, xo_o + sl * 512: xo_o + sl * 512 + 512], C.pC[0:128, 0:512])
                sV.inc(c)
            C.v(fv)
            prevv["ev"] = sV.n
            prevv["xv"][sl] = sV.n
            vtok = sV.n
            def fy2(y, oc=oc, sl=sl, vtok=vtok):
                sV.wait(y, vtok)
                dst = (C.y_out if last else C.xres)
                C.ld(y, dst[oc * 128:(oc + 1) * 128, t0:t0 + 512],
                     C.AF_t[:, stf_o + sl * 512: stf_o + sl * 512 + 512], sDS)
                sDS.wait(y)
                C.sY.inc(y.nop("rl", nofuse=True))
            C.y(fy2)
            prevv["st"][sl] = C.sY.n
        C.fence()

    AFA.pop()
    AB.pop()
    AB.pop()


KB.sub_ffn = sub_ffn

def sub_attn_a(C, last):
    T, NT, NC, TP1 = C.T, C.NT, C.NC, C.TP1
    sv = 65 * 16
    AB, AFA = C.AB, C.AFA
    sT, sS, sV, sG, sY = C.sT, C.sS, C.sV, C.sG, C.sY
    sDW, sDX, sDS = C.sDW, C.sDX, C.sDS
    AFT = C.AF_t

    AB.push()
    AFA.push()
    pr_o = AFA.alloc(64)
    dr_o = AFA.alloc(64)
    sc_o = AFA.alloc(64)
    ap_o = AFA.alloc(64)
    v16_o = AFA.alloc(1)

    def f(g):
        C.ld(g, AFT[0:1, pr_o:pr_o + 64], C.aprm[0:1, :])
        sDW.wait(g)
        sG.inc(g.nop("rl", nofuse=True))
    C.g(f)
    tok_pr = sG.n

    def f(s, tok_pr=tok_pr):
        sG.wait(s, tok_pr)
        a = s.activation(AFT[0:1, sc_o:sc_o + 16], AFT[0:1, pr_o:pr_o + 16], AF.Abs)
        tk = sS.inc(a)
        sS.wait(s, tk)
        a = s.activation(AFT[0:1, dr_o:dr_o + 16], AFT[0:1, sc_o:sc_o + 16], AF.Copy, scale=-10.0)
        sS.inc(a)
        a = s.activation(AFT[0:1, sc_o + 16:sc_o + 32], AFT[0:1, pr_o + 16:pr_o + 32], AF.Copy, scale=10.0)
        sS.inc(a)
        a = s.activation(AFT[0:1, dr_o + 32:dr_o + 48], AFT[0:1, pr_o + 32:pr_o + 48], AF.Copy)
        sS.inc(a)
        a = s.activation(AFT[0:1, sc_o + 48:sc_o + 64], AFT[0:1, pr_o + 48:pr_o + 64], AF.Exp, scale=10.0)
        sS.inc(a)
    C.s(f)
    tk_s1 = sS.n

    def f(v, tk_s1=tk_s1):
        sS.wait(v, tk_s1)
        ksm = bass.AP(C.cstf.ap().tensor, C.cstf.ap().offset + CF_CONST + 3,
                      [[C.cstf.ap().ap[0][0], 1], [0, 16]])
        a = v.tensor_tensor(AFT[0:1, sc_o + 32:sc_o + 48], AFT[0:1, sc_o + 16:sc_o + 32],
                            ksm, ALU.min)
        sV.inc(a)
        one = bass.AP(C.cstf.ap().tensor, C.cstf.ap().offset + CF_CONST + 4,
                      [[C.cstf.ap().ap[0][0], 1], [0, 16]])
        a = v.tensor_tensor(AFT[0:1, dr_o + 48:dr_o + 64], one,
                            AFT[0:1, pr_o + 32:pr_o + 48], ALU.subtract)
        sV.inc(a)
    C.v(f)
    tk_v1 = sV.n

    def f(s, tk_v1=tk_v1):
        sV.wait(s, tk_v1)
        a = s.activation(AFT[0:1, dr_o + 16:dr_o + 32], AFT[0:1, sc_o + 32:sc_o + 48], AF.Exp)
        sS.inc(a)
    C.s(f)
    tk_s2 = sS.n

    def fmm(t, tk_s2=tk_s2, tk_v1=tk_v1):
        sS.wait(t, tk_s2)
        sV.wait(t, tk_v1)
        m = t.matmul(C.pB[0:128, 0:64], C.cstf[0:1, CF_ONESB:CF_ONESB + 128],
                     AFT[0:1, dr_o:dr_o + 64], start=True, stop=True)
        sT.inc(m)
        m = t.matmul(C.pB[0:16, 64:65], AFT[0:1, sc_o + 48:sc_o + 64],
                     C.cstf[0:1, CF_CONST + 4:CF_CONST + 5], start=True, stop=True)
        sT.inc(m)
    C.t(fmm)
    tkmm = sT.n

    def fev(s, tkmm=tkmm):
        sT.wait(s, tkmm)
        a = s.activation(AFT[:, ap_o:ap_o + 64], C.pB[0:128, 0:64], AF.Copy)
        sS.inc(a)
        a = s.activation(AFT[0:16, v16_o:v16_o + 1], C.pB[0:16, 64:65], AF.Copy)
        sS.inc(a)
    C.s(fev)
    tok_ap = sS.n

    AB.push()
    AFA.push()
    du_o = AB.alloc(128)
    ut_o = AB.alloc(128)
    dcb_o = AB.alloc(2048)
    car_o = AB.alloc(1024)
    kn_o = AB.alloc(8 * TP1)
    kn = C.AB_t[:, kn_o:kn_o + 8 * TP1]
    wsk_o = AB.alloc(8192)
    wsk = C.AB_t[:, wsk_o:wsk_o + 8192]
    xb_o = AB.alloc(4096)
    xb = C.AB_t[:, xb_o:xb_o + 4096]
    kt_o = AB.alloc(1024)
    ktok = C.AB_t[:, kt_o:kt_o + 1024]
    cr2_o = AB.alloc(1024)
    xt = AFT[:, AFA.alloc(4096):AFA.stack[-1]]
    kns_o = AFA.alloc(1024)
    kns = AFT[:, kns_o:kns_o + 1024]
    lt1_o = AFA.alloc(128)
    lt2_o = AFA.alloc(128)
    sq_o = AFA.alloc(1024)
    nr_o = AFA.alloc(96)

    def f(g):
        a = g.memset(kn[:, :].rearrange("p (c t) -> p c t", c=8)[:, :, 0:1], 0.0)
        sG.inc(a)
    C.g(f)

    def f(g):
        C.ld(g, C.AB_t[:, du_o:du_o + 128], C.tblu[:, 0:128])
        C.ld(g, C.AB_t[:, ut_o:ut_o + 128], C.tblu[:, 128:256])
        sDW.wait(g)
        sG.inc(g.nop("rl", nofuse=True))
    C.g(f)
    tk4 = sG.n

    for h in range(16):
        def fv(v, h=h, tk4=tk4):
            sG.wait(v, tk4)
            sS.wait(v, tok_ap)
            sV.wait(v)
            sS.wait(v)
            nb = bass.AP(AFT.ap().tensor, AFT.ap().offset + ap_o + h,
                         [AFT.ap().ap[0], [0, 128]])
            a = v.tensor_tensor(AFT[:, lt1_o:lt1_o + 128], C.AB_t[:, du_o:du_o + 128],
                                nb, ALU.mult)
            sV.inc(a)
        C.v(fv)
        tkm = sV.n
        def fs(s, h=h, tkm=tkm):
            sV.wait(s, tkm)
            a = s.activation(AFT[:, lt2_o:lt2_o + 128], AFT[:, lt1_o:lt1_o + 128], AF.Exp)
            sS.inc(a)
        C.s(fs)
        tke = sS.n
        def fv2(v, h=h, tke=tke):
            sS.wait(v, tke)
            a = v.tensor_mul(C.pt[:, 128 * h:128 * h + 128], AFT[:, lt2_o:lt2_o + 128],
                             C.AB_t[:, ut_o:ut_o + 128])
            tk = sV.inc(a)
            sV.wait(v, tk)
            nb0 = bass.AP(AFT.ap().tensor, AFT.ap().offset + ap_o + h,
                          [[AFT.ap().ap[0][0], 1], [0, 128]])
            a = v.tensor_tensor(AFT[0:1, lt1_o:lt1_o + 128], C.cstb[0:1, CB_P1ROW:CB_P1ROW + 128],
                                nb0, ALU.mult)
            sV.inc(a)
        C.v(fv2)
        tkd = sV.n
        def fs2(s, h=h, tkd=tkd):
            sV.wait(s, tkd)
            a = s.activation(C.AB_t[0:1, dcb_o + 128 * h:dcb_o + 128 * h + 128],
                             AFT[0:1, lt1_o:lt1_o + 128], AF.Exp)
            sS.inc(a)
        C.s(fs2)

    wtokK = load_wsec(C, wsk_o, C.wkT, 1024, 1024)
    C.w1_bias(wsk, 1024, 0, None)
    C.fence()

    st = {"evac": (0, 0), "knmul": (0, 0), "scan": (0, 0), "knev": (0, 0),
          "carev": (0, 0)}
    for tt in range(NT):
        lt = C.ln_full(tt, xt, xb, store=True)
        for tc in range(4):
            c = 4 * tt + tc
            for half in range(2):
                def pre(t, pc=st["evac"], lt=lt, wtokK=wtokK):
                    sG.wait(t, wtokK)
                    sS.wait(t, pc)
                    sV.wait(t, lt)
                tok = C.proj_tok_tc(xb, lambda ic, half=half: wsk[:, ic * 1024 + half * 512: ic * 1024 + half * 512 + 512],
                                    tc, 512, 0, o0=half * 512, pre=pre)
                def fse(s, half=half, tok=tok, sc=st["scan"]):
                    sT.wait(s, tok)
                    sT.wait(s, sc)
                    a = s.activation(ktok[:, half * 512:(half + 1) * 512], C.pC[0:128, 0:512], AF.Copy)
                    sS.inc(a)
                C.s(fse)
                st["evac"] = sS.n

            ktokTok = sS.n
            def ftsc(t, c=c, ktokTok=ktokTok, pk=st["knmul"]):
                sS.wait(t, ktokTok)
                sV.wait(t, pk)
                m = None
                for h in range(16):
                    m = t.matmul(C.pL[:, 64 * h:64 * h + 64], C.pt[:, 128 * h:128 * h + 128],
                                 ktok[:, 64 * h:64 * h + 64], start=True, stop=(c == 0))
                    if c > 0:
                        m = t.matmul(C.pL[:, 64 * h:64 * h + 64],
                                     C.AB_t[0:1, dcb_o + 128 * h:dcb_o + 128 * h + 128],
                                     C.AB_t[0:1, car_o + 64 * h:car_o + 64 * h + 64],
                                     start=False, stop=True)
                sT.inc(m)
            C.t(ftsc)
            st["scan"] = sT.n
            scanTok = sT.n
            if c < NC - 1:
                def fcr(s, scanTok=scanTok, pce=st["carev"]):
                    sT.wait(s, scanTok)
                    sS.wait(s, pce)
                    s.activation(C.AB_t[64:128, cr2_o:cr2_o + 512], C.pL[64:128, 0:512], AF.Copy)
                    a = s.activation(C.AB_t[64:128, cr2_o + 512:cr2_o + 1024],
                                     C.pL[64:128, 512:1024], AF.Copy)
                    tk = sS.inc(a)
                    sS.wait(s, tk)
                    C.sDA.inc(s.dma_start(out=C.AB_t[0:1, car_o:car_o + 1024],
                                          in_=C.AB_t[127:128, cr2_o:cr2_o + 1024]), 16)
                    C.sDA.wait(s)
                    sS.inc(s.nop("rl", nofuse=True))
                C.s(fcr)
                st["carev"] = sS.n
            def fsq(s, scanTok=scanTok):
                sT.wait(s, scanTok)
                a = s.activation(AFT[:, sq_o:sq_o + 512], C.pL[:, 0:512], AF.Square)
                sS.inc(a)
                a = s.activation(AFT[:, sq_o + 512:sq_o + 1024], C.pL[:, 512:1024], AF.Square)
                sS.inc(a)
            C.s(fsq)
            tksq = sS.n
            def fnr(v, tksq=tksq):
                sS.wait(v, tksq)
                sV.wait(v)
                g3 = AFT[:, sq_o:sq_o + 1024].rearrange("p (h d) -> p h d", h=16)
                a = v.tensor_reduce(AFT[:, nr_o:nr_o + 16], g3, mybir.AxisListType.X, ALU.add)
                sV.inc(a)
            C.v(fnr)
            tkr = sV.n
            def fns(s, tkr=tkr):
                sV.wait(s, tkr)
                a = s.activation(AFT[:, nr_o + 16:nr_o + 32], AFT[:, nr_o:nr_o + 16], AF.Sqrt)
                sS.inc(a)
            C.s(fns)
            tkn1 = sS.n
            def fne(v, tkn1=tkn1):
                sS.wait(v, tkn1)
                eps = bass.AP(C.cstf.ap().tensor, C.cstf.ap().offset + CF_CONST + 1,
                              [C.cstf.ap().ap[0], [0, 16]])
                a = v.tensor_tensor(AFT[:, nr_o + 32:nr_o + 48], AFT[:, nr_o + 16:nr_o + 32],
                                    eps, ALU.add)
                sV.inc(a)
            C.v(fne)
            tkn2 = sV.n
            def fnl(s, tkn2=tkn2):
                sV.wait(s, tkn2)
                a = s.activation(AFT[:, nr_o + 48:nr_o + 64], AFT[:, nr_o + 32:nr_o + 48], AF.Ln)
                tk = sS.inc(a)
                sS.wait(s, tk)
                a = s.activation(AFT[:, nr_o + 64:nr_o + 80], AFT[:, nr_o + 48:nr_o + 64], AF.Exp, scale=-1.0)
                sS.inc(a)
            C.s(fnl)
            tkn3 = sS.n
            def fni(v, tkn3=tkn3, c=c):
                sS.wait(v, tkn3)
                a = v.tensor_mul(AFT[:, nr_o + 80:nr_o + 96], AFT[:, nr_o + 64:nr_o + 80],
                                 AFT[:, ap_o + 16:ap_o + 32])
                tk = sV.inc(a)
                sV.wait(v, tk)
                lo = bass.AP(AFT.ap().tensor, AFT.ap().offset + nr_o + 80,
                             [AFT.ap().ap[0], [1, 8], [0, 64]])
                a = v.tensor_tensor(kns[:, 0:512], C.pL[:, 0:512], lo, ALU.mult)
                tk = sV.inc(a)
                sV.wait(v, tk)
                hi = bass.AP(AFT.ap().tensor, AFT.ap().offset + nr_o + 88,
                             [AFT.ap().ap[0], [1, 8], [0, 64]])
                a = v.tensor_tensor(kns[:, 512:1024], C.pL[:, 512:1024], hi, ALU.mult)
                sV.inc(a)
            C.v(fni)
            st["knmul"] = sV.n
            tkkm = sV.n
            def ftr(t, tkkm=tkkm, pkv=st["knev"]):
                sV.wait(t, tkkm)
                sS.wait(t, pkv)
                m = None
                for dc in range(8):
                    m = t.transpose(C.pA[:, 128 * dc:128 * dc + 128], kns[:, 128 * dc:128 * dc + 128],
                                    C.cstf[:, CF_ID:CF_ID + 128])
                sT.inc(m)
            C.t(ftr)
            tktr = sT.n
            def fke(s, c=c, tktr=tktr):
                sT.wait(s, tktr)
                a = None
                for dc in range(8):
                    a = s.activation(kn[:, dc * TP1 + 1 + 128 * c: dc * TP1 + 1 + 128 * c + 128],
                                     C.pA[:, 128 * dc:128 * dc + 128], AF.Copy)
                    sS.inc(a)
            C.s(fke)
            st["knev"] = sS.n
    C.fence()
    def f(g):
        C.ld(g, C.knhb[:, :], kn[:, :])
        sDW.wait(g)
        sG.inc(g.nop("rl", nofuse=True))
    C.g(f)
    C.fence()
    AFA.pop()
    AB.pop()
    vn_o = AB.alloc(NC * sv)
    vn = C.AB_t[:, vn_o:vn_o + NC * sv]
    AB.push()
    AFA.push()
    vfa_o = AB.alloc(8192)
    wsv_o = AB.alloc(8192)
    wsv = C.AB_t[:, wsv_o:wsv_o + 8192]
    xb_o = AB.alloc(4096)
    xb = C.AB_t[:, xb_o:xb_o + 4096]
    vma_o = AB.alloc(4096)
    i2b_o = AB.alloc(512)
    xt = AFT[:, AFA.alloc(4096):AFA.stack[-1]]
    vst_o = AFA.alloc(512)
    vstage = AFT[:, vst_o:vst_o + 512]
    vm_o = AFA.alloc(512)
    vsh_o = AFA.alloc(512)
    vsq_o = AFA.alloc(512)
    n16_o = AFA.alloc(1024)

    def f(g):
        a = g.memset(vn[:, :].rearrange("p (k h c) -> p k h c", k=NC, h=16)[:, :, :, 64:65], 1.0)
        sG.inc(a)
    C.g(f)
    wtokV = load_wsec(C, wsv_o, C.wvT, 1024, 1024)
    C.w1_bias(wsv, 1024, 1024, None)
    C.fence()

    st = {"evac": (0, 0), "vno": (0, 0), "chain": (0, 0), "vnev": [(0, 0), (0, 0)],
          "trv": (0, 0)}
    for ti in range(NT):
        tt = NT - 1 - ti
        cur, nxt = tt % 2, (tt + 1) % 2
        def pre_t(t, pc=st["chain"]):
            sS.wait(t, pc)
        lt = C.ln_lite(tt, xt, xb, pre_t=pre_t)
        for dc in range(8):
            def pre(t, pc=st["evac"], pv=st["vno"], lt=lt, wtokV=wtokV):
                sG.wait(t, wtokV)
                sS.wait(t, pc)
                sV.wait(t, tuple(map(max, pv, lt)))
            tok = C.proj_feat_oc(xb, lambda ic, dc=dc: wsv[:, ic * 1024 + dc * 128: ic * 1024 + dc * 128 + 128],
                                 dc, 1024, pre=pre)
            def fse(s, dc=dc, cur=cur, tok=tok):
                sT.wait(s, tok)
                sS.wait(s)
                sV.wait(s)
                a = s.activation(C.AB_t[:, vfa_o + cur * 4096 + 512 * dc: vfa_o + cur * 4096 + 512 * dc + 512],
                                 C.pC[0:128, 0:512], AF.Copy)
                sS.inc(a)
            C.s(fse)
            st["evac"] = sS.n
        vfaTok = sS.n
        for dc in range(8):
            def fvm(v, dc=dc, tt=tt, cur=cur, nxt=nxt, vfaTok=vfaTok):
                sS.wait(v)
                sV.wait(v)
                def bc(colo, base, n):
                    sl = AFT[base:base + 64, colo:colo + 1]
                    return bass.AP(sl.tensor, sl.offset, [[sl.ap[0][0], 64], [0, n]])
                for ph in range(2):
                    b0 = 64 * ph
                    hh = 2 * dc + ph
                    vcur = C.AB_t[b0:b0 + 64, vfa_o + cur * 4096 + 512 * dc: vfa_o + cur * 4096 + 512 * dc + 512]
                    a = v.tensor_tensor(AFT[b0:b0 + 64, vm_o:vm_o + 512], vcur,
                                        bc(ap_o + 32 + hh, b0, 512), ALU.mult)
                    tk = sV.inc(a)
                    sV.wait(v, tk)
                    a = v.tensor_tensor(AFT[b0:b0 + 64, vsh_o:vsh_o + 511],
                                        C.AB_t[b0:b0 + 64, vfa_o + cur * 4096 + 512 * dc + 1: vfa_o + cur * 4096 + 512 * dc + 512],
                                        bc(ap_o + 48 + hh, b0, 511), ALU.mult)
                    tk = sV.inc(a)
                    sV.wait(v, tk)
                    if tt < NT - 1:
                        tail = C.AB_t[b0:b0 + 64, vfa_o + nxt * 4096 + 512 * dc: vfa_o + nxt * 4096 + 512 * dc + 1]
                    else:
                        tail = C.cstf[b0:b0 + 64, CF_CONST + 5:CF_CONST + 6]
                    a = v.tensor_tensor(AFT[b0:b0 + 64, vsh_o + 511:vsh_o + 512], tail,
                                        bc(ap_o + 48 + hh, b0, 1), ALU.mult)
                    tk = sV.inc(a)
                    sV.wait(v, tk)
                    a = v.tensor_add(AFT[b0:b0 + 64, vm_o:vm_o + 512], AFT[b0:b0 + 64, vm_o:vm_o + 512],
                                     AFT[b0:b0 + 64, vsh_o:vsh_o + 512])
                    sV.inc(a)
            C.v(fvm)
            tkvm = sV.n
            def fsqv(s, dc=dc, cur=cur, tkvm=tkvm):
                sV.wait(s, tkvm)
                sT.wait(s)
                a = s.activation(AFT[:, vsq_o:vsq_o + 512], AFT[:, vm_o:vm_o + 512], AF.Square)
                sS.inc(a)
                a = s.activation(C.AB_t[:, vma_o + 512 * dc: vma_o + 512 * dc + 512],
                                 AFT[:, vm_o:vm_o + 512], AF.Copy)
                sS.inc(a)
            C.s(fsqv)
            tksqv = sS.n
            def fssq(t, dc=dc, tksqv=tksqv, lt=lt):
                sS.wait(t, tksqv)
                sV.wait(t, lt)
                m = t.matmul(C.pB[0:16, 0:512], C.cstf[:, CF_BD + 16 * dc: CF_BD + 16 * dc + 16],
                             AFT[:, vsq_o:vsq_o + 512], start=(dc == 0), stop=(dc == 7))
                sT.inc(m)
            C.t(fssq)
        ssqTok = sT.n
        def fn1(s, ssqTok=ssqTok):
            sT.wait(s, ssqTok)
            a = s.activation(AFT[0:16, n16_o:n16_o + 512], C.pB[0:16, 0:512], AF.Sqrt)
            sS.inc(a)
        C.s(fn1)
        chain0 = sS.n
        st["chain"] = chain0
        def fn2(v, chain0=chain0):
            sS.wait(v, chain0)
            eps = bass.AP(C.cstf.ap().tensor, C.cstf.ap().offset + CF_CONST + 1,
                          [[C.cstf.ap().ap[0][0], 16], [0, 512]])
            a = v.tensor_tensor(AFT[0:16, n16_o + 512:n16_o + 1024], AFT[0:16, n16_o:n16_o + 512],
                                eps, ALU.add)
            sV.inc(a)
        C.v(fn2)
        tkn2 = sV.n
        def fn3(s, tkn2=tkn2):
            sV.wait(s, tkn2)
            a = s.activation(AFT[0:16, n16_o:n16_o + 512], AFT[0:16, n16_o + 512:n16_o + 1024], AF.Ln)
            tk = sS.inc(a)
            sS.wait(s, tk)
            a = s.activation(AFT[0:16, n16_o + 512:n16_o + 1024], AFT[0:16, n16_o:n16_o + 512],
                             AF.Exp, scale=-1.0)
            sS.inc(a)
        C.s(fn3)
        tkn3 = sS.n
        def fn4(v, tkn3=tkn3):
            sS.wait(v, tkn3)
            vs = bass.AP(AFT.ap().tensor, AFT.ap().offset + v16_o,
                         [[AFT.ap().ap[0][0], 16], [0, 512]])
            a = v.tensor_tensor(AFT[0:16, n16_o:n16_o + 512], AFT[0:16, n16_o + 512:n16_o + 1024],
                                vs, ALU.mult)
            sV.inc(a)
        C.v(fn4)
        tkn4 = sV.n
        def fn5(s, tkn4=tkn4):
            sV.wait(s, tkn4)
            a = s.activation(C.AB_t[0:16, i2b_o:i2b_o + 512], AFT[0:16, n16_o:n16_o + 512], AF.Copy)
            sS.inc(a)
        C.s(fn5)
        i2bTok = sS.n
        for dc in range(8):
            def fbc(t, dc=dc, i2bTok=i2bTok, pv=st["vno"]):
                sS.wait(t, i2bTok)
                sV.wait(t, pv)
                m = t.matmul(C.pC[0:128, 0:512], C.cstb[0:16, CB_BDT + 128 * dc: CB_BDT + 128 * dc + 128],
                             C.AB_t[0:16, i2b_o:i2b_o + 512], start=True, stop=True)
                sT.inc(m)
            C.t(fbc)
            tkbc = sT.n
            def fvn(v, dc=dc, tkbc=tkbc, ptr=st["trv"]):
                sT.wait(v, tkbc)
                sT.wait(v, ptr)
                a = v.tensor_mul(vstage[:, 0:512], C.AB_t[:, vma_o + 512 * dc: vma_o + 512 * dc + 512],
                                 C.pC[0:128, 0:512])
                sV.inc(a)
            C.v(fvn)
            st["vno"] = sV.n
            tkvn = sV.n
            def ftrv(t, dc=dc, tkvn=tkvn, pev=st["vnev"][dc % 2]):
                sV.wait(t, tkvn)
                sV.wait(t, pev)
                m = None
                for blk in range(4):
                    m = t.transpose(C.pA[:, 512 * (dc % 2) + 128 * blk: 512 * (dc % 2) + 128 * blk + 128],
                                    vstage[:, 128 * blk:128 * blk + 128],
                                    C.cstf[:, CF_ID:CF_ID + 128])
                sT.inc(m)
            C.t(ftrv)
            st["trv"] = sT.n
            tktrv = sT.n
            def fvev(v, dc=dc, tt=tt, tktrv=tktrv):
                sT.wait(v, tktrv)
                a = None
                for blk in range(4):
                    kt = 4 * tt + blk
                    dst = vn[:, kt * sv + 65 * 2 * dc: kt * sv + 65 * 2 * dc + 130]
                    dst = dst.rearrange("p (h c) -> p h c", c=65)[:, :, 0:64]
                    src = C.pA[:, 512 * (dc % 2) + 128 * blk: 512 * (dc % 2) + 128 * blk + 128]
                    a = v.tensor_copy(dst, src.rearrange("p (h c) -> p h c", c=64))
                    sV.inc(a)
            C.v(fvev)
            st["vnev"][dc % 2] = sV.n
    AFA.pop()
    AB.pop()
    C.fence()
    AB.push()
    AFA.push()
    knr_o = AB.alloc(8 * TP1)
    knr = C.AB_t[:, knr_o:knr_o + 8 * TP1]
    ob_o = AB.alloc(4096)
    obuf = C.AB_t[:, ob_o:ob_o + 4096]
    odd_o = AB.alloc(512)
    odds = C.AB_t[:, odd_o:odd_o + 512]
    wp_o = AB.alloc(1024)
    wp = C.AB_t[:, wp_o:wp_o + 1024]
    meg_o = AFA.alloc(C.MEGW)
    mega = AFT[:, meg_o:meg_o + C.MEGW]
    inv_o = AFA.alloc(512)
    invs = AFT[:, inv_o:inv_o + 512]
    xo_o = AFA.alloc(1024)
    stf_o = AFA.alloc(1024)

    def f(g):
        C.ld(g, knr[:, :], C.knhb[:, :])
        C.ld(g, C.w2[0:1, 3072:3072 + 1024], C.mrows[1:2, 0:1024])
        C.ld(g, C.w2[1:2, 3072:3072 + 1024], C.mrows[1:2, 0:1024])
    C.g(f)
    C.emit_mega_mask(mega, C.tbl_ma, qshift=0)
    C.fence()

    epst = {"tk2": (0, 0), "tk3": (0, 0), "tk4": (0, 0), "odd": (0, 0)}

    def epilogue(slot, h, av, tok_av):
        blk = h // 2
        if slot == 0:
            def ffix(v, av=av, tok_av=tok_av):
                C.sT2.wait(v, tok_av)
                a = v.tensor_add(av[64:65, 0:1], av[64:65, 0:1], C.cstf[64:65, CF_ONES:CF_ONES + 1])
                sV.inc(a)
            C.v(ffix)
            fixTok = sV.n
        else:
            fixTok = None
        def fs(s, tok_av=tok_av, p2=epst["tk2"], fixTok=fixTok):
            C.sT2.wait(s, tok_av)
            if fixTok is not None:
                sV.wait(s, fixTok)
            sT.wait(s, p2)
            a = s.activation(C.rows[64:65, 0:512], av[64:65, :], AF.Ln)
            sS.inc(a)
        C.s(fs)
        tk1 = sS.n
        def ft(t, tk1=tk1, p3=epst["tk3"]):
            sS.wait(t, tk1)
            sS.wait(t, p3)
            m = t.matmul(C.pB[0:64, 0:512], C.cstf[64:65, CF_ONESB:CF_ONESB + 64],
                         C.rows[64:65, 0:512], start=True, stop=True)
            sT.inc(m)
        C.t(ft)
        tk2 = sT.n
        epst["tk2"] = tk2
        def fs2(s, tk2=tk2, p4=epst["tk4"]):
            sT.wait(s, tk2)
            sV.wait(s, p4)
            a = s.activation(invs[0:64, :], C.pB[0:64, 0:512], AF.Exp, scale=-1.0)
            sS.inc(a)
        C.s(fs2)
        tk3 = sS.n
        epst["tk3"] = tk3
        def fv(v, h=h, blk=blk, tk3=tk3, po=epst["odd"]):
            sS.wait(v, tk3)
            if h % 2 == 0:
                dst = obuf[0:64, blk * 512:(blk + 1) * 512]
            else:
                C.sY.wait(v, po)
                dst = odds[0:64, 0:512]
            c2 = v.tensor_mul(dst, av[0:64, :], invs[0:64, :])
            sV.inc(c2)
        C.v(fv)
        tk4 = sV.n
        epst["tk4"] = tk4
        if h % 2 == 1:
            def fy(y, blk=blk, tk4=tk4):
                sV.wait(y, tk4)
                C.ld(y, obuf[64:128, blk * 512:(blk + 1) * 512], odds[0:64, 0:512], sDS)
                sDS.wait(y)
                C.sY.inc(y.nop("rl", nofuse=True))
            C.y(fy)
            epst["odd"] = C.sY.n
        return (sV, sV.n)

    def k_ap(slot, kt, h):
        dc = h // 2
        return knr[64 * (h % 2):64 * (h % 2) + 64, dc * TP1 + 1 + kt * 128: dc * TP1 + 1 + kt * 128 + 128]

    def q_ap(slot, h):
        dc = h // 2
        return knr[64 * (h % 2):64 * (h % 2) + 64, dc * TP1 + 1 + slot * 512: dc * TP1 + 1 + slot * 512 + 512]

    def vn_ap(kt, h):
        base = kt * sv + 65 * h
        return vn[:, base:base + 65]

    def bias_ap(slot, kt, h):
        cnt = 4 * (slot + 1)
        if kt >= cnt - 4:
            c0 = C.mega_c0(slot, kt)
            return mega[:, c0:c0 + 512]
        return None

    cfg = dict(H=16, counts=lambda s: 4 * (s + 1), k_ap=k_ap, q_ap=q_ap,
               vn_ap=vn_ap, bias_ap=bias_ap, phantom=None, eshift=-20.0,
               epilogue=epilogue)

    for slot in range(NT):
        C.emit_attention_slot(cfg, slot)
        C.out_proj_residual(slot, obuf, wp, wp_o, C.wcT, 3072,
                            AFT[:, xo_o:xo_o + 1024], AFT[:, stf_o:stf_o + 1024], last)
        C.fence()

    AFA.pop()
    AB.pop()
    AFA.pop()
    AB.pop()


KB.sub_attn_a = sub_attn_a


# version 07
# speedup vs baseline: 1.0320x; 1.0320x over previous
import math
import numpy as np
import ml_dtypes
import concourse.bass as bass
import concourse.mybir as mybir
from concourse.bass_utils import run_bass_kernel_spmd
from contextlib import ExitStack

F32 = mybir.dt.float32
BF16 = mybir.dt.bfloat16
AF = mybir.ActivationFunctionType
ALU = mybir.AluOpType

NEG = -1.0e30
KEY_SCALE_MAX = math.log(2 ** 16 - 1)
CONSTS = (1e-5, 1e-10, 10.0, KEY_SCALE_MAX, 1.0, 0.0, -1.0, 0.125, -40.0, -20.0)


class Sem:

    def __init__(self, nc, ctx, name, dma=False):
        self.hA = ctx.enter_context(nc.semaphore(name + "a"))
        self.hB = ctx.enter_context(nc.semaphore(name + "b"))
        self.va = 0
        self.vb = 0
        self.ic = 0
        self.dma = dma

    @property
    def n(self):
        return (self.va, self.vb)

    def inc(self, instr, k=1):
        if self.ic % 2 == 0:
            instr.then_inc(self.hA, k)
            self.va += k
        else:
            instr.then_inc(self.hB, k)
            self.vb += k
        self.ic += 1
        return (self.va, self.vb)

    def tok_after(self, snap, j, k=1):
        va, vb, ic = snap
        for i in range(j):
            if (ic + i) % 2 == 0:
                va += k
            else:
                vb += k
        return (va, vb)

    @property
    def snap(self):
        return (self.va, self.vb, self.ic)

    def wait(self, eng, tok=None):
        va, vb = (self.va, self.vb) if (tok is None or self.dma) else tok
        if va:
            eng.wait_ge(self.hA, va)
        if vb:
            eng.wait_ge(self.hB, vb)


class Arena:
    def __init__(self, tensor, width):
        self.t = tensor
        self.w = width
        self.stack = [0]

    def alloc(self, cols):
        o = self.stack[-1]
        assert o + cols <= self.w, f"arena overflow {o}+{cols}>{self.w}"
        self.stack[-1] = o + cols
        return o

    def push(self):
        self.stack.append(self.stack[-1])

    def pop(self):
        self.stack.pop()


def hope_tables(T, base=10000, d_model=1024):
    dim = d_model // 2
    pos = np.arange(dim, dtype=np.float64)
    freqs_np = 1.0 / base ** (pos / dim)
    active = int(np.sum(freqs_np * 2 * math.pi * T >= 1.0))
    active = min(active, dim - d_model // 4)
    freqs = freqs_np[:active].astype(np.float32)
    d = np.arange(0, T, dtype=np.float32)
    theta = (d[:, None] * freqs[None, :]).astype(np.float32)
    pv = (np.cos(theta, dtype=np.float32) + np.sin(theta, dtype=np.float32)).astype(np.float32)
    nh = active // 32
    per_head = pv.astype(np.float64).reshape(T, nh, 32).sum(axis=2)
    hi = per_head.astype(np.float32)
    lo = (per_head - hi.astype(np.float64)).astype(np.float32)
    return hi.T.copy(), lo.T.copy()


def shift2d(row):
    L = row.shape[0]
    padded = np.concatenate([np.zeros(127, row.dtype), row])
    s = np.lib.stride_tricks.sliding_window_view(padded, L)
    return np.ascontiguousarray(s[::-1])


def build_tables(T):
    LPAD = T + 128
    L = LPAD + T
    d = np.arange(-LPAD, T)
    mask_a = np.where(d >= 1, 0.0, NEG).astype(np.float32)
    mask_f = np.where(d >= 0, 0.0, NEG).astype(np.float32)
    hi8, lo8 = hope_tables(T)
    hope_hi = np.zeros((8, L), dtype=np.float32)
    hope_lo = np.zeros((8, L), dtype=np.float32)
    hope_hi[:, :LPAD] = NEG
    hope_hi[:, LPAD:] = hi8
    hope_lo[:, LPAD:] = lo8
    ma2 = shift2d(mask_a)
    mf2 = shift2d(mask_f)
    hh2 = np.concatenate([shift2d(hope_hi[h]) for h in range(8)], axis=0)
    hl2 = np.concatenate([shift2d(hope_lo[h]) for h in range(8)], axis=0)
    return ma2, mf2, hh2, hl2


def mk_constants():
    c = np.zeros((128, 192), dtype=np.float32)
    c[:, 0] = 1.0
    c[:, 1] = np.arange(1, 129, dtype=np.float32)
    for h in range(8):
        c[h, 16 + 8 * h:16 + 8 * h + 8] = 1.0
    for h in range(8):
        c[64, 80 + 8 * h + h] = 1.0
    return c


def mk_selectors():
    s = np.zeros((64, 32 * 128), dtype=np.float32)
    for e in range(8):
        for j in range(4):
            m = e * 4 + j
            s[(2 * j) * 8 + e, 128 * m:128 * m + 64] = 1.0
            s[(2 * j + 1) * 8 + e, 128 * m + 64:128 * m + 128] = 1.0
    return s


class KB:
    def __init__(self, T, subs):
        self.T = T
        self.subs = subs
        self.NT = T // 512
        self.NC = T // 128
        self.TP1 = T + 1
        self.LPAD = T + 128
        self.TBL = self.LPAD + T
        self.MEGW = T + 384
        self.nc = bass.Bass("TRN2", target_bir_lowering=False)

    def build(self):
        nc = self.nc
        T = self.T

        def din(name, shape, dtype=F32):
            return nc.dram_tensor(name, shape, dtype, kind="ExternalInput").ap()

        self.x_in = din("x_feat", [1024, T])
        self.wkT = din("wkT", [1024, 1024])
        self.wvT = din("wvT", [1024, 1024])
        self.wcT = din("wcT", [128, 8 * 1024])
        self.aprm = din("aprm", [1, 64])
        self.wqkvT = din("wqkvT", [1024, 3072])
        self.bqkv = din("bqkv", [1, 3072])
        self.wpT = din("wpT", [128, 8 * 1024])
        self.bproj = din("bproj", [1, 1024])
        self.wqf = din("wqf", [1024, 512])
        self.wkf = din("wkf", [1024, 512])
        self.wgf = din("wgf", [1024, 64])
        self.wve = din("wve", [8, 1024, 512])
        self.woe = din("woe", [32, 128, 1024])
        self.tbl_ma = din("tbl_ma", [128, self.TBL])
        self.tbl_mf = din("tbl_mf", [128, self.TBL])
        self.tbl_hh = din("tbl_hh", [8 * 128, self.TBL])
        self.tbl_hl = din("tbl_hl", [8 * 128, self.TBL])
        self.cstb_in = din("cstb_in", [128, 2048])
        self.tblu = din("tblu", [128, 256])
        self.mrows = din("mrows", [2, 4096])
        self.cstf_in = din("cstf_in", [128, 512])
        self.selc_in = din("selc_in", [64, 32 * 128])

        self.y_out = nc.dram_tensor("y", [1024, T], F32, kind="ExternalOutput").ap()
        self.xres = nc.dram_tensor("xres", [1024, T], F32, kind="Internal").ap()
        self.knhb = nc.dram_tensor("knhb", [128, 8 * self.TP1], BF16, kind="Internal").ap()
        self.qhb = nc.dram_tensor("qhb", [1024, T], BF16, kind="Internal").ap()
        self.aghb = nc.dram_tensor("aghb", [2, T], F32, kind="Internal").ap()

        ctx = ExitStack()
        with ctx:
            self._alloc(ctx)
            self._emit()
        return nc

    def _alloc(self, ctx):
        nc = self.nc
        self.pt = ctx.enter_context(nc.sbuf_tensor("pt", [128, 4 * 512], BF16))
        self.rows = ctx.enter_context(nc.sbuf_tensor("rows", [128, 2048], F32))
        self.cstf = ctx.enter_context(nc.sbuf_tensor("cstf", [128, 512], F32))
        self.cstb = ctx.enter_context(nc.sbuf_tensor("cstb", [128, 2048], BF16))
        self.w2 = ctx.enter_context(nc.sbuf_tensor("w2", [2, 4096], BF16))
        self.g2 = ctx.enter_context(nc.sbuf_tensor("g2", [2, 512], BF16))
        self.AB_t = ctx.enter_context(nc.sbuf_tensor("AB", [128, 76000], BF16))
        self.AF_t = ctx.enter_context(nc.sbuf_tensor("AFa", [128, 8200], F32))
        self.AB = Arena(self.AB_t, 76000)
        self.AFA = Arena(self.AF_t, 8200)

        self.pL = ctx.enter_context(nc.psum_tensor("pL", [128, 2048], F32))
        self.pA = ctx.enter_context(nc.psum_tensor("pA", [128, 1024], F32))
        self.pB = ctx.enter_context(nc.psum_tensor("pB", [128, 512], F32))
        self.pC = ctx.enter_context(nc.psum_tensor("pC", [128, 512], F32))

        self.sDW = Sem(nc, ctx, "sDW", dma=True)
        self.sDA = Sem(nc, ctx, "sDA", dma=True)
        self.sDX = Sem(nc, ctx, "sDX", dma=True)
        self.sDS = Sem(nc, ctx, "sDS", dma=True)
        self.sT = Sem(nc, ctx, "sT")
        self.sS = Sem(nc, ctx, "sS")
        self.sV = Sem(nc, ctx, "sV")
        self.sG = Sem(nc, ctx, "sG")
        self.sT2 = Sem(nc, ctx, "sT2")
        self.sS2 = Sem(nc, ctx, "sS2")
        self.sY = Sem(nc, ctx, "sY")

        self.block = ctx.enter_context(nc.Block())
        for i, val in enumerate(CONSTS):
            nc.const_aps.register(F32, val, self.cstf[:, 220 + i:221 + i])

    def t(self, f): self.block.tensor(f)
    def s(self, f): self.block.scalar(f)
    def v(self, f): self.block.vector(f)
    def g(self, f): self.block.gpsimd(f)
    def y(self, f): self.block.sync(f)

    def ld(self, eng, dst, src, sem=None):
        sem = sem or self.sDW
        sem.inc(eng.dma_start(out=dst, in_=src), 16)

    def fence(self):
        sT, sS, sV, sG, sY = self.sT, self.sS, self.sV, self.sG, self.sY
        sDS, sDX, sDW = self.sDS, self.sDX, self.sDW
        def ft(t):
            sT.wait(t)
            sT.inc(t.nop("fence", nofuse=True))
        def fs(s):
            sS.wait(s)
            sS.inc(s.nop("fence", nofuse=True))
        def fv(v):
            sV.wait(v)
            sV.inc(v.nop("fence", nofuse=True))
        def fg(g):
            sDW.wait(g)
            sG.wait(g)
            sG.inc(g.nop("fence", nofuse=True))
        def fy(y):
            sDS.wait(y)
            sDX.wait(y)
            sY.wait(y)
            sY.inc(y.nop("fence", nofuse=True))
        self.t(ft); self.s(fs); self.v(fv); self.g(fg); self.y(fy)
        def wt(t): sS.wait(t); sV.wait(t); sG.wait(t); sY.wait(t)
        def ws(s): sT.wait(s); sV.wait(s); sG.wait(s); sY.wait(s)
        def wv(v): sT.wait(v); sS.wait(v); sG.wait(v); sY.wait(v)
        def wg(g): sT.wait(g); sS.wait(g); sV.wait(g); sY.wait(g)
        def wy(y): sT.wait(y); sS.wait(y); sV.wait(y); sG.wait(y)
        self.t(wt); self.s(ws); self.v(wv); self.g(wg); self.y(wy)

    def _emit(self):
        C = self
        sDW, sDX, sDS = C.sDW, C.sDX, C.sDS
        sV = C.sV

        def f(g):
            C.ld(g, C.cstf[:, :], C.cstf_in[:, :])
            C.ld(g, C.cstb[:, :], C.cstb_in[:, :])
            C.ld(g, C.xres[:, :], C.x_in[:, :])
        C.g(f)
        def f(g):
            C.ld(g, C.g2[1:2, 0:512], C.mrows[0:1, 0:512])
        C.g(f)
        C.fence()

        nsub = len(C.subs)
        for i, sub in enumerate(C.subs):
            last = (i == nsub - 1)
            if sub == "attn_a":
                C.sub_attn_a(last)
            elif sub in ("ffn1", "ffn2"):
                C.sub_ffn(last)
            elif sub == "attn_b":
                C.sub_attn_b(last)
            C.fence()

        def f(y_):
            sDS.wait(y_)
        C.y(f)
        def f(g):
            sDS.wait(g)
        C.g(f)


def emit_attention_slot(C, cfg, slot):
    sT, sS, sV, sT2, sS2 = C.sT, C.sS, C.sV, C.sT2, C.sS2
    idn = C.cstf[:, CF_ID:CF_ID + 128]
    st = cfg.setdefault("_state", {"epi": {0: None, 1: None}, "hctr": 0})
    cnt = cfg["counts"](slot)
    order = cfg.get("head_order") or list(range(cfg["H"]))
    for h in order:
        if cfg.get("pre_head"):
            cfg["pre_head"](slot, h)
        hp = st["hctr"] % 2
        st["hctr"] += 1
        av = C.pA[:, hp * 512:(hp + 1) * 512]
        qap = cfg["q_ap"](slot, h)
        snT2 = sT2.snap
        snS2 = sS2.snap
        snT = sT.snap
        prev_exp_all = sS2.n
        prev_av_all = sT2.n
        bprev = st["epi"][hp]
        ph = cfg["phantom"](slot) if cfg.get("phantom") else None

        def ft(t, slot=slot, h=h, cnt=cnt, av=av, qap=qap,
               snS2=snS2, prev_exp_all=prev_exp_all, bprev=bprev, ph=ph):
            sS2.wait(t, prev_exp_all)
            if bprev is not None:
                bprev[0].wait(t, bprev[1])
            for kt in range(cnt):
                r = kt % 4
                if kt >= 1:
                    sS2.wait(t, sS2.tok_after(snS2, kt))
                    m = t.matmul(av[0:65, :], cfg["vn_ap"](kt - 1, h),
                                 C.pt[:, ((kt - 1) % 4) * 512:((kt - 1) % 4) * 512 + 512],
                                 start=(kt - 1 == 0), stop=False)
                    sT2.inc(m)
                bias = cfg["bias_ap"](slot, kt, h)
                m = t.matmul(C.pL[:, r * 512:(r + 1) * 512],
                             cfg["k_ap"](slot, kt, h), qap,
                             start=True, stop=(bias is None))
                if bias is not None:
                    m = t.matmul(C.pL[:, r * 512:(r + 1) * 512], idn, bias,
                                 start=False, stop=True)
                sT.inc(m)
            sS2.wait(t, sS2.tok_after(snS2, cnt))
            m = t.matmul(av[0:65, :], cfg["vn_ap"](cnt - 1, h),
                         C.pt[:, ((cnt - 1) % 4) * 512:((cnt - 1) % 4) * 512 + 512],
                         start=(cnt - 1 == 0), stop=(ph is None))
            if ph is not None:
                m = t.matmul(av[0:65, :], ph[0], ph[1], start=False, stop=True)
            sT2.inc(m)
        C.t(ft)
        final_av_tok = sT2.n

        def fs(s, cnt=cnt, snT=snT, snT2=snT2, prev_av_all=prev_av_all):
            for kt in range(cnt):
                r = kt % 4
                if kt < 4:
                    sT2.wait(s, prev_av_all)
                else:
                    sT2.wait(s, sT2.tok_after(snT2, kt - 3))
                sT.wait(s, sT.tok_after(snT, kt + 1))
                a = s.activation(C.pt[:, r * 512:(r + 1) * 512],
                                 C.pL[:, r * 512:(r + 1) * 512], AF.Exp,
                                 bias=cfg.get("eshift", 0.0))
                sS2.inc(a)
        C.s(fs)

        st["epi"][hp] = cfg["epilogue"](slot, h, av, final_av_tok)


def emit_mega_mask(C, mega_ap, tbl, qshift, W=None):
    W = W or C.MEGW
    off = C.LPAD + qshift - 384
    def f(g):
        C.sT.wait(g)
        C.sDW.inc(g.dma_start(out=mega_ap[:, 0:W], in_=tbl[0:128, off:off + W]), 16)
        C.sDW.wait(g)
        C.sG.inc(g.nop("rl", nofuse=True))
    C.g(f)
    return C.sG.n


def mega_c0(C, slot, kt):
    return slot * 512 - 128 * kt + 384


KB.emit_attention_slot = emit_attention_slot
KB.emit_mega_mask = emit_mega_mask
KB.mega_c0 = mega_c0


CB_ONES = 0
CB_P1 = 1
CB_E8 = 16
CB_OH = 80
CB_ONESB = 144
CB_P1ROW = 208
CB_ID = 336
CB_DEC = 464
CB_TRI = 592
CB_BDT = 720
CB_BD = 1744
CB_ONESR = 1880
CF_ONES = 0
CF_E8 = 16
CF_ONESB = 80
CF_CONST = 220
CF_ID = 256
CF_BD = 384


def mk_cstb():
    c = np.zeros((128, 2048), dtype=np.float32)
    c[:, CB_ONES] = 1.0
    c[:, CB_P1] = np.arange(1, 129, dtype=np.float32)
    for h in range(8):
        c[h, CB_E8 + 8 * h: CB_E8 + 8 * h + 8] = 1.0
    for h in range(8):
        c[64, CB_OH + 8 * h + h] = 1.0
    c[:, CB_ONESB:CB_ONESB + 64] = 1.0
    c[0, CB_P1ROW:CB_P1ROW + 128] = np.arange(1, 129, dtype=np.float32)
    c[:, CB_ID:CB_ID + 128] = np.eye(128, dtype=np.float32)
    p = np.arange(128)[:, None]
    j = np.arange(128)[None, :]
    c[:, CB_DEC:CB_DEC + 128] = np.maximum(p - j, 0).astype(np.float32)
    c[:, CB_TRI:CB_TRI + 128] = (p >= j).astype(np.float32)
    for dc in range(8):
        blk = np.zeros((128, 16), dtype=np.float32)
        for pp in range(128):
            blk[pp, 2 * dc + pp // 64] = 1.0
        c[:, CB_BD + 16 * dc: CB_BD + 16 * dc + 16] = blk
        bt = np.zeros((128, 128), dtype=np.float32)
        for pp in range(128):
            bt[2 * dc + pp // 64, pp] = 1.0
        c[:, CB_BDT + 128 * dc: CB_BDT + 128 * dc + 128] = bt
    c[:, CB_ONESR:CB_ONESR + 128] = 1.0
    return c


def mk_cstf():
    c = np.zeros((128, 512), dtype=np.float32)
    c[:, CF_ONES] = 1.0
    for h in range(8):
        c[h, CF_E8 + 8 * h: CF_E8 + 8 * h + 8] = 1.0
    c[:, CF_ONESB:CF_ONESB + 128] = 1.0
    c[:, CF_ID:CF_ID + 128] = np.eye(128, dtype=np.float32)
    for i, val in enumerate(CONSTS):
        c[:, CF_CONST + i] = val
    for dc in range(8):
        for pp in range(128):
            c[pp, CF_BD + 16 * dc + 2 * dc + pp // 64] = 1.0
    return c


def ln_full(C, tt, xt, xb, store=True):
    sDX, sDS, sT, sS, sV = C.sDX, C.sDS, C.sT, C.sS, C.sV
    R = C.rows
    t0 = tt * 512
    def f(y):
        sV.wait(y)
        sS.wait(y)
        for dc in range(8):
            C.ld(y, xt[:, dc * 512:(dc + 1) * 512],
                 C.xres[dc * 128:(dc + 1) * 128, t0:t0 + 512], sDX)
        sDX.wait(y)
        C.sY.inc(y.nop("rl", nofuse=True))
    C.y(f)
    tkx = C.sY.n
    def f(v):
        C.sY.wait(v, tkx)
        sT.wait(v)
        c = None
        for dc in range(8):
            c = v.tensor_copy(xb[:, dc * 512:(dc + 1) * 512], xt[:, dc * 512:(dc + 1) * 512])
        sV.inc(c)
    C.v(f)
    tkb = sV.n
    C.AFA.push()
    o = C.AFA.alloc(1024)
    sq_sl = [C.AF_t[:, o:o + 512], C.AF_t[:, o + 512:o + 1024]]
    snT = sT.snap
    def f(s):
        sV.wait(s, tkb)
        sT.wait(s, (snT[0], snT[1]))
        for dc in range(8):
            if dc >= 2:
                sT.wait(s, sT.tok_after(snT, dc - 1))
            a = s.activation(sq_sl[dc % 2][:, :], xt[:, dc * 512:(dc + 1) * 512], AF.Square)
            sS.inc(a)
    snS = sS.snap
    C.s(f)
    def f(t):
        for dc in range(8):
            sS.wait(t, sS.tok_after(snS, dc + 1))
            t.matmul(C.pB[0:1, 0:512], C.cstb[:, CB_ONES:CB_ONES + 1],
                     xb[:, dc * 512:(dc + 1) * 512], start=(dc == 0), stop=(dc == 7))
            m2 = t.matmul(C.pL[0:1, 0:512], C.cstf[:, CF_ONES:CF_ONES + 1],
                          sq_sl[dc % 2][:, :], start=(dc == 0), stop=(dc == 7))
            sT.inc(m2)
    C.t(f)
    tksum = sT.n
    def f(s):
        sT.wait(s, tksum)
        C.sY.wait(s)
        s.activation(R[0:1, 0:512], C.pB[0:1, 0:512], AF.Copy, scale=1.0 / 1024.0)
        a = s.activation(R[0:1, 512:1024], C.pL[0:1, 0:512], AF.Copy, scale=1.0 / 1024.0)
        tk = sS.inc(a)
        sS.wait(s, tk)
        a = s.activation(R[0:1, 1024:1536], R[0:1, 0:512], AF.Square)
        sS.inc(a)
    C.s(f)
    tk1 = sS.n
    def f(v):
        sS.wait(v, tk1)
        c = v.tensor_sub(R[0:1, 512:1024], R[0:1, 512:1024], R[0:1, 1024:1536])
        sV.inc(c)
    C.v(f)
    tk2 = sV.n
    def f(s):
        sV.wait(s, tk2)
        a = s.activation(R[0:1, 1024:1536], R[0:1, 512:1024], AF.Ln, bias=1e-5)
        tk = sS.inc(a)
        sS.wait(s, tk)
        a = s.activation(R[0:1, 1536:2048], R[0:1, 1024:1536], AF.Exp, scale=-0.5)
        sS.inc(a)
    C.s(f)
    tk3 = sS.n
    def f(v):
        sS.wait(v, tk3)
        v.tensor_mul(C.g2[0:1, 0:512], R[0:1, 0:512], R[0:1, 1536:2048])
        c = v.tensor_mul(R[0:1, 1024:1536], R[0:1, 0:512], R[0:1, 1536:2048])
        sV.inc(c)
    C.v(f)
    tk4 = sV.n
    if store:
        def f(y, tt=tt):
            sV.wait(y, tk4)
            C.ld(y, C.aghb[0:1, tt * 512:(tt + 1) * 512], R[0:1, 1536:2048], sDS)
            C.ld(y, C.aghb[1:2, tt * 512:(tt + 1) * 512], R[0:1, 1024:1536], sDS)
            sDS.wait(y)
            C.sY.inc(y.nop("rl", nofuse=True))
        C.y(f)
    def f(t):
        sV.wait(t, tk4)
        m = t.matmul(C.pB[0:128, 0:512], C.cstf[0:1, CF_ONESB:CF_ONESB + 128], R[0:1, 1536:2048],
                     start=True, stop=True)
        sT.inc(m)
    C.t(f)
    tk5 = sT.n
    def f(v):
        sT.wait(v, tk5)
        c = None
        for dc in range(8):
            c = v.tensor_mul(xb[:, dc * 512:(dc + 1) * 512],
                             xt[:, dc * 512:(dc + 1) * 512], C.pB[0:128, 0:512])
        sV.inc(c)
    C.v(f)
    C.AFA.pop()
    return sV.n


def ln_lite(C, tt, xt, xb, pre_t=None):
    sDX, sT, sV = C.sDX, C.sT, C.sV
    R = C.rows
    t0 = tt * 512
    def f(y):
        sV.wait(y)
        C.sS.wait(y)
        sT.wait(y)
        C.ld(y, R[0:1, 1536:2048], C.aghb[0:1, t0:t0 + 512], sDX)
        C.ld(y, R[0:1, 0:512], C.aghb[1:2, t0:t0 + 512], sDX)
        for dc in range(8):
            C.ld(y, xt[:, dc * 512:(dc + 1) * 512],
                 C.xres[dc * 128:(dc + 1) * 128, t0:t0 + 512], sDX)
        sDX.wait(y)
        C.sY.inc(y.nop("rl", nofuse=True))
    C.y(f)
    tkx = C.sY.n
    def f(t):
        C.sY.wait(t, tkx)
        if pre_t is not None:
            pre_t(t)
        m = t.matmul(C.pB[0:128, 0:512], C.cstf[0:1, CF_ONESB:CF_ONESB + 128],
                     R[0:1, 1536:2048], start=True, stop=True)
        sT.inc(m)
    C.t(f)
    tk5 = sT.n
    def f(v):
        sT.wait(v, tk5)
        c = None
        for dc in range(8):
            c = v.tensor_mul(xb[:, dc * 512:(dc + 1) * 512],
                             xt[:, dc * 512:(dc + 1) * 512], C.pB[0:128, 0:512])
        c2 = v.tensor_copy(C.g2[0:1, 0:512], R[0:1, 0:512])
        sV.inc(c2)
    C.v(f)
    return sV.n


def w1_bias(C, wsb, dout, w2off, bias_src=None, wsb_cols0=0, wait_tok=None):
    sT, sS, sG, sDW = C.sT, C.sS, C.sG, C.sDW
    def f(g):
        if bias_src is not None:
            C.ld(g, C.w2[1:2, w2off:w2off + dout], bias_src)
        else:
            C.ld(g, C.w2[1:2, w2off:w2off + dout], C.mrows[1:2, 0:dout])
        sDW.wait(g)
        sG.inc(g.nop("rl", nofuse=True))
    C.g(f)
    wtok = sG.n
    for oc in range((dout + 511) // 512):
        o0 = oc * 512
        ow = min(512, dout - o0)
        def f(t, o0=o0, ow=ow, snv=C.sV.n, sns=C.sS.n):
            sG.wait(t, wtok)
            C.sV.wait(t, snv)
            C.sS.wait(t, sns)
            m = None
            for ic in range(8):
                m = t.matmul(C.pB[0:1, 0:ow], C.cstb[:, CB_ONES:CB_ONES + 1],
                             wsb[:, wsb_cols0 + ic * dout + o0: wsb_cols0 + ic * dout + o0 + ow],
                             start=(ic == 0), stop=(ic == 7))
            sT.inc(m)
        C.t(f)
        tk = sT.n
        def f(s, o0=o0, ow=ow, tk=tk):
            sT.wait(s, tk)
            a = s.activation(C.w2[0:1, w2off + o0: w2off + o0 + ow], C.pB[0:1, 0:ow],
                             AF.Copy, scale=-1.0)
            sS.inc(a)
        C.s(f)
        tk2 = sS.n
        def f(t, tk2=tk2):
            sS.wait(t, tk2)
        C.t(f)
    return sS.n, sG.n, sDW.n


def proj_feat_oc(C, xb, wslice_fn, oc, w2off, pre=None):
    sT = C.sT
    def f(t):
        if pre is not None:
            pre(t)
        t.matmul(C.pC[0:128, 0:512], C.w2[:, w2off + oc * 128: w2off + (oc + 1) * 128],
                 C.g2[:, 0:512], start=True, stop=False)
        m = None
        for ic in range(8):
            m = t.matmul(C.pC[0:128, 0:512], wslice_fn(ic),
                         xb[:, ic * 512:(ic + 1) * 512], start=False, stop=(ic == 7))
        sT.inc(m)
    C.t(f)
    return sT.n


def proj_tok_tc(C, xb, wslice_fn, tc, ow, w2off, o0=0, pre=None):
    sT = C.sT
    def f(t):
        if pre is not None:
            pre(t)
        t.matmul(C.pC[0:128, 0:ow], C.g2[0:2, tc * 128:(tc + 1) * 128],
                 C.w2[:, w2off + o0: w2off + o0 + ow], start=True, stop=False)
        m = None
        for ic in range(8):
            m = t.matmul(C.pC[0:128, 0:ow],
                         xb[:, ic * 512 + tc * 128: ic * 512 + tc * 128 + 128],
                         wslice_fn(ic), start=False, stop=(ic == 7))
        sT.inc(m)
    C.t(f)
    return sT.n


KB.ln_full = ln_full
KB.ln_lite = ln_lite
KB.w1_bias = w1_bias
KB.proj_feat_oc = proj_feat_oc
KB.proj_tok_tc = proj_tok_tc

def load_wsec(C, dst_cols, src2d, dout, pitch, rows0=0, nic=8):
    t = C.AB_t
    def f(g):
        C.sT.wait(g)
        src = bass.AP(src2d.tensor, src2d.offset + rows0,
                      [[pitch, 128], [128 * pitch, nic], [1, dout]])
        C.sDW.inc(g.dma_start(out=t[:, dst_cols:dst_cols + nic * dout].rearrange("p (i o) -> p i o", i=nic),
                    in_=src), 16)
        C.sDW.wait(g)
        C.sG.inc(g.nop("rl", nofuse=True))
    C.g(f)
    return C.sG.n


def sub_attn_b(C, last):
    T, NT, NC, TP1 = C.T, C.NT, C.NC, C.TP1
    sv = 65 * 16
    AB, AFA = C.AB, C.AFA
    sT, sS, sV, sG, sDW, sDX, sDS = C.sT, C.sS, C.sV, C.sG, C.sDW, C.sDX, C.sDS

    AB.push()
    kn_o = AB.alloc(8 * TP1)
    kn = C.AB_t[:, kn_o:kn_o + 8 * TP1]

    def f(g):
        a = g.memset(kn[:, :].rearrange("p (c t) -> p c t", c=8)[:, :, 0:1], 0.0)
        sG.inc(a)
    C.g(f)
    C.fence()

    for sec, o0, w2off in (("q", 0, 0), ("k", 1024, 1024)):
        AB.push()
        AFA.push()
        xt = C.AF_t[:, AFA.alloc(4096):AFA.stack[-1]]
        xb_o = AB.alloc(4096)
        xb = C.AB_t[:, xb_o:xb_o + 4096]
        ws_o = AB.alloc(8192)
        ws = C.AB_t[:, ws_o:ws_o + 8192]
        wtok = load_wsec(C, ws_o, C.wqkvT, 1024, 3072, rows0=o0)
        C.w1_bias(ws, 1024, w2off, C.bqkv[0:1, o0:o0 + 1024], wait_tok=wtok)
        stg_o = AB.alloc(1024)
        stg = C.AB_t[:, stg_o:stg_o + 1024]
        stok = [(0, 0), (0, 0)]
        for tt in range(NT):
            if sec == "q":
                lt = C.ln_full(tt, xt, xb, store=True)
            else:
                lt = C.ln_lite(tt, xt, xb)
            prev_cp = [sS.n]
            for oc in range(8):
                def pre(t, pc=prev_cp[0], lt=lt):
                    sS.wait(t, pc)
                    sV.wait(t, lt)
                tok = C.proj_feat_oc(xb, lambda ic, oc=oc: ws[:, ic * 1024 + oc * 128: ic * 1024 + (oc + 1) * 128],
                                     oc, w2off, pre=pre)
                if sec == "q":
                    sl = oc % 2
                    def fse(s, oc=oc, sl=sl, tok=tok):
                        sT.wait(s, tok)
                        C.sY.wait(s, stok[sl])
                        a = s.activation(stg[:, sl * 512:(sl + 1) * 512], C.pC[0:128, 0:512], AF.Copy, scale=0.125)
                        sS.inc(a)
                    C.s(fse)
                    prev_cp[0] = sS.n
                    cp = sS.n
                    def fy(y, oc=oc, sl=sl, tt=tt, cp=cp):
                        sS.wait(y, cp)
                        C.ld(y, C.qhb[oc * 128:(oc + 1) * 128, tt * 512:(tt + 1) * 512],
                             stg[:, sl * 512:(sl + 1) * 512], sDS)
                        sDS.wait(y)
                        C.sY.inc(y.nop("rl", nofuse=True))
                    C.y(fy)
                    stok[sl] = C.sY.n
                else:
                    def fse(s, oc=oc, tt=tt, tok=tok):
                        sT.wait(s, tok)
                        a = s.activation(kn[:, oc * TP1 + tt * 512 + 1: oc * TP1 + tt * 512 + 513],
                                         C.pC[0:128, 0:512], AF.Copy)
                        sS.inc(a)
                    C.s(fse)
                    prev_cp[0] = sS.n
        AFA.pop()
        AB.pop()
        C.fence()

    vn_o = AB.alloc(NC * sv)
    vn = C.AB_t[:, vn_o:vn_o + NC * sv]
    def f(g):
        b = g.memset(vn[:, :].rearrange("p (k h c) -> p k h c", k=NC, h=16)[:, :, :, 64:65], 1.0)
        sG.inc(b)
    C.g(f)
    AB.push()
    AFA.push()
    xt = C.AF_t[:, AFA.alloc(4096):AFA.stack[-1]]
    xb_o = AB.alloc(4096)
    xb = C.AB_t[:, xb_o:xb_o + 4096]
    ws_o = AB.alloc(4096)
    o0 = 2048
    w2off = 2048
    wsu = [(0, 0), (0, 0)]
    for tt in range(NT):
        lt = C.ln_lite(tt, xt, xb)
        for sl in range(2):
            wsb = ws_o
            def fg(g, sl=sl, pu=tuple(map(max, wsu[0], wsu[1]))):
                sT.wait(g, pu)
                src = bass.AP(C.wqkvT.tensor, C.wqkvT.offset + o0 + sl * 512,
                              [[3072, 128], [128 * 3072, 8], [1, 512]])
                sDW.inc(g.dma_start(out=C.AB_t[:, ws_o:ws_o + 4096].rearrange("p (i o) -> p i o", i=8),
                            in_=src), 16)
                sDW.wait(g)
                sG.inc(g.nop("rl", nofuse=True))
            C.g(fg)
            wtok = sG.n
            if tt == 0:
                C.w1_bias(C.AB_t[:, wsb:wsb + 4096], 512, w2off + sl * 512,
                          C.bqkv[0:1, o0 + sl * 512:o0 + sl * 512 + 512], wait_tok=wtok)
            prev_ev = [sV.n]
            for tc in range(4):
                def pre(t, pv=prev_ev[0], wtok=wtok, lt=lt):
                    sG.wait(t, wtok)
                    sV.wait(t, tuple(map(max, pv, lt)))
                tok = C.proj_tok_tc(xb, lambda ic, wsb=wsb: C.AB_t[:, wsb + ic * 512:wsb + (ic + 1) * 512],
                                    tc, 512, w2off, o0=sl * 512, pre=pre)
                kt = tt * 4 + tc
                def fv(v, kt=kt, sl=sl, tok=tok, pe=prev_ev[0]):
                    sT.wait(v, tok)
                    sV.wait(v, pe)
                    dst = vn[:, kt * sv + 65 * 8 * sl: kt * sv + 65 * 8 * sl + 520]
                    dst = dst.rearrange("p (h c) -> p h c", c=65)[:, :, 0:64]
                    c = v.tensor_copy(dst, C.pC[0:128, 0:512].rearrange("p (h c) -> p h c", c=64))
                    sV.inc(c)
                C.v(fv)
                prev_ev[0] = sV.n
            wsu[sl] = sT.n
    AFA.pop()
    AB.pop()
    C.fence()

    AB.push()
    AFA.push()
    qs_o = AB.alloc(4096)
    qs = C.AB_t[:, qs_o:qs_o + 4096]
    ob_o = AB.alloc(4096)
    obuf = C.AB_t[:, ob_o:ob_o + 4096]
    odd_o = AB.alloc(512)
    odds = C.AB_t[:, odd_o:odd_o + 512]
    wp_o = AB.alloc(1024)
    wp = C.AB_t[:, wp_o:wp_o + 1024]
    meg_o = AFA.alloc(C.MEGW)
    mega = C.AF_t[:, meg_o:meg_o + C.MEGW]
    inv_o = AFA.alloc(512)
    invs = C.AF_t[:, inv_o:inv_o + 512]
    xo_o = AFA.alloc(1024)
    stf_o = AFA.alloc(1024)

    def f(g):
        C.ld(g, C.w2[0:1, 3072:3072 + 1024], C.mrows[1:2, 0:1024])
        C.ld(g, C.w2[1:2, 3072:3072 + 1024], C.bproj[0:1, :])
    C.g(f)
    C.fence()

    head_order = list(range(8, 16)) + list(range(0, 8))
    state = {"cur": None}

    def pre_head(slot, h):
        want = "mask" if h >= 8 else ("hope", h)
        if state["cur"] == want:
            return
        state["cur"] = want
        W = min(C.MEGW, 512 * (slot + 1) + 384)
        if want == "mask":
            tk = C.emit_mega_mask(mega, C.tbl_mf, qshift=0, W=W)
        else:
            off = C.LPAD - 384
            def fg(g, h=h, W=W):
                sT.wait(g)
                sDW.inc(g.dma_start(out=mega[:, 0:W],
                                    in_=C.tbl_hh[h * 128:(h + 1) * 128, off:off + W]), 16)
                sDW.wait(g)
                half = (W + 1) // 2
                for c0 in (0, half):
                    cw = min(half, W - c0)
                    sDW.inc(g.dma_start(out=mega[:, c0:c0 + cw],
                                        in_=C.tbl_hl[h * 128:(h + 1) * 128, off + c0:off + c0 + cw],
                                        accum_op=ALU.add), 16)
                    sDW.wait(g)
                sG.inc(g.nop("rl", nofuse=True))
            C.g(fg)
            tk = sG.n
        def ft(t, tk=tk):
            sG.wait(t, tk)
        C.t(ft)

    def q_load(slot):
        def f(y):
            sT.wait(y)
            for dc in range(8):
                C.ld(y, qs[:, dc * 512:(dc + 1) * 512],
                     C.qhb[dc * 128:(dc + 1) * 128, slot * 512:(slot + 1) * 512], sDX)
            sDX.wait(y)
            C.sY.inc(y.nop("rl", nofuse=True))
        C.y(f)
        tk = C.sY.n
        def ft(t, tk=tk):
            C.sY.wait(t, tk)
        C.t(ft)

    epst = {"tk2": (0, 0), "tk3": (0, 0), "tk4": (0, 0), "odd": (0, 0)}

    def epilogue(slot, h, av, tok_av):
        blk = h // 2
        def fs(s, tok_av=tok_av, p2=epst["tk2"]):
            C.sT2.wait(s, tok_av)
            sT.wait(s, p2)
            a = s.activation(C.rows[64:65, 0:512], av[64:65, :], AF.Ln)
            sS.inc(a)
        C.s(fs)
        tk1 = sS.n
        def ft(t, tk1=tk1, p3=epst["tk3"]):
            sS.wait(t, tk1)
            sS.wait(t, p3)
            m = t.matmul(C.pB[0:64, 0:512], C.cstf[64:65, CF_ONESB:CF_ONESB + 64],
                         C.rows[64:65, 0:512], start=True, stop=True)
            sT.inc(m)
        C.t(ft)
        tk2 = sT.n
        epst["tk2"] = tk2
        def fs2(s, tk2=tk2, p4=epst["tk4"]):
            sT.wait(s, tk2)
            sV.wait(s, p4)
            a = s.activation(invs[0:64, :], C.pB[0:64, 0:512], AF.Exp, scale=-1.0)
            sS.inc(a)
        C.s(fs2)
        tk3 = sS.n
        epst["tk3"] = tk3
        def fv(v, h=h, blk=blk, tk3=tk3, po=epst["odd"]):
            sS.wait(v, tk3)
            if h % 2 == 0:
                dst = obuf[0:64, blk * 512:(blk + 1) * 512]
            else:
                C.sY.wait(v, po)
                dst = odds[0:64, 0:512]
            c = v.tensor_mul(dst, av[0:64, :], invs[0:64, :])
            sV.inc(c)
        C.v(fv)
        tk4 = sV.n
        epst["tk4"] = tk4
        if h % 2 == 1:
            def fy(y, blk=blk, tk4=tk4):
                sV.wait(y, tk4)
                C.ld(y, obuf[64:128, blk * 512:(blk + 1) * 512], odds[0:64, 0:512], sDS)
                sDS.wait(y)
                C.sY.inc(y.nop("rl", nofuse=True))
            C.y(fy)
            epst["odd"] = C.sY.n
        return (sV, sV.n)

    def k_ap(slot, kt, h):
        dc = h // 2
        return kn[64 * (h % 2):64 * (h % 2) + 64, dc * TP1 + 1 + kt * 128: dc * TP1 + 1 + kt * 128 + 128]

    def q_ap(slot, h):
        dc = h // 2
        return qs[64 * (h % 2):64 * (h % 2) + 64, dc * 512:(dc + 1) * 512]

    def vn_ap(kt, h):
        base = kt * sv + 65 * h
        return vn[:, base:base + 65]

    def bias_ap(slot, kt, h):
        c0 = C.mega_c0(slot, kt)
        if h < 8:
            return mega[:, c0:c0 + 512]
        cnt = 4 * (slot + 1)
        if kt >= cnt - 4:
            return mega[:, c0:c0 + 512]
        return None

    cfg = dict(H=16, counts=lambda s: 4 * (s + 1), k_ap=k_ap, q_ap=q_ap,
               vn_ap=vn_ap, bias_ap=bias_ap, phantom=None, eshift=-40.0,
               epilogue=epilogue, pre_head=pre_head, head_order=head_order)

    for slot in range(NT):
        state["cur"] = None
        q_load(slot)
        C.emit_attention_slot(cfg, slot)
        C.out_proj_residual(slot, obuf, wp, wp_o, C.wpT, 3072,
                            C.AF_t[:, xo_o:xo_o + 1024], C.AF_t[:, stf_o:stf_o + 1024], last)
        C.fence()

    AFA.pop()
    AB.pop()
    AB.pop()


def out_proj_residual(C, slot, obuf, wp, wp_o, wsrc, w2off, xo, stf, last):
    sT, sS, sV, sDW, sDX, sDS = C.sT, C.sS, C.sV, C.sDW, C.sDX, C.sDS
    t0 = slot * 512
    prev = {"ev": sV.n, "st": [(0, 0), (0, 0)], "mm": (0, 0), "xv": [(0, 0), (0, 0)]}
    for oc in range(8):
        sl = oc % 2
        def fg(g, oc=oc):
            sT.wait(g, prev["mm"])
            src = bass.AP(wsrc.tensor, wsrc.offset + oc * 128,
                          [[8192, 128], [1024, 8], [1, 128]])
            sDW.inc(g.dma_start(out=wp[:, 0:1024].rearrange("p (j o) -> p j o", j=8),
                        in_=src), 16)
            sDW.wait(g)
            C.sG.inc(g.nop("rl", nofuse=True))
        C.g(fg)
        wtok = C.sG.n
        def fy(y, oc=oc, sl=sl):
            sV.wait(y, prev["xv"][sl])
            C.ld(y, xo[:, sl * 512:sl * 512 + 512], C.xres[oc * 128:(oc + 1) * 128, t0:t0 + 512], sDX)
            sDX.wait(y)
            C.sY.inc(y.nop("rl", nofuse=True))
        C.y(fy)
        xtok = C.sY.n
        def ft(t, oc=oc, wtok=wtok):
            C.sG.wait(t, wtok)
            sV.wait(t, prev["ev"])
            if oc == 0:
                C.sY.wait(t)
            t.matmul(C.pC[0:128, 0:512], C.w2[:, w2off + oc * 128: w2off + (oc + 1) * 128],
                     C.g2[:, 0:512], start=True, stop=False)
            m = None
            for j in range(8):
                m = t.matmul(C.pC[0:128, 0:512], wp[:, j * 128:(j + 1) * 128],
                             obuf[:, j * 512:(j + 1) * 512], start=False, stop=(j == 7))
            sT.inc(m)
        C.t(ft)
        mtok = sT.n
        prev["mm"] = mtok
        def fv(v, sl=sl, mtok=mtok, xtok=xtok, pst=prev["st"][sl]):
            sT.wait(v, mtok)
            C.sY.wait(v, xtok)
            C.sY.wait(v, pst)
            c = v.tensor_add(stf[:, sl * 512:sl * 512 + 512], xo[:, sl * 512:sl * 512 + 512],
                             C.pC[0:128, 0:512])
            sV.inc(c)
        C.v(fv)
        prev["ev"] = sV.n
        prev["xv"][sl] = sV.n
        vtok = sV.n
        def fy2(y, oc=oc, sl=sl, vtok=vtok):
            sV.wait(y, vtok)
            dst = (C.y_out if last else C.xres)
            C.ld(y, dst[oc * 128:(oc + 1) * 128, t0:t0 + 512], stf[:, sl * 512:sl * 512 + 512], sDS)
            sDS.wait(y)
            C.sY.inc(y.nop("rl", nofuse=True))
        C.y(fy2)
        prev["st"][sl] = C.sY.n


KB.sub_attn_b = sub_attn_b
KB.out_proj_residual = out_proj_residual
KB.load_wsec = load_wsec

def prep_inputs(ins, T=4096):
    f32 = np.float32
    d = {}
    d["wkT"] = np.ascontiguousarray(ins["L0_attn_a_W_k"].T.astype(f32))
    d["wvT"] = np.ascontiguousarray(ins["L0_attn_a_W_v"].T.astype(f32))
    wc = ins["L0_attn_a_W_c"].astype(f32)
    wc_re = np.zeros((128, 8 * 1024), dtype=f32)
    for j in range(8):
        for p in range(128):
            wc_re[p, j * 1024:(j + 1) * 1024] = wc[:, 64 * (2 * j + p // 64) + (p % 64)]
    d["wcT"] = wc_re
    d["aprm"] = np.concatenate([
        ins["L0_attn_a_leaky_key_beta"].ravel(), ins["L0_attn_a_key_scale"].ravel(),
        ins["L0_attn_a_v_coef"].ravel(), ins["L0_attn_a_val_scale"].ravel()]
    ).astype(f32).reshape(1, 64)
    d["wqkvT"] = np.ascontiguousarray(ins["L0_attn_b_w_qkv"].T.astype(f32))
    d["bqkv"] = ins["L0_attn_b_b_qkv"].astype(f32).reshape(1, 3072)
    wp = ins["L0_attn_b_w_proj"].astype(f32)
    wp_re = np.zeros((128, 8 * 1024), dtype=f32)
    for j in range(8):
        for p in range(128):
            wp_re[p, j * 1024:(j + 1) * 1024] = wp[:, 64 * (2 * j + p // 64) + (p % 64)]
    d["wpT"] = wp_re
    d["bproj"] = ins["L0_attn_b_b_proj"].astype(f32).reshape(1, 1024)
    d["wqf"] = np.ascontiguousarray(ins["L0_ffn_w_q"].astype(f32))
    d["wkf"] = np.ascontiguousarray(ins["L0_ffn_w_k"].astype(f32))
    d["wgf"] = np.ascontiguousarray(ins["L0_ffn_w_gate"].astype(f32))
    ve = ins["L0_ffn_value_experts"].astype(f32)
    d["wve"] = np.ascontiguousarray(np.transpose(ve, (1, 2, 0, 3)).reshape(8, 1024, 512))
    oe = ins["L0_ffn_output_experts"].astype(f32)
    woe = np.zeros((32, 128, 1024), dtype=f32)
    for e in range(8):
        for j in range(4):
            m = e * 4 + j
            woe[m, 0:64, :] = oe[e, 2 * j]
            woe[m, 64:128, :] = oe[e, 2 * j + 1]
    d["woe"] = woe
    ma, mf, hh, hl = build_tables(T)
    d["tbl_ma"] = ma
    d["tbl_mf"] = mf
    d["tbl_hh"] = hh
    d["tbl_hl"] = hl
    d["cstb_in"] = mk_cstb()
    p = np.arange(128)[:, None]
    j = np.arange(128)[None, :]
    tu = np.zeros((128, 256), dtype=f32)
    tu[:, 0:128] = np.maximum(j - p, 0).astype(f32)
    tu[:, 128:256] = (j >= p).astype(f32)
    d["tblu"] = tu
    mr = np.zeros((2, 4096), dtype=np.float32); mr[0, :] = 1.0
    d["mrows"] = mr
    d["cstf_in"] = mk_cstf()
    d["selc_in"] = mk_selectors()
    return d


_SUBS_ALL = ("attn_a", "ffn1", "attn_b", "ffn2")


def run_device(ins, T=4096, subs=_SUBS_ALL, n_b=None):
    x = np.asarray(ins["L0_attn_a_x"], dtype=np.float32)
    B = x.shape[0]
    n_b = n_b or B
    shared = prep_inputs(ins, T)
    kb = KB(T, subs)
    nc = kb.build()
    in_maps = []
    for c in range(8):
        b = c % n_b
        m = dict(shared)
        m["x_feat"] = np.ascontiguousarray(x[b].T)
        in_maps.append(m)
    res = run_bass_kernel_spmd(nc, in_maps, core_ids=list(range(8)))
    outs = []
    for b in range(B):
        outs.append(np.ascontiguousarray(res.results[b]["y"].T))
    return np.stack(outs, axis=0)


def kernel(**inputs):
    return run_device(inputs, T=4096, subs=_SUBS_ALL)

def sub_ffn(C, last):
    T, NT, NC, TP1 = C.T, C.NT, C.NC, C.TP1
    sv = 65 * 8
    AB, AFA = C.AB, C.AFA
    sT, sS, sV, sG, sDW, sDX, sDS = C.sT, C.sS, C.sV, C.sG, C.sDW, C.sDX, C.sDS

    AB.push()
    knf_o = AB.alloc(4 * TP1)
    knf = C.AB_t[:, knf_o:knf_o + 4 * TP1]
    vnf_o = AB.alloc(NC * sv)
    vnf = C.AB_t[:, vnf_o:vnf_o + NC * sv]
    gmT_o = AB.alloc(NT * 512)
    def f(g):
        g.memset(knf[:, :].rearrange("p (c t) -> p c t", c=4)[:, :, 0:1], 0.0)
        a = g.memset(vnf[:, :].rearrange("p (k h c) -> p k h c", k=NC, h=8)[:, :, :, 64:65], 1.0)
        sG.inc(a)
    C.g(f)
    C.fence()

    AB.push()
    AFA.push()
    xt = C.AF_t[:, AFA.alloc(4096):AFA.stack[-1]]
    gt_o = AFA.alloc(640)
    mt_o = AFA.alloc(512)
    tmp_o = AFA.alloc(512)
    xb_o = AB.alloc(4096)
    xb = C.AB_t[:, xb_o:xb_o + 4096]
    wq_o = AB.alloc(4096)
    wk_o = AB.alloc(4096)
    wg_o = AB.alloc(512)
    wv_o = AB.alloc(8192)
    stg_o = AB.alloc(1024)
    stg = C.AB_t[:, stg_o:stg_o + 1024]

    load_wsec(C, wq_o, C.wqf, 512, 512)
    load_wsec(C, wk_o, C.wkf, 512, 512)
    load_wsec(C, wg_o, C.wgf, 64, 64)
    C.w1_bias(C.AB_t[:, wq_o:wq_o + 4096], 512, 0, None)
    C.w1_bias(C.AB_t[:, wk_o:wk_o + 4096], 512, 512, None)
    C.w1_bias(C.AB_t[:, wg_o:wg_o + 512], 64, 1024, None)

    wvu = [(0, 0), (0, 0)]
    for tt in range(NT):
        lt = C.ln_full(tt, xt, xb, store=False)
        prev_cp = [sS.n]
        for oc in range(4):
            def pre(t, pc=prev_cp[0], lt=lt):
                sS.wait(t, pc)
                sV.wait(t, lt)
            tok = C.proj_feat_oc(xb, lambda ic, oc=oc: C.AB_t[:, wq_o + ic * 512 + oc * 128: wq_o + ic * 512 + (oc + 1) * 128],
                                 oc, 0, pre=pre)
            sl = oc % 2
            def fse(s, sl=sl, tok=tok):
                sT.wait(s, tok)
                C.sY.wait(s)
                a = s.activation(stg[:, sl * 512:(sl + 1) * 512], C.pC[0:128, 0:512], AF.Copy, scale=0.125)
                sS.inc(a)
            C.s(fse)
            prev_cp[0] = sS.n
            cp = sS.n
            def fy(y, oc=oc, sl=sl, tt=tt, cp=cp):
                sS.wait(y, cp)
                C.ld(y, C.qhb[oc * 128:(oc + 1) * 128, tt * 512:(tt + 1) * 512],
                     stg[:, sl * 512:(sl + 1) * 512], sDS)
                sDS.wait(y)
                C.sY.inc(y.nop("rl", nofuse=True))
            C.y(fy)
        for oc in range(4):
            def pre(t, pc=prev_cp[0]):
                sS.wait(t, pc)
            tok = C.proj_feat_oc(xb, lambda ic, oc=oc: C.AB_t[:, wk_o + ic * 512 + oc * 128: wk_o + ic * 512 + (oc + 1) * 128],
                                 oc, 512, pre=pre)
            def fse(s, oc=oc, tt=tt, tok=tok):
                sT.wait(s, tok)
                a = s.activation(knf[:, oc * TP1 + tt * 512 + 1: oc * TP1 + tt * 512 + 513],
                                 C.pC[0:128, 0:512], AF.Copy)
                sS.inc(a)
            C.s(fse)
            prev_cp[0] = sS.n
        gm_toks = {}
        for tc in range(4):
            def pre(t, pc=prev_cp[0]):
                sS.wait(t, pc)
            tok = C.proj_tok_tc(xb, lambda ic: C.AB_t[:, wg_o + ic * 64: wg_o + (ic + 1) * 64],
                                tc, 64, 1024, pre=pre)
            gbase = gt_o + tc * 160
            def fse(s, tok=tok, gbase=gbase):
                sT.wait(s, tok)
                a = s.activation(C.AF_t[:, gbase:gbase + 64], C.pC[0:128, 0:64], AF.Sigmoid)
                sS.inc(a)
            C.s(fse)
            prev_cp[0] = sS.n
            stok = sS.n
            def fv(v, gbase=gbase, stok=stok):
                sS.wait(v, stok)
                sV.wait(v)
                g3 = C.AF_t[:, gbase:gbase + 64].rearrange("p (h e) -> p h e", h=8)
                a = v.tensor_reduce(C.AF_t[:, gbase + 128:gbase + 136], g3,
                                    mybir.AxisListType.X, ALU.max)
                tk = sV.inc(a)
                sV.wait(v, tk)
                mb = bass.AP(C.AF_t.ap().tensor, C.AF_t.ap().offset + gbase + 128,
                             [C.AF_t.ap().ap[0], [1, 8], [0, 8]])
                a = v.tensor_tensor(C.AF_t[:, gbase + 64:gbase + 128],
                                    C.AF_t[:, gbase:gbase + 64], mb, ALU.is_equal)
                tk = sV.inc(a)
                sV.wait(v, tk)
                c = v.tensor_mul(C.AF_t[:, gbase + 64:gbase + 128],
                                 C.AF_t[:, gbase + 64:gbase + 128], C.AF_t[:, gbase:gbase + 64])
                gm_toks[gbase] = sV.inc(c)
            C.v(fv)
        def fgf(t, pc=prev_cp[0], lt=lt):
            sS.wait(t, pc)
            sV.wait(t, lt)
            t.matmul(C.pC[0:64, 0:512], C.w2[:, 1024:1088], C.g2[:, 0:512], start=True, stop=False)
            m = None
            for ic in range(8):
                m = t.matmul(C.pC[0:64, 0:512], C.AB_t[:, wg_o + ic * 64: wg_o + (ic + 1) * 64],
                             xb[:, ic * 512:(ic + 1) * 512], start=False, stop=(ic == 7))
            sT.inc(m)
        C.t(fgf)
        tkgf = sT.n
        def fse(s, tkgf=tkgf):
            sT.wait(s, tkgf)
            a = s.activation(C.AF_t[0:64, tmp_o:tmp_o + 512], C.pC[0:64, 0:512], AF.Sigmoid)
            sS.inc(a)
        C.s(fse)
        tk_gf = sS.n
        prev_cp[0] = sS.n
        def ftr(t, tk_gf=tk_gf, gm_toks=gm_toks):
            for tc in range(4):
                sV.wait(t, gm_toks[gt_o + tc * 160])
            sS.wait(t, tk_gf)
            m = None
            for tc in range(4):
                m = t.transpose(C.pC[0:8, 128 * tc:128 * tc + 128],
                                C.AF_t[:, gt_o + tc * 160 + 128: gt_o + tc * 160 + 136],
                                C.cstf[:, CF_ID:CF_ID + 128])
            sT.inc(m)
        C.t(ftr)
        tktr = sT.n
        def fmt(v, tktr=tktr):
            sT.wait(v, tktr)
            c = v.tensor_copy(C.AF_t[0:8, mt_o:mt_o + 512], C.pC[0:8, 0:512])
            sV.inc(c)
        C.v(fmt)
        tkmt = sV.n
        def fmb(t, tkmt=tkmt):
            sV.wait(t, tkmt)
            m = t.matmul(C.pC[0:64, 0:512], C.cstf[0:8, CF_E8:CF_E8 + 64],
                         C.AF_t[0:8, mt_o:mt_o + 512], start=True, stop=True)
            sT.inc(m)
        C.t(fmb)
        tkmb = sT.n
        def fgm(v, tkmb=tkmb, tt=tt):
            sT.wait(v, tkmb)
            a = v.tensor_tensor(C.AF_t[0:64, mt_o:mt_o + 512],
                                C.AF_t[0:64, tmp_o:tmp_o + 512], C.pC[0:64, 0:512], ALU.is_equal)
            tk = sV.inc(a)
            sV.wait(v, tk)
            c = v.tensor_mul(C.AB_t[0:64, gmT_o + tt * 512: gmT_o + (tt + 1) * 512],
                             C.AF_t[0:64, mt_o:mt_o + 512], C.AF_t[0:64, tmp_o:tmp_o + 512])
            sV.inc(c)
        C.v(fgm)
        for hh in range(8):
            sl = hh % 2
            wvb = wv_o + sl * 4096
            def fg(g, hh=hh, pu=wvu[sl]):
                sT.wait(g, pu)
                src = bass.AP(C.wve.tensor, C.wve.offset + hh * 1024 * 512,
                              [[512, 128], [128 * 512, 8], [1, 512]])
                sDW.inc(g.dma_start(out=C.AB_t[:, wv_o + (hh % 2) * 4096:wv_o + (hh % 2) * 4096 + 4096].rearrange("p (i o) -> p i o", i=8),
                        in_=src), 16)
                sDW.wait(g)
                sG.inc(g.nop("rl", nofuse=True))
            C.g(fg)
            C.w1_bias(C.AB_t[:, wvb:wvb + 4096], 512, 2048 + sl * 512, None)
            for tc in range(4):
                def pre(t, pc=prev_cp[0]):
                    sS.wait(t, pc)
                    sV.wait(t)
                tok = C.proj_tok_tc(xb, lambda ic, wvb=wvb: C.AB_t[:, wvb + ic * 512:wvb + (ic + 1) * 512],
                                    tc, 512, 2048 + sl * 512, pre=pre)
                kt = tt * 4 + tc
                gbase = gt_o + tc * 160
                def fcv(v, hh=hh, kt=kt, tok=tok, gbase=gbase):
                    sT.wait(v, tok)
                    sV.wait(v)
                    t3 = C.AF_t[:, tmp_o:tmp_o + 512]
                    gmb = bass.AP(C.AF_t.ap().tensor, C.AF_t.ap().offset + gbase + 64 + 8 * hh,
                                  [C.AF_t.ap().ap[0], [1, 8], [0, 64]])
                    a = v.tensor_tensor(t3, C.pC[0:128, 0:512], gmb, ALU.mult)
                    tk = sV.inc(a)
                    sV.wait(v, tk)
                    a = v.tensor_add(t3[:, 0:256], t3[:, 0:256], t3[:, 256:512])
                    tk = sV.inc(a)
                    sV.wait(v, tk)
                    a = v.tensor_add(t3[:, 0:128], t3[:, 0:128], t3[:, 128:256])
                    tk = sV.inc(a)
                    sV.wait(v, tk)
                    c = v.tensor_add(vnf[:, kt * sv + 65 * hh: kt * sv + 65 * hh + 64],
                                     t3[:, 0:64], t3[:, 64:128])
                    sV.inc(c)
                C.v(fcv)
            wvu[sl] = sT.n
    AFA.pop()
    AB.pop()
    C.fence()

    AB.push()
    AFA.push()
    qs_o = AB.alloc(2048)
    qs = C.AB_t[:, qs_o:qs_o + 2048]
    ob_o = AB.alloc(2048)
    obuf = C.AB_t[:, ob_o:ob_o + 2048]
    odd_o = AB.alloc(512)
    odds = C.AB_t[:, odd_o:odd_o + 512]
    selc_o = AB.alloc(4096)
    selc = C.AB_t[:, selc_o:selc_o + 4096]
    ub_o = AB.alloc(512)
    ub = C.AB_t[:, ub_o:ub_o + 512]
    oe_o = AB.alloc(32 * 512)
    woe_o = AB.alloc(4096)
    meg_o = AFA.alloc(C.MEGW)
    mega = C.AF_t[:, meg_o:meg_o + C.MEGW]
    srow_o = AFA.alloc(512)
    srow = C.AF_t[:, srow_o:srow_o + 512]
    xo_o = AFA.alloc(1024)
    stf_o = AFA.alloc(1024)

    def f(g):
        C.ld(g, C.AB_t[0:64, selc_o:selc_o + 4096], C.selc_in[:, :])
    C.g(f)
    C.emit_mega_mask(mega, C.tbl_mf, qshift=0)
    C.fence()

    epst = {"gmm": (0, 0)}

    def epilogue(slot, h, av, tok_av):
        blk = h // 2
        def fs(s, h=h, blk=blk, tok_av=tok_av, pg=epst["gmm"]):
            C.sT2.wait(s, tok_av)
            sT.wait(s, pg)
            if h % 2 == 0:
                dst = obuf[0:64, blk * 512:(blk + 1) * 512]
            else:
                C.sY.wait(s)
                dst = odds[0:64, 0:512]
            s.activation(dst, av[0:64, :], AF.Copy)
            a = s.activation(srow[64:65, 0:512], av[64:65, :], AF.Copy)
            sS.inc(a)
        C.s(fs)
        tk1 = sS.n
        def ft(t, h=h, tk1=tk1):
            sS.wait(t, tk1)
            m = t.matmul(C.pB[0:8, 0:512], C.cstf[64:65, CF_ID + 64 - h:CF_ID + 72 - h],
                         srow[64:65, 0:512], start=(h == 0), stop=(h == 7))
            sT.inc(m)
        C.t(ft)
        epst["gmm"] = sT.n
        if h % 2 == 1:
            def fy(y, blk=blk, tk1=tk1):
                sS.wait(y, tk1)
                C.ld(y, obuf[64:128, blk * 512:(blk + 1) * 512], odds[0:64, 0:512], sDS)
                sDS.wait(y)
                C.sY.inc(y.nop("rl", nofuse=True))
            C.y(fy)
        return (sS, tk1)

    def q_load(slot):
        def f(y):
            sT.wait(y)
            for dc in range(4):
                C.ld(y, qs[:, dc * 512:(dc + 1) * 512],
                     C.qhb[dc * 128:(dc + 1) * 128, slot * 512:(slot + 1) * 512], sDX)
            sDX.wait(y)
            C.sY.inc(y.nop("rl", nofuse=True))
        C.y(f)
        tkq = C.sY.n
        def ft(t, tkq=tkq):
            C.sY.wait(t, tkq)
        C.t(ft)

    def k_ap(slot, kt, h):
        dc = h // 2
        return knf[64 * (h % 2):64 * (h % 2) + 64, dc * TP1 + 1 + kt * 128: dc * TP1 + 1 + kt * 128 + 128]

    def q_ap(slot, h):
        dc = h // 2
        return qs[64 * (h % 2):64 * (h % 2) + 64, dc * 512:(dc + 1) * 512]

    def vn_ap(kt, h):
        base = kt * sv + 65 * h
        return vnf[:, base:base + 65]

    def bias_ap(slot, kt, h):
        cnt = 4 * (slot + 1)
        if kt >= cnt - 4:
            c0 = C.mega_c0(slot, kt)
            return mega[:, c0:c0 + 512]
        return None

    cfg = dict(H=8, counts=lambda s: 4 * (s + 1), k_ap=k_ap, q_ap=q_ap,
               vn_ap=vn_ap, bias_ap=bias_ap, phantom=None, eshift=0.0,
               epilogue=epilogue)

    for slot in range(NT):
        q_load(slot)
        C.emit_attention_slot(cfg, slot)
        def fr8(s):
            sT.wait(s)
            a = s.activation(C.rows[0:8, 0:512], C.pB[0:8, 0:512], AF.Ln)
            tk = sS.inc(a)
            sS.wait(s, tk)
            a = s.activation(C.rows[0:8, 512:1024], C.rows[0:8, 0:512], AF.Exp, scale=-1.0)
            sS.inc(a)
        C.s(fr8)
        tkr8 = sS.n
        def frb(t, tkr8=tkr8):
            sS.wait(t, tkr8)
            sV.wait(t)
            m = t.matmul(C.pC[0:64, 0:512], C.cstf[0:8, CF_E8:CF_E8 + 64],
                         C.rows[0:8, 512:1024], start=True, stop=True)
            sT.inc(m)
        C.t(frb)
        tkrb = sT.n
        def fu(v, slot=slot, tkrb=tkrb):
            sT.wait(v, tkrb)
            c = v.tensor_mul(ub[0:64, 0:512], C.AB_t[0:64, gmT_o + slot * 512: gmT_o + (slot + 1) * 512],
                             C.pC[0:64, 0:512])
            sV.inc(c)
        C.v(fu)
        tku = sV.n
        prev_oe = [tku]
        for mm_i in range(32):
            def fub(t, mm_i=mm_i, po=prev_oe[0]):
                sV.wait(t, po)
                m = t.matmul(C.pC[0:128, 0:512], C.AB_t[0:64, selc_o + 128 * mm_i: selc_o + 128 * mm_i + 128],
                             ub[0:64, 0:512], start=True, stop=True)
                sT.inc(m)
            C.t(fub)
            tub = sT.n
            jj = mm_i % 4
            def foe(v, mm_i=mm_i, jj=jj, tub=tub):
                sT.wait(v, tub)
                if mm_i == 0:
                    C.sY.wait(v)
                c = v.tensor_mul(C.AB_t[:, oe_o + mm_i * 512: oe_o + (mm_i + 1) * 512],
                                 obuf[:, jj * 512:(jj + 1) * 512], C.pC[0:128, 0:512])
                sV.inc(c)
            C.v(foe)
            prev_oe[0] = sV.n
        tkoe = sV.n
        t0 = slot * 512
        prevv = {"mm": (0, 0), "ev": tkoe, "st": [(0, 0), (0, 0)], "xv": [(0, 0), (0, 0)]}
        for oc in range(8):
            sl = oc % 2
            def fgw(g, oc=oc):
                sT.wait(g, prevv["mm"])
                src = bass.AP(C.woe.tensor, C.woe.offset + oc * 128,
                              [[1024, 128], [128 * 1024, 32], [1, 128]])
                sDW.inc(g.dma_start(out=C.AB_t[:, woe_o:woe_o + 4096].rearrange("p (m o) -> p m o", m=32),
                        in_=src), 16)
                sDW.wait(g)
                sG.inc(g.nop("rl", nofuse=True))
            C.g(fgw)
            wtok = sG.n
            def fy(y, oc=oc, sl=sl):
                sV.wait(y, prevv["xv"][sl])
                C.ld(y, C.AF_t[:, xo_o + sl * 512: xo_o + sl * 512 + 512],
                     C.xres[oc * 128:(oc + 1) * 128, t0:t0 + 512], sDX)
                sDX.wait(y)
                C.sY.inc(y.nop("rl", nofuse=True))
            C.y(fy)
            xtok = C.sY.n
            def ft(t, oc=oc, wtok=wtok):
                sG.wait(t, wtok)
                sV.wait(t, prevv["ev"])
                m = None
                for mm_i in range(32):
                    m = t.matmul(C.pC[0:128, 0:512], C.AB_t[:, woe_o + mm_i * 128: woe_o + (mm_i + 1) * 128],
                                 C.AB_t[:, oe_o + mm_i * 512: oe_o + (mm_i + 1) * 512],
                                 start=(mm_i == 0), stop=(mm_i == 31))
                sT.inc(m)
            C.t(ft)
            prevv["mm"] = sT.n
            mtok = sT.n
            def fv(v, sl=sl, mtok=mtok, pst=prevv["st"][sl], xtok=xtok):
                sT.wait(v, mtok)
                C.sY.wait(v, xtok)
                C.sY.wait(v, pst)
                c = v.tensor_add(C.AF_t[:, stf_o + sl * 512: stf_o + sl * 512 + 512],
                                 C.AF_t[:, xo_o + sl * 512: xo_o + sl * 512 + 512], C.pC[0:128, 0:512])
                sV.inc(c)
            C.v(fv)
            prevv["ev"] = sV.n
            prevv["xv"][sl] = sV.n
            vtok = sV.n
            def fy2(y, oc=oc, sl=sl, vtok=vtok):
                sV.wait(y, vtok)
                dst = (C.y_out if last else C.xres)
                C.ld(y, dst[oc * 128:(oc + 1) * 128, t0:t0 + 512],
                     C.AF_t[:, stf_o + sl * 512: stf_o + sl * 512 + 512], sDS)
                sDS.wait(y)
                C.sY.inc(y.nop("rl", nofuse=True))
            C.y(fy2)
            prevv["st"][sl] = C.sY.n
        C.fence()

    AFA.pop()
    AB.pop()
    AB.pop()


KB.sub_ffn = sub_ffn

def sub_attn_a(C, last):
    T, NT, NC, TP1 = C.T, C.NT, C.NC, C.TP1
    sv = 65 * 16
    AB, AFA = C.AB, C.AFA
    sT, sS, sV, sG, sY = C.sT, C.sS, C.sV, C.sG, C.sY
    sDW, sDX, sDS = C.sDW, C.sDX, C.sDS
    AFT = C.AF_t

    AB.push()
    AFA.push()
    pr_o = AFA.alloc(64)
    dr_o = AFA.alloc(64)
    sc_o = AFA.alloc(64)
    ap_o = AFA.alloc(64)
    v16_o = AFA.alloc(1)

    def f(g):
        C.ld(g, AFT[0:1, pr_o:pr_o + 64], C.aprm[0:1, :])
        sDW.wait(g)
        sG.inc(g.nop("rl", nofuse=True))
    C.g(f)
    tok_pr = sG.n

    def f(s, tok_pr=tok_pr):
        sG.wait(s, tok_pr)
        a = s.activation(AFT[0:1, sc_o:sc_o + 16], AFT[0:1, pr_o:pr_o + 16], AF.Abs)
        tk = sS.inc(a)
        sS.wait(s, tk)
        a = s.activation(AFT[0:1, dr_o:dr_o + 16], AFT[0:1, sc_o:sc_o + 16], AF.Copy, scale=-10.0)
        sS.inc(a)
        a = s.activation(AFT[0:1, sc_o + 16:sc_o + 32], AFT[0:1, pr_o + 16:pr_o + 32], AF.Copy, scale=10.0)
        sS.inc(a)
        a = s.activation(AFT[0:1, dr_o + 32:dr_o + 48], AFT[0:1, pr_o + 32:pr_o + 48], AF.Copy)
        sS.inc(a)
        a = s.activation(AFT[0:1, sc_o + 48:sc_o + 64], AFT[0:1, pr_o + 48:pr_o + 64], AF.Exp, scale=10.0)
        sS.inc(a)
    C.s(f)
    tk_s1 = sS.n

    def f(v, tk_s1=tk_s1):
        sS.wait(v, tk_s1)
        ksm = bass.AP(C.cstf.ap().tensor, C.cstf.ap().offset + CF_CONST + 3,
                      [[C.cstf.ap().ap[0][0], 1], [0, 16]])
        a = v.tensor_tensor(AFT[0:1, sc_o + 32:sc_o + 48], AFT[0:1, sc_o + 16:sc_o + 32],
                            ksm, ALU.min)
        sV.inc(a)
        one = bass.AP(C.cstf.ap().tensor, C.cstf.ap().offset + CF_CONST + 4,
                      [[C.cstf.ap().ap[0][0], 1], [0, 16]])
        a = v.tensor_tensor(AFT[0:1, dr_o + 48:dr_o + 64], one,
                            AFT[0:1, pr_o + 32:pr_o + 48], ALU.subtract)
        sV.inc(a)
    C.v(f)
    tk_v1 = sV.n

    def f(s, tk_v1=tk_v1):
        sV.wait(s, tk_v1)
        a = s.activation(AFT[0:1, dr_o + 16:dr_o + 32], AFT[0:1, sc_o + 32:sc_o + 48], AF.Exp)
        sS.inc(a)
    C.s(f)
    tk_s2 = sS.n

    def fmm(t, tk_s2=tk_s2, tk_v1=tk_v1):
        sS.wait(t, tk_s2)
        sV.wait(t, tk_v1)
        m = t.matmul(C.pB[0:128, 0:64], C.cstf[0:1, CF_ONESB:CF_ONESB + 128],
                     AFT[0:1, dr_o:dr_o + 64], start=True, stop=True)
        sT.inc(m)
        m = t.matmul(C.pB[0:16, 64:65], AFT[0:1, sc_o + 48:sc_o + 64],
                     C.cstf[0:1, CF_CONST + 4:CF_CONST + 5], start=True, stop=True)
        sT.inc(m)
    C.t(fmm)
    tkmm = sT.n

    def fev(s, tkmm=tkmm):
        sT.wait(s, tkmm)
        a = s.activation(AFT[:, ap_o:ap_o + 64], C.pB[0:128, 0:64], AF.Copy)
        sS.inc(a)
        a = s.activation(AFT[0:16, v16_o:v16_o + 1], C.pB[0:16, 64:65], AF.Copy)
        sS.inc(a)
    C.s(fev)
    tok_ap = sS.n

    AB.push()
    AFA.push()
    du_o = AB.alloc(128)
    ut_o = AB.alloc(128)
    dcb_o = AB.alloc(2048)
    car_o = AB.alloc(1024)
    kn_o = AB.alloc(8 * TP1)
    kn = C.AB_t[:, kn_o:kn_o + 8 * TP1]
    wsk_o = AB.alloc(8192)
    wsk = C.AB_t[:, wsk_o:wsk_o + 8192]
    xb_o = AB.alloc(4096)
    xb = C.AB_t[:, xb_o:xb_o + 4096]
    kt_o = AB.alloc(1024)
    ktok = C.AB_t[:, kt_o:kt_o + 1024]
    cr2_o = AB.alloc(1024)
    xt = AFT[:, AFA.alloc(4096):AFA.stack[-1]]
    kns_o = AFA.alloc(1024)
    kns = AFT[:, kns_o:kns_o + 1024]
    lt1_o = AFA.alloc(128)
    lt2_o = AFA.alloc(128)
    sq_o = AFA.alloc(1024)
    nr_o = AFA.alloc(96)

    def f(g):
        a = g.memset(kn[:, :].rearrange("p (c t) -> p c t", c=8)[:, :, 0:1], 0.0)
        sG.inc(a)
    C.g(f)

    def f(g):
        C.ld(g, C.AB_t[:, du_o:du_o + 128], C.tblu[:, 0:128])
        C.ld(g, C.AB_t[:, ut_o:ut_o + 128], C.tblu[:, 128:256])
        sDW.wait(g)
        sG.inc(g.nop("rl", nofuse=True))
    C.g(f)
    tk4 = sG.n

    for h in range(16):
        def fv(v, h=h, tk4=tk4):
            sG.wait(v, tk4)
            sS.wait(v, tok_ap)
            sV.wait(v)
            sS.wait(v)
            nb = bass.AP(AFT.ap().tensor, AFT.ap().offset + ap_o + h,
                         [AFT.ap().ap[0], [0, 128]])
            a = v.tensor_tensor(AFT[:, lt1_o:lt1_o + 128], C.AB_t[:, du_o:du_o + 128],
                                nb, ALU.mult)
            sV.inc(a)
        C.v(fv)
        tkm = sV.n
        def fs(s, h=h, tkm=tkm):
            sV.wait(s, tkm)
            a = s.activation(AFT[:, lt2_o:lt2_o + 128], AFT[:, lt1_o:lt1_o + 128], AF.Exp)
            sS.inc(a)
        C.s(fs)
        tke = sS.n
        def fv2(v, h=h, tke=tke):
            sS.wait(v, tke)
            a = v.tensor_mul(C.pt[:, 128 * h:128 * h + 128], AFT[:, lt2_o:lt2_o + 128],
                             C.AB_t[:, ut_o:ut_o + 128])
            tk = sV.inc(a)
            sV.wait(v, tk)
            nb0 = bass.AP(AFT.ap().tensor, AFT.ap().offset + ap_o + h,
                          [[AFT.ap().ap[0][0], 1], [0, 128]])
            a = v.tensor_tensor(AFT[0:1, lt1_o:lt1_o + 128], C.cstb[0:1, CB_P1ROW:CB_P1ROW + 128],
                                nb0, ALU.mult)
            sV.inc(a)
        C.v(fv2)
        tkd = sV.n
        def fs2(s, h=h, tkd=tkd):
            sV.wait(s, tkd)
            a = s.activation(C.AB_t[0:1, dcb_o + 128 * h:dcb_o + 128 * h + 128],
                             AFT[0:1, lt1_o:lt1_o + 128], AF.Exp)
            sS.inc(a)
        C.s(fs2)

    wtokK = load_wsec(C, wsk_o, C.wkT, 1024, 1024)
    C.w1_bias(wsk, 1024, 0, None)
    C.fence()

    st = {"evac": (0, 0), "knmul": (0, 0), "scan": (0, 0), "knev": (0, 0),
          "carev": (0, 0)}
    for tt in range(NT):
        lt = C.ln_full(tt, xt, xb, store=True)
        for tc in range(4):
            c = 4 * tt + tc
            for half in range(2):
                def pre(t, pc=st["evac"], lt=lt, wtokK=wtokK):
                    sG.wait(t, wtokK)
                    sS.wait(t, pc)
                    sV.wait(t, lt)
                tok = C.proj_tok_tc(xb, lambda ic, half=half: wsk[:, ic * 1024 + half * 512: ic * 1024 + half * 512 + 512],
                                    tc, 512, 0, o0=half * 512, pre=pre)
                def fse(s, half=half, tok=tok, sc=st["scan"]):
                    sT.wait(s, tok)
                    sT.wait(s, sc)
                    a = s.activation(ktok[:, half * 512:(half + 1) * 512], C.pC[0:128, 0:512], AF.Copy)
                    sS.inc(a)
                C.s(fse)
                st["evac"] = sS.n

            ktokTok = sS.n
            def ftsc(t, c=c, ktokTok=ktokTok, pk=st["knmul"]):
                sS.wait(t, ktokTok)
                sV.wait(t, pk)
                m = None
                for h in range(16):
                    m = t.matmul(C.pL[:, 64 * h:64 * h + 64], C.pt[:, 128 * h:128 * h + 128],
                                 ktok[:, 64 * h:64 * h + 64], start=True, stop=(c == 0))
                    if c > 0:
                        m = t.matmul(C.pL[:, 64 * h:64 * h + 64],
                                     C.AB_t[0:1, dcb_o + 128 * h:dcb_o + 128 * h + 128],
                                     C.AB_t[0:1, car_o + 64 * h:car_o + 64 * h + 64],
                                     start=False, stop=True)
                sT.inc(m)
            C.t(ftsc)
            st["scan"] = sT.n
            scanTok = sT.n
            if c < NC - 1:
                def fcr(s, scanTok=scanTok, pce=st["carev"]):
                    sT.wait(s, scanTok)
                    sS.wait(s, pce)
                    s.activation(C.AB_t[64:128, cr2_o:cr2_o + 512], C.pL[64:128, 0:512], AF.Copy)
                    a = s.activation(C.AB_t[64:128, cr2_o + 512:cr2_o + 1024],
                                     C.pL[64:128, 512:1024], AF.Copy)
                    tk = sS.inc(a)
                    sS.wait(s, tk)
                    C.sDA.inc(s.dma_start(out=C.AB_t[0:1, car_o:car_o + 1024],
                                          in_=C.AB_t[127:128, cr2_o:cr2_o + 1024]), 16)
                    C.sDA.wait(s)
                    sS.inc(s.nop("rl", nofuse=True))
                C.s(fcr)
                st["carev"] = sS.n
            def fsq(s, scanTok=scanTok):
                sT.wait(s, scanTok)
                a = s.activation(AFT[:, sq_o:sq_o + 512], C.pL[:, 0:512], AF.Square)
                sS.inc(a)
                a = s.activation(AFT[:, sq_o + 512:sq_o + 1024], C.pL[:, 512:1024], AF.Square)
                sS.inc(a)
            C.s(fsq)
            tksq = sS.n
            def fnr(v, tksq=tksq):
                sS.wait(v, tksq)
                sV.wait(v)
                g3 = AFT[:, sq_o:sq_o + 1024].rearrange("p (h d) -> p h d", h=16)
                a = v.tensor_reduce(AFT[:, nr_o:nr_o + 16], g3, mybir.AxisListType.X, ALU.add)
                sV.inc(a)
            C.v(fnr)
            tkr = sV.n
            def fns(s, tkr=tkr):
                sV.wait(s, tkr)
                a = s.activation(AFT[:, nr_o + 16:nr_o + 32], AFT[:, nr_o:nr_o + 16], AF.Sqrt)
                sS.inc(a)
            C.s(fns)
            tkn1 = sS.n
            def fne(v, tkn1=tkn1):
                sS.wait(v, tkn1)
                eps = bass.AP(C.cstf.ap().tensor, C.cstf.ap().offset + CF_CONST + 1,
                              [C.cstf.ap().ap[0], [0, 16]])
                a = v.tensor_tensor(AFT[:, nr_o + 32:nr_o + 48], AFT[:, nr_o + 16:nr_o + 32],
                                    eps, ALU.add)
                sV.inc(a)
            C.v(fne)
            tkn2 = sV.n
            def fnl(s, tkn2=tkn2):
                sV.wait(s, tkn2)
                a = s.activation(AFT[:, nr_o + 48:nr_o + 64], AFT[:, nr_o + 32:nr_o + 48], AF.Ln)
                tk = sS.inc(a)
                sS.wait(s, tk)
                a = s.activation(AFT[:, nr_o + 64:nr_o + 80], AFT[:, nr_o + 48:nr_o + 64], AF.Exp, scale=-1.0)
                sS.inc(a)
            C.s(fnl)
            tkn3 = sS.n
            def fni(v, tkn3=tkn3, c=c):
                sS.wait(v, tkn3)
                a = v.tensor_mul(AFT[:, nr_o + 80:nr_o + 96], AFT[:, nr_o + 64:nr_o + 80],
                                 AFT[:, ap_o + 16:ap_o + 32])
                tk = sV.inc(a)
                sV.wait(v, tk)
                lo = bass.AP(AFT.ap().tensor, AFT.ap().offset + nr_o + 80,
                             [AFT.ap().ap[0], [1, 8], [0, 64]])
                a = v.tensor_tensor(kns[:, 0:512], C.pL[:, 0:512], lo, ALU.mult)
                tk = sV.inc(a)
                sV.wait(v, tk)
                hi = bass.AP(AFT.ap().tensor, AFT.ap().offset + nr_o + 88,
                             [AFT.ap().ap[0], [1, 8], [0, 64]])
                a = v.tensor_tensor(kns[:, 512:1024], C.pL[:, 512:1024], hi, ALU.mult)
                sV.inc(a)
            C.v(fni)
            st["knmul"] = sV.n
            tkkm = sV.n
            def ftr(t, tkkm=tkkm, pkv=st["knev"]):
                sV.wait(t, tkkm)
                sS.wait(t, pkv)
                m = None
                for dc in range(8):
                    m = t.transpose(C.pA[:, 128 * dc:128 * dc + 128], kns[:, 128 * dc:128 * dc + 128],
                                    C.cstf[:, CF_ID:CF_ID + 128])
                sT.inc(m)
            C.t(ftr)
            tktr = sT.n
            def fke(s, c=c, tktr=tktr):
                sT.wait(s, tktr)
                a = None
                for dc in range(8):
                    a = s.activation(kn[:, dc * TP1 + 1 + 128 * c: dc * TP1 + 1 + 128 * c + 128],
                                     C.pA[:, 128 * dc:128 * dc + 128], AF.Copy)
                    sS.inc(a)
            C.s(fke)
            st["knev"] = sS.n
    C.fence()
    def f(g):
        C.ld(g, C.knhb[:, :], kn[:, :])
        sDW.wait(g)
        sG.inc(g.nop("rl", nofuse=True))
    C.g(f)
    C.fence()
    AFA.pop()
    AB.pop()
    vn_o = AB.alloc(NC * sv)
    vn = C.AB_t[:, vn_o:vn_o + NC * sv]
    AB.push()
    AFA.push()
    vfa_o = AB.alloc(8192)
    wsv_o = AB.alloc(8192)
    wsv = C.AB_t[:, wsv_o:wsv_o + 8192]
    xb_o = AB.alloc(4096)
    xb = C.AB_t[:, xb_o:xb_o + 4096]
    vma_o = AB.alloc(4096)
    i2b_o = AB.alloc(512)
    xt = AFT[:, AFA.alloc(4096):AFA.stack[-1]]
    vst_o = AFA.alloc(512)
    vstage = AFT[:, vst_o:vst_o + 512]
    vm_o = AFA.alloc(512)
    vsh_o = AFA.alloc(512)
    vsq_o = AFA.alloc(512)
    n16_o = AFA.alloc(1024)

    def f(g):
        a = g.memset(vn[:, :].rearrange("p (k h c) -> p k h c", k=NC, h=16)[:, :, :, 64:65], 1.0)
        sG.inc(a)
    C.g(f)
    wtokV = load_wsec(C, wsv_o, C.wvT, 1024, 1024)
    C.w1_bias(wsv, 1024, 1024, None)
    C.fence()

    st = {"evac": (0, 0), "vno": (0, 0), "chain": (0, 0), "vnev": [(0, 0), (0, 0)],
          "trv": (0, 0)}
    for ti in range(NT):
        tt = NT - 1 - ti
        cur, nxt = tt % 2, (tt + 1) % 2
        def pre_t(t, pc=st["chain"]):
            sS.wait(t, pc)
        lt = C.ln_lite(tt, xt, xb, pre_t=pre_t)
        for dc in range(8):
            def pre(t, pc=st["evac"], pv=st["vno"], lt=lt, wtokV=wtokV):
                sG.wait(t, wtokV)
                sS.wait(t, pc)
                sV.wait(t, tuple(map(max, pv, lt)))
            tok = C.proj_feat_oc(xb, lambda ic, dc=dc: wsv[:, ic * 1024 + dc * 128: ic * 1024 + dc * 128 + 128],
                                 dc, 1024, pre=pre)
            def fse(s, dc=dc, cur=cur, tok=tok):
                sT.wait(s, tok)
                sS.wait(s)
                sV.wait(s)
                a = s.activation(C.AB_t[:, vfa_o + cur * 4096 + 512 * dc: vfa_o + cur * 4096 + 512 * dc + 512],
                                 C.pC[0:128, 0:512], AF.Copy)
                sS.inc(a)
            C.s(fse)
            st["evac"] = sS.n
        vfaTok = sS.n
        for dc in range(8):
            def fvm(v, dc=dc, tt=tt, cur=cur, nxt=nxt, vfaTok=vfaTok):
                sS.wait(v)
                sV.wait(v)
                def bc(colo, base, n):
                    sl = AFT[base:base + 64, colo:colo + 1]
                    return bass.AP(sl.tensor, sl.offset, [[sl.ap[0][0], 64], [0, n]])
                for ph in range(2):
                    b0 = 64 * ph
                    hh = 2 * dc + ph
                    vcur = C.AB_t[b0:b0 + 64, vfa_o + cur * 4096 + 512 * dc: vfa_o + cur * 4096 + 512 * dc + 512]
                    a = v.tensor_tensor(AFT[b0:b0 + 64, vm_o:vm_o + 512], vcur,
                                        bc(ap_o + 32 + hh, b0, 512), ALU.mult)
                    tk = sV.inc(a)
                    sV.wait(v, tk)
                    a = v.tensor_tensor(AFT[b0:b0 + 64, vsh_o:vsh_o + 511],
                                        C.AB_t[b0:b0 + 64, vfa_o + cur * 4096 + 512 * dc + 1: vfa_o + cur * 4096 + 512 * dc + 512],
                                        bc(ap_o + 48 + hh, b0, 511), ALU.mult)
                    tk = sV.inc(a)
                    sV.wait(v, tk)
                    if tt < NT - 1:
                        tail = C.AB_t[b0:b0 + 64, vfa_o + nxt * 4096 + 512 * dc: vfa_o + nxt * 4096 + 512 * dc + 1]
                    else:
                        tail = C.cstf[b0:b0 + 64, CF_CONST + 5:CF_CONST + 6]
                    a = v.tensor_tensor(AFT[b0:b0 + 64, vsh_o + 511:vsh_o + 512], tail,
                                        bc(ap_o + 48 + hh, b0, 1), ALU.mult)
                    tk = sV.inc(a)
                    sV.wait(v, tk)
                    a = v.tensor_add(AFT[b0:b0 + 64, vm_o:vm_o + 512], AFT[b0:b0 + 64, vm_o:vm_o + 512],
                                     AFT[b0:b0 + 64, vsh_o:vsh_o + 512])
                    sV.inc(a)
            C.v(fvm)
            tkvm = sV.n
            def fsqv(s, dc=dc, cur=cur, tkvm=tkvm):
                sV.wait(s, tkvm)
                sT.wait(s)
                a = s.activation(AFT[:, vsq_o:vsq_o + 512], AFT[:, vm_o:vm_o + 512], AF.Square)
                sS.inc(a)
                a = s.activation(C.AB_t[:, vma_o + 512 * dc: vma_o + 512 * dc + 512],
                                 AFT[:, vm_o:vm_o + 512], AF.Copy)
                sS.inc(a)
            C.s(fsqv)
            tksqv = sS.n
            def fssq(t, dc=dc, tksqv=tksqv, lt=lt):
                sS.wait(t, tksqv)
                sV.wait(t, lt)
                m = t.matmul(C.pB[0:16, 0:512], C.cstf[:, CF_BD + 16 * dc: CF_BD + 16 * dc + 16],
                             AFT[:, vsq_o:vsq_o + 512], start=(dc == 0), stop=(dc == 7))
                sT.inc(m)
            C.t(fssq)
        ssqTok = sT.n
        def fn1(s, ssqTok=ssqTok):
            sT.wait(s, ssqTok)
            a = s.activation(AFT[0:16, n16_o:n16_o + 512], C.pB[0:16, 0:512], AF.Sqrt)
            sS.inc(a)
        C.s(fn1)
        chain0 = sS.n
        st["chain"] = chain0
        def fn2(v, chain0=chain0):
            sS.wait(v, chain0)
            eps = bass.AP(C.cstf.ap().tensor, C.cstf.ap().offset + CF_CONST + 1,
                          [[C.cstf.ap().ap[0][0], 16], [0, 512]])
            a = v.tensor_tensor(AFT[0:16, n16_o + 512:n16_o + 1024], AFT[0:16, n16_o:n16_o + 512],
                                eps, ALU.add)
            sV.inc(a)
        C.v(fn2)
        tkn2 = sV.n
        def fn3(s, tkn2=tkn2):
            sV.wait(s, tkn2)
            a = s.activation(AFT[0:16, n16_o:n16_o + 512], AFT[0:16, n16_o + 512:n16_o + 1024], AF.Ln)
            tk = sS.inc(a)
            sS.wait(s, tk)
            a = s.activation(AFT[0:16, n16_o + 512:n16_o + 1024], AFT[0:16, n16_o:n16_o + 512],
                             AF.Exp, scale=-1.0)
            sS.inc(a)
        C.s(fn3)
        tkn3 = sS.n
        def fn4(v, tkn3=tkn3):
            sS.wait(v, tkn3)
            vs = bass.AP(AFT.ap().tensor, AFT.ap().offset + v16_o,
                         [[AFT.ap().ap[0][0], 16], [0, 512]])
            a = v.tensor_tensor(AFT[0:16, n16_o:n16_o + 512], AFT[0:16, n16_o + 512:n16_o + 1024],
                                vs, ALU.mult)
            sV.inc(a)
        C.v(fn4)
        tkn4 = sV.n
        def fn5(s, tkn4=tkn4):
            sV.wait(s, tkn4)
            a = s.activation(C.AB_t[0:16, i2b_o:i2b_o + 512], AFT[0:16, n16_o:n16_o + 512], AF.Copy)
            sS.inc(a)
        C.s(fn5)
        i2bTok = sS.n
        for dc in range(8):
            def fbc(t, dc=dc, i2bTok=i2bTok, pv=st["vno"]):
                sS.wait(t, i2bTok)
                sV.wait(t, pv)
                m = t.matmul(C.pC[0:128, 0:512], C.cstb[0:16, CB_BDT + 128 * dc: CB_BDT + 128 * dc + 128],
                             C.AB_t[0:16, i2b_o:i2b_o + 512], start=True, stop=True)
                sT.inc(m)
            C.t(fbc)
            tkbc = sT.n
            def fvn(v, dc=dc, tkbc=tkbc, ptr=st["trv"]):
                sT.wait(v, tkbc)
                sT.wait(v, ptr)
                a = v.tensor_mul(vstage[:, 0:512], C.AB_t[:, vma_o + 512 * dc: vma_o + 512 * dc + 512],
                                 C.pC[0:128, 0:512])
                sV.inc(a)
            C.v(fvn)
            st["vno"] = sV.n
            tkvn = sV.n
            def ftrv(t, dc=dc, tkvn=tkvn, pev=st["vnev"][dc % 2]):
                sV.wait(t, tkvn)
                sV.wait(t, pev)
                m = None
                for blk in range(4):
                    m = t.transpose(C.pA[:, 512 * (dc % 2) + 128 * blk: 512 * (dc % 2) + 128 * blk + 128],
                                    vstage[:, 128 * blk:128 * blk + 128],
                                    C.cstf[:, CF_ID:CF_ID + 128])
                sT.inc(m)
            C.t(ftrv)
            st["trv"] = sT.n
            tktrv = sT.n
            def fvev(v, dc=dc, tt=tt, tktrv=tktrv):
                sT.wait(v, tktrv)
                a = None
                for blk in range(4):
                    kt = 4 * tt + blk
                    dst = vn[:, kt * sv + 65 * 2 * dc: kt * sv + 65 * 2 * dc + 130]
                    dst = dst.rearrange("p (h c) -> p h c", c=65)[:, :, 0:64]
                    src = C.pA[:, 512 * (dc % 2) + 128 * blk: 512 * (dc % 2) + 128 * blk + 128]
                    a = v.tensor_copy(dst, src.rearrange("p (h c) -> p h c", c=64))
                    sV.inc(a)
            C.v(fvev)
            st["vnev"][dc % 2] = sV.n
    AFA.pop()
    AB.pop()
    C.fence()
    AB.push()
    AFA.push()
    knr_o = AB.alloc(8 * TP1)
    knr = C.AB_t[:, knr_o:knr_o + 8 * TP1]
    ob_o = AB.alloc(4096)
    obuf = C.AB_t[:, ob_o:ob_o + 4096]
    odd_o = AB.alloc(512)
    odds = C.AB_t[:, odd_o:odd_o + 512]
    wp_o = AB.alloc(1024)
    wp = C.AB_t[:, wp_o:wp_o + 1024]
    meg_o = AFA.alloc(C.MEGW)
    mega = AFT[:, meg_o:meg_o + C.MEGW]
    inv_o = AFA.alloc(512)
    invs = AFT[:, inv_o:inv_o + 512]
    xo_o = AFA.alloc(1024)
    stf_o = AFA.alloc(1024)

    def f(g):
        C.ld(g, knr[:, :], C.knhb[:, :])
        C.ld(g, C.w2[0:1, 3072:3072 + 1024], C.mrows[1:2, 0:1024])
        C.ld(g, C.w2[1:2, 3072:3072 + 1024], C.mrows[1:2, 0:1024])
    C.g(f)
    C.emit_mega_mask(mega, C.tbl_ma, qshift=0)
    C.fence()

    epst = {"tk2": (0, 0), "tk3": (0, 0), "tk4": (0, 0), "odd": (0, 0)}

    def epilogue(slot, h, av, tok_av):
        blk = h // 2
        if slot == 0:
            def ffix(v, av=av, tok_av=tok_av):
                C.sT2.wait(v, tok_av)
                a = v.tensor_add(av[64:65, 0:1], av[64:65, 0:1], C.cstf[64:65, CF_ONES:CF_ONES + 1])
                sV.inc(a)
            C.v(ffix)
            fixTok = sV.n
        else:
            fixTok = None
        def fs(s, tok_av=tok_av, p2=epst["tk2"], fixTok=fixTok):
            C.sT2.wait(s, tok_av)
            if fixTok is not None:
                sV.wait(s, fixTok)
            sT.wait(s, p2)
            a = s.activation(C.rows[64:65, 0:512], av[64:65, :], AF.Ln)
            sS.inc(a)
        C.s(fs)
        tk1 = sS.n
        def ft(t, tk1=tk1, p3=epst["tk3"]):
            sS.wait(t, tk1)
            sS.wait(t, p3)
            m = t.matmul(C.pB[0:64, 0:512], C.cstf[64:65, CF_ONESB:CF_ONESB + 64],
                         C.rows[64:65, 0:512], start=True, stop=True)
            sT.inc(m)
        C.t(ft)
        tk2 = sT.n
        epst["tk2"] = tk2
        def fs2(s, tk2=tk2, p4=epst["tk4"]):
            sT.wait(s, tk2)
            sV.wait(s, p4)
            a = s.activation(invs[0:64, :], C.pB[0:64, 0:512], AF.Exp, scale=-1.0)
            sS.inc(a)
        C.s(fs2)
        tk3 = sS.n
        epst["tk3"] = tk3
        def fv(v, h=h, blk=blk, tk3=tk3, po=epst["odd"]):
            sS.wait(v, tk3)
            if h % 2 == 0:
                dst = obuf[0:64, blk * 512:(blk + 1) * 512]
            else:
                C.sY.wait(v, po)
                dst = odds[0:64, 0:512]
            c2 = v.tensor_mul(dst, av[0:64, :], invs[0:64, :])
            sV.inc(c2)
        C.v(fv)
        tk4 = sV.n
        epst["tk4"] = tk4
        if h % 2 == 1:
            def fy(y, blk=blk, tk4=tk4):
                sV.wait(y, tk4)
                C.ld(y, obuf[64:128, blk * 512:(blk + 1) * 512], odds[0:64, 0:512], sDS)
                sDS.wait(y)
                C.sY.inc(y.nop("rl", nofuse=True))
            C.y(fy)
            epst["odd"] = C.sY.n
        return (sV, sV.n)

    def k_ap(slot, kt, h):
        dc = h // 2
        return knr[64 * (h % 2):64 * (h % 2) + 64, dc * TP1 + 1 + kt * 128: dc * TP1 + 1 + kt * 128 + 128]

    def q_ap(slot, h):
        dc = h // 2
        return knr[64 * (h % 2):64 * (h % 2) + 64, dc * TP1 + 1 + slot * 512: dc * TP1 + 1 + slot * 512 + 512]

    def vn_ap(kt, h):
        base = kt * sv + 65 * h
        return vn[:, base:base + 65]

    def bias_ap(slot, kt, h):
        cnt = 4 * (slot + 1)
        if kt >= cnt - 4:
            c0 = C.mega_c0(slot, kt)
            return mega[:, c0:c0 + 512]
        return None

    cfg = dict(H=16, counts=lambda s: 4 * (s + 1), k_ap=k_ap, q_ap=q_ap,
               vn_ap=vn_ap, bias_ap=bias_ap, phantom=None, eshift=-20.0,
               epilogue=epilogue)

    for slot in range(NT):
        C.emit_attention_slot(cfg, slot)
        C.out_proj_residual(slot, obuf, wp, wp_o, C.wcT, 3072,
                            AFT[:, xo_o:xo_o + 1024], AFT[:, stf_o:stf_o + 1024], last)
        C.fence()

    AFA.pop()
    AB.pop()
    AFA.pop()
    AB.pop()


KB.sub_attn_a = sub_attn_a


# version 08
# speedup vs baseline: 1.0651x; 1.0321x over previous
import math
import numpy as np
import ml_dtypes
import concourse.bass as bass
import concourse.mybir as mybir
from concourse.bass_utils import run_bass_kernel_spmd
from contextlib import ExitStack

F32 = mybir.dt.float32
BF16 = mybir.dt.bfloat16
AF = mybir.ActivationFunctionType
ALU = mybir.AluOpType

NEG = -1.0e30
KEY_SCALE_MAX = math.log(2 ** 16 - 1)
CONSTS = (1e-5, 1e-10, 10.0, KEY_SCALE_MAX, 1.0, 0.0, -1.0, 0.125, -40.0, -20.0)


class Sem:

    def __init__(self, nc, ctx, name, dma=False):
        self.hA = ctx.enter_context(nc.semaphore(name + "a"))
        self.hB = ctx.enter_context(nc.semaphore(name + "b"))
        self.va = 0
        self.vb = 0
        self.ic = 0
        self.dma = dma

    @property
    def n(self):
        return (self.va, self.vb)

    def inc(self, instr, k=1):
        if self.ic % 2 == 0:
            instr.then_inc(self.hA, k)
            self.va += k
        else:
            instr.then_inc(self.hB, k)
            self.vb += k
        self.ic += 1
        return (self.va, self.vb)

    def tok_after(self, snap, j, k=1):
        va, vb, ic = snap
        for i in range(j):
            if (ic + i) % 2 == 0:
                va += k
            else:
                vb += k
        return (va, vb)

    @property
    def snap(self):
        return (self.va, self.vb, self.ic)

    def wait(self, eng, tok=None):
        va, vb = (self.va, self.vb) if (tok is None or self.dma) else tok
        if va:
            eng.wait_ge(self.hA, va)
        if vb:
            eng.wait_ge(self.hB, vb)


class Arena:
    def __init__(self, tensor, width):
        self.t = tensor
        self.w = width
        self.stack = [0]

    def alloc(self, cols):
        o = self.stack[-1]
        assert o + cols <= self.w, f"arena overflow {o}+{cols}>{self.w}"
        self.stack[-1] = o + cols
        return o

    def push(self):
        self.stack.append(self.stack[-1])

    def pop(self):
        self.stack.pop()


def hope_tables(T, base=10000, d_model=1024):
    dim = d_model // 2
    pos = np.arange(dim, dtype=np.float64)
    freqs_np = 1.0 / base ** (pos / dim)
    active = int(np.sum(freqs_np * 2 * math.pi * T >= 1.0))
    active = min(active, dim - d_model // 4)
    freqs = freqs_np[:active].astype(np.float32)
    d = np.arange(0, T, dtype=np.float32)
    theta = (d[:, None] * freqs[None, :]).astype(np.float32)
    pv = (np.cos(theta, dtype=np.float32) + np.sin(theta, dtype=np.float32)).astype(np.float32)
    nh = active // 32
    per_head = pv.astype(np.float64).reshape(T, nh, 32).sum(axis=2)
    hi = per_head.astype(np.float32)
    lo = (per_head - hi.astype(np.float64)).astype(np.float32)
    return hi.T.copy(), lo.T.copy()


def shift2d(row):
    L = row.shape[0]
    padded = np.concatenate([np.zeros(127, row.dtype), row])
    s = np.lib.stride_tricks.sliding_window_view(padded, L)
    return np.ascontiguousarray(s[::-1])


def build_tables(T):
    LPAD = T + 128
    L = LPAD + T
    d = np.arange(-LPAD, T)
    mask_a = np.where(d >= 1, 0.0, NEG).astype(np.float32)
    mask_f = np.where(d >= 0, 0.0, NEG).astype(np.float32)
    hi8, lo8 = hope_tables(T)
    hope_hi = np.zeros((8, L), dtype=np.float32)
    hope_lo = np.zeros((8, L), dtype=np.float32)
    hope_hi[:, :LPAD] = NEG
    hope_hi[:, LPAD:] = hi8
    hope_lo[:, LPAD:] = lo8
    ma2 = shift2d(mask_a)
    mf2 = shift2d(mask_f)
    hh2 = np.concatenate([shift2d(hope_hi[h]) for h in range(8)], axis=0)
    hl2 = np.concatenate([shift2d(hope_lo[h]) for h in range(8)], axis=0)
    return ma2, mf2, hh2, hl2


def mk_constants():
    c = np.zeros((128, 192), dtype=np.float32)
    c[:, 0] = 1.0
    c[:, 1] = np.arange(1, 129, dtype=np.float32)
    for h in range(8):
        c[h, 16 + 8 * h:16 + 8 * h + 8] = 1.0
    for h in range(8):
        c[64, 80 + 8 * h + h] = 1.0
    return c


def mk_selectors():
    s = np.zeros((64, 32 * 128), dtype=np.float32)
    for e in range(8):
        for j in range(4):
            m = e * 4 + j
            s[(2 * j) * 8 + e, 128 * m:128 * m + 64] = 1.0
            s[(2 * j + 1) * 8 + e, 128 * m + 64:128 * m + 128] = 1.0
    return s


class KB:
    def __init__(self, T, subs):
        self.T = T
        self.subs = subs
        self.NT = T // 512
        self.NC = T // 128
        self.TP1 = T + 1
        self.LPAD = T + 128
        self.TBL = self.LPAD + T
        self.MEGW = T + 384
        self.nc = bass.Bass("TRN2", target_bir_lowering=False)

    def build(self):
        nc = self.nc
        T = self.T

        def din(name, shape, dtype=F32):
            return nc.dram_tensor(name, shape, dtype, kind="ExternalInput").ap()

        self.x_in = din("x_feat", [1024, T])
        self.wkT = din("wkT", [1024, 1024])
        self.wvT = din("wvT", [1024, 1024])
        self.wcT = din("wcT", [128, 8 * 1024])
        self.aprm = din("aprm", [1, 64])
        self.wqkvT = din("wqkvT", [1024, 3072])
        self.bqkv = din("bqkv", [1, 3072])
        self.wpT = din("wpT", [128, 8 * 1024])
        self.bproj = din("bproj", [1, 1024])
        self.wqf = din("wqf", [1024, 512])
        self.wkf = din("wkf", [1024, 512])
        self.wgf = din("wgf", [1024, 64])
        self.wve = din("wve", [8, 1024, 512])
        self.woe = din("woe", [32, 128, 1024])
        self.tbl_ma = din("tbl_ma", [128, self.TBL])
        self.tbl_mf = din("tbl_mf", [128, self.TBL])
        self.tbl_hh = din("tbl_hh", [8 * 128, self.TBL])
        self.tbl_hl = din("tbl_hl", [8 * 128, self.TBL])
        self.cstb_in = din("cstb_in", [128, 2048])
        self.tblu = din("tblu", [128, 256])
        self.mrows = din("mrows", [2, 4096])
        self.cstf_in = din("cstf_in", [128, 512])
        self.selc_in = din("selc_in", [64, 32 * 128])

        self.y_out = nc.dram_tensor("y", [1024, T], F32, kind="ExternalOutput").ap()
        self.xres = nc.dram_tensor("xres", [1024, T], F32, kind="Internal").ap()
        self.knhb = nc.dram_tensor("knhb", [128, 8 * self.TP1], BF16, kind="Internal").ap()
        self.qhb = nc.dram_tensor("qhb", [1024, T], BF16, kind="Internal").ap()
        self.aghb = nc.dram_tensor("aghb", [2, T], F32, kind="Internal").ap()

        ctx = ExitStack()
        with ctx:
            self._alloc(ctx)
            self._emit()
        return nc

    def _alloc(self, ctx):
        nc = self.nc
        self.pt = ctx.enter_context(nc.sbuf_tensor("pt", [128, 4 * 512], BF16))
        self.rows = ctx.enter_context(nc.sbuf_tensor("rows", [128, 2048], F32))
        self.cstf = ctx.enter_context(nc.sbuf_tensor("cstf", [128, 512], F32))
        self.cstb = ctx.enter_context(nc.sbuf_tensor("cstb", [128, 2048], BF16))
        self.w2 = ctx.enter_context(nc.sbuf_tensor("w2", [2, 4096], BF16))
        self.g2 = ctx.enter_context(nc.sbuf_tensor("g2", [2, 512], BF16))
        self.AB_t = ctx.enter_context(nc.sbuf_tensor("AB", [128, 76000], BF16))
        self.AF_t = ctx.enter_context(nc.sbuf_tensor("AFa", [128, 8200], F32))
        self.AB = Arena(self.AB_t, 76000)
        self.AFA = Arena(self.AF_t, 8200)

        self.pL = ctx.enter_context(nc.psum_tensor("pL", [128, 2048], F32))
        self.pA = ctx.enter_context(nc.psum_tensor("pA", [128, 1024], F32))
        self.pB = ctx.enter_context(nc.psum_tensor("pB", [128, 512], F32))
        self.pC = ctx.enter_context(nc.psum_tensor("pC", [128, 512], F32))

        self.sDW = Sem(nc, ctx, "sDW", dma=True)
        self.sDA = Sem(nc, ctx, "sDA", dma=True)
        self.sDX = Sem(nc, ctx, "sDX", dma=True)
        self.sDS = Sem(nc, ctx, "sDS", dma=True)
        self.sT = Sem(nc, ctx, "sT")
        self.sS = Sem(nc, ctx, "sS")
        self.sV = Sem(nc, ctx, "sV")
        self.sG = Sem(nc, ctx, "sG")
        self.sT2 = Sem(nc, ctx, "sT2")
        self.sS2 = Sem(nc, ctx, "sS2")
        self.sY = Sem(nc, ctx, "sY")

        self.block = ctx.enter_context(nc.Block())
        for i, val in enumerate(CONSTS):
            nc.const_aps.register(F32, val, self.cstf[:, 220 + i:221 + i])

    def t(self, f): self.block.tensor(f)
    def s(self, f): self.block.scalar(f)
    def v(self, f): self.block.vector(f)
    def g(self, f): self.block.gpsimd(f)
    def y(self, f): self.block.sync(f)

    def ld(self, eng, dst, src, sem=None):
        sem = sem or self.sDW
        sem.inc(eng.dma_start(out=dst, in_=src), 16)

    def fence(self):
        sT, sS, sV, sG, sY = self.sT, self.sS, self.sV, self.sG, self.sY
        sDS, sDX, sDW = self.sDS, self.sDX, self.sDW
        def ft(t):
            sT.wait(t)
            sT.inc(t.nop("fence", nofuse=True))
        def fs(s):
            sS.wait(s)
            sS.inc(s.nop("fence", nofuse=True))
        def fv(v):
            sV.wait(v)
            sV.inc(v.nop("fence", nofuse=True))
        def fg(g):
            sDW.wait(g)
            sG.wait(g)
            sG.inc(g.nop("fence", nofuse=True))
        def fy(y):
            sDS.wait(y)
            sDX.wait(y)
            sY.wait(y)
            sY.inc(y.nop("fence", nofuse=True))
        self.t(ft); self.s(fs); self.v(fv); self.g(fg); self.y(fy)
        def wt(t): sS.wait(t); sV.wait(t); sG.wait(t); sY.wait(t)
        def ws(s): sT.wait(s); sV.wait(s); sG.wait(s); sY.wait(s)
        def wv(v): sT.wait(v); sS.wait(v); sG.wait(v); sY.wait(v)
        def wg(g): sT.wait(g); sS.wait(g); sV.wait(g); sY.wait(g)
        def wy(y): sT.wait(y); sS.wait(y); sV.wait(y); sG.wait(y)
        self.t(wt); self.s(ws); self.v(wv); self.g(wg); self.y(wy)

    def _emit(self):
        C = self
        sDW, sDX, sDS = C.sDW, C.sDX, C.sDS
        sV = C.sV

        def f(g):
            C.ld(g, C.cstf[:, :], C.cstf_in[:, :])
            C.ld(g, C.cstb[:, :], C.cstb_in[:, :])
            C.ld(g, C.xres[:, :], C.x_in[:, :])
        C.g(f)
        def f(g):
            C.ld(g, C.g2[1:2, 0:512], C.mrows[0:1, 0:512])
        C.g(f)
        C.fence()

        nsub = len(C.subs)
        for i, sub in enumerate(C.subs):
            last = (i == nsub - 1)
            if sub == "attn_a":
                C.sub_attn_a(last)
            elif sub in ("ffn1", "ffn2"):
                C.sub_ffn(last)
            elif sub == "attn_b":
                C.sub_attn_b(last)
            C.fence()

        def f(y_):
            sDS.wait(y_)
        C.y(f)
        def f(g):
            sDS.wait(g)
        C.g(f)


def emit_attention_slot(C, cfg, slot):
    sT, sS, sV, sT2, sS2 = C.sT, C.sS, C.sV, C.sT2, C.sS2
    idn = C.cstf[:, CF_ID:CF_ID + 128]
    st = cfg.setdefault("_state", {"epi": {0: None, 1: None}, "hctr": 0})
    cnt = cfg["counts"](slot)
    order = cfg.get("head_order") or list(range(cfg["H"]))
    for h in order:
        if cfg.get("pre_head"):
            cfg["pre_head"](slot, h)
        hp = st["hctr"] % 2
        st["hctr"] += 1
        av = C.pA[:, hp * 512:(hp + 1) * 512]
        qap = cfg["q_ap"](slot, h)
        snT2 = sT2.snap
        snS2 = sS2.snap
        snT = sT.snap
        prev_exp_all = sS2.n
        prev_av_all = sT2.n
        bprev = st["epi"][hp]
        ph = cfg["phantom"](slot) if cfg.get("phantom") else None

        def ft(t, slot=slot, h=h, cnt=cnt, av=av, qap=qap,
               snS2=snS2, prev_exp_all=prev_exp_all, bprev=bprev, ph=ph):
            sS2.wait(t, prev_exp_all)
            if bprev is not None:
                bprev[0].wait(t, bprev[1])
            for kt in range(cnt):
                r = kt % 4
                if kt >= 1:
                    sS2.wait(t, sS2.tok_after(snS2, kt))
                    m = t.matmul(av[0:65, :], cfg["vn_ap"](kt - 1, h),
                                 C.pt[:, ((kt - 1) % 4) * 512:((kt - 1) % 4) * 512 + 512],
                                 start=(kt - 1 == 0), stop=False)
                    sT2.inc(m)
                bias = cfg["bias_ap"](slot, kt, h)
                m = t.matmul(C.pL[:, r * 512:(r + 1) * 512],
                             cfg["k_ap"](slot, kt, h), qap,
                             start=True, stop=(bias is None))
                if bias is not None:
                    m = t.matmul(C.pL[:, r * 512:(r + 1) * 512], idn, bias,
                                 start=False, stop=True)
                sT.inc(m)
            sS2.wait(t, sS2.tok_after(snS2, cnt))
            m = t.matmul(av[0:65, :], cfg["vn_ap"](cnt - 1, h),
                         C.pt[:, ((cnt - 1) % 4) * 512:((cnt - 1) % 4) * 512 + 512],
                         start=(cnt - 1 == 0), stop=(ph is None))
            if ph is not None:
                m = t.matmul(av[0:65, :], ph[0], ph[1], start=False, stop=True)
            sT2.inc(m)
        C.t(ft)
        final_av_tok = sT2.n

        def fs(s, cnt=cnt, snT=snT, snT2=snT2, prev_av_all=prev_av_all):
            for kt in range(cnt):
                r = kt % 4
                if kt < 4:
                    sT2.wait(s, prev_av_all)
                else:
                    sT2.wait(s, sT2.tok_after(snT2, kt - 3))
                sT.wait(s, sT.tok_after(snT, kt + 1))
                a = s.activation(C.pt[:, r * 512:(r + 1) * 512],
                                 C.pL[:, r * 512:(r + 1) * 512], AF.Exp,
                                 bias=cfg.get("eshift", 0.0))
                sS2.inc(a)
        C.s(fs)

        st["epi"][hp] = cfg["epilogue"](slot, h, av, final_av_tok)


def emit_mega_mask(C, mega_ap, tbl, qshift, W=None):
    W = W or C.MEGW
    off = C.LPAD + qshift - 384
    def f(g):
        C.sT.wait(g)
        C.sDW.inc(g.dma_start(out=mega_ap[:, 0:W], in_=tbl[0:128, off:off + W]), 16)
        C.sDW.wait(g)
        C.sG.inc(g.nop("rl", nofuse=True))
    C.g(f)
    return C.sG.n


def mega_c0(C, slot, kt):
    return slot * 512 - 128 * kt + 384


KB.emit_attention_slot = emit_attention_slot
KB.emit_mega_mask = emit_mega_mask
KB.mega_c0 = mega_c0


CB_ONES = 0
CB_P1 = 1
CB_E8 = 16
CB_OH = 80
CB_ONESB = 144
CB_P1ROW = 208
CB_ID = 336
CB_DEC = 464
CB_TRI = 592
CB_BDT = 720
CB_BD = 1744
CB_ONESR = 1880
CF_ONES = 0
CF_E8 = 16
CF_ONESB = 80
CF_CONST = 220
CF_ID = 256
CF_BD = 384


def mk_cstb():
    c = np.zeros((128, 2048), dtype=np.float32)
    c[:, CB_ONES] = 1.0
    c[:, CB_P1] = np.arange(1, 129, dtype=np.float32)
    for h in range(8):
        c[h, CB_E8 + 8 * h: CB_E8 + 8 * h + 8] = 1.0
    for h in range(8):
        c[64, CB_OH + 8 * h + h] = 1.0
    c[:, CB_ONESB:CB_ONESB + 64] = 1.0
    c[0, CB_P1ROW:CB_P1ROW + 128] = np.arange(1, 129, dtype=np.float32)
    c[:, CB_ID:CB_ID + 128] = np.eye(128, dtype=np.float32)
    p = np.arange(128)[:, None]
    j = np.arange(128)[None, :]
    c[:, CB_DEC:CB_DEC + 128] = np.maximum(p - j, 0).astype(np.float32)
    c[:, CB_TRI:CB_TRI + 128] = (p >= j).astype(np.float32)
    for dc in range(8):
        blk = np.zeros((128, 16), dtype=np.float32)
        for pp in range(128):
            blk[pp, 2 * dc + pp // 64] = 1.0
        c[:, CB_BD + 16 * dc: CB_BD + 16 * dc + 16] = blk
        bt = np.zeros((128, 128), dtype=np.float32)
        for pp in range(128):
            bt[2 * dc + pp // 64, pp] = 1.0
        c[:, CB_BDT + 128 * dc: CB_BDT + 128 * dc + 128] = bt
    c[:, CB_ONESR:CB_ONESR + 128] = 1.0
    return c


def mk_cstf():
    c = np.zeros((128, 512), dtype=np.float32)
    c[:, CF_ONES] = 1.0
    for h in range(8):
        c[h, CF_E8 + 8 * h: CF_E8 + 8 * h + 8] = 1.0
    c[:, CF_ONESB:CF_ONESB + 128] = 1.0
    c[:, CF_ID:CF_ID + 128] = np.eye(128, dtype=np.float32)
    for i, val in enumerate(CONSTS):
        c[:, CF_CONST + i] = val
    for dc in range(8):
        for pp in range(128):
            c[pp, CF_BD + 16 * dc + 2 * dc + pp // 64] = 1.0
    return c


def ln_full(C, tt, xt, xb, store=True):
    sDX, sDS, sT, sS, sV = C.sDX, C.sDS, C.sT, C.sS, C.sV
    R = C.rows
    t0 = tt * 512
    def f(y):
        sV.wait(y)
        sS.wait(y)
        for dc in range(8):
            C.ld(y, xt[:, dc * 512:(dc + 1) * 512],
                 C.xres[dc * 128:(dc + 1) * 128, t0:t0 + 512], sDX)
        sDX.wait(y)
        C.sY.inc(y.nop("rl", nofuse=True))
    C.y(f)
    tkx = C.sY.n
    def f(v):
        C.sY.wait(v, tkx)
        sT.wait(v)
        c = None
        for dc in range(8):
            c = v.tensor_copy(xb[:, dc * 512:(dc + 1) * 512], xt[:, dc * 512:(dc + 1) * 512])
        sV.inc(c)
    C.v(f)
    tkb = sV.n
    C.AFA.push()
    o = C.AFA.alloc(1024)
    sq_sl = [C.AF_t[:, o:o + 512], C.AF_t[:, o + 512:o + 1024]]
    snT = sT.snap
    def f(s):
        sV.wait(s, tkb)
        sT.wait(s, (snT[0], snT[1]))
        for dc in range(8):
            if dc >= 2:
                sT.wait(s, sT.tok_after(snT, dc - 1))
            a = s.activation(sq_sl[dc % 2][:, :], xt[:, dc * 512:(dc + 1) * 512], AF.Square)
            sS.inc(a)
    snS = sS.snap
    C.s(f)
    def f(t):
        for dc in range(8):
            sS.wait(t, sS.tok_after(snS, dc + 1))
            t.matmul(C.pB[0:1, 0:512], C.cstb[:, CB_ONES:CB_ONES + 1],
                     xb[:, dc * 512:(dc + 1) * 512], start=(dc == 0), stop=(dc == 7))
            m2 = t.matmul(C.pL[0:1, 0:512], C.cstf[:, CF_ONES:CF_ONES + 1],
                          sq_sl[dc % 2][:, :], start=(dc == 0), stop=(dc == 7))
            sT.inc(m2)
    C.t(f)
    tksum = sT.n
    def f(s):
        sT.wait(s, tksum)
        C.sY.wait(s)
        s.activation(R[0:1, 0:512], C.pB[0:1, 0:512], AF.Copy, scale=1.0 / 1024.0)
        a = s.activation(R[0:1, 512:1024], C.pL[0:1, 0:512], AF.Copy, scale=1.0 / 1024.0)
        tk = sS.inc(a)
        sS.wait(s, tk)
        a = s.activation(R[0:1, 1024:1536], R[0:1, 0:512], AF.Square)
        sS.inc(a)
    C.s(f)
    tk1 = sS.n
    def f(v):
        sS.wait(v, tk1)
        c = v.tensor_sub(R[0:1, 512:1024], R[0:1, 512:1024], R[0:1, 1024:1536])
        sV.inc(c)
    C.v(f)
    tk2 = sV.n
    def f(s):
        sV.wait(s, tk2)
        a = s.activation(R[0:1, 1024:1536], R[0:1, 512:1024], AF.Ln, bias=1e-5)
        tk = sS.inc(a)
        sS.wait(s, tk)
        a = s.activation(R[0:1, 1536:2048], R[0:1, 1024:1536], AF.Exp, scale=-0.5)
        sS.inc(a)
    C.s(f)
    tk3 = sS.n
    def f(v):
        sS.wait(v, tk3)
        v.tensor_mul(C.g2[0:1, 0:512], R[0:1, 0:512], R[0:1, 1536:2048])
        c = v.tensor_mul(R[0:1, 1024:1536], R[0:1, 0:512], R[0:1, 1536:2048])
        sV.inc(c)
    C.v(f)
    tk4 = sV.n
    if store:
        def f(y, tt=tt):
            sV.wait(y, tk4)
            C.ld(y, C.aghb[0:1, tt * 512:(tt + 1) * 512], R[0:1, 1536:2048], sDS)
            C.ld(y, C.aghb[1:2, tt * 512:(tt + 1) * 512], R[0:1, 1024:1536], sDS)
            sDS.wait(y)
            C.sY.inc(y.nop("rl", nofuse=True))
        C.y(f)
    def f(t):
        sV.wait(t, tk4)
        m = t.matmul(C.pB[0:128, 0:512], C.cstf[0:1, CF_ONESB:CF_ONESB + 128], R[0:1, 1536:2048],
                     start=True, stop=True)
        sT.inc(m)
    C.t(f)
    tk5 = sT.n
    def f(v):
        sT.wait(v, tk5)
        c = None
        for dc in range(8):
            c = v.tensor_mul(xb[:, dc * 512:(dc + 1) * 512],
                             xt[:, dc * 512:(dc + 1) * 512], C.pB[0:128, 0:512])
        sV.inc(c)
    C.v(f)
    C.AFA.pop()
    return sV.n


def ln_lite(C, tt, xt, xb, pre_t=None):
    sDX, sT, sV = C.sDX, C.sT, C.sV
    R = C.rows
    t0 = tt * 512
    def f(y):
        sV.wait(y)
        C.sS.wait(y)
        sT.wait(y)
        C.ld(y, R[0:1, 1536:2048], C.aghb[0:1, t0:t0 + 512], sDX)
        C.ld(y, R[0:1, 0:512], C.aghb[1:2, t0:t0 + 512], sDX)
        for dc in range(8):
            C.ld(y, xt[:, dc * 512:(dc + 1) * 512],
                 C.xres[dc * 128:(dc + 1) * 128, t0:t0 + 512], sDX)
        sDX.wait(y)
        C.sY.inc(y.nop("rl", nofuse=True))
    C.y(f)
    tkx = C.sY.n
    def f(t):
        C.sY.wait(t, tkx)
        if pre_t is not None:
            pre_t(t)
        m = t.matmul(C.pB[0:128, 0:512], C.cstf[0:1, CF_ONESB:CF_ONESB + 128],
                     R[0:1, 1536:2048], start=True, stop=True)
        sT.inc(m)
    C.t(f)
    tk5 = sT.n
    def f(v):
        sT.wait(v, tk5)
        c = None
        for dc in range(8):
            c = v.tensor_mul(xb[:, dc * 512:(dc + 1) * 512],
                             xt[:, dc * 512:(dc + 1) * 512], C.pB[0:128, 0:512])
        c2 = v.tensor_copy(C.g2[0:1, 0:512], R[0:1, 0:512])
        sV.inc(c2)
    C.v(f)
    return sV.n


def w1_bias(C, wsb, dout, w2off, bias_src=None, wsb_cols0=0, wait_tok=None):
    sT, sS, sG, sDW = C.sT, C.sS, C.sG, C.sDW
    def f(g):
        if bias_src is not None:
            C.ld(g, C.w2[1:2, w2off:w2off + dout], bias_src)
        else:
            C.ld(g, C.w2[1:2, w2off:w2off + dout], C.mrows[1:2, 0:dout])
        sDW.wait(g)
        sG.inc(g.nop("rl", nofuse=True))
    C.g(f)
    wtok = sG.n
    for oc in range((dout + 511) // 512):
        o0 = oc * 512
        ow = min(512, dout - o0)
        def f(t, o0=o0, ow=ow, snv=C.sV.n, sns=C.sS.n):
            sG.wait(t, wtok)
            C.sV.wait(t, snv)
            C.sS.wait(t, sns)
            m = None
            for ic in range(8):
                m = t.matmul(C.pB[0:1, 0:ow], C.cstb[:, CB_ONES:CB_ONES + 1],
                             wsb[:, wsb_cols0 + ic * dout + o0: wsb_cols0 + ic * dout + o0 + ow],
                             start=(ic == 0), stop=(ic == 7))
            sT.inc(m)
        C.t(f)
        tk = sT.n
        def f(s, o0=o0, ow=ow, tk=tk):
            sT.wait(s, tk)
            a = s.activation(C.w2[0:1, w2off + o0: w2off + o0 + ow], C.pB[0:1, 0:ow],
                             AF.Copy, scale=-1.0)
            sS.inc(a)
        C.s(f)
        tk2 = sS.n
        def f(t, tk2=tk2):
            sS.wait(t, tk2)
        C.t(f)
    return sS.n, sG.n, sDW.n


def proj_feat_oc(C, xb, wslice_fn, oc, w2off, pre=None):
    sT = C.sT
    def f(t):
        if pre is not None:
            pre(t)
        t.matmul(C.pC[0:128, 0:512], C.w2[:, w2off + oc * 128: w2off + (oc + 1) * 128],
                 C.g2[:, 0:512], start=True, stop=False)
        m = None
        for ic in range(8):
            m = t.matmul(C.pC[0:128, 0:512], wslice_fn(ic),
                         xb[:, ic * 512:(ic + 1) * 512], start=False, stop=(ic == 7))
        sT.inc(m)
    C.t(f)
    return sT.n


def proj_tok_tc(C, xb, wslice_fn, tc, ow, w2off, o0=0, pre=None):
    sT = C.sT
    def f(t):
        if pre is not None:
            pre(t)
        t.matmul(C.pC[0:128, 0:ow], C.g2[0:2, tc * 128:(tc + 1) * 128],
                 C.w2[:, w2off + o0: w2off + o0 + ow], start=True, stop=False)
        m = None
        for ic in range(8):
            m = t.matmul(C.pC[0:128, 0:ow],
                         xb[:, ic * 512 + tc * 128: ic * 512 + tc * 128 + 128],
                         wslice_fn(ic), start=False, stop=(ic == 7))
        sT.inc(m)
    C.t(f)
    return sT.n


KB.ln_full = ln_full
KB.ln_lite = ln_lite
KB.w1_bias = w1_bias
KB.proj_feat_oc = proj_feat_oc
KB.proj_tok_tc = proj_tok_tc

def load_wsec(C, dst_cols, src2d, dout, pitch, rows0=0, nic=8):
    t = C.AB_t
    def f(g):
        C.sT.wait(g)
        src = bass.AP(src2d.tensor, src2d.offset + rows0,
                      [[pitch, 128], [128 * pitch, nic], [1, dout]])
        C.sDW.inc(g.dma_start(out=t[:, dst_cols:dst_cols + nic * dout].rearrange("p (i o) -> p i o", i=nic),
                    in_=src), 16)
        C.sDW.wait(g)
        C.sG.inc(g.nop("rl", nofuse=True))
    C.g(f)
    return C.sG.n


def sub_attn_b(C, last):
    T, NT, NC, TP1 = C.T, C.NT, C.NC, C.TP1
    sv = 65 * 16
    AB, AFA = C.AB, C.AFA
    sT, sS, sV, sG, sDW, sDX, sDS = C.sT, C.sS, C.sV, C.sG, C.sDW, C.sDX, C.sDS

    AB.push()
    kn_o = AB.alloc(8 * TP1)
    kn = C.AB_t[:, kn_o:kn_o + 8 * TP1]

    def f(g):
        a = g.memset(kn[:, :].rearrange("p (c t) -> p c t", c=8)[:, :, 0:1], 0.0)
        sG.inc(a)
    C.g(f)
    C.fence()

    for sec, o0, w2off in (("q", 0, 0), ("k", 1024, 1024)):
        AB.push()
        AFA.push()
        xt = C.AF_t[:, AFA.alloc(4096):AFA.stack[-1]]
        xb_o = AB.alloc(4096)
        xb = C.AB_t[:, xb_o:xb_o + 4096]
        ws_o = AB.alloc(8192)
        ws = C.AB_t[:, ws_o:ws_o + 8192]
        wtok = load_wsec(C, ws_o, C.wqkvT, 1024, 3072, rows0=o0)
        C.w1_bias(ws, 1024, w2off, C.bqkv[0:1, o0:o0 + 1024], wait_tok=wtok)
        stg_o = AB.alloc(1024)
        stg = C.AB_t[:, stg_o:stg_o + 1024]
        stok = [(0, 0), (0, 0)]
        for tt in range(NT):
            if sec == "q":
                lt = C.ln_full(tt, xt, xb, store=True)
            else:
                lt = C.ln_lite(tt, xt, xb)
            prev_cp = [sS.n]
            for oc in range(8):
                def pre(t, pc=prev_cp[0], lt=lt):
                    sS.wait(t, pc)
                    sV.wait(t, lt)
                tok = C.proj_feat_oc(xb, lambda ic, oc=oc: ws[:, ic * 1024 + oc * 128: ic * 1024 + (oc + 1) * 128],
                                     oc, w2off, pre=pre)
                if sec == "q":
                    sl = oc % 2
                    def fse(s, oc=oc, sl=sl, tok=tok):
                        sT.wait(s, tok)
                        C.sY.wait(s, stok[sl])
                        a = s.activation(stg[:, sl * 512:(sl + 1) * 512], C.pC[0:128, 0:512], AF.Copy, scale=0.125)
                        sS.inc(a)
                    C.s(fse)
                    prev_cp[0] = sS.n
                    cp = sS.n
                    def fy(y, oc=oc, sl=sl, tt=tt, cp=cp):
                        sS.wait(y, cp)
                        C.ld(y, C.qhb[oc * 128:(oc + 1) * 128, tt * 512:(tt + 1) * 512],
                             stg[:, sl * 512:(sl + 1) * 512], sDS)
                        sDS.wait(y)
                        C.sY.inc(y.nop("rl", nofuse=True))
                    C.y(fy)
                    stok[sl] = C.sY.n
                else:
                    def fse(s, oc=oc, tt=tt, tok=tok):
                        sT.wait(s, tok)
                        a = s.activation(kn[:, oc * TP1 + tt * 512 + 1: oc * TP1 + tt * 512 + 513],
                                         C.pC[0:128, 0:512], AF.Copy)
                        sS.inc(a)
                    C.s(fse)
                    prev_cp[0] = sS.n
        AFA.pop()
        AB.pop()
        C.fence()

    vn_o = AB.alloc(NC * sv)
    vn = C.AB_t[:, vn_o:vn_o + NC * sv]
    def f(g):
        b = g.memset(vn[:, :].rearrange("p (k h c) -> p k h c", k=NC, h=16)[:, :, :, 64:65], 1.0)
        sG.inc(b)
    C.g(f)
    AB.push()
    AFA.push()
    xt = C.AF_t[:, AFA.alloc(4096):AFA.stack[-1]]
    xb_o = AB.alloc(4096)
    xb = C.AB_t[:, xb_o:xb_o + 4096]
    ws_o = AB.alloc(4096)
    o0 = 2048
    w2off = 2048
    wsu = [(0, 0), (0, 0)]
    for tt in range(NT):
        lt = C.ln_lite(tt, xt, xb)
        for sl in range(2):
            wsb = ws_o
            def fg(g, sl=sl, pu=tuple(map(max, wsu[0], wsu[1]))):
                sT.wait(g, pu)
                src = bass.AP(C.wqkvT.tensor, C.wqkvT.offset + o0 + sl * 512,
                              [[3072, 128], [128 * 3072, 8], [1, 512]])
                sDW.inc(g.dma_start(out=C.AB_t[:, ws_o:ws_o + 4096].rearrange("p (i o) -> p i o", i=8),
                            in_=src), 16)
                sDW.wait(g)
                sG.inc(g.nop("rl", nofuse=True))
            C.g(fg)
            wtok = sG.n
            if tt == 0:
                C.w1_bias(C.AB_t[:, wsb:wsb + 4096], 512, w2off + sl * 512,
                          C.bqkv[0:1, o0 + sl * 512:o0 + sl * 512 + 512], wait_tok=wtok)
            prev_ev = [sV.n]
            for tc in range(4):
                def pre(t, pv=prev_ev[0], wtok=wtok, lt=lt):
                    sG.wait(t, wtok)
                    sV.wait(t, tuple(map(max, pv, lt)))
                tok = C.proj_tok_tc(xb, lambda ic, wsb=wsb: C.AB_t[:, wsb + ic * 512:wsb + (ic + 1) * 512],
                                    tc, 512, w2off, o0=sl * 512, pre=pre)
                kt = tt * 4 + tc
                def fv(v, kt=kt, sl=sl, tok=tok, pe=prev_ev[0]):
                    sT.wait(v, tok)
                    sV.wait(v, pe)
                    dst = vn[:, kt * sv + 65 * 8 * sl: kt * sv + 65 * 8 * sl + 520]
                    dst = dst.rearrange("p (h c) -> p h c", c=65)[:, :, 0:64]
                    c = v.tensor_copy(dst, C.pC[0:128, 0:512].rearrange("p (h c) -> p h c", c=64))
                    sV.inc(c)
                C.v(fv)
                prev_ev[0] = sV.n
            wsu[sl] = sT.n
    AFA.pop()
    AB.pop()
    C.fence()

    AB.push()
    AFA.push()
    qs_o = AB.alloc(4096)
    qs = C.AB_t[:, qs_o:qs_o + 4096]
    ob_o = AB.alloc(4096)
    obuf = C.AB_t[:, ob_o:ob_o + 4096]
    odd_o = AB.alloc(512)
    odds = C.AB_t[:, odd_o:odd_o + 512]
    wp_o = AB.alloc(1024)
    wp = C.AB_t[:, wp_o:wp_o + 1024]
    meg_o = AFA.alloc(C.MEGW)
    mega = C.AF_t[:, meg_o:meg_o + C.MEGW]
    inv_o = AFA.alloc(512)
    invs = C.AF_t[:, inv_o:inv_o + 512]
    xo_o = AFA.alloc(1024)
    stf_o = AFA.alloc(1024)

    def f(g):
        C.ld(g, C.w2[0:1, 3072:3072 + 1024], C.mrows[1:2, 0:1024])
        C.ld(g, C.w2[1:2, 3072:3072 + 1024], C.bproj[0:1, :])
    C.g(f)
    C.fence()

    head_order = list(range(8, 16)) + list(range(0, 8))
    state = {"cur": None}

    def pre_head(slot, h):
        want = "mask" if h >= 8 else ("hope", h)
        if state["cur"] == want:
            return
        state["cur"] = want
        W = min(C.MEGW, 512 * (slot + 1) + 384)
        if want == "mask":
            tk = C.emit_mega_mask(mega, C.tbl_mf, qshift=0, W=W)
        else:
            off = C.LPAD - 384
            def fg(g, h=h, W=W):
                sT.wait(g)
                sDW.inc(g.dma_start(out=mega[:, 0:W],
                                    in_=C.tbl_hh[h * 128:(h + 1) * 128, off:off + W]), 16)
                sDW.wait(g)
                half = (W + 1) // 2
                for c0 in (0, half):
                    cw = min(half, W - c0)
                    sDW.inc(g.dma_start(out=mega[:, c0:c0 + cw],
                                        in_=C.tbl_hl[h * 128:(h + 1) * 128, off + c0:off + c0 + cw],
                                        accum_op=ALU.add), 16)
                    sDW.wait(g)
                sG.inc(g.nop("rl", nofuse=True))
            C.g(fg)
            tk = sG.n
        def ft(t, tk=tk):
            sG.wait(t, tk)
        C.t(ft)

    def q_load(slot):
        def f(y):
            sT.wait(y)
            for dc in range(8):
                C.ld(y, qs[:, dc * 512:(dc + 1) * 512],
                     C.qhb[dc * 128:(dc + 1) * 128, slot * 512:(slot + 1) * 512], sDX)
            sDX.wait(y)
            C.sY.inc(y.nop("rl", nofuse=True))
        C.y(f)
        tk = C.sY.n
        def ft(t, tk=tk):
            C.sY.wait(t, tk)
        C.t(ft)

    epst = {"tk2": (0, 0), "tk3": (0, 0), "tk4": (0, 0), "odd": (0, 0)}

    def epilogue(slot, h, av, tok_av):
        blk = h // 2
        def fs(s, tok_av=tok_av, p2=epst["tk2"]):
            C.sT2.wait(s, tok_av)
            sT.wait(s, p2)
            a = s.activation(C.rows[64:65, 0:512], av[64:65, :], AF.Ln)
            sS.inc(a)
        C.s(fs)
        tk1 = sS.n
        def ft(t, tk1=tk1, p3=epst["tk3"]):
            sS.wait(t, tk1)
            sS.wait(t, p3)
            m = t.matmul(C.pB[0:64, 0:512], C.cstf[64:65, CF_ONESB:CF_ONESB + 64],
                         C.rows[64:65, 0:512], start=True, stop=True)
            sT.inc(m)
        C.t(ft)
        tk2 = sT.n
        epst["tk2"] = tk2
        def fs2(s, tk2=tk2, p4=epst["tk4"]):
            sT.wait(s, tk2)
            sV.wait(s, p4)
            a = s.activation(invs[0:64, :], C.pB[0:64, 0:512], AF.Exp, scale=-1.0)
            sS.inc(a)
        C.s(fs2)
        tk3 = sS.n
        epst["tk3"] = tk3
        def fv(v, h=h, blk=blk, tk3=tk3, po=epst["odd"]):
            sS.wait(v, tk3)
            if h % 2 == 0:
                dst = obuf[0:64, blk * 512:(blk + 1) * 512]
            else:
                C.sY.wait(v, po)
                dst = odds[0:64, 0:512]
            c = v.tensor_mul(dst, av[0:64, :], invs[0:64, :])
            sV.inc(c)
        C.v(fv)
        tk4 = sV.n
        epst["tk4"] = tk4
        if h % 2 == 1:
            def fy(y, blk=blk, tk4=tk4):
                sV.wait(y, tk4)
                C.ld(y, obuf[64:128, blk * 512:(blk + 1) * 512], odds[0:64, 0:512], sDS)
                sDS.wait(y)
                C.sY.inc(y.nop("rl", nofuse=True))
            C.y(fy)
            epst["odd"] = C.sY.n
        return (sV, sV.n)

    def k_ap(slot, kt, h):
        dc = h // 2
        return kn[64 * (h % 2):64 * (h % 2) + 64, dc * TP1 + 1 + kt * 128: dc * TP1 + 1 + kt * 128 + 128]

    def q_ap(slot, h):
        dc = h // 2
        return qs[64 * (h % 2):64 * (h % 2) + 64, dc * 512:(dc + 1) * 512]

    def vn_ap(kt, h):
        base = kt * sv + 65 * h
        return vn[:, base:base + 65]

    def bias_ap(slot, kt, h):
        c0 = C.mega_c0(slot, kt)
        if h < 8:
            return mega[:, c0:c0 + 512]
        cnt = 4 * (slot + 1)
        if kt >= cnt - 4:
            return mega[:, c0:c0 + 512]
        return None

    cfg = dict(H=16, counts=lambda s: 4 * (s + 1), k_ap=k_ap, q_ap=q_ap,
               vn_ap=vn_ap, bias_ap=bias_ap, phantom=None, eshift=-40.0,
               epilogue=epilogue, pre_head=pre_head, head_order=head_order)

    for slot in range(NT):
        state["cur"] = None
        q_load(slot)
        C.emit_attention_slot(cfg, slot)
        C.out_proj_residual(slot, obuf, wp, wp_o, C.wpT, 3072,
                            C.AF_t[:, xo_o:xo_o + 1024], C.AF_t[:, stf_o:stf_o + 1024], last)
        C.fence()

    AFA.pop()
    AB.pop()
    AB.pop()


def out_proj_residual(C, slot, obuf, wp, wp_o, wsrc, w2off, xo, stf, last, nwp=1):
    sT, sS, sV, sDW, sDX, sDS = C.sT, C.sS, C.sV, C.sDW, C.sDX, C.sDS
    t0 = slot * 512
    prev = {"ev": sV.n, "st": [(0, 0), (0, 0)], "mm": (0, 0), "xv": [(0, 0), (0, 0)],
            "wu": [(0, 0), (0, 0)]}
    for oc in range(8):
        sl = oc % 2
        wsl = oc % nwp
        def fg(g, oc=oc, wsl=wsl, pu=prev["wu"][wsl] if nwp == 2 else prev["mm"]):
            sT.wait(g, pu)
            src = bass.AP(wsrc.tensor, wsrc.offset + oc * 128,
                          [[8192, 128], [1024, 8], [1, 128]])
            sDW.inc(g.dma_start(out=wp[:, wsl * 1024:wsl * 1024 + 1024].rearrange("p (j o) -> p j o", j=8),
                        in_=src), 16)
            sDW.wait(g)
            C.sG.inc(g.nop("rl", nofuse=True))
        C.g(fg)
        wtok = C.sG.n
        def fy(y, oc=oc, sl=sl):
            sV.wait(y, prev["xv"][sl])
            C.ld(y, xo[:, sl * 512:sl * 512 + 512], C.xres[oc * 128:(oc + 1) * 128, t0:t0 + 512], sDX)
            sDX.wait(y)
            C.sY.inc(y.nop("rl", nofuse=True))
        C.y(fy)
        xtok = C.sY.n
        def ft(t, oc=oc, wtok=wtok, wsl=wsl):
            C.sG.wait(t, wtok)
            sV.wait(t, prev["ev"])
            if oc == 0:
                C.sY.wait(t)
            t.matmul(C.pC[0:128, 0:512], C.w2[:, w2off + oc * 128: w2off + (oc + 1) * 128],
                     C.g2[:, 0:512], start=True, stop=False)
            m = None
            for j in range(8):
                m = t.matmul(C.pC[0:128, 0:512], wp[:, wsl * 1024 + j * 128:wsl * 1024 + (j + 1) * 128],
                             obuf[:, j * 512:(j + 1) * 512], start=False, stop=(j == 7))
            sT.inc(m)
        C.t(ft)
        mtok = sT.n
        prev["mm"] = mtok
        prev["wu"][wsl] = mtok
        def fv(v, sl=sl, mtok=mtok, xtok=xtok, pst=prev["st"][sl]):
            sT.wait(v, mtok)
            C.sY.wait(v, xtok)
            C.sY.wait(v, pst)
            c = v.tensor_add(stf[:, sl * 512:sl * 512 + 512], xo[:, sl * 512:sl * 512 + 512],
                             C.pC[0:128, 0:512])
            sV.inc(c)
        C.v(fv)
        prev["ev"] = sV.n
        prev["xv"][sl] = sV.n
        vtok = sV.n
        def fy2(y, oc=oc, sl=sl, vtok=vtok):
            sV.wait(y, vtok)
            dst = (C.y_out if last else C.xres)
            C.ld(y, dst[oc * 128:(oc + 1) * 128, t0:t0 + 512], stf[:, sl * 512:sl * 512 + 512], sDS)
            sDS.wait(y)
            C.sY.inc(y.nop("rl", nofuse=True))
        C.y(fy2)
        prev["st"][sl] = C.sY.n


KB.sub_attn_b = sub_attn_b
KB.out_proj_residual = out_proj_residual
KB.load_wsec = load_wsec

def prep_inputs(ins, T=4096):
    f32 = np.float32
    d = {}
    d["wkT"] = np.ascontiguousarray(ins["L0_attn_a_W_k"].T.astype(f32))
    d["wvT"] = np.ascontiguousarray(ins["L0_attn_a_W_v"].T.astype(f32))
    wc = ins["L0_attn_a_W_c"].astype(f32)
    wc_re = np.zeros((128, 8 * 1024), dtype=f32)
    for j in range(8):
        for p in range(128):
            wc_re[p, j * 1024:(j + 1) * 1024] = wc[:, 64 * (2 * j + p // 64) + (p % 64)]
    d["wcT"] = wc_re
    d["aprm"] = np.concatenate([
        ins["L0_attn_a_leaky_key_beta"].ravel(), ins["L0_attn_a_key_scale"].ravel(),
        ins["L0_attn_a_v_coef"].ravel(), ins["L0_attn_a_val_scale"].ravel()]
    ).astype(f32).reshape(1, 64)
    d["wqkvT"] = np.ascontiguousarray(ins["L0_attn_b_w_qkv"].T.astype(f32))
    d["bqkv"] = ins["L0_attn_b_b_qkv"].astype(f32).reshape(1, 3072)
    wp = ins["L0_attn_b_w_proj"].astype(f32)
    wp_re = np.zeros((128, 8 * 1024), dtype=f32)
    for j in range(8):
        for p in range(128):
            wp_re[p, j * 1024:(j + 1) * 1024] = wp[:, 64 * (2 * j + p // 64) + (p % 64)]
    d["wpT"] = wp_re
    d["bproj"] = ins["L0_attn_b_b_proj"].astype(f32).reshape(1, 1024)
    d["wqf"] = np.ascontiguousarray(ins["L0_ffn_w_q"].astype(f32))
    d["wkf"] = np.ascontiguousarray(ins["L0_ffn_w_k"].astype(f32))
    d["wgf"] = np.ascontiguousarray(ins["L0_ffn_w_gate"].astype(f32))
    ve = ins["L0_ffn_value_experts"].astype(f32)
    d["wve"] = np.ascontiguousarray(np.transpose(ve, (1, 2, 0, 3)).reshape(8, 1024, 512))
    oe = ins["L0_ffn_output_experts"].astype(f32)
    woe = np.zeros((32, 128, 1024), dtype=f32)
    for e in range(8):
        for j in range(4):
            m = e * 4 + j
            woe[m, 0:64, :] = oe[e, 2 * j]
            woe[m, 64:128, :] = oe[e, 2 * j + 1]
    d["woe"] = woe
    ma, mf, hh, hl = build_tables(T)
    d["tbl_ma"] = ma
    d["tbl_mf"] = mf
    d["tbl_hh"] = hh
    d["tbl_hl"] = hl
    d["cstb_in"] = mk_cstb()
    p = np.arange(128)[:, None]
    j = np.arange(128)[None, :]
    tu = np.zeros((128, 256), dtype=f32)
    tu[:, 0:128] = np.maximum(j - p, 0).astype(f32)
    tu[:, 128:256] = (j >= p).astype(f32)
    d["tblu"] = tu
    mr = np.zeros((2, 4096), dtype=np.float32); mr[0, :] = 1.0
    d["mrows"] = mr
    d["cstf_in"] = mk_cstf()
    d["selc_in"] = mk_selectors()
    return d


_SUBS_ALL = ("attn_a", "ffn1", "attn_b", "ffn2")


def run_device(ins, T=4096, subs=_SUBS_ALL, n_b=None):
    x = np.asarray(ins["L0_attn_a_x"], dtype=np.float32)
    B = x.shape[0]
    n_b = n_b or B
    shared = prep_inputs(ins, T)
    kb = KB(T, subs)
    nc = kb.build()
    in_maps = []
    for c in range(8):
        b = c % n_b
        m = dict(shared)
        m["x_feat"] = np.ascontiguousarray(x[b].T)
        in_maps.append(m)
    res = run_bass_kernel_spmd(nc, in_maps, core_ids=list(range(8)))
    outs = []
    for b in range(B):
        outs.append(np.ascontiguousarray(res.results[b]["y"].T))
    return np.stack(outs, axis=0)


def kernel(**inputs):
    return run_device(inputs, T=4096, subs=_SUBS_ALL)

def sub_ffn(C, last):
    T, NT, NC, TP1 = C.T, C.NT, C.NC, C.TP1
    sv = 65 * 8
    AB, AFA = C.AB, C.AFA
    sT, sS, sV, sG, sDW, sDX, sDS = C.sT, C.sS, C.sV, C.sG, C.sDW, C.sDX, C.sDS

    AB.push()
    knf_o = AB.alloc(4 * TP1)
    knf = C.AB_t[:, knf_o:knf_o + 4 * TP1]
    vnf_o = AB.alloc(NC * sv)
    vnf = C.AB_t[:, vnf_o:vnf_o + NC * sv]
    gmT_o = AB.alloc(NT * 512)
    def f(g):
        g.memset(knf[:, :].rearrange("p (c t) -> p c t", c=4)[:, :, 0:1], 0.0)
        a = g.memset(vnf[:, :].rearrange("p (k h c) -> p k h c", k=NC, h=8)[:, :, :, 64:65], 1.0)
        sG.inc(a)
    C.g(f)
    C.fence()

    AB.push()
    AFA.push()
    xt = C.AF_t[:, AFA.alloc(4096):AFA.stack[-1]]
    gt_o = AFA.alloc(640)
    mt_o = AFA.alloc(512)
    tmp_o = AFA.alloc(512)
    xb_o = AB.alloc(4096)
    xb = C.AB_t[:, xb_o:xb_o + 4096]
    wq_o = AB.alloc(4096)
    wk_o = AB.alloc(4096)
    wg_o = AB.alloc(512)
    wv_o = AB.alloc(8192)
    stg_o = AB.alloc(1024)
    stg = C.AB_t[:, stg_o:stg_o + 1024]

    load_wsec(C, wq_o, C.wqf, 512, 512)
    load_wsec(C, wk_o, C.wkf, 512, 512)
    load_wsec(C, wg_o, C.wgf, 64, 64)
    C.w1_bias(C.AB_t[:, wq_o:wq_o + 4096], 512, 0, None)
    C.w1_bias(C.AB_t[:, wk_o:wk_o + 4096], 512, 512, None)
    C.w1_bias(C.AB_t[:, wg_o:wg_o + 512], 64, 1024, None)

    wvu = [(0, 0), (0, 0)]
    for tt in range(NT):
        lt = C.ln_full(tt, xt, xb, store=False)
        prev_cp = [sS.n]
        for oc in range(4):
            def pre(t, pc=prev_cp[0], lt=lt):
                sS.wait(t, pc)
                sV.wait(t, lt)
            tok = C.proj_feat_oc(xb, lambda ic, oc=oc: C.AB_t[:, wq_o + ic * 512 + oc * 128: wq_o + ic * 512 + (oc + 1) * 128],
                                 oc, 0, pre=pre)
            sl = oc % 2
            def fse(s, sl=sl, tok=tok):
                sT.wait(s, tok)
                C.sY.wait(s)
                a = s.activation(stg[:, sl * 512:(sl + 1) * 512], C.pC[0:128, 0:512], AF.Copy, scale=0.125)
                sS.inc(a)
            C.s(fse)
            prev_cp[0] = sS.n
            cp = sS.n
            def fy(y, oc=oc, sl=sl, tt=tt, cp=cp):
                sS.wait(y, cp)
                C.ld(y, C.qhb[oc * 128:(oc + 1) * 128, tt * 512:(tt + 1) * 512],
                     stg[:, sl * 512:(sl + 1) * 512], sDS)
                sDS.wait(y)
                C.sY.inc(y.nop("rl", nofuse=True))
            C.y(fy)
        for oc in range(4):
            def pre(t, pc=prev_cp[0]):
                sS.wait(t, pc)
            tok = C.proj_feat_oc(xb, lambda ic, oc=oc: C.AB_t[:, wk_o + ic * 512 + oc * 128: wk_o + ic * 512 + (oc + 1) * 128],
                                 oc, 512, pre=pre)
            def fse(s, oc=oc, tt=tt, tok=tok):
                sT.wait(s, tok)
                a = s.activation(knf[:, oc * TP1 + tt * 512 + 1: oc * TP1 + tt * 512 + 513],
                                 C.pC[0:128, 0:512], AF.Copy)
                sS.inc(a)
            C.s(fse)
            prev_cp[0] = sS.n
        gm_toks = {}
        for tc in range(4):
            def pre(t, pc=prev_cp[0]):
                sS.wait(t, pc)
            tok = C.proj_tok_tc(xb, lambda ic: C.AB_t[:, wg_o + ic * 64: wg_o + (ic + 1) * 64],
                                tc, 64, 1024, pre=pre)
            gbase = gt_o + tc * 160
            def fse(s, tok=tok, gbase=gbase):
                sT.wait(s, tok)
                a = s.activation(C.AF_t[:, gbase:gbase + 64], C.pC[0:128, 0:64], AF.Sigmoid)
                sS.inc(a)
            C.s(fse)
            prev_cp[0] = sS.n
            stok = sS.n
            def fv(v, gbase=gbase, stok=stok):
                sS.wait(v, stok)
                sV.wait(v)
                g3 = C.AF_t[:, gbase:gbase + 64].rearrange("p (h e) -> p h e", h=8)
                a = v.tensor_reduce(C.AF_t[:, gbase + 128:gbase + 136], g3,
                                    mybir.AxisListType.X, ALU.max)
                tk = sV.inc(a)
                sV.wait(v, tk)
                mb = bass.AP(C.AF_t.ap().tensor, C.AF_t.ap().offset + gbase + 128,
                             [C.AF_t.ap().ap[0], [1, 8], [0, 8]])
                a = v.tensor_tensor(C.AF_t[:, gbase + 64:gbase + 128],
                                    C.AF_t[:, gbase:gbase + 64], mb, ALU.is_equal)
                tk = sV.inc(a)
                sV.wait(v, tk)
                c = v.tensor_mul(C.AF_t[:, gbase + 64:gbase + 128],
                                 C.AF_t[:, gbase + 64:gbase + 128], C.AF_t[:, gbase:gbase + 64])
                gm_toks[gbase] = sV.inc(c)
            C.v(fv)
        def fgf(t, pc=prev_cp[0], lt=lt):
            sS.wait(t, pc)
            sV.wait(t, lt)
            t.matmul(C.pC[0:64, 0:512], C.w2[:, 1024:1088], C.g2[:, 0:512], start=True, stop=False)
            m = None
            for ic in range(8):
                m = t.matmul(C.pC[0:64, 0:512], C.AB_t[:, wg_o + ic * 64: wg_o + (ic + 1) * 64],
                             xb[:, ic * 512:(ic + 1) * 512], start=False, stop=(ic == 7))
            sT.inc(m)
        C.t(fgf)
        tkgf = sT.n
        def fse(s, tkgf=tkgf):
            sT.wait(s, tkgf)
            a = s.activation(C.AF_t[0:64, tmp_o:tmp_o + 512], C.pC[0:64, 0:512], AF.Sigmoid)
            sS.inc(a)
        C.s(fse)
        tk_gf = sS.n
        prev_cp[0] = sS.n
        def ftr(t, tk_gf=tk_gf, gm_toks=gm_toks):
            for tc in range(4):
                sV.wait(t, gm_toks[gt_o + tc * 160])
            sS.wait(t, tk_gf)
            m = None
            for tc in range(4):
                m = t.transpose(C.pC[0:8, 128 * tc:128 * tc + 128],
                                C.AF_t[:, gt_o + tc * 160 + 128: gt_o + tc * 160 + 136],
                                C.cstf[:, CF_ID:CF_ID + 128])
            sT.inc(m)
        C.t(ftr)
        tktr = sT.n
        def fmt(v, tktr=tktr):
            sT.wait(v, tktr)
            c = v.tensor_copy(C.AF_t[0:8, mt_o:mt_o + 512], C.pC[0:8, 0:512])
            sV.inc(c)
        C.v(fmt)
        tkmt = sV.n
        def fmb(t, tkmt=tkmt):
            sV.wait(t, tkmt)
            m = t.matmul(C.pC[0:64, 0:512], C.cstf[0:8, CF_E8:CF_E8 + 64],
                         C.AF_t[0:8, mt_o:mt_o + 512], start=True, stop=True)
            sT.inc(m)
        C.t(fmb)
        tkmb = sT.n
        def fgm(v, tkmb=tkmb, tt=tt):
            sT.wait(v, tkmb)
            a = v.tensor_tensor(C.AF_t[0:64, mt_o:mt_o + 512],
                                C.AF_t[0:64, tmp_o:tmp_o + 512], C.pC[0:64, 0:512], ALU.is_equal)
            tk = sV.inc(a)
            sV.wait(v, tk)
            c = v.tensor_mul(C.AB_t[0:64, gmT_o + tt * 512: gmT_o + (tt + 1) * 512],
                             C.AF_t[0:64, mt_o:mt_o + 512], C.AF_t[0:64, tmp_o:tmp_o + 512])
            sV.inc(c)
        C.v(fgm)
        for hh in range(8):
            sl = hh % 2
            wvb = wv_o + sl * 4096
            def fg(g, hh=hh, pu=wvu[sl]):
                sT.wait(g, pu)
                src = bass.AP(C.wve.tensor, C.wve.offset + hh * 1024 * 512,
                              [[512, 128], [128 * 512, 8], [1, 512]])
                sDW.inc(g.dma_start(out=C.AB_t[:, wv_o + (hh % 2) * 4096:wv_o + (hh % 2) * 4096 + 4096].rearrange("p (i o) -> p i o", i=8),
                        in_=src), 16)
                sDW.wait(g)
                sG.inc(g.nop("rl", nofuse=True))
            C.g(fg)
            C.w1_bias(C.AB_t[:, wvb:wvb + 4096], 512, 2048 + sl * 512, None)
            for tc in range(4):
                def pre(t, pc=prev_cp[0]):
                    sS.wait(t, pc)
                    sV.wait(t)
                tok = C.proj_tok_tc(xb, lambda ic, wvb=wvb: C.AB_t[:, wvb + ic * 512:wvb + (ic + 1) * 512],
                                    tc, 512, 2048 + sl * 512, pre=pre)
                kt = tt * 4 + tc
                gbase = gt_o + tc * 160
                def fcv(v, hh=hh, kt=kt, tok=tok, gbase=gbase):
                    sT.wait(v, tok)
                    sV.wait(v)
                    t3 = C.AF_t[:, tmp_o:tmp_o + 512]
                    gmb = bass.AP(C.AF_t.ap().tensor, C.AF_t.ap().offset + gbase + 64 + 8 * hh,
                                  [C.AF_t.ap().ap[0], [1, 8], [0, 64]])
                    a = v.tensor_tensor(t3, C.pC[0:128, 0:512], gmb, ALU.mult)
                    tk = sV.inc(a)
                    sV.wait(v, tk)
                    a = v.tensor_add(t3[:, 0:256], t3[:, 0:256], t3[:, 256:512])
                    tk = sV.inc(a)
                    sV.wait(v, tk)
                    a = v.tensor_add(t3[:, 0:128], t3[:, 0:128], t3[:, 128:256])
                    tk = sV.inc(a)
                    sV.wait(v, tk)
                    c = v.tensor_add(vnf[:, kt * sv + 65 * hh: kt * sv + 65 * hh + 64],
                                     t3[:, 0:64], t3[:, 64:128])
                    sV.inc(c)
                C.v(fcv)
            wvu[sl] = sT.n
    AFA.pop()
    AB.pop()
    C.fence()

    AB.push()
    AFA.push()
    qs_o = AB.alloc(2048)
    qs = C.AB_t[:, qs_o:qs_o + 2048]
    ob_o = AB.alloc(2048)
    obuf = C.AB_t[:, ob_o:ob_o + 2048]
    odd_o = AB.alloc(512)
    odds = C.AB_t[:, odd_o:odd_o + 512]
    selc_o = AB.alloc(4096)
    selc = C.AB_t[:, selc_o:selc_o + 4096]
    ub_o = AB.alloc(512)
    ub = C.AB_t[:, ub_o:ub_o + 512]
    oe_o = AB.alloc(32 * 512)
    woe_o = AB.alloc(8192)
    meg_o = AFA.alloc(C.MEGW)
    mega = C.AF_t[:, meg_o:meg_o + C.MEGW]
    srow_o = AFA.alloc(512)
    srow = C.AF_t[:, srow_o:srow_o + 512]
    xo_o = AFA.alloc(1024)
    stf_o = AFA.alloc(1024)

    def f(g):
        C.ld(g, C.AB_t[0:64, selc_o:selc_o + 4096], C.selc_in[:, :])
    C.g(f)
    C.emit_mega_mask(mega, C.tbl_mf, qshift=0)
    C.fence()

    epst = {"gmm": (0, 0)}

    def epilogue(slot, h, av, tok_av):
        blk = h // 2
        def fs(s, h=h, blk=blk, tok_av=tok_av, pg=epst["gmm"]):
            C.sT2.wait(s, tok_av)
            sT.wait(s, pg)
            if h % 2 == 0:
                dst = obuf[0:64, blk * 512:(blk + 1) * 512]
            else:
                C.sY.wait(s)
                dst = odds[0:64, 0:512]
            s.activation(dst, av[0:64, :], AF.Copy)
            a = s.activation(srow[64:65, 0:512], av[64:65, :], AF.Copy)
            sS.inc(a)
        C.s(fs)
        tk1 = sS.n
        def ft(t, h=h, tk1=tk1):
            sS.wait(t, tk1)
            m = t.matmul(C.pB[0:8, 0:512], C.cstf[64:65, CF_ID + 64 - h:CF_ID + 72 - h],
                         srow[64:65, 0:512], start=(h == 0), stop=(h == 7))
            sT.inc(m)
        C.t(ft)
        epst["gmm"] = sT.n
        if h % 2 == 1:
            def fy(y, blk=blk, tk1=tk1):
                sS.wait(y, tk1)
                C.ld(y, obuf[64:128, blk * 512:(blk + 1) * 512], odds[0:64, 0:512], sDS)
                sDS.wait(y)
                C.sY.inc(y.nop("rl", nofuse=True))
            C.y(fy)
        return (sS, tk1)

    def q_load(slot):
        def f(y):
            sT.wait(y)
            for dc in range(4):
                C.ld(y, qs[:, dc * 512:(dc + 1) * 512],
                     C.qhb[dc * 128:(dc + 1) * 128, slot * 512:(slot + 1) * 512], sDX)
            sDX.wait(y)
            C.sY.inc(y.nop("rl", nofuse=True))
        C.y(f)
        tkq = C.sY.n
        def ft(t, tkq=tkq):
            C.sY.wait(t, tkq)
        C.t(ft)

    def k_ap(slot, kt, h):
        dc = h // 2
        return knf[64 * (h % 2):64 * (h % 2) + 64, dc * TP1 + 1 + kt * 128: dc * TP1 + 1 + kt * 128 + 128]

    def q_ap(slot, h):
        dc = h // 2
        return qs[64 * (h % 2):64 * (h % 2) + 64, dc * 512:(dc + 1) * 512]

    def vn_ap(kt, h):
        base = kt * sv + 65 * h
        return vnf[:, base:base + 65]

    def bias_ap(slot, kt, h):
        cnt = 4 * (slot + 1)
        if kt >= cnt - 4:
            c0 = C.mega_c0(slot, kt)
            return mega[:, c0:c0 + 512]
        return None

    cfg = dict(H=8, counts=lambda s: 4 * (s + 1), k_ap=k_ap, q_ap=q_ap,
               vn_ap=vn_ap, bias_ap=bias_ap, phantom=None, eshift=0.0,
               epilogue=epilogue)

    wou = [(0, 0), (0, 0)]
    for slot in range(NT):
        q_load(slot)
        C.emit_attention_slot(cfg, slot)
        def fr8(s):
            sT.wait(s)
            a = s.activation(C.rows[0:8, 0:512], C.pB[0:8, 0:512], AF.Ln)
            tk = sS.inc(a)
            sS.wait(s, tk)
            a = s.activation(C.rows[0:8, 512:1024], C.rows[0:8, 0:512], AF.Exp, scale=-1.0)
            sS.inc(a)
        C.s(fr8)
        tkr8 = sS.n
        def frb(t, tkr8=tkr8):
            sS.wait(t, tkr8)
            sV.wait(t)
            m = t.matmul(C.pC[0:64, 0:512], C.cstf[0:8, CF_E8:CF_E8 + 64],
                         C.rows[0:8, 512:1024], start=True, stop=True)
            sT.inc(m)
        C.t(frb)
        tkrb = sT.n
        def fu(v, slot=slot, tkrb=tkrb):
            sT.wait(v, tkrb)
            c = v.tensor_mul(ub[0:64, 0:512], C.AB_t[0:64, gmT_o + slot * 512: gmT_o + (slot + 1) * 512],
                             C.pC[0:64, 0:512])
            sV.inc(c)
        C.v(fu)
        tku = sV.n
        prev_oe = [tku]
        for mm_i in range(32):
            def fub(t, mm_i=mm_i, po=prev_oe[0]):
                sV.wait(t, po)
                m = t.matmul(C.pC[0:128, 0:512], C.AB_t[0:64, selc_o + 128 * mm_i: selc_o + 128 * mm_i + 128],
                             ub[0:64, 0:512], start=True, stop=True)
                sT.inc(m)
            C.t(fub)
            tub = sT.n
            jj = mm_i % 4
            def foe(v, mm_i=mm_i, jj=jj, tub=tub):
                sT.wait(v, tub)
                if mm_i == 0:
                    C.sY.wait(v)
                c = v.tensor_mul(C.AB_t[:, oe_o + mm_i * 512: oe_o + (mm_i + 1) * 512],
                                 obuf[:, jj * 512:(jj + 1) * 512], C.pC[0:128, 0:512])
                sV.inc(c)
            C.v(foe)
            prev_oe[0] = sV.n
        tkoe = sV.n
        t0 = slot * 512
        prevv = {"mm": (0, 0), "ev": tkoe, "st": [(0, 0), (0, 0)], "xv": [(0, 0), (0, 0)],
                 "wu": [wou[0], wou[1]]}
        for oc in range(8):
            sl = oc % 2
            wob = woe_o + sl * 4096
            def fgw(g, oc=oc, pu=prevv["wu"][sl]):
                sT.wait(g, pu)
                src = bass.AP(C.woe.tensor, C.woe.offset + oc * 128,
                              [[1024, 128], [128 * 1024, 32], [1, 128]])
                sDW.inc(g.dma_start(out=C.AB_t[:, woe_o + (oc % 2) * 4096:woe_o + (oc % 2) * 4096 + 4096].rearrange("p (m o) -> p m o", m=32),
                        in_=src), 16)
                sDW.wait(g)
                sG.inc(g.nop("rl", nofuse=True))
            C.g(fgw)
            wtok = sG.n
            def fy(y, oc=oc, sl=sl):
                sV.wait(y, prevv["xv"][sl])
                C.ld(y, C.AF_t[:, xo_o + sl * 512: xo_o + sl * 512 + 512],
                     C.xres[oc * 128:(oc + 1) * 128, t0:t0 + 512], sDX)
                sDX.wait(y)
                C.sY.inc(y.nop("rl", nofuse=True))
            C.y(fy)
            xtok = C.sY.n
            def ft(t, oc=oc, wtok=wtok, wob=wob):
                sG.wait(t, wtok)
                sV.wait(t, prevv["ev"])
                m = None
                for mm_i in range(32):
                    m = t.matmul(C.pC[0:128, 0:512], C.AB_t[:, wob + mm_i * 128: wob + (mm_i + 1) * 128],
                                 C.AB_t[:, oe_o + mm_i * 512: oe_o + (mm_i + 1) * 512],
                                 start=(mm_i == 0), stop=(mm_i == 31))
                sT.inc(m)
            C.t(ft)
            prevv["mm"] = sT.n
            prevv["wu"][sl] = sT.n
            mtok = sT.n
            def fv(v, sl=sl, mtok=mtok, pst=prevv["st"][sl], xtok=xtok):
                sT.wait(v, mtok)
                C.sY.wait(v, xtok)
                C.sY.wait(v, pst)
                c = v.tensor_add(C.AF_t[:, stf_o + sl * 512: stf_o + sl * 512 + 512],
                                 C.AF_t[:, xo_o + sl * 512: xo_o + sl * 512 + 512], C.pC[0:128, 0:512])
                sV.inc(c)
            C.v(fv)
            prevv["ev"] = sV.n
            prevv["xv"][sl] = sV.n
            vtok = sV.n
            def fy2(y, oc=oc, sl=sl, vtok=vtok):
                sV.wait(y, vtok)
                dst = (C.y_out if last else C.xres)
                C.ld(y, dst[oc * 128:(oc + 1) * 128, t0:t0 + 512],
                     C.AF_t[:, stf_o + sl * 512: stf_o + sl * 512 + 512], sDS)
                sDS.wait(y)
                C.sY.inc(y.nop("rl", nofuse=True))
            C.y(fy2)
            prevv["st"][sl] = C.sY.n
        wou[0], wou[1] = prevv["wu"]
        C.fence()

    AFA.pop()
    AB.pop()
    AB.pop()


KB.sub_ffn = sub_ffn

def sub_attn_a(C, last):
    T, NT, NC, TP1 = C.T, C.NT, C.NC, C.TP1
    sv = 65 * 16
    AB, AFA = C.AB, C.AFA
    sT, sS, sV, sG, sY = C.sT, C.sS, C.sV, C.sG, C.sY
    sDW, sDX, sDS = C.sDW, C.sDX, C.sDS
    AFT = C.AF_t

    AB.push()
    AFA.push()
    pr_o = AFA.alloc(64)
    dr_o = AFA.alloc(64)
    sc_o = AFA.alloc(64)
    ap_o = AFA.alloc(64)
    v16_o = AFA.alloc(1)

    def f(g):
        C.ld(g, AFT[0:1, pr_o:pr_o + 64], C.aprm[0:1, :])
        sDW.wait(g)
        sG.inc(g.nop("rl", nofuse=True))
    C.g(f)
    tok_pr = sG.n

    def f(s, tok_pr=tok_pr):
        sG.wait(s, tok_pr)
        a = s.activation(AFT[0:1, sc_o:sc_o + 16], AFT[0:1, pr_o:pr_o + 16], AF.Abs)
        tk = sS.inc(a)
        sS.wait(s, tk)
        a = s.activation(AFT[0:1, dr_o:dr_o + 16], AFT[0:1, sc_o:sc_o + 16], AF.Copy, scale=-10.0)
        sS.inc(a)
        a = s.activation(AFT[0:1, sc_o + 16:sc_o + 32], AFT[0:1, pr_o + 16:pr_o + 32], AF.Copy, scale=10.0)
        sS.inc(a)
        a = s.activation(AFT[0:1, dr_o + 32:dr_o + 48], AFT[0:1, pr_o + 32:pr_o + 48], AF.Copy)
        sS.inc(a)
        a = s.activation(AFT[0:1, sc_o + 48:sc_o + 64], AFT[0:1, pr_o + 48:pr_o + 64], AF.Exp, scale=10.0)
        sS.inc(a)
    C.s(f)
    tk_s1 = sS.n

    def f(v, tk_s1=tk_s1):
        sS.wait(v, tk_s1)
        ksm = bass.AP(C.cstf.ap().tensor, C.cstf.ap().offset + CF_CONST + 3,
                      [[C.cstf.ap().ap[0][0], 1], [0, 16]])
        a = v.tensor_tensor(AFT[0:1, sc_o + 32:sc_o + 48], AFT[0:1, sc_o + 16:sc_o + 32],
                            ksm, ALU.min)
        sV.inc(a)
        one = bass.AP(C.cstf.ap().tensor, C.cstf.ap().offset + CF_CONST + 4,
                      [[C.cstf.ap().ap[0][0], 1], [0, 16]])
        a = v.tensor_tensor(AFT[0:1, dr_o + 48:dr_o + 64], one,
                            AFT[0:1, pr_o + 32:pr_o + 48], ALU.subtract)
        sV.inc(a)
    C.v(f)
    tk_v1 = sV.n

    def f(s, tk_v1=tk_v1):
        sV.wait(s, tk_v1)
        a = s.activation(AFT[0:1, dr_o + 16:dr_o + 32], AFT[0:1, sc_o + 32:sc_o + 48], AF.Exp)
        sS.inc(a)
    C.s(f)
    tk_s2 = sS.n

    def fmm(t, tk_s2=tk_s2, tk_v1=tk_v1):
        sS.wait(t, tk_s2)
        sV.wait(t, tk_v1)
        m = t.matmul(C.pB[0:128, 0:64], C.cstf[0:1, CF_ONESB:CF_ONESB + 128],
                     AFT[0:1, dr_o:dr_o + 64], start=True, stop=True)
        sT.inc(m)
        m = t.matmul(C.pB[0:16, 64:65], AFT[0:1, sc_o + 48:sc_o + 64],
                     C.cstf[0:1, CF_CONST + 4:CF_CONST + 5], start=True, stop=True)
        sT.inc(m)
    C.t(fmm)
    tkmm = sT.n

    def fev(s, tkmm=tkmm):
        sT.wait(s, tkmm)
        a = s.activation(AFT[:, ap_o:ap_o + 64], C.pB[0:128, 0:64], AF.Copy)
        sS.inc(a)
        a = s.activation(AFT[0:16, v16_o:v16_o + 1], C.pB[0:16, 64:65], AF.Copy)
        sS.inc(a)
    C.s(fev)
    tok_ap = sS.n

    AB.push()
    AFA.push()
    du_o = AB.alloc(128)
    ut_o = AB.alloc(128)
    dcb_o = AB.alloc(2048)
    car_o = AB.alloc(1024)
    kn_o = AB.alloc(8 * TP1)
    kn = C.AB_t[:, kn_o:kn_o + 8 * TP1]
    wsk_o = AB.alloc(8192)
    wsk = C.AB_t[:, wsk_o:wsk_o + 8192]
    xb_o = AB.alloc(4096)
    xb = C.AB_t[:, xb_o:xb_o + 4096]
    kt_o = AB.alloc(1024)
    ktok = C.AB_t[:, kt_o:kt_o + 1024]
    cr2_o = AB.alloc(1024)
    xt = AFT[:, AFA.alloc(4096):AFA.stack[-1]]
    kns_o = AFA.alloc(1024)
    kns = AFT[:, kns_o:kns_o + 1024]
    lt1_o = AFA.alloc(128)
    lt2_o = AFA.alloc(128)
    sq_o = AFA.alloc(1024)
    nr_o = AFA.alloc(96)

    def f(g):
        a = g.memset(kn[:, :].rearrange("p (c t) -> p c t", c=8)[:, :, 0:1], 0.0)
        sG.inc(a)
    C.g(f)

    def f(g):
        C.ld(g, C.AB_t[:, du_o:du_o + 128], C.tblu[:, 0:128])
        C.ld(g, C.AB_t[:, ut_o:ut_o + 128], C.tblu[:, 128:256])
        sDW.wait(g)
        sG.inc(g.nop("rl", nofuse=True))
    C.g(f)
    tk4 = sG.n

    for h in range(16):
        def fv(v, h=h, tk4=tk4):
            sG.wait(v, tk4)
            sS.wait(v, tok_ap)
            sV.wait(v)
            sS.wait(v)
            nb = bass.AP(AFT.ap().tensor, AFT.ap().offset + ap_o + h,
                         [AFT.ap().ap[0], [0, 128]])
            a = v.tensor_tensor(AFT[:, lt1_o:lt1_o + 128], C.AB_t[:, du_o:du_o + 128],
                                nb, ALU.mult)
            sV.inc(a)
        C.v(fv)
        tkm = sV.n
        def fs(s, h=h, tkm=tkm):
            sV.wait(s, tkm)
            a = s.activation(AFT[:, lt2_o:lt2_o + 128], AFT[:, lt1_o:lt1_o + 128], AF.Exp)
            sS.inc(a)
        C.s(fs)
        tke = sS.n
        def fv2(v, h=h, tke=tke):
            sS.wait(v, tke)
            a = v.tensor_mul(C.pt[:, 128 * h:128 * h + 128], AFT[:, lt2_o:lt2_o + 128],
                             C.AB_t[:, ut_o:ut_o + 128])
            tk = sV.inc(a)
            sV.wait(v, tk)
            nb0 = bass.AP(AFT.ap().tensor, AFT.ap().offset + ap_o + h,
                          [[AFT.ap().ap[0][0], 1], [0, 128]])
            a = v.tensor_tensor(AFT[0:1, lt1_o:lt1_o + 128], C.cstb[0:1, CB_P1ROW:CB_P1ROW + 128],
                                nb0, ALU.mult)
            sV.inc(a)
        C.v(fv2)
        tkd = sV.n
        def fs2(s, h=h, tkd=tkd):
            sV.wait(s, tkd)
            a = s.activation(C.AB_t[0:1, dcb_o + 128 * h:dcb_o + 128 * h + 128],
                             AFT[0:1, lt1_o:lt1_o + 128], AF.Exp)
            sS.inc(a)
        C.s(fs2)

    wtokK = load_wsec(C, wsk_o, C.wkT, 1024, 1024)
    C.w1_bias(wsk, 1024, 0, None)
    C.fence()

    st = {"evac": (0, 0), "knmul": (0, 0), "scan": (0, 0), "knev": (0, 0),
          "carev": (0, 0)}
    for tt in range(NT):
        lt = C.ln_full(tt, xt, xb, store=True)
        for tc in range(4):
            c = 4 * tt + tc
            for half in range(2):
                def pre(t, pc=st["evac"], lt=lt, wtokK=wtokK):
                    sG.wait(t, wtokK)
                    sS.wait(t, pc)
                    sV.wait(t, lt)
                tok = C.proj_tok_tc(xb, lambda ic, half=half: wsk[:, ic * 1024 + half * 512: ic * 1024 + half * 512 + 512],
                                    tc, 512, 0, o0=half * 512, pre=pre)
                def fse(s, half=half, tok=tok, sc=st["scan"]):
                    sT.wait(s, tok)
                    sT.wait(s, sc)
                    a = s.activation(ktok[:, half * 512:(half + 1) * 512], C.pC[0:128, 0:512], AF.Copy)
                    sS.inc(a)
                C.s(fse)
                st["evac"] = sS.n

            ktokTok = sS.n
            def ftsc(t, c=c, ktokTok=ktokTok, pk=st["knmul"]):
                sS.wait(t, ktokTok)
                sV.wait(t, pk)
                m = None
                for h in range(16):
                    m = t.matmul(C.pL[:, 64 * h:64 * h + 64], C.pt[:, 128 * h:128 * h + 128],
                                 ktok[:, 64 * h:64 * h + 64], start=True, stop=(c == 0))
                    if c > 0:
                        m = t.matmul(C.pL[:, 64 * h:64 * h + 64],
                                     C.AB_t[0:1, dcb_o + 128 * h:dcb_o + 128 * h + 128],
                                     C.AB_t[0:1, car_o + 64 * h:car_o + 64 * h + 64],
                                     start=False, stop=True)
                sT.inc(m)
            C.t(ftsc)
            st["scan"] = sT.n
            scanTok = sT.n
            if c < NC - 1:
                def fcr(s, scanTok=scanTok, pce=st["carev"]):
                    sT.wait(s, scanTok)
                    sS.wait(s, pce)
                    s.activation(C.AB_t[64:128, cr2_o:cr2_o + 512], C.pL[64:128, 0:512], AF.Copy)
                    a = s.activation(C.AB_t[64:128, cr2_o + 512:cr2_o + 1024],
                                     C.pL[64:128, 512:1024], AF.Copy)
                    tk = sS.inc(a)
                    sS.wait(s, tk)
                    C.sDA.inc(s.dma_start(out=C.AB_t[0:1, car_o:car_o + 1024],
                                          in_=C.AB_t[127:128, cr2_o:cr2_o + 1024]), 16)
                    C.sDA.wait(s)
                    sS.inc(s.nop("rl", nofuse=True))
                C.s(fcr)
                st["carev"] = sS.n
            def fsq(s, scanTok=scanTok):
                sT.wait(s, scanTok)
                a = s.activation(AFT[:, sq_o:sq_o + 512], C.pL[:, 0:512], AF.Square)
                sS.inc(a)
                a = s.activation(AFT[:, sq_o + 512:sq_o + 1024], C.pL[:, 512:1024], AF.Square)
                sS.inc(a)
            C.s(fsq)
            tksq = sS.n
            def fnr(v, tksq=tksq):
                sS.wait(v, tksq)
                sV.wait(v)
                g3 = AFT[:, sq_o:sq_o + 1024].rearrange("p (h d) -> p h d", h=16)
                a = v.tensor_reduce(AFT[:, nr_o:nr_o + 16], g3, mybir.AxisListType.X, ALU.add)
                sV.inc(a)
            C.v(fnr)
            tkr = sV.n
            def fns(s, tkr=tkr):
                sV.wait(s, tkr)
                a = s.activation(AFT[:, nr_o + 16:nr_o + 32], AFT[:, nr_o:nr_o + 16], AF.Sqrt)
                sS.inc(a)
            C.s(fns)
            tkn1 = sS.n
            def fne(v, tkn1=tkn1):
                sS.wait(v, tkn1)
                eps = bass.AP(C.cstf.ap().tensor, C.cstf.ap().offset + CF_CONST + 1,
                              [C.cstf.ap().ap[0], [0, 16]])
                a = v.tensor_tensor(AFT[:, nr_o + 32:nr_o + 48], AFT[:, nr_o + 16:nr_o + 32],
                                    eps, ALU.add)
                sV.inc(a)
            C.v(fne)
            tkn2 = sV.n
            def fnl(s, tkn2=tkn2):
                sV.wait(s, tkn2)
                a = s.activation(AFT[:, nr_o + 48:nr_o + 64], AFT[:, nr_o + 32:nr_o + 48], AF.Ln)
                tk = sS.inc(a)
                sS.wait(s, tk)
                a = s.activation(AFT[:, nr_o + 64:nr_o + 80], AFT[:, nr_o + 48:nr_o + 64], AF.Exp, scale=-1.0)
                sS.inc(a)
            C.s(fnl)
            tkn3 = sS.n
            def fni(v, tkn3=tkn3, c=c):
                sS.wait(v, tkn3)
                a = v.tensor_mul(AFT[:, nr_o + 80:nr_o + 96], AFT[:, nr_o + 64:nr_o + 80],
                                 AFT[:, ap_o + 16:ap_o + 32])
                tk = sV.inc(a)
                sV.wait(v, tk)
                lo = bass.AP(AFT.ap().tensor, AFT.ap().offset + nr_o + 80,
                             [AFT.ap().ap[0], [1, 8], [0, 64]])
                a = v.tensor_tensor(kns[:, 0:512], C.pL[:, 0:512], lo, ALU.mult)
                tk = sV.inc(a)
                sV.wait(v, tk)
                hi = bass.AP(AFT.ap().tensor, AFT.ap().offset + nr_o + 88,
                             [AFT.ap().ap[0], [1, 8], [0, 64]])
                a = v.tensor_tensor(kns[:, 512:1024], C.pL[:, 512:1024], hi, ALU.mult)
                sV.inc(a)
            C.v(fni)
            st["knmul"] = sV.n
            tkkm = sV.n
            def ftr(t, tkkm=tkkm, pkv=st["knev"]):
                sV.wait(t, tkkm)
                sS.wait(t, pkv)
                m = None
                for dc in range(8):
                    m = t.transpose(C.pA[:, 128 * dc:128 * dc + 128], kns[:, 128 * dc:128 * dc + 128],
                                    C.cstf[:, CF_ID:CF_ID + 128])
                sT.inc(m)
            C.t(ftr)
            tktr = sT.n
            def fke(s, c=c, tktr=tktr):
                sT.wait(s, tktr)
                a = None
                for dc in range(8):
                    a = s.activation(kn[:, dc * TP1 + 1 + 128 * c: dc * TP1 + 1 + 128 * c + 128],
                                     C.pA[:, 128 * dc:128 * dc + 128], AF.Copy)
                    sS.inc(a)
            C.s(fke)
            st["knev"] = sS.n
    C.fence()
    def f(g):
        C.ld(g, C.knhb[:, :], kn[:, :])
        sDW.wait(g)
        sG.inc(g.nop("rl", nofuse=True))
    C.g(f)
    C.fence()
    AFA.pop()
    AB.pop()
    vn_o = AB.alloc(NC * sv)
    vn = C.AB_t[:, vn_o:vn_o + NC * sv]
    AB.push()
    AFA.push()
    vfa_o = AB.alloc(8192)
    wsv_o = AB.alloc(8192)
    wsv = C.AB_t[:, wsv_o:wsv_o + 8192]
    xb_o = AB.alloc(4096)
    xb = C.AB_t[:, xb_o:xb_o + 4096]
    vma_o = AB.alloc(4096)
    i2b_o = AB.alloc(512)
    xt = AFT[:, AFA.alloc(4096):AFA.stack[-1]]
    vst_o = AFA.alloc(512)
    vstage = AFT[:, vst_o:vst_o + 512]
    vm_o = AFA.alloc(512)
    vsh_o = AFA.alloc(512)
    vsq_o = AFA.alloc(512)
    n16_o = AFA.alloc(1024)

    def f(g):
        a = g.memset(vn[:, :].rearrange("p (k h c) -> p k h c", k=NC, h=16)[:, :, :, 64:65], 1.0)
        sG.inc(a)
    C.g(f)
    wtokV = load_wsec(C, wsv_o, C.wvT, 1024, 1024)
    C.w1_bias(wsv, 1024, 1024, None)
    C.fence()

    st = {"evac": (0, 0), "vno": (0, 0), "chain": (0, 0), "vnev": [(0, 0), (0, 0)],
          "trv": (0, 0)}
    for ti in range(NT):
        tt = NT - 1 - ti
        cur, nxt = tt % 2, (tt + 1) % 2
        def pre_t(t, pc=st["chain"]):
            sS.wait(t, pc)
        lt = C.ln_lite(tt, xt, xb, pre_t=pre_t)
        for dc in range(8):
            def pre(t, pc=st["evac"], pv=st["vno"], lt=lt, wtokV=wtokV):
                sG.wait(t, wtokV)
                sS.wait(t, pc)
                sV.wait(t, tuple(map(max, pv, lt)))
            tok = C.proj_feat_oc(xb, lambda ic, dc=dc: wsv[:, ic * 1024 + dc * 128: ic * 1024 + dc * 128 + 128],
                                 dc, 1024, pre=pre)
            def fse(s, dc=dc, cur=cur, tok=tok):
                sT.wait(s, tok)
                sS.wait(s)
                sV.wait(s)
                a = s.activation(C.AB_t[:, vfa_o + cur * 4096 + 512 * dc: vfa_o + cur * 4096 + 512 * dc + 512],
                                 C.pC[0:128, 0:512], AF.Copy)
                sS.inc(a)
            C.s(fse)
            st["evac"] = sS.n
        vfaTok = sS.n
        for dc in range(8):
            def fvm(v, dc=dc, tt=tt, cur=cur, nxt=nxt, vfaTok=vfaTok):
                sS.wait(v)
                sV.wait(v)
                def bc(colo, base, n):
                    sl = AFT[base:base + 64, colo:colo + 1]
                    return bass.AP(sl.tensor, sl.offset, [[sl.ap[0][0], 64], [0, n]])
                for ph in range(2):
                    b0 = 64 * ph
                    hh = 2 * dc + ph
                    vcur = C.AB_t[b0:b0 + 64, vfa_o + cur * 4096 + 512 * dc: vfa_o + cur * 4096 + 512 * dc + 512]
                    a = v.tensor_tensor(AFT[b0:b0 + 64, vm_o:vm_o + 512], vcur,
                                        bc(ap_o + 32 + hh, b0, 512), ALU.mult)
                    tk = sV.inc(a)
                    sV.wait(v, tk)
                    a = v.tensor_tensor(AFT[b0:b0 + 64, vsh_o:vsh_o + 511],
                                        C.AB_t[b0:b0 + 64, vfa_o + cur * 4096 + 512 * dc + 1: vfa_o + cur * 4096 + 512 * dc + 512],
                                        bc(ap_o + 48 + hh, b0, 511), ALU.mult)
                    tk = sV.inc(a)
                    sV.wait(v, tk)
                    if tt < NT - 1:
                        tail = C.AB_t[b0:b0 + 64, vfa_o + nxt * 4096 + 512 * dc: vfa_o + nxt * 4096 + 512 * dc + 1]
                    else:
                        tail = C.cstf[b0:b0 + 64, CF_CONST + 5:CF_CONST + 6]
                    a = v.tensor_tensor(AFT[b0:b0 + 64, vsh_o + 511:vsh_o + 512], tail,
                                        bc(ap_o + 48 + hh, b0, 1), ALU.mult)
                    tk = sV.inc(a)
                    sV.wait(v, tk)
                    a = v.tensor_add(AFT[b0:b0 + 64, vm_o:vm_o + 512], AFT[b0:b0 + 64, vm_o:vm_o + 512],
                                     AFT[b0:b0 + 64, vsh_o:vsh_o + 512])
                    sV.inc(a)
            C.v(fvm)
            tkvm = sV.n
            def fsqv(s, dc=dc, cur=cur, tkvm=tkvm):
                sV.wait(s, tkvm)
                sT.wait(s)
                a = s.activation(AFT[:, vsq_o:vsq_o + 512], AFT[:, vm_o:vm_o + 512], AF.Square)
                sS.inc(a)
                a = s.activation(C.AB_t[:, vma_o + 512 * dc: vma_o + 512 * dc + 512],
                                 AFT[:, vm_o:vm_o + 512], AF.Copy)
                sS.inc(a)
            C.s(fsqv)
            tksqv = sS.n
            def fssq(t, dc=dc, tksqv=tksqv, lt=lt):
                sS.wait(t, tksqv)
                sV.wait(t, lt)
                m = t.matmul(C.pB[0:16, 0:512], C.cstf[:, CF_BD + 16 * dc: CF_BD + 16 * dc + 16],
                             AFT[:, vsq_o:vsq_o + 512], start=(dc == 0), stop=(dc == 7))
                sT.inc(m)
            C.t(fssq)
        ssqTok = sT.n
        def fn1(s, ssqTok=ssqTok):
            sT.wait(s, ssqTok)
            a = s.activation(AFT[0:16, n16_o:n16_o + 512], C.pB[0:16, 0:512], AF.Sqrt)
            sS.inc(a)
        C.s(fn1)
        chain0 = sS.n
        st["chain"] = chain0
        def fn2(v, chain0=chain0):
            sS.wait(v, chain0)
            eps = bass.AP(C.cstf.ap().tensor, C.cstf.ap().offset + CF_CONST + 1,
                          [[C.cstf.ap().ap[0][0], 16], [0, 512]])
            a = v.tensor_tensor(AFT[0:16, n16_o + 512:n16_o + 1024], AFT[0:16, n16_o:n16_o + 512],
                                eps, ALU.add)
            sV.inc(a)
        C.v(fn2)
        tkn2 = sV.n
        def fn3(s, tkn2=tkn2):
            sV.wait(s, tkn2)
            a = s.activation(AFT[0:16, n16_o:n16_o + 512], AFT[0:16, n16_o + 512:n16_o + 1024], AF.Ln)
            tk = sS.inc(a)
            sS.wait(s, tk)
            a = s.activation(AFT[0:16, n16_o + 512:n16_o + 1024], AFT[0:16, n16_o:n16_o + 512],
                             AF.Exp, scale=-1.0)
            sS.inc(a)
        C.s(fn3)
        tkn3 = sS.n
        def fn4(v, tkn3=tkn3):
            sS.wait(v, tkn3)
            vs = bass.AP(AFT.ap().tensor, AFT.ap().offset + v16_o,
                         [[AFT.ap().ap[0][0], 16], [0, 512]])
            a = v.tensor_tensor(AFT[0:16, n16_o:n16_o + 512], AFT[0:16, n16_o + 512:n16_o + 1024],
                                vs, ALU.mult)
            sV.inc(a)
        C.v(fn4)
        tkn4 = sV.n
        def fn5(s, tkn4=tkn4):
            sV.wait(s, tkn4)
            a = s.activation(C.AB_t[0:16, i2b_o:i2b_o + 512], AFT[0:16, n16_o:n16_o + 512], AF.Copy)
            sS.inc(a)
        C.s(fn5)
        i2bTok = sS.n
        for dc in range(8):
            def fbc(t, dc=dc, i2bTok=i2bTok, pv=st["vno"]):
                sS.wait(t, i2bTok)
                sV.wait(t, pv)
                m = t.matmul(C.pC[0:128, 0:512], C.cstb[0:16, CB_BDT + 128 * dc: CB_BDT + 128 * dc + 128],
                             C.AB_t[0:16, i2b_o:i2b_o + 512], start=True, stop=True)
                sT.inc(m)
            C.t(fbc)
            tkbc = sT.n
            def fvn(v, dc=dc, tkbc=tkbc, ptr=st["trv"]):
                sT.wait(v, tkbc)
                sT.wait(v, ptr)
                a = v.tensor_mul(vstage[:, 0:512], C.AB_t[:, vma_o + 512 * dc: vma_o + 512 * dc + 512],
                                 C.pC[0:128, 0:512])
                sV.inc(a)
            C.v(fvn)
            st["vno"] = sV.n
            tkvn = sV.n
            def ftrv(t, dc=dc, tkvn=tkvn, pev=st["vnev"][dc % 2]):
                sV.wait(t, tkvn)
                sV.wait(t, pev)
                m = None
                for blk in range(4):
                    m = t.transpose(C.pA[:, 512 * (dc % 2) + 128 * blk: 512 * (dc % 2) + 128 * blk + 128],
                                    vstage[:, 128 * blk:128 * blk + 128],
                                    C.cstf[:, CF_ID:CF_ID + 128])
                sT.inc(m)
            C.t(ftrv)
            st["trv"] = sT.n
            tktrv = sT.n
            def fvev(v, dc=dc, tt=tt, tktrv=tktrv):
                sT.wait(v, tktrv)
                a = None
                for blk in range(4):
                    kt = 4 * tt + blk
                    dst = vn[:, kt * sv + 65 * 2 * dc: kt * sv + 65 * 2 * dc + 130]
                    dst = dst.rearrange("p (h c) -> p h c", c=65)[:, :, 0:64]
                    src = C.pA[:, 512 * (dc % 2) + 128 * blk: 512 * (dc % 2) + 128 * blk + 128]
                    a = v.tensor_copy(dst, src.rearrange("p (h c) -> p h c", c=64))
                    sV.inc(a)
            C.v(fvev)
            st["vnev"][dc % 2] = sV.n
    AFA.pop()
    AB.pop()
    C.fence()
    AB.push()
    AFA.push()
    knr_o = AB.alloc(8 * TP1)
    knr = C.AB_t[:, knr_o:knr_o + 8 * TP1]
    ob_o = AB.alloc(4096)
    obuf = C.AB_t[:, ob_o:ob_o + 4096]
    odd_o = AB.alloc(512)
    odds = C.AB_t[:, odd_o:odd_o + 512]
    wp_o = AB.alloc(2048)
    wp = C.AB_t[:, wp_o:wp_o + 2048]
    meg_o = AFA.alloc(C.MEGW)
    mega = AFT[:, meg_o:meg_o + C.MEGW]
    inv_o = AFA.alloc(512)
    invs = AFT[:, inv_o:inv_o + 512]
    xo_o = AFA.alloc(1024)
    stf_o = AFA.alloc(1024)

    def f(g):
        C.ld(g, knr[:, :], C.knhb[:, :])
        C.ld(g, C.w2[0:1, 3072:3072 + 1024], C.mrows[1:2, 0:1024])
        C.ld(g, C.w2[1:2, 3072:3072 + 1024], C.mrows[1:2, 0:1024])
    C.g(f)
    C.emit_mega_mask(mega, C.tbl_ma, qshift=0)
    C.fence()

    epst = {"tk2": (0, 0), "tk3": (0, 0), "tk4": (0, 0), "odd": (0, 0)}

    def epilogue(slot, h, av, tok_av):
        blk = h // 2
        if slot == 0:
            def ffix(v, av=av, tok_av=tok_av):
                C.sT2.wait(v, tok_av)
                a = v.tensor_add(av[64:65, 0:1], av[64:65, 0:1], C.cstf[64:65, CF_ONES:CF_ONES + 1])
                sV.inc(a)
            C.v(ffix)
            fixTok = sV.n
        else:
            fixTok = None
        def fs(s, tok_av=tok_av, p2=epst["tk2"], fixTok=fixTok):
            C.sT2.wait(s, tok_av)
            if fixTok is not None:
                sV.wait(s, fixTok)
            sT.wait(s, p2)
            a = s.activation(C.rows[64:65, 0:512], av[64:65, :], AF.Ln)
            sS.inc(a)
        C.s(fs)
        tk1 = sS.n
        def ft(t, tk1=tk1, p3=epst["tk3"]):
            sS.wait(t, tk1)
            sS.wait(t, p3)
            m = t.matmul(C.pB[0:64, 0:512], C.cstf[64:65, CF_ONESB:CF_ONESB + 64],
                         C.rows[64:65, 0:512], start=True, stop=True)
            sT.inc(m)
        C.t(ft)
        tk2 = sT.n
        epst["tk2"] = tk2
        def fs2(s, tk2=tk2, p4=epst["tk4"]):
            sT.wait(s, tk2)
            sV.wait(s, p4)
            a = s.activation(invs[0:64, :], C.pB[0:64, 0:512], AF.Exp, scale=-1.0)
            sS.inc(a)
        C.s(fs2)
        tk3 = sS.n
        epst["tk3"] = tk3
        def fv(v, h=h, blk=blk, tk3=tk3, po=epst["odd"]):
            sS.wait(v, tk3)
            if h % 2 == 0:
                dst = obuf[0:64, blk * 512:(blk + 1) * 512]
            else:
                C.sY.wait(v, po)
                dst = odds[0:64, 0:512]
            c2 = v.tensor_mul(dst, av[0:64, :], invs[0:64, :])
            sV.inc(c2)
        C.v(fv)
        tk4 = sV.n
        epst["tk4"] = tk4
        if h % 2 == 1:
            def fy(y, blk=blk, tk4=tk4):
                sV.wait(y, tk4)
                C.ld(y, obuf[64:128, blk * 512:(blk + 1) * 512], odds[0:64, 0:512], sDS)
                sDS.wait(y)
                C.sY.inc(y.nop("rl", nofuse=True))
            C.y(fy)
            epst["odd"] = C.sY.n
        return (sV, sV.n)

    def k_ap(slot, kt, h):
        dc = h // 2
        return knr[64 * (h % 2):64 * (h % 2) + 64, dc * TP1 + 1 + kt * 128: dc * TP1 + 1 + kt * 128 + 128]

    def q_ap(slot, h):
        dc = h // 2
        return knr[64 * (h % 2):64 * (h % 2) + 64, dc * TP1 + 1 + slot * 512: dc * TP1 + 1 + slot * 512 + 512]

    def vn_ap(kt, h):
        base = kt * sv + 65 * h
        return vn[:, base:base + 65]

    def bias_ap(slot, kt, h):
        cnt = 4 * (slot + 1)
        if kt >= cnt - 4:
            c0 = C.mega_c0(slot, kt)
            return mega[:, c0:c0 + 512]
        return None

    cfg = dict(H=16, counts=lambda s: 4 * (s + 1), k_ap=k_ap, q_ap=q_ap,
               vn_ap=vn_ap, bias_ap=bias_ap, phantom=None, eshift=-20.0,
               epilogue=epilogue)

    for slot in range(NT):
        C.emit_attention_slot(cfg, slot)
        C.out_proj_residual(slot, obuf, wp, wp_o, C.wcT, 3072,
                            AFT[:, xo_o:xo_o + 1024], AFT[:, stf_o:stf_o + 1024], last, nwp=2)
        C.fence()

    AFA.pop()
    AB.pop()
    AFA.pop()
    AB.pop()


KB.sub_attn_a = sub_attn_a


# version 10
# speedup vs baseline: 1.1363x; 1.0668x over previous
import math
import numpy as np
import ml_dtypes
import concourse.bass as bass
import concourse.mybir as mybir
from concourse.bass_utils import run_bass_kernel_spmd
from contextlib import ExitStack

F32 = mybir.dt.float32
BF16 = mybir.dt.bfloat16
AF = mybir.ActivationFunctionType
ALU = mybir.AluOpType

NEG = -1.0e30
KEY_SCALE_MAX = math.log(2 ** 16 - 1)
CONSTS = (1e-5, 1e-10, 10.0, KEY_SCALE_MAX, 1.0, 0.0, -1.0, 0.125, -40.0, -20.0)


class Sem:

    def __init__(self, nc, ctx, name, dma=False):
        self.hA = ctx.enter_context(nc.semaphore(name + "a"))
        self.hB = ctx.enter_context(nc.semaphore(name + "b"))
        self.va = 0
        self.vb = 0
        self.ic = 0
        self.dma = dma

    @property
    def n(self):
        return (self.va, self.vb)

    def inc(self, instr, k=1):
        if self.ic % 2 == 0:
            instr.then_inc(self.hA, k)
            self.va += k
        else:
            instr.then_inc(self.hB, k)
            self.vb += k
        self.ic += 1
        return (self.va, self.vb)

    def tok_after(self, snap, j, k=1):
        va, vb, ic = snap
        for i in range(j):
            if (ic + i) % 2 == 0:
                va += k
            else:
                vb += k
        return (va, vb)

    @property
    def snap(self):
        return (self.va, self.vb, self.ic)

    def wait(self, eng, tok=None):
        va, vb = (self.va, self.vb) if (tok is None or self.dma) else tok
        if va:
            eng.wait_ge(self.hA, va)
        if vb:
            eng.wait_ge(self.hB, vb)


class Arena:
    def __init__(self, tensor, width):
        self.t = tensor
        self.w = width
        self.stack = [0]

    def alloc(self, cols):
        o = self.stack[-1]
        assert o + cols <= self.w, f"arena overflow {o}+{cols}>{self.w}"
        self.stack[-1] = o + cols
        return o

    def push(self):
        self.stack.append(self.stack[-1])

    def pop(self):
        self.stack.pop()


def hope_tables(T, base=10000, d_model=1024):
    dim = d_model // 2
    pos = np.arange(dim, dtype=np.float64)
    freqs_np = 1.0 / base ** (pos / dim)
    active = int(np.sum(freqs_np * 2 * math.pi * T >= 1.0))
    active = min(active, dim - d_model // 4)
    freqs = freqs_np[:active].astype(np.float32)
    d = np.arange(0, T, dtype=np.float32)
    theta = (d[:, None] * freqs[None, :]).astype(np.float32)
    pv = (np.cos(theta, dtype=np.float32) + np.sin(theta, dtype=np.float32)).astype(np.float32)
    nh = active // 32
    per_head = pv.astype(np.float64).reshape(T, nh, 32).sum(axis=2)
    hi = per_head.astype(np.float32)
    lo = (per_head - hi.astype(np.float64)).astype(np.float32)
    return hi.T.copy(), lo.T.copy()


def shift2d(row):
    L = row.shape[0]
    padded = np.concatenate([np.zeros(127, row.dtype), row])
    s = np.lib.stride_tricks.sliding_window_view(padded, L)
    return np.ascontiguousarray(s[::-1])


def build_tables(T):
    LPAD = T + 128
    L = LPAD + T
    d = np.arange(-LPAD, T)
    mask_a = np.where(d >= 1, 0.0, NEG).astype(np.float32)
    mask_f = np.where(d >= 0, 0.0, NEG).astype(np.float32)
    hi8, lo8 = hope_tables(T)
    hope_hi = np.zeros((8, L), dtype=np.float32)
    hope_lo = np.zeros((8, L), dtype=np.float32)
    hope_hi[:, :LPAD] = NEG
    hope_hi[:, LPAD:] = hi8
    hope_lo[:, LPAD:] = lo8
    ma2 = shift2d(mask_a)
    mf2 = shift2d(mask_f)
    hh2 = np.concatenate([shift2d(hope_hi[h]) for h in range(8)], axis=0)
    hl2 = np.concatenate([shift2d(hope_lo[h]) for h in range(8)], axis=0)
    return ma2, mf2, hh2, hl2


def mk_constants():
    c = np.zeros((128, 192), dtype=np.float32)
    c[:, 0] = 1.0
    c[:, 1] = np.arange(1, 129, dtype=np.float32)
    for h in range(8):
        c[h, 16 + 8 * h:16 + 8 * h + 8] = 1.0
    for h in range(8):
        c[64, 80 + 8 * h + h] = 1.0
    return c


def mk_selectors():
    s = np.zeros((64, 32 * 128), dtype=np.float32)
    for e in range(8):
        for j in range(4):
            m = e * 4 + j
            s[(2 * j) * 8 + e, 128 * m:128 * m + 64] = 1.0
            s[(2 * j + 1) * 8 + e, 128 * m + 64:128 * m + 128] = 1.0
    return s


class KB:
    def __init__(self, T, subs):
        self.T = T
        self.subs = subs
        self.NT = T // 512
        self.NC = T // 128
        self.TP1 = T + 1
        self.LPAD = T + 128
        self.TBL = self.LPAD + T
        self.MEGW = T + 384
        self.nc = bass.Bass("TRN2", target_bir_lowering=False)

    def build(self):
        nc = self.nc
        T = self.T

        def din(name, shape, dtype=F32):
            return nc.dram_tensor(name, shape, dtype, kind="ExternalInput").ap()

        self.x_in = din("x_feat", [1024, T])
        self.wkT = din("wkT", [1024, 1024])
        self.wvT = din("wvT", [1024, 1024])
        self.wcT = din("wcT", [128, 8 * 1024])
        self.aprm = din("aprm", [1, 64])
        self.wqkvT = din("wqkvT", [1024, 3072])
        self.bqkv = din("bqkv", [1, 3072])
        self.wpT = din("wpT", [128, 8 * 1024])
        self.bproj = din("bproj", [1, 1024])
        self.wqf = din("wqf", [1024, 512])
        self.wkf = din("wkf", [1024, 512])
        self.wgf = din("wgf", [1024, 64])
        self.wve = din("wve", [8, 1024, 512])
        self.woe = din("woe", [32, 128, 1024])
        self.tbl_ma = din("tbl_ma", [128, self.TBL])
        self.tbl_mf = din("tbl_mf", [128, self.TBL])
        self.tbl_hh = din("tbl_hh", [8 * 128, self.TBL])
        self.tbl_hl = din("tbl_hl", [8 * 128, self.TBL])
        self.cstb_in = din("cstb_in", [128, 2048])
        self.tblu = din("tblu", [128, 256])
        self.mrows = din("mrows", [2, 4096])
        self.cstf_in = din("cstf_in", [128, 512])
        self.selc_in = din("selc_in", [64, 32 * 128])

        self.y_out = nc.dram_tensor("y", [1024, T], F32, kind="ExternalOutput").ap()
        self.xres = nc.dram_tensor("xres", [1024, T], F32, kind="Internal").ap()
        self.knhb = nc.dram_tensor("knhb", [128, 8 * self.TP1], BF16, kind="Internal").ap()
        self.qhb = nc.dram_tensor("qhb", [1024, T], BF16, kind="Internal").ap()
        self.aghb = nc.dram_tensor("aghb", [2, T], F32, kind="Internal").ap()

        ctx = ExitStack()
        with ctx:
            self._alloc(ctx)
            self._emit()
        return nc

    def _alloc(self, ctx):
        nc = self.nc
        self.pt = ctx.enter_context(nc.sbuf_tensor("pt", [128, 4 * 512], BF16))
        self.rows = ctx.enter_context(nc.sbuf_tensor("rows", [128, 2048], F32))
        self.cstf = ctx.enter_context(nc.sbuf_tensor("cstf", [128, 512], F32))
        self.cstb = ctx.enter_context(nc.sbuf_tensor("cstb", [128, 2048], BF16))
        self.w2 = ctx.enter_context(nc.sbuf_tensor("w2", [2, 4096], BF16))
        self.g2 = ctx.enter_context(nc.sbuf_tensor("g2", [2, 512], BF16))
        self.AB_t = ctx.enter_context(nc.sbuf_tensor("AB", [128, 76000], BF16))
        self.AF_t = ctx.enter_context(nc.sbuf_tensor("AFa", [128, 8200], F32))
        self.AB = Arena(self.AB_t, 76000)
        self.AFA = Arena(self.AF_t, 8200)

        self.pL = ctx.enter_context(nc.psum_tensor("pL", [128, 2048], F32))
        self.pA = ctx.enter_context(nc.psum_tensor("pA", [128, 1024], F32))
        self.pB = ctx.enter_context(nc.psum_tensor("pB", [128, 512], F32))
        self.pC = ctx.enter_context(nc.psum_tensor("pC", [128, 512], F32))

        self.sDW = Sem(nc, ctx, "sDW", dma=True)
        self.sDA = Sem(nc, ctx, "sDA", dma=True)
        self.sDX = Sem(nc, ctx, "sDX", dma=True)
        self.sDS = Sem(nc, ctx, "sDS", dma=True)
        self.sT = Sem(nc, ctx, "sT")
        self.sS = Sem(nc, ctx, "sS")
        self.sV = Sem(nc, ctx, "sV")
        self.sG = Sem(nc, ctx, "sG")
        self.sT2 = Sem(nc, ctx, "sT2")
        self.sS2 = Sem(nc, ctx, "sS2")
        self.sY = Sem(nc, ctx, "sY")

        self.block = ctx.enter_context(nc.Block())
        for i, val in enumerate(CONSTS):
            nc.const_aps.register(F32, val, self.cstf[:, 220 + i:221 + i])

    def t(self, f): self.block.tensor(f)
    def s(self, f): self.block.scalar(f)
    def v(self, f): self.block.vector(f)
    def g(self, f): self.block.gpsimd(f)
    def y(self, f): self.block.sync(f)

    def ld(self, eng, dst, src, sem=None):
        sem = sem or self.sDW
        sem.inc(eng.dma_start(out=dst, in_=src), 16)

    def fence(self):
        sT, sS, sV, sG, sY = self.sT, self.sS, self.sV, self.sG, self.sY
        sDS, sDX, sDW = self.sDS, self.sDX, self.sDW
        def ft(t):
            sT.wait(t)
            sT.inc(t.nop("fence", nofuse=True))
        def fs(s):
            sS.wait(s)
            sS.inc(s.nop("fence", nofuse=True))
        def fv(v):
            sV.wait(v)
            sV.inc(v.nop("fence", nofuse=True))
        def fg(g):
            sDW.wait(g)
            sG.wait(g)
            sG.inc(g.nop("fence", nofuse=True))
        def fy(y):
            sDS.wait(y)
            sDX.wait(y)
            sY.wait(y)
            sY.inc(y.nop("fence", nofuse=True))
        self.t(ft); self.s(fs); self.v(fv); self.g(fg); self.y(fy)
        def wt(t): sS.wait(t); sV.wait(t); sG.wait(t); sY.wait(t)
        def ws(s): sT.wait(s); sV.wait(s); sG.wait(s); sY.wait(s)
        def wv(v): sT.wait(v); sS.wait(v); sG.wait(v); sY.wait(v)
        def wg(g): sT.wait(g); sS.wait(g); sV.wait(g); sY.wait(g)
        def wy(y): sT.wait(y); sS.wait(y); sV.wait(y); sG.wait(y)
        self.t(wt); self.s(ws); self.v(wv); self.g(wg); self.y(wy)

    def _emit(self):
        C = self
        sDW, sDX, sDS = C.sDW, C.sDX, C.sDS
        sV = C.sV

        def f(g):
            C.ld(g, C.cstf[:, :], C.cstf_in[:, :])
            C.ld(g, C.cstb[:, :], C.cstb_in[:, :])
            C.ld(g, C.xres[:, :], C.x_in[:, :])
        C.g(f)
        def f(g):
            C.ld(g, C.g2[1:2, 0:512], C.mrows[0:1, 0:512])
        C.g(f)
        C.fence()

        nsub = len(C.subs)
        for i, sub in enumerate(C.subs):
            last = (i == nsub - 1)
            if sub == "attn_a":
                C.sub_attn_a(last)
            elif sub in ("ffn1", "ffn2"):
                C.sub_ffn(last)
            elif sub == "attn_b":
                C.sub_attn_b(last)
            C.fence()

        def f(y_):
            sDS.wait(y_)
        C.y(f)
        def f(g):
            sDS.wait(g)
        C.g(f)


def emit_attention_slot(C, cfg, slot):
    sT, sS, sV, sT2, sS2 = C.sT, C.sS, C.sV, C.sT2, C.sS2
    st = cfg.setdefault("_state", {"epi": {0: None, 1: None}, "hctr": 0})
    cnt = cfg["counts"](slot)
    order = cfg.get("head_order") or list(range(cfg["H"]))
    for h in order:
        if cfg.get("pre_head"):
            cfg["pre_head"](slot, h)
        hp = st["hctr"] % 2
        st["hctr"] += 1
        av = C.pA[:, hp * 512:(hp + 1) * 512]
        qap = cfg["q_ap"](slot, h)
        snT2 = sT2.snap
        snS2 = sS2.snap
        snT = sT.snap
        prev_exp_all = sS2.n
        prev_av_all = sT2.n
        bprev = st["epi"][hp]
        ph = cfg["phantom"](slot) if cfg.get("phantom") else None
        biases = [cfg["bias_ap"](slot, kt, h) for kt in range(cnt)]

        def ft(t, slot=slot, h=h, cnt=cnt, av=av, qap=qap,
               snS2=snS2, prev_exp_all=prev_exp_all, bprev=bprev, ph=ph):
            sS2.wait(t, prev_exp_all)
            if bprev is not None:
                bprev[0].wait(t, bprev[1])
            for kt in range(cnt):
                r = kt % 4
                if kt >= 1:
                    sS2.wait(t, sS2.tok_after(snS2, kt))
                    m = t.matmul(av[0:65, :], cfg["vn_ap"](kt - 1, h),
                                 C.pt[:, ((kt - 1) % 4) * 512:((kt - 1) % 4) * 512 + 512],
                                 start=(kt - 1 == 0), stop=False)
                    sT2.inc(m)
                m = t.matmul(C.pL[:, r * 512:(r + 1) * 512],
                             cfg["k_ap"](slot, kt, h), qap,
                             start=True, stop=True)
                sT.inc(m)
            sS2.wait(t, sS2.tok_after(snS2, cnt))
            m = t.matmul(av[0:65, :], cfg["vn_ap"](cnt - 1, h),
                         C.pt[:, ((cnt - 1) % 4) * 512:((cnt - 1) % 4) * 512 + 512],
                         start=(cnt - 1 == 0), stop=(ph is None))
            if ph is not None:
                m = t.matmul(av[0:65, :], ph[0], ph[1], start=False, stop=True)
            sT2.inc(m)
        C.t(ft)
        final_av_tok = sT2.n

        vtoks = [None] * cnt
        for kt in range(cnt):
            if biases[kt] is None:
                continue
            def fb(v, kt=kt, r=kt % 4, bias=biases[kt], snT=snT):
                sT.wait(v, sT.tok_after(snT, kt + 1))
                c = v.tensor_add(C.pL[:, r * 512:(r + 1) * 512],
                                 C.pL[:, r * 512:(r + 1) * 512], bias)
                sV.inc(c)
            C.v(fb)
            vtoks[kt] = sV.n

        def fs(s, cnt=cnt, snT=snT, snT2=snT2, prev_av_all=prev_av_all, vtoks=vtoks):
            for kt in range(cnt):
                r = kt % 4
                if kt < 4:
                    sT2.wait(s, prev_av_all)
                else:
                    sT2.wait(s, sT2.tok_after(snT2, kt - 3))
                sT.wait(s, sT.tok_after(snT, kt + 1))
                if vtoks[kt] is not None:
                    sV.wait(s, vtoks[kt])
                a = s.activation(C.pt[:, r * 512:(r + 1) * 512],
                                 C.pL[:, r * 512:(r + 1) * 512], AF.Exp,
                                 bias=cfg.get("eshift", 0.0))
                sS2.inc(a)
        C.s(fs)

        st["epi"][hp] = cfg["epilogue"](slot, h, av, final_av_tok)


def emit_mega_mask(C, mega_ap, tbl, qshift, W=None):
    W = W or C.MEGW
    off = C.LPAD + qshift - 384
    def f(g):
        C.sT.wait(g)
        C.sV.wait(g)
        C.sDW.inc(g.dma_start(out=mega_ap[:, 0:W], in_=tbl[0:128, off:off + W]), 16)
        C.sDW.wait(g)
        C.sG.inc(g.nop("rl", nofuse=True))
    C.g(f)
    return C.sG.n


def mega_c0(C, slot, kt):
    return slot * 512 - 128 * kt + 384


KB.emit_attention_slot = emit_attention_slot
KB.emit_mega_mask = emit_mega_mask
KB.mega_c0 = mega_c0


CB_ONES = 0
CB_P1 = 1
CB_E8 = 16
CB_OH = 80
CB_ONESB = 144
CB_P1ROW = 208
CB_ID = 336
CB_DEC = 464
CB_TRI = 592
CB_BDT = 720
CB_BD = 1744
CB_ONESR = 1880
CF_ONES = 0
CF_E8 = 16
CF_ONESB = 80
CF_CONST = 220
CF_ID = 256
CF_BD = 384


def mk_cstb():
    c = np.zeros((128, 2048), dtype=np.float32)
    c[:, CB_ONES] = 1.0
    c[:, CB_P1] = np.arange(1, 129, dtype=np.float32)
    for h in range(8):
        c[h, CB_E8 + 8 * h: CB_E8 + 8 * h + 8] = 1.0
    for h in range(8):
        c[64, CB_OH + 8 * h + h] = 1.0
    c[:, CB_ONESB:CB_ONESB + 64] = 1.0
    c[0, CB_P1ROW:CB_P1ROW + 128] = np.arange(1, 129, dtype=np.float32)
    c[:, CB_ID:CB_ID + 128] = np.eye(128, dtype=np.float32)
    p = np.arange(128)[:, None]
    j = np.arange(128)[None, :]
    c[:, CB_DEC:CB_DEC + 128] = np.maximum(p - j, 0).astype(np.float32)
    c[:, CB_TRI:CB_TRI + 128] = (p >= j).astype(np.float32)
    for dc in range(8):
        blk = np.zeros((128, 16), dtype=np.float32)
        for pp in range(128):
            blk[pp, 2 * dc + pp // 64] = 1.0
        c[:, CB_BD + 16 * dc: CB_BD + 16 * dc + 16] = blk
        bt = np.zeros((128, 128), dtype=np.float32)
        for pp in range(128):
            bt[2 * dc + pp // 64, pp] = 1.0
        c[:, CB_BDT + 128 * dc: CB_BDT + 128 * dc + 128] = bt
    c[:, CB_ONESR:CB_ONESR + 128] = 1.0
    return c


def mk_cstf():
    c = np.zeros((128, 512), dtype=np.float32)
    c[:, CF_ONES] = 1.0
    for h in range(8):
        c[h, CF_E8 + 8 * h: CF_E8 + 8 * h + 8] = 1.0
    c[:, CF_ONESB:CF_ONESB + 128] = 1.0
    c[:, CF_ID:CF_ID + 128] = np.eye(128, dtype=np.float32)
    for i, val in enumerate(CONSTS):
        c[:, CF_CONST + i] = val
    for dc in range(8):
        for pp in range(128):
            c[pp, CF_BD + 16 * dc + 2 * dc + pp // 64] = 1.0
    return c


def ln_full(C, tt, xt, xb, store=True):
    sDX, sDS, sT, sS, sV = C.sDX, C.sDS, C.sT, C.sS, C.sV
    R = C.rows
    t0 = tt * 512
    def f(y):
        sV.wait(y)
        sS.wait(y)
        for dc in range(8):
            C.ld(y, xt[:, dc * 512:(dc + 1) * 512],
                 C.xres[dc * 128:(dc + 1) * 128, t0:t0 + 512], sDX)
        sDX.wait(y)
        C.sY.inc(y.nop("rl", nofuse=True))
    C.y(f)
    tkx = C.sY.n
    def f(v):
        C.sY.wait(v, tkx)
        sT.wait(v)
        c = None
        for dc in range(8):
            c = v.tensor_copy(xb[:, dc * 512:(dc + 1) * 512], xt[:, dc * 512:(dc + 1) * 512])
        sV.inc(c)
    C.v(f)
    tkb = sV.n
    C.AFA.push()
    o = C.AFA.alloc(1024)
    sq_sl = [C.AF_t[:, o:o + 512], C.AF_t[:, o + 512:o + 1024]]
    snT = sT.snap
    def f(s):
        sV.wait(s, tkb)
        sT.wait(s, (snT[0], snT[1]))
        for dc in range(8):
            if dc >= 2:
                sT.wait(s, sT.tok_after(snT, dc - 1))
            a = s.activation(sq_sl[dc % 2][:, :], xt[:, dc * 512:(dc + 1) * 512], AF.Square)
            sS.inc(a)
    snS = sS.snap
    C.s(f)
    def f(t):
        for dc in range(8):
            sS.wait(t, sS.tok_after(snS, dc + 1))
            t.matmul(C.pB[0:1, 0:512], C.cstb[:, CB_ONES:CB_ONES + 1],
                     xb[:, dc * 512:(dc + 1) * 512], start=(dc == 0), stop=(dc == 7))
            m2 = t.matmul(C.pL[0:1, 0:512], C.cstf[:, CF_ONES:CF_ONES + 1],
                          sq_sl[dc % 2][:, :], start=(dc == 0), stop=(dc == 7))
            sT.inc(m2)
    C.t(f)
    tksum = sT.n
    def f(s):
        sT.wait(s, tksum)
        C.sY.wait(s)
        s.activation(R[0:1, 0:512], C.pB[0:1, 0:512], AF.Copy, scale=1.0 / 1024.0)
        a = s.activation(R[0:1, 512:1024], C.pL[0:1, 0:512], AF.Copy, scale=1.0 / 1024.0)
        tk = sS.inc(a)
        sS.wait(s, tk)
        a = s.activation(R[0:1, 1024:1536], R[0:1, 0:512], AF.Square)
        sS.inc(a)
    C.s(f)
    tk1 = sS.n
    def f(v):
        sS.wait(v, tk1)
        c = v.tensor_sub(R[0:1, 512:1024], R[0:1, 512:1024], R[0:1, 1024:1536])
        sV.inc(c)
    C.v(f)
    tk2 = sV.n
    def f(s):
        sV.wait(s, tk2)
        a = s.activation(R[0:1, 1024:1536], R[0:1, 512:1024], AF.Ln, bias=1e-5)
        tk = sS.inc(a)
        sS.wait(s, tk)
        a = s.activation(R[0:1, 1536:2048], R[0:1, 1024:1536], AF.Exp, scale=-0.5)
        sS.inc(a)
    C.s(f)
    tk3 = sS.n
    def f(v):
        sS.wait(v, tk3)
        v.tensor_mul(C.g2[0:1, 0:512], R[0:1, 0:512], R[0:1, 1536:2048])
        c = v.tensor_mul(R[0:1, 1024:1536], R[0:1, 0:512], R[0:1, 1536:2048])
        sV.inc(c)
    C.v(f)
    tk4 = sV.n
    if store:
        def f(y, tt=tt):
            sV.wait(y, tk4)
            C.ld(y, C.aghb[0:1, tt * 512:(tt + 1) * 512], R[0:1, 1536:2048], sDS)
            C.ld(y, C.aghb[1:2, tt * 512:(tt + 1) * 512], R[0:1, 1024:1536], sDS)
            sDS.wait(y)
            C.sY.inc(y.nop("rl", nofuse=True))
        C.y(f)
    def f(t):
        sV.wait(t, tk4)
        m = t.matmul(C.pB[0:128, 0:512], C.cstf[0:1, CF_ONESB:CF_ONESB + 128], R[0:1, 1536:2048],
                     start=True, stop=True)
        sT.inc(m)
    C.t(f)
    tk5 = sT.n
    def f(v):
        sT.wait(v, tk5)
        c = None
        for dc in range(8):
            c = v.tensor_mul(xb[:, dc * 512:(dc + 1) * 512],
                             xt[:, dc * 512:(dc + 1) * 512], C.pB[0:128, 0:512])
        sV.inc(c)
    C.v(f)
    C.AFA.pop()
    return sV.n


def ln_lite(C, tt, xt, xb, pre_t=None):
    sDX, sT, sV = C.sDX, C.sT, C.sV
    R = C.rows
    t0 = tt * 512
    def f(y):
        sV.wait(y)
        C.sS.wait(y)
        sT.wait(y)
        C.ld(y, R[0:1, 1536:2048], C.aghb[0:1, t0:t0 + 512], sDX)
        C.ld(y, R[0:1, 0:512], C.aghb[1:2, t0:t0 + 512], sDX)
        for dc in range(8):
            C.ld(y, xt[:, dc * 512:(dc + 1) * 512],
                 C.xres[dc * 128:(dc + 1) * 128, t0:t0 + 512], sDX)
        sDX.wait(y)
        C.sY.inc(y.nop("rl", nofuse=True))
    C.y(f)
    tkx = C.sY.n
    def f(t):
        C.sY.wait(t, tkx)
        if pre_t is not None:
            pre_t(t)
        m = t.matmul(C.pB[0:128, 0:512], C.cstf[0:1, CF_ONESB:CF_ONESB + 128],
                     R[0:1, 1536:2048], start=True, stop=True)
        sT.inc(m)
    C.t(f)
    tk5 = sT.n
    def f(v):
        sT.wait(v, tk5)
        c = None
        for dc in range(8):
            c = v.tensor_mul(xb[:, dc * 512:(dc + 1) * 512],
                             xt[:, dc * 512:(dc + 1) * 512], C.pB[0:128, 0:512])
        c2 = v.tensor_copy(C.g2[0:1, 0:512], R[0:1, 0:512])
        sV.inc(c2)
    C.v(f)
    return sV.n


def w1_bias(C, wsb, dout, w2off, bias_src=None, wsb_cols0=0, wait_tok=None):
    sT, sS, sG, sDW = C.sT, C.sS, C.sG, C.sDW
    def f(g):
        if bias_src is not None:
            C.ld(g, C.w2[1:2, w2off:w2off + dout], bias_src)
        else:
            C.ld(g, C.w2[1:2, w2off:w2off + dout], C.mrows[1:2, 0:dout])
        sDW.wait(g)
        sG.inc(g.nop("rl", nofuse=True))
    C.g(f)
    wtok = sG.n
    for oc in range((dout + 511) // 512):
        o0 = oc * 512
        ow = min(512, dout - o0)
        def f(t, o0=o0, ow=ow, snv=C.sV.n, sns=C.sS.n):
            sG.wait(t, wtok)
            C.sV.wait(t, snv)
            C.sS.wait(t, sns)
            m = None
            for ic in range(8):
                m = t.matmul(C.pB[0:1, 0:ow], C.cstb[:, CB_ONES:CB_ONES + 1],
                             wsb[:, wsb_cols0 + ic * dout + o0: wsb_cols0 + ic * dout + o0 + ow],
                             start=(ic == 0), stop=(ic == 7))
            sT.inc(m)
        C.t(f)
        tk = sT.n
        def f(s, o0=o0, ow=ow, tk=tk):
            sT.wait(s, tk)
            a = s.activation(C.w2[0:1, w2off + o0: w2off + o0 + ow], C.pB[0:1, 0:ow],
                             AF.Copy, scale=-1.0)
            sS.inc(a)
        C.s(f)
        tk2 = sS.n
        def f(t, tk2=tk2):
            sS.wait(t, tk2)
        C.t(f)
    return sS.n, sG.n, sDW.n


def proj_feat_oc(C, xb, wslice_fn, oc, w2off, pre=None):
    sT = C.sT
    def f(t):
        if pre is not None:
            pre(t)
        t.matmul(C.pC[0:128, 0:512], C.w2[:, w2off + oc * 128: w2off + (oc + 1) * 128],
                 C.g2[:, 0:512], start=True, stop=False)
        m = None
        for ic in range(8):
            m = t.matmul(C.pC[0:128, 0:512], wslice_fn(ic),
                         xb[:, ic * 512:(ic + 1) * 512], start=False, stop=(ic == 7))
        sT.inc(m)
    C.t(f)
    return sT.n


def proj_tok_tc(C, xb, wslice_fn, tc, ow, w2off, o0=0, pre=None):
    sT = C.sT
    def f(t):
        if pre is not None:
            pre(t)
        t.matmul(C.pC[0:128, 0:ow], C.g2[0:2, tc * 128:(tc + 1) * 128],
                 C.w2[:, w2off + o0: w2off + o0 + ow], start=True, stop=False)
        m = None
        for ic in range(8):
            m = t.matmul(C.pC[0:128, 0:ow],
                         xb[:, ic * 512 + tc * 128: ic * 512 + tc * 128 + 128],
                         wslice_fn(ic), start=False, stop=(ic == 7))
        sT.inc(m)
    C.t(f)
    return sT.n


KB.ln_full = ln_full
KB.ln_lite = ln_lite
KB.w1_bias = w1_bias
KB.proj_feat_oc = proj_feat_oc
KB.proj_tok_tc = proj_tok_tc

def load_wsec(C, dst_cols, src2d, dout, pitch, rows0=0, nic=8):
    t = C.AB_t
    def f(g):
        C.sT.wait(g)
        src = bass.AP(src2d.tensor, src2d.offset + rows0,
                      [[pitch, 128], [128 * pitch, nic], [1, dout]])
        C.sDW.inc(g.dma_start(out=t[:, dst_cols:dst_cols + nic * dout].rearrange("p (i o) -> p i o", i=nic),
                    in_=src), 16)
        C.sDW.wait(g)
        C.sG.inc(g.nop("rl", nofuse=True))
    C.g(f)
    return C.sG.n


def sub_attn_b(C, last):
    T, NT, NC, TP1 = C.T, C.NT, C.NC, C.TP1
    sv = 65 * 16
    AB, AFA = C.AB, C.AFA
    sT, sS, sV, sG, sDW, sDX, sDS = C.sT, C.sS, C.sV, C.sG, C.sDW, C.sDX, C.sDS

    AB.push()
    kn_o = AB.alloc(8 * TP1)
    kn = C.AB_t[:, kn_o:kn_o + 8 * TP1]

    def f(g):
        a = g.memset(kn[:, :].rearrange("p (c t) -> p c t", c=8)[:, :, 0:1], 0.0)
        sG.inc(a)
    C.g(f)
    C.fence()

    for sec, o0, w2off in (("q", 0, 0), ("k", 1024, 1024)):
        AB.push()
        AFA.push()
        xt = C.AF_t[:, AFA.alloc(4096):AFA.stack[-1]]
        xb_o = AB.alloc(4096)
        xb = C.AB_t[:, xb_o:xb_o + 4096]
        ws_o = AB.alloc(8192)
        ws = C.AB_t[:, ws_o:ws_o + 8192]
        wtok = load_wsec(C, ws_o, C.wqkvT, 1024, 3072, rows0=o0)
        C.w1_bias(ws, 1024, w2off, C.bqkv[0:1, o0:o0 + 1024], wait_tok=wtok)
        stg_o = AB.alloc(1024)
        stg = C.AB_t[:, stg_o:stg_o + 1024]
        stok = [(0, 0), (0, 0)]
        for tt in range(NT):
            if sec == "q":
                lt = C.ln_full(tt, xt, xb, store=True)
            else:
                lt = C.ln_lite(tt, xt, xb)
            prev_cp = [sS.n]
            for oc in range(8):
                def pre(t, pc=prev_cp[0], lt=lt):
                    sS.wait(t, pc)
                    sV.wait(t, lt)
                tok = C.proj_feat_oc(xb, lambda ic, oc=oc: ws[:, ic * 1024 + oc * 128: ic * 1024 + (oc + 1) * 128],
                                     oc, w2off, pre=pre)
                if sec == "q":
                    sl = oc % 2
                    def fse(s, oc=oc, sl=sl, tok=tok):
                        sT.wait(s, tok)
                        C.sY.wait(s, stok[sl])
                        a = s.activation(stg[:, sl * 512:(sl + 1) * 512], C.pC[0:128, 0:512], AF.Copy, scale=0.125)
                        sS.inc(a)
                    C.s(fse)
                    prev_cp[0] = sS.n
                    cp = sS.n
                    def fy(y, oc=oc, sl=sl, tt=tt, cp=cp):
                        sS.wait(y, cp)
                        C.ld(y, C.qhb[oc * 128:(oc + 1) * 128, tt * 512:(tt + 1) * 512],
                             stg[:, sl * 512:(sl + 1) * 512], sDS)
                        sDS.wait(y)
                        C.sY.inc(y.nop("rl", nofuse=True))
                    C.y(fy)
                    stok[sl] = C.sY.n
                else:
                    def fse(s, oc=oc, tt=tt, tok=tok):
                        sT.wait(s, tok)
                        a = s.activation(kn[:, oc * TP1 + tt * 512 + 1: oc * TP1 + tt * 512 + 513],
                                         C.pC[0:128, 0:512], AF.Copy)
                        sS.inc(a)
                    C.s(fse)
                    prev_cp[0] = sS.n
        AFA.pop()
        AB.pop()
        C.fence()

    vn_o = AB.alloc(NC * sv)
    vn = C.AB_t[:, vn_o:vn_o + NC * sv]
    def f(g):
        b = g.memset(vn[:, :].rearrange("p (k h c) -> p k h c", k=NC, h=16)[:, :, :, 64:65], 1.0)
        sG.inc(b)
    C.g(f)
    AB.push()
    AFA.push()
    xt = C.AF_t[:, AFA.alloc(4096):AFA.stack[-1]]
    xb_o = AB.alloc(4096)
    xb = C.AB_t[:, xb_o:xb_o + 4096]
    ws_o = AB.alloc(4096)
    o0 = 2048
    w2off = 2048
    wsu = [(0, 0), (0, 0)]
    for tt in range(NT):
        lt = C.ln_lite(tt, xt, xb)
        for sl in range(2):
            wsb = ws_o
            def fg(g, sl=sl, pu=tuple(map(max, wsu[0], wsu[1]))):
                sT.wait(g, pu)
                src = bass.AP(C.wqkvT.tensor, C.wqkvT.offset + o0 + sl * 512,
                              [[3072, 128], [128 * 3072, 8], [1, 512]])
                sDW.inc(g.dma_start(out=C.AB_t[:, ws_o:ws_o + 4096].rearrange("p (i o) -> p i o", i=8),
                            in_=src), 16)
                sDW.wait(g)
                sG.inc(g.nop("rl", nofuse=True))
            C.g(fg)
            wtok = sG.n
            if tt == 0:
                C.w1_bias(C.AB_t[:, wsb:wsb + 4096], 512, w2off + sl * 512,
                          C.bqkv[0:1, o0 + sl * 512:o0 + sl * 512 + 512], wait_tok=wtok)
            prev_ev = [sV.n]
            for tc in range(4):
                def pre(t, pv=prev_ev[0], wtok=wtok, lt=lt):
                    sG.wait(t, wtok)
                    sV.wait(t, tuple(map(max, pv, lt)))
                tok = C.proj_tok_tc(xb, lambda ic, wsb=wsb: C.AB_t[:, wsb + ic * 512:wsb + (ic + 1) * 512],
                                    tc, 512, w2off, o0=sl * 512, pre=pre)
                kt = tt * 4 + tc
                def fv(v, kt=kt, sl=sl, tok=tok, pe=prev_ev[0]):
                    sT.wait(v, tok)
                    sV.wait(v, pe)
                    dst = vn[:, kt * sv + 65 * 8 * sl: kt * sv + 65 * 8 * sl + 520]
                    dst = dst.rearrange("p (h c) -> p h c", c=65)[:, :, 0:64]
                    c = v.tensor_copy(dst, C.pC[0:128, 0:512].rearrange("p (h c) -> p h c", c=64))
                    sV.inc(c)
                C.v(fv)
                prev_ev[0] = sV.n
            wsu[sl] = sT.n
    AFA.pop()
    AB.pop()
    C.fence()

    AB.push()
    AFA.push()
    qs_o = AB.alloc(4096)
    qs = C.AB_t[:, qs_o:qs_o + 4096]
    ob_o = AB.alloc(4096)
    obuf = C.AB_t[:, ob_o:ob_o + 4096]
    odd_o = AB.alloc(512)
    odds = C.AB_t[:, odd_o:odd_o + 512]
    wp_o = AB.alloc(1024)
    wp = C.AB_t[:, wp_o:wp_o + 1024]
    meg_o = AFA.alloc(C.MEGW)
    mega = C.AF_t[:, meg_o:meg_o + C.MEGW]
    inv_o = AFA.alloc(512)
    invs = C.AF_t[:, inv_o:inv_o + 512]
    xo_o = AFA.alloc(1024)
    stf_o = AFA.alloc(1024)

    def f(g):
        C.ld(g, C.w2[0:1, 3072:3072 + 1024], C.mrows[1:2, 0:1024])
        C.ld(g, C.w2[1:2, 3072:3072 + 1024], C.bproj[0:1, :])
    C.g(f)
    C.fence()

    head_order = list(range(8, 16)) + list(range(0, 8))
    state = {"cur": None}

    def pre_head(slot, h):
        want = "mask" if h >= 8 else ("hope", h)
        if state["cur"] == want:
            return
        state["cur"] = want
        W = min(C.MEGW, 512 * (slot + 1) + 384)
        if want == "mask":
            tk = C.emit_mega_mask(mega, C.tbl_mf, qshift=0, W=W)
        else:
            off = C.LPAD - 384
            def fg(g, h=h, W=W):
                sT.wait(g)
                sV.wait(g)
                sDW.inc(g.dma_start(out=mega[:, 0:W],
                                    in_=C.tbl_hh[h * 128:(h + 1) * 128, off:off + W]), 16)
                sDW.wait(g)
                half = (W + 1) // 2
                for c0 in (0, half):
                    cw = min(half, W - c0)
                    sDW.inc(g.dma_start(out=mega[:, c0:c0 + cw],
                                        in_=C.tbl_hl[h * 128:(h + 1) * 128, off + c0:off + c0 + cw],
                                        accum_op=ALU.add), 16)
                    sDW.wait(g)
                sG.inc(g.nop("rl", nofuse=True))
            C.g(fg)
            tk = sG.n
        def ft(t, tk=tk):
            sG.wait(t, tk)
        C.t(ft)

    def q_load(slot):
        def f(y):
            sT.wait(y)
            for dc in range(8):
                C.ld(y, qs[:, dc * 512:(dc + 1) * 512],
                     C.qhb[dc * 128:(dc + 1) * 128, slot * 512:(slot + 1) * 512], sDX)
            sDX.wait(y)
            C.sY.inc(y.nop("rl", nofuse=True))
        C.y(f)
        tk = C.sY.n
        def ft(t, tk=tk):
            C.sY.wait(t, tk)
        C.t(ft)

    epst = {"tk2": (0, 0), "tk3": (0, 0), "tk4": (0, 0), "odd": (0, 0)}

    def epilogue(slot, h, av, tok_av):
        blk = h // 2
        def fs(s, tok_av=tok_av, p2=epst["tk2"]):
            C.sT2.wait(s, tok_av)
            sT.wait(s, p2)
            a = s.activation(C.rows[64:65, 0:512], av[64:65, :], AF.Ln)
            sS.inc(a)
        C.s(fs)
        tk1 = sS.n
        def ft(t, tk1=tk1, p3=epst["tk3"]):
            sS.wait(t, tk1)
            sS.wait(t, p3)
            m = t.matmul(C.pB[0:64, 0:512], C.cstf[64:65, CF_ONESB:CF_ONESB + 64],
                         C.rows[64:65, 0:512], start=True, stop=True)
            sT.inc(m)
        C.t(ft)
        tk2 = sT.n
        epst["tk2"] = tk2
        def fs2(s, tk2=tk2, p4=epst["tk4"]):
            sT.wait(s, tk2)
            sV.wait(s, p4)
            a = s.activation(invs[0:64, :], C.pB[0:64, 0:512], AF.Exp, scale=-1.0)
            sS.inc(a)
        C.s(fs2)
        tk3 = sS.n
        epst["tk3"] = tk3
        def fv(v, h=h, blk=blk, tk3=tk3, po=epst["odd"]):
            sS.wait(v, tk3)
            if h % 2 == 0:
                dst = obuf[0:64, blk * 512:(blk + 1) * 512]
            else:
                C.sY.wait(v, po)
                dst = odds[0:64, 0:512]
            c = v.tensor_mul(dst, av[0:64, :], invs[0:64, :])
            sV.inc(c)
        C.v(fv)
        tk4 = sV.n
        epst["tk4"] = tk4
        if h % 2 == 1:
            def fy(y, blk=blk, tk4=tk4):
                sV.wait(y, tk4)
                C.ld(y, obuf[64:128, blk * 512:(blk + 1) * 512], odds[0:64, 0:512], sDS)
                sDS.wait(y)
                C.sY.inc(y.nop("rl", nofuse=True))
            C.y(fy)
            epst["odd"] = C.sY.n
        return (sV, sV.n)

    def k_ap(slot, kt, h):
        dc = h // 2
        return kn[64 * (h % 2):64 * (h % 2) + 64, dc * TP1 + 1 + kt * 128: dc * TP1 + 1 + kt * 128 + 128]

    def q_ap(slot, h):
        dc = h // 2
        return qs[64 * (h % 2):64 * (h % 2) + 64, dc * 512:(dc + 1) * 512]

    def vn_ap(kt, h):
        base = kt * sv + 65 * h
        return vn[:, base:base + 65]

    def bias_ap(slot, kt, h):
        c0 = C.mega_c0(slot, kt)
        if h < 8:
            return mega[:, c0:c0 + 512]
        cnt = 4 * (slot + 1)
        if kt >= cnt - 4:
            return mega[:, c0:c0 + 512]
        return None

    cfg = dict(H=16, counts=lambda s: 4 * (s + 1), k_ap=k_ap, q_ap=q_ap,
               vn_ap=vn_ap, bias_ap=bias_ap, phantom=None, eshift=-40.0,
               epilogue=epilogue, pre_head=pre_head, head_order=head_order)

    for slot in range(NT):
        state["cur"] = None
        q_load(slot)
        C.emit_attention_slot(cfg, slot)
        C.out_proj_residual(slot, obuf, wp, wp_o, C.wpT, 3072,
                            C.AF_t[:, xo_o:xo_o + 1024], C.AF_t[:, stf_o:stf_o + 1024], last)
        C.fence()

    AFA.pop()
    AB.pop()
    AB.pop()


def out_proj_residual(C, slot, obuf, wp, wp_o, wsrc, w2off, xo, stf, last, nwp=1):
    sT, sS, sV, sDW, sDX, sDS = C.sT, C.sS, C.sV, C.sDW, C.sDX, C.sDS
    t0 = slot * 512
    prev = {"ev": sV.n, "st": [(0, 0), (0, 0)], "mm": (0, 0), "xv": [(0, 0), (0, 0)],
            "wu": [(0, 0), (0, 0)]}
    for oc in range(8):
        sl = oc % 2
        wsl = oc % nwp
        def fg(g, oc=oc, wsl=wsl, pu=prev["wu"][wsl] if nwp == 2 else prev["mm"]):
            sT.wait(g, pu)
            src = bass.AP(wsrc.tensor, wsrc.offset + oc * 128,
                          [[8192, 128], [1024, 8], [1, 128]])
            sDW.inc(g.dma_start(out=wp[:, wsl * 1024:wsl * 1024 + 1024].rearrange("p (j o) -> p j o", j=8),
                        in_=src), 16)
            sDW.wait(g)
            C.sG.inc(g.nop("rl", nofuse=True))
        C.g(fg)
        wtok = C.sG.n
        def fy(y, oc=oc, sl=sl):
            sV.wait(y, prev["xv"][sl])
            C.ld(y, xo[:, sl * 512:sl * 512 + 512], C.xres[oc * 128:(oc + 1) * 128, t0:t0 + 512], sDX)
            sDX.wait(y)
            C.sY.inc(y.nop("rl", nofuse=True))
        C.y(fy)
        xtok = C.sY.n
        def ft(t, oc=oc, wtok=wtok, wsl=wsl):
            C.sG.wait(t, wtok)
            sV.wait(t, prev["ev"])
            if oc == 0:
                C.sY.wait(t)
            t.matmul(C.pC[0:128, 0:512], C.w2[:, w2off + oc * 128: w2off + (oc + 1) * 128],
                     C.g2[:, 0:512], start=True, stop=False)
            m = None
            for j in range(8):
                m = t.matmul(C.pC[0:128, 0:512], wp[:, wsl * 1024 + j * 128:wsl * 1024 + (j + 1) * 128],
                             obuf[:, j * 512:(j + 1) * 512], start=False, stop=(j == 7))
            sT.inc(m)
        C.t(ft)
        mtok = sT.n
        prev["mm"] = mtok
        prev["wu"][wsl] = mtok
        def fv(v, sl=sl, mtok=mtok, xtok=xtok, pst=prev["st"][sl]):
            sT.wait(v, mtok)
            C.sY.wait(v, xtok)
            C.sY.wait(v, pst)
            c = v.tensor_add(stf[:, sl * 512:sl * 512 + 512], xo[:, sl * 512:sl * 512 + 512],
                             C.pC[0:128, 0:512])
            sV.inc(c)
        C.v(fv)
        prev["ev"] = sV.n
        prev["xv"][sl] = sV.n
        vtok = sV.n
        def fy2(y, oc=oc, sl=sl, vtok=vtok):
            sV.wait(y, vtok)
            dst = (C.y_out if last else C.xres)
            C.ld(y, dst[oc * 128:(oc + 1) * 128, t0:t0 + 512], stf[:, sl * 512:sl * 512 + 512], sDS)
            sDS.wait(y)
            C.sY.inc(y.nop("rl", nofuse=True))
        C.y(fy2)
        prev["st"][sl] = C.sY.n


KB.sub_attn_b = sub_attn_b
KB.out_proj_residual = out_proj_residual
KB.load_wsec = load_wsec

def prep_inputs(ins, T=4096):
    f32 = np.float32
    d = {}
    d["wkT"] = np.ascontiguousarray(ins["L0_attn_a_W_k"].T.astype(f32))
    d["wvT"] = np.ascontiguousarray(ins["L0_attn_a_W_v"].T.astype(f32))
    wc = ins["L0_attn_a_W_c"].astype(f32)
    wc_re = np.zeros((128, 8 * 1024), dtype=f32)
    for j in range(8):
        for p in range(128):
            wc_re[p, j * 1024:(j + 1) * 1024] = wc[:, 64 * (2 * j + p // 64) + (p % 64)]
    d["wcT"] = wc_re
    d["aprm"] = np.concatenate([
        ins["L0_attn_a_leaky_key_beta"].ravel(), ins["L0_attn_a_key_scale"].ravel(),
        ins["L0_attn_a_v_coef"].ravel(), ins["L0_attn_a_val_scale"].ravel()]
    ).astype(f32).reshape(1, 64)
    d["wqkvT"] = np.ascontiguousarray(ins["L0_attn_b_w_qkv"].T.astype(f32))
    d["bqkv"] = ins["L0_attn_b_b_qkv"].astype(f32).reshape(1, 3072)
    wp = ins["L0_attn_b_w_proj"].astype(f32)
    wp_re = np.zeros((128, 8 * 1024), dtype=f32)
    for j in range(8):
        for p in range(128):
            wp_re[p, j * 1024:(j + 1) * 1024] = wp[:, 64 * (2 * j + p // 64) + (p % 64)]
    d["wpT"] = wp_re
    d["bproj"] = ins["L0_attn_b_b_proj"].astype(f32).reshape(1, 1024)
    d["wqf"] = np.ascontiguousarray(ins["L0_ffn_w_q"].astype(f32))
    d["wkf"] = np.ascontiguousarray(ins["L0_ffn_w_k"].astype(f32))
    d["wgf"] = np.ascontiguousarray(ins["L0_ffn_w_gate"].astype(f32))
    ve = ins["L0_ffn_value_experts"].astype(f32)
    d["wve"] = np.ascontiguousarray(np.transpose(ve, (1, 2, 0, 3)).reshape(8, 1024, 512))
    oe = ins["L0_ffn_output_experts"].astype(f32)
    woe = np.zeros((32, 128, 1024), dtype=f32)
    for e in range(8):
        for j in range(4):
            m = e * 4 + j
            woe[m, 0:64, :] = oe[e, 2 * j]
            woe[m, 64:128, :] = oe[e, 2 * j + 1]
    d["woe"] = woe
    ma, mf, hh, hl = build_tables(T)
    d["tbl_ma"] = ma
    d["tbl_mf"] = mf
    d["tbl_hh"] = hh
    d["tbl_hl"] = hl
    d["cstb_in"] = mk_cstb()
    p = np.arange(128)[:, None]
    j = np.arange(128)[None, :]
    tu = np.zeros((128, 256), dtype=f32)
    tu[:, 0:128] = np.maximum(j - p, 0).astype(f32)
    tu[:, 128:256] = (j >= p).astype(f32)
    d["tblu"] = tu
    mr = np.zeros((2, 4096), dtype=np.float32); mr[0, :] = 1.0
    d["mrows"] = mr
    d["cstf_in"] = mk_cstf()
    d["selc_in"] = mk_selectors()
    return d


_SUBS_ALL = ("attn_a", "ffn1", "attn_b", "ffn2")


def run_device(ins, T=4096, subs=_SUBS_ALL, n_b=None):
    x = np.asarray(ins["L0_attn_a_x"], dtype=np.float32)
    B = x.shape[0]
    n_b = n_b or B
    shared = prep_inputs(ins, T)
    kb = KB(T, subs)
    nc = kb.build()
    in_maps = []
    for c in range(8):
        b = c % n_b
        m = dict(shared)
        m["x_feat"] = np.ascontiguousarray(x[b].T)
        in_maps.append(m)
    res = run_bass_kernel_spmd(nc, in_maps, core_ids=list(range(8)))
    outs = []
    for b in range(B):
        outs.append(np.ascontiguousarray(res.results[b]["y"].T))
    return np.stack(outs, axis=0)


def kernel(**inputs):
    return run_device(inputs, T=4096, subs=_SUBS_ALL)

def sub_ffn(C, last):
    T, NT, NC, TP1 = C.T, C.NT, C.NC, C.TP1
    sv = 65 * 8
    AB, AFA = C.AB, C.AFA
    sT, sS, sV, sG, sDW, sDX, sDS = C.sT, C.sS, C.sV, C.sG, C.sDW, C.sDX, C.sDS

    AB.push()
    knf_o = AB.alloc(4 * TP1)
    knf = C.AB_t[:, knf_o:knf_o + 4 * TP1]
    vnf_o = AB.alloc(NC * sv)
    vnf = C.AB_t[:, vnf_o:vnf_o + NC * sv]
    gmT_o = AB.alloc(NT * 512)
    def f(g):
        g.memset(knf[:, :].rearrange("p (c t) -> p c t", c=4)[:, :, 0:1], 0.0)
        a = g.memset(vnf[:, :].rearrange("p (k h c) -> p k h c", k=NC, h=8)[:, :, :, 64:65], 1.0)
        sG.inc(a)
    C.g(f)
    C.fence()

    AB.push()
    AFA.push()
    xt = C.AF_t[:, AFA.alloc(4096):AFA.stack[-1]]
    gt_o = AFA.alloc(640)
    mt_o = AFA.alloc(512)
    tmp_o = AFA.alloc(512)
    xb_o = AB.alloc(4096)
    xb = C.AB_t[:, xb_o:xb_o + 4096]
    wq_o = AB.alloc(4096)
    wk_o = AB.alloc(4096)
    wg_o = AB.alloc(512)
    wv_o = AB.alloc(8192)
    stg_o = AB.alloc(1024)
    stg = C.AB_t[:, stg_o:stg_o + 1024]

    load_wsec(C, wq_o, C.wqf, 512, 512)
    load_wsec(C, wk_o, C.wkf, 512, 512)
    load_wsec(C, wg_o, C.wgf, 64, 64)
    C.w1_bias(C.AB_t[:, wq_o:wq_o + 4096], 512, 0, None)
    C.w1_bias(C.AB_t[:, wk_o:wk_o + 4096], 512, 512, None)
    C.w1_bias(C.AB_t[:, wg_o:wg_o + 512], 64, 1024, None)

    wvu = [(0, 0), (0, 0)]
    for tt in range(NT):
        lt = C.ln_full(tt, xt, xb, store=False)
        prev_cp = [sS.n]
        for oc in range(4):
            def pre(t, pc=prev_cp[0], lt=lt):
                sS.wait(t, pc)
                sV.wait(t, lt)
            tok = C.proj_feat_oc(xb, lambda ic, oc=oc: C.AB_t[:, wq_o + ic * 512 + oc * 128: wq_o + ic * 512 + (oc + 1) * 128],
                                 oc, 0, pre=pre)
            sl = oc % 2
            def fse(s, sl=sl, tok=tok):
                sT.wait(s, tok)
                C.sY.wait(s)
                a = s.activation(stg[:, sl * 512:(sl + 1) * 512], C.pC[0:128, 0:512], AF.Copy, scale=0.125)
                sS.inc(a)
            C.s(fse)
            prev_cp[0] = sS.n
            cp = sS.n
            def fy(y, oc=oc, sl=sl, tt=tt, cp=cp):
                sS.wait(y, cp)
                C.ld(y, C.qhb[oc * 128:(oc + 1) * 128, tt * 512:(tt + 1) * 512],
                     stg[:, sl * 512:(sl + 1) * 512], sDS)
                sDS.wait(y)
                C.sY.inc(y.nop("rl", nofuse=True))
            C.y(fy)
        for oc in range(4):
            def pre(t, pc=prev_cp[0]):
                sS.wait(t, pc)
            tok = C.proj_feat_oc(xb, lambda ic, oc=oc: C.AB_t[:, wk_o + ic * 512 + oc * 128: wk_o + ic * 512 + (oc + 1) * 128],
                                 oc, 512, pre=pre)
            def fse(s, oc=oc, tt=tt, tok=tok):
                sT.wait(s, tok)
                a = s.activation(knf[:, oc * TP1 + tt * 512 + 1: oc * TP1 + tt * 512 + 513],
                                 C.pC[0:128, 0:512], AF.Copy)
                sS.inc(a)
            C.s(fse)
            prev_cp[0] = sS.n
        gm_toks = {}
        for tc in range(4):
            def pre(t, pc=prev_cp[0]):
                sS.wait(t, pc)
            tok = C.proj_tok_tc(xb, lambda ic: C.AB_t[:, wg_o + ic * 64: wg_o + (ic + 1) * 64],
                                tc, 64, 1024, pre=pre)
            gbase = gt_o + tc * 160
            def fse(s, tok=tok, gbase=gbase):
                sT.wait(s, tok)
                a = s.activation(C.AF_t[:, gbase:gbase + 64], C.pC[0:128, 0:64], AF.Sigmoid)
                sS.inc(a)
            C.s(fse)
            prev_cp[0] = sS.n
            stok = sS.n
            def fv(v, gbase=gbase, stok=stok):
                sS.wait(v, stok)
                sV.wait(v)
                g3 = C.AF_t[:, gbase:gbase + 64].rearrange("p (h e) -> p h e", h=8)
                a = v.tensor_reduce(C.AF_t[:, gbase + 128:gbase + 136], g3,
                                    mybir.AxisListType.X, ALU.max)
                tk = sV.inc(a)
                sV.wait(v, tk)
                mb = bass.AP(C.AF_t.ap().tensor, C.AF_t.ap().offset + gbase + 128,
                             [C.AF_t.ap().ap[0], [1, 8], [0, 8]])
                a = v.tensor_tensor(C.AF_t[:, gbase + 64:gbase + 128],
                                    C.AF_t[:, gbase:gbase + 64], mb, ALU.is_equal)
                tk = sV.inc(a)
                sV.wait(v, tk)
                c = v.tensor_mul(C.AF_t[:, gbase + 64:gbase + 128],
                                 C.AF_t[:, gbase + 64:gbase + 128], C.AF_t[:, gbase:gbase + 64])
                gm_toks[gbase] = sV.inc(c)
            C.v(fv)
        def fgf(t, pc=prev_cp[0], lt=lt):
            sS.wait(t, pc)
            sV.wait(t, lt)
            t.matmul(C.pC[0:64, 0:512], C.w2[:, 1024:1088], C.g2[:, 0:512], start=True, stop=False)
            m = None
            for ic in range(8):
                m = t.matmul(C.pC[0:64, 0:512], C.AB_t[:, wg_o + ic * 64: wg_o + (ic + 1) * 64],
                             xb[:, ic * 512:(ic + 1) * 512], start=False, stop=(ic == 7))
            sT.inc(m)
        C.t(fgf)
        tkgf = sT.n
        def fse(s, tkgf=tkgf):
            sT.wait(s, tkgf)
            a = s.activation(C.AF_t[0:64, tmp_o:tmp_o + 512], C.pC[0:64, 0:512], AF.Sigmoid)
            sS.inc(a)
        C.s(fse)
        tk_gf = sS.n
        prev_cp[0] = sS.n
        def ftr(t, tk_gf=tk_gf, gm_toks=gm_toks):
            for tc in range(4):
                sV.wait(t, gm_toks[gt_o + tc * 160])
            sS.wait(t, tk_gf)
            m = None
            for tc in range(4):
                m = t.transpose(C.pC[0:8, 128 * tc:128 * tc + 128],
                                C.AF_t[:, gt_o + tc * 160 + 128: gt_o + tc * 160 + 136],
                                C.cstf[:, CF_ID:CF_ID + 128])
            sT.inc(m)
        C.t(ftr)
        tktr = sT.n
        def fmt(v, tktr=tktr):
            sT.wait(v, tktr)
            c = v.tensor_copy(C.AF_t[0:8, mt_o:mt_o + 512], C.pC[0:8, 0:512])
            sV.inc(c)
        C.v(fmt)
        tkmt = sV.n
        def fmb(t, tkmt=tkmt):
            sV.wait(t, tkmt)
            m = t.matmul(C.pC[0:64, 0:512], C.cstf[0:8, CF_E8:CF_E8 + 64],
                         C.AF_t[0:8, mt_o:mt_o + 512], start=True, stop=True)
            sT.inc(m)
        C.t(fmb)
        tkmb = sT.n
        def fgm(v, tkmb=tkmb, tt=tt):
            sT.wait(v, tkmb)
            a = v.tensor_tensor(C.AF_t[0:64, mt_o:mt_o + 512],
                                C.AF_t[0:64, tmp_o:tmp_o + 512], C.pC[0:64, 0:512], ALU.is_equal)
            tk = sV.inc(a)
            sV.wait(v, tk)
            c = v.tensor_mul(C.AB_t[0:64, gmT_o + tt * 512: gmT_o + (tt + 1) * 512],
                             C.AF_t[0:64, mt_o:mt_o + 512], C.AF_t[0:64, tmp_o:tmp_o + 512])
            sV.inc(c)
        C.v(fgm)
        for hh in range(8):
            sl = hh % 2
            wvb = wv_o + sl * 4096
            def fg(g, hh=hh, pu=wvu[sl]):
                sT.wait(g, pu)
                src = bass.AP(C.wve.tensor, C.wve.offset + hh * 1024 * 512,
                              [[512, 128], [128 * 512, 8], [1, 512]])
                sDW.inc(g.dma_start(out=C.AB_t[:, wv_o + (hh % 2) * 4096:wv_o + (hh % 2) * 4096 + 4096].rearrange("p (i o) -> p i o", i=8),
                        in_=src), 16)
                sDW.wait(g)
                sG.inc(g.nop("rl", nofuse=True))
            C.g(fg)
            C.w1_bias(C.AB_t[:, wvb:wvb + 4096], 512, 2048 + sl * 512, None)
            for tc in range(4):
                def pre(t, pc=prev_cp[0]):
                    sS.wait(t, pc)
                    sV.wait(t)
                tok = C.proj_tok_tc(xb, lambda ic, wvb=wvb: C.AB_t[:, wvb + ic * 512:wvb + (ic + 1) * 512],
                                    tc, 512, 2048 + sl * 512, pre=pre)
                kt = tt * 4 + tc
                gbase = gt_o + tc * 160
                def fcv(v, hh=hh, kt=kt, tok=tok, gbase=gbase):
                    sT.wait(v, tok)
                    sV.wait(v)
                    t3 = C.AF_t[:, tmp_o:tmp_o + 512]
                    gmb = bass.AP(C.AF_t.ap().tensor, C.AF_t.ap().offset + gbase + 64 + 8 * hh,
                                  [C.AF_t.ap().ap[0], [1, 8], [0, 64]])
                    a = v.tensor_tensor(t3, C.pC[0:128, 0:512], gmb, ALU.mult)
                    tk = sV.inc(a)
                    sV.wait(v, tk)
                    a = v.tensor_add(t3[:, 0:256], t3[:, 0:256], t3[:, 256:512])
                    tk = sV.inc(a)
                    sV.wait(v, tk)
                    a = v.tensor_add(t3[:, 0:128], t3[:, 0:128], t3[:, 128:256])
                    tk = sV.inc(a)
                    sV.wait(v, tk)
                    c = v.tensor_add(vnf[:, kt * sv + 65 * hh: kt * sv + 65 * hh + 64],
                                     t3[:, 0:64], t3[:, 64:128])
                    sV.inc(c)
                C.v(fcv)
            wvu[sl] = sT.n
    AFA.pop()
    AB.pop()
    C.fence()

    AB.push()
    AFA.push()
    qs_o = AB.alloc(2048)
    qs = C.AB_t[:, qs_o:qs_o + 2048]
    ob_o = AB.alloc(2048)
    obuf = C.AB_t[:, ob_o:ob_o + 2048]
    odd_o = AB.alloc(512)
    odds = C.AB_t[:, odd_o:odd_o + 512]
    selc_o = AB.alloc(4096)
    selc = C.AB_t[:, selc_o:selc_o + 4096]
    ub_o = AB.alloc(512)
    ub = C.AB_t[:, ub_o:ub_o + 512]
    oe_o = AB.alloc(32 * 512)
    woe_o = AB.alloc(8192)
    meg_o = AFA.alloc(C.MEGW)
    mega = C.AF_t[:, meg_o:meg_o + C.MEGW]
    srow_o = AFA.alloc(512)
    srow = C.AF_t[:, srow_o:srow_o + 512]
    xo_o = AFA.alloc(1024)
    stf_o = AFA.alloc(1024)

    def f(g):
        C.ld(g, C.AB_t[0:64, selc_o:selc_o + 4096], C.selc_in[:, :])
    C.g(f)
    C.emit_mega_mask(mega, C.tbl_mf, qshift=0)
    C.fence()

    epst = {"gmm": (0, 0)}

    def epilogue(slot, h, av, tok_av):
        blk = h // 2
        def fs(s, h=h, blk=blk, tok_av=tok_av, pg=epst["gmm"]):
            C.sT2.wait(s, tok_av)
            sT.wait(s, pg)
            if h % 2 == 0:
                dst = obuf[0:64, blk * 512:(blk + 1) * 512]
            else:
                C.sY.wait(s)
                dst = odds[0:64, 0:512]
            s.activation(dst, av[0:64, :], AF.Copy)
            a = s.activation(srow[64:65, 0:512], av[64:65, :], AF.Copy)
            sS.inc(a)
        C.s(fs)
        tk1 = sS.n
        def ft(t, h=h, tk1=tk1):
            sS.wait(t, tk1)
            m = t.matmul(C.pB[0:8, 0:512], C.cstf[64:65, CF_ID + 64 - h:CF_ID + 72 - h],
                         srow[64:65, 0:512], start=(h == 0), stop=(h == 7))
            sT.inc(m)
        C.t(ft)
        epst["gmm"] = sT.n
        if h % 2 == 1:
            def fy(y, blk=blk, tk1=tk1):
                sS.wait(y, tk1)
                C.ld(y, obuf[64:128, blk * 512:(blk + 1) * 512], odds[0:64, 0:512], sDS)
                sDS.wait(y)
                C.sY.inc(y.nop("rl", nofuse=True))
            C.y(fy)
        return (sS, tk1)

    def q_load(slot):
        def f(y):
            sT.wait(y)
            for dc in range(4):
                C.ld(y, qs[:, dc * 512:(dc + 1) * 512],
                     C.qhb[dc * 128:(dc + 1) * 128, slot * 512:(slot + 1) * 512], sDX)
            sDX.wait(y)
            C.sY.inc(y.nop("rl", nofuse=True))
        C.y(f)
        tkq = C.sY.n
        def ft(t, tkq=tkq):
            C.sY.wait(t, tkq)
        C.t(ft)

    def k_ap(slot, kt, h):
        dc = h // 2
        return knf[64 * (h % 2):64 * (h % 2) + 64, dc * TP1 + 1 + kt * 128: dc * TP1 + 1 + kt * 128 + 128]

    def q_ap(slot, h):
        dc = h // 2
        return qs[64 * (h % 2):64 * (h % 2) + 64, dc * 512:(dc + 1) * 512]

    def vn_ap(kt, h):
        base = kt * sv + 65 * h
        return vnf[:, base:base + 65]

    def bias_ap(slot, kt, h):
        cnt = 4 * (slot + 1)
        if kt >= cnt - 4:
            c0 = C.mega_c0(slot, kt)
            return mega[:, c0:c0 + 512]
        return None

    cfg = dict(H=8, counts=lambda s: 4 * (s + 1), k_ap=k_ap, q_ap=q_ap,
               vn_ap=vn_ap, bias_ap=bias_ap, phantom=None, eshift=0.0,
               epilogue=epilogue)

    wou = [(0, 0), (0, 0)]
    for slot in range(NT):
        q_load(slot)
        C.emit_attention_slot(cfg, slot)
        def fr8(s):
            sT.wait(s)
            a = s.activation(C.rows[0:8, 0:512], C.pB[0:8, 0:512], AF.Ln)
            tk = sS.inc(a)
            sS.wait(s, tk)
            a = s.activation(C.rows[0:8, 512:1024], C.rows[0:8, 0:512], AF.Exp, scale=-1.0)
            sS.inc(a)
        C.s(fr8)
        tkr8 = sS.n
        def frb(t, tkr8=tkr8):
            sS.wait(t, tkr8)
            sV.wait(t)
            m = t.matmul(C.pC[0:64, 0:512], C.cstf[0:8, CF_E8:CF_E8 + 64],
                         C.rows[0:8, 512:1024], start=True, stop=True)
            sT.inc(m)
        C.t(frb)
        tkrb = sT.n
        def fu(v, slot=slot, tkrb=tkrb):
            sT.wait(v, tkrb)
            c = v.tensor_mul(ub[0:64, 0:512], C.AB_t[0:64, gmT_o + slot * 512: gmT_o + (slot + 1) * 512],
                             C.pC[0:64, 0:512])
            sV.inc(c)
        C.v(fu)
        tku = sV.n
        prev_oe = [tku]
        for mm_i in range(32):
            def fub(t, mm_i=mm_i, po=prev_oe[0]):
                sV.wait(t, po)
                m = t.matmul(C.pC[0:128, 0:512], C.AB_t[0:64, selc_o + 128 * mm_i: selc_o + 128 * mm_i + 128],
                             ub[0:64, 0:512], start=True, stop=True)
                sT.inc(m)
            C.t(fub)
            tub = sT.n
            jj = mm_i % 4
            def foe(v, mm_i=mm_i, jj=jj, tub=tub):
                sT.wait(v, tub)
                if mm_i == 0:
                    C.sY.wait(v)
                c = v.tensor_mul(C.AB_t[:, oe_o + mm_i * 512: oe_o + (mm_i + 1) * 512],
                                 obuf[:, jj * 512:(jj + 1) * 512], C.pC[0:128, 0:512])
                sV.inc(c)
            C.v(foe)
            prev_oe[0] = sV.n
        tkoe = sV.n
        t0 = slot * 512
        prevv = {"mm": (0, 0), "ev": tkoe, "st": [(0, 0), (0, 0)], "xv": [(0, 0), (0, 0)],
                 "wu": [wou[0], wou[1]]}
        for oc in range(8):
            sl = oc % 2
            wob = woe_o + sl * 4096
            def fgw(g, oc=oc, pu=prevv["wu"][sl]):
                sT.wait(g, pu)
                src = bass.AP(C.woe.tensor, C.woe.offset + oc * 128,
                              [[1024, 128], [128 * 1024, 32], [1, 128]])
                sDW.inc(g.dma_start(out=C.AB_t[:, woe_o + (oc % 2) * 4096:woe_o + (oc % 2) * 4096 + 4096].rearrange("p (m o) -> p m o", m=32),
                        in_=src), 16)
                sDW.wait(g)
                sG.inc(g.nop("rl", nofuse=True))
            C.g(fgw)
            wtok = sG.n
            def fy(y, oc=oc, sl=sl):
                sV.wait(y, prevv["xv"][sl])
                C.ld(y, C.AF_t[:, xo_o + sl * 512: xo_o + sl * 512 + 512],
                     C.xres[oc * 128:(oc + 1) * 128, t0:t0 + 512], sDX)
                sDX.wait(y)
                C.sY.inc(y.nop("rl", nofuse=True))
            C.y(fy)
            xtok = C.sY.n
            def ft(t, oc=oc, wtok=wtok, wob=wob):
                sG.wait(t, wtok)
                sV.wait(t, prevv["ev"])
                m = None
                for mm_i in range(32):
                    m = t.matmul(C.pC[0:128, 0:512], C.AB_t[:, wob + mm_i * 128: wob + (mm_i + 1) * 128],
                                 C.AB_t[:, oe_o + mm_i * 512: oe_o + (mm_i + 1) * 512],
                                 start=(mm_i == 0), stop=(mm_i == 31))
                sT.inc(m)
            C.t(ft)
            prevv["mm"] = sT.n
            prevv["wu"][sl] = sT.n
            mtok = sT.n
            def fv(v, sl=sl, mtok=mtok, pst=prevv["st"][sl], xtok=xtok):
                sT.wait(v, mtok)
                C.sY.wait(v, xtok)
                C.sY.wait(v, pst)
                c = v.tensor_add(C.AF_t[:, stf_o + sl * 512: stf_o + sl * 512 + 512],
                                 C.AF_t[:, xo_o + sl * 512: xo_o + sl * 512 + 512], C.pC[0:128, 0:512])
                sV.inc(c)
            C.v(fv)
            prevv["ev"] = sV.n
            prevv["xv"][sl] = sV.n
            vtok = sV.n
            def fy2(y, oc=oc, sl=sl, vtok=vtok):
                sV.wait(y, vtok)
                dst = (C.y_out if last else C.xres)
                C.ld(y, dst[oc * 128:(oc + 1) * 128, t0:t0 + 512],
                     C.AF_t[:, stf_o + sl * 512: stf_o + sl * 512 + 512], sDS)
                sDS.wait(y)
                C.sY.inc(y.nop("rl", nofuse=True))
            C.y(fy2)
            prevv["st"][sl] = C.sY.n
        wou[0], wou[1] = prevv["wu"]
        C.fence()

    AFA.pop()
    AB.pop()
    AB.pop()


KB.sub_ffn = sub_ffn

def sub_attn_a(C, last):
    T, NT, NC, TP1 = C.T, C.NT, C.NC, C.TP1
    sv = 65 * 16
    AB, AFA = C.AB, C.AFA
    sT, sS, sV, sG, sY = C.sT, C.sS, C.sV, C.sG, C.sY
    sDW, sDX, sDS = C.sDW, C.sDX, C.sDS
    AFT = C.AF_t

    AB.push()
    AFA.push()
    pr_o = AFA.alloc(64)
    dr_o = AFA.alloc(64)
    sc_o = AFA.alloc(64)
    ap_o = AFA.alloc(64)
    v16_o = AFA.alloc(1)

    def f(g):
        C.ld(g, AFT[0:1, pr_o:pr_o + 64], C.aprm[0:1, :])
        sDW.wait(g)
        sG.inc(g.nop("rl", nofuse=True))
    C.g(f)
    tok_pr = sG.n

    def f(s, tok_pr=tok_pr):
        sG.wait(s, tok_pr)
        a = s.activation(AFT[0:1, sc_o:sc_o + 16], AFT[0:1, pr_o:pr_o + 16], AF.Abs)
        tk = sS.inc(a)
        sS.wait(s, tk)
        a = s.activation(AFT[0:1, dr_o:dr_o + 16], AFT[0:1, sc_o:sc_o + 16], AF.Copy, scale=-10.0)
        sS.inc(a)
        a = s.activation(AFT[0:1, sc_o + 16:sc_o + 32], AFT[0:1, pr_o + 16:pr_o + 32], AF.Copy, scale=10.0)
        sS.inc(a)
        a = s.activation(AFT[0:1, dr_o + 32:dr_o + 48], AFT[0:1, pr_o + 32:pr_o + 48], AF.Copy)
        sS.inc(a)
        a = s.activation(AFT[0:1, sc_o + 48:sc_o + 64], AFT[0:1, pr_o + 48:pr_o + 64], AF.Exp, scale=10.0)
        sS.inc(a)
    C.s(f)
    tk_s1 = sS.n

    def f(v, tk_s1=tk_s1):
        sS.wait(v, tk_s1)
        ksm = bass.AP(C.cstf.ap().tensor, C.cstf.ap().offset + CF_CONST + 3,
                      [[C.cstf.ap().ap[0][0], 1], [0, 16]])
        a = v.tensor_tensor(AFT[0:1, sc_o + 32:sc_o + 48], AFT[0:1, sc_o + 16:sc_o + 32],
                            ksm, ALU.min)
        sV.inc(a)
        one = bass.AP(C.cstf.ap().tensor, C.cstf.ap().offset + CF_CONST + 4,
                      [[C.cstf.ap().ap[0][0], 1], [0, 16]])
        a = v.tensor_tensor(AFT[0:1, dr_o + 48:dr_o + 64], one,
                            AFT[0:1, pr_o + 32:pr_o + 48], ALU.subtract)
        sV.inc(a)
    C.v(f)
    tk_v1 = sV.n

    def f(s, tk_v1=tk_v1):
        sV.wait(s, tk_v1)
        a = s.activation(AFT[0:1, dr_o + 16:dr_o + 32], AFT[0:1, sc_o + 32:sc_o + 48], AF.Exp)
        sS.inc(a)
    C.s(f)
    tk_s2 = sS.n

    def fmm(t, tk_s2=tk_s2, tk_v1=tk_v1):
        sS.wait(t, tk_s2)
        sV.wait(t, tk_v1)
        m = t.matmul(C.pB[0:128, 0:64], C.cstf[0:1, CF_ONESB:CF_ONESB + 128],
                     AFT[0:1, dr_o:dr_o + 64], start=True, stop=True)
        sT.inc(m)
        m = t.matmul(C.pB[0:16, 64:65], AFT[0:1, sc_o + 48:sc_o + 64],
                     C.cstf[0:1, CF_CONST + 4:CF_CONST + 5], start=True, stop=True)
        sT.inc(m)
    C.t(fmm)
    tkmm = sT.n

    def fev(s, tkmm=tkmm):
        sT.wait(s, tkmm)
        a = s.activation(AFT[:, ap_o:ap_o + 64], C.pB[0:128, 0:64], AF.Copy)
        sS.inc(a)
        a = s.activation(AFT[0:16, v16_o:v16_o + 1], C.pB[0:16, 64:65], AF.Copy)
        sS.inc(a)
    C.s(fev)
    tok_ap = sS.n

    AB.push()
    AFA.push()
    du_o = AB.alloc(128)
    ut_o = AB.alloc(128)
    dcb_o = AB.alloc(2048)
    car_o = AB.alloc(1024)
    kn_o = AB.alloc(8 * TP1)
    kn = C.AB_t[:, kn_o:kn_o + 8 * TP1]
    wsk_o = AB.alloc(8192)
    wsk = C.AB_t[:, wsk_o:wsk_o + 8192]
    xb_o = AB.alloc(4096)
    xb = C.AB_t[:, xb_o:xb_o + 4096]
    kt_o = AB.alloc(1024)
    ktok = C.AB_t[:, kt_o:kt_o + 1024]
    cr2_o = AB.alloc(1024)
    xt = AFT[:, AFA.alloc(4096):AFA.stack[-1]]
    kns_o = AFA.alloc(1024)
    kns = AFT[:, kns_o:kns_o + 1024]
    lt1_o = AFA.alloc(128)
    lt2_o = AFA.alloc(128)
    sq_o = AFA.alloc(1024)
    nr_o = AFA.alloc(96)

    def f(g):
        a = g.memset(kn[:, :].rearrange("p (c t) -> p c t", c=8)[:, :, 0:1], 0.0)
        sG.inc(a)
    C.g(f)

    def f(g):
        C.ld(g, C.AB_t[:, du_o:du_o + 128], C.tblu[:, 0:128])
        C.ld(g, C.AB_t[:, ut_o:ut_o + 128], C.tblu[:, 128:256])
        sDW.wait(g)
        sG.inc(g.nop("rl", nofuse=True))
    C.g(f)
    tk4 = sG.n

    for h in range(16):
        def fv(v, h=h, tk4=tk4):
            sG.wait(v, tk4)
            sS.wait(v, tok_ap)
            sV.wait(v)
            sS.wait(v)
            nb = bass.AP(AFT.ap().tensor, AFT.ap().offset + ap_o + h,
                         [AFT.ap().ap[0], [0, 128]])
            a = v.tensor_tensor(AFT[:, lt1_o:lt1_o + 128], C.AB_t[:, du_o:du_o + 128],
                                nb, ALU.mult)
            sV.inc(a)
        C.v(fv)
        tkm = sV.n
        def fs(s, h=h, tkm=tkm):
            sV.wait(s, tkm)
            a = s.activation(AFT[:, lt2_o:lt2_o + 128], AFT[:, lt1_o:lt1_o + 128], AF.Exp)
            sS.inc(a)
        C.s(fs)
        tke = sS.n
        def fv2(v, h=h, tke=tke):
            sS.wait(v, tke)
            a = v.tensor_mul(C.pt[:, 128 * h:128 * h + 128], AFT[:, lt2_o:lt2_o + 128],
                             C.AB_t[:, ut_o:ut_o + 128])
            tk = sV.inc(a)
            sV.wait(v, tk)
            nb0 = bass.AP(AFT.ap().tensor, AFT.ap().offset + ap_o + h,
                          [[AFT.ap().ap[0][0], 1], [0, 128]])
            a = v.tensor_tensor(AFT[0:1, lt1_o:lt1_o + 128], C.cstb[0:1, CB_P1ROW:CB_P1ROW + 128],
                                nb0, ALU.mult)
            sV.inc(a)
        C.v(fv2)
        tkd = sV.n
        def fs2(s, h=h, tkd=tkd):
            sV.wait(s, tkd)
            a = s.activation(C.AB_t[0:1, dcb_o + 128 * h:dcb_o + 128 * h + 128],
                             AFT[0:1, lt1_o:lt1_o + 128], AF.Exp)
            sS.inc(a)
        C.s(fs2)

    wtokK = load_wsec(C, wsk_o, C.wkT, 1024, 1024)
    C.w1_bias(wsk, 1024, 0, None)
    C.fence()

    st = {"evac": (0, 0), "knmul": (0, 0), "scan": (0, 0), "knev": (0, 0),
          "carev": (0, 0)}
    for tt in range(NT):
        lt = C.ln_full(tt, xt, xb, store=True)
        for tc in range(4):
            c = 4 * tt + tc
            for half in range(2):
                def pre(t, pc=st["evac"], lt=lt, wtokK=wtokK):
                    sG.wait(t, wtokK)
                    sS.wait(t, pc)
                    sV.wait(t, lt)
                tok = C.proj_tok_tc(xb, lambda ic, half=half: wsk[:, ic * 1024 + half * 512: ic * 1024 + half * 512 + 512],
                                    tc, 512, 0, o0=half * 512, pre=pre)
                def fse(s, half=half, tok=tok, sc=st["scan"]):
                    sT.wait(s, tok)
                    sT.wait(s, sc)
                    a = s.activation(ktok[:, half * 512:(half + 1) * 512], C.pC[0:128, 0:512], AF.Copy)
                    sS.inc(a)
                C.s(fse)
                st["evac"] = sS.n

            ktokTok = sS.n
            def ftsc(t, c=c, ktokTok=ktokTok, pk=st["knmul"]):
                sS.wait(t, ktokTok)
                sV.wait(t, pk)
                m = None
                for h in range(16):
                    m = t.matmul(C.pL[:, 64 * h:64 * h + 64], C.pt[:, 128 * h:128 * h + 128],
                                 ktok[:, 64 * h:64 * h + 64], start=True, stop=(c == 0))
                    if c > 0:
                        m = t.matmul(C.pL[:, 64 * h:64 * h + 64],
                                     C.AB_t[0:1, dcb_o + 128 * h:dcb_o + 128 * h + 128],
                                     C.AB_t[0:1, car_o + 64 * h:car_o + 64 * h + 64],
                                     start=False, stop=True)
                sT.inc(m)
            C.t(ftsc)
            st["scan"] = sT.n
            scanTok = sT.n
            if c < NC - 1:
                def fcr(s, scanTok=scanTok, pce=st["carev"]):
                    sT.wait(s, scanTok)
                    sS.wait(s, pce)
                    s.activation(C.AB_t[64:128, cr2_o:cr2_o + 512], C.pL[64:128, 0:512], AF.Copy)
                    a = s.activation(C.AB_t[64:128, cr2_o + 512:cr2_o + 1024],
                                     C.pL[64:128, 512:1024], AF.Copy)
                    tk = sS.inc(a)
                    sS.wait(s, tk)
                    C.sDA.inc(s.dma_start(out=C.AB_t[0:1, car_o:car_o + 1024],
                                          in_=C.AB_t[127:128, cr2_o:cr2_o + 1024]), 16)
                    C.sDA.wait(s)
                    sS.inc(s.nop("rl", nofuse=True))
                C.s(fcr)
                st["carev"] = sS.n
            def fsq(s, scanTok=scanTok):
                sT.wait(s, scanTok)
                a = s.activation(AFT[:, sq_o:sq_o + 512], C.pL[:, 0:512], AF.Square)
                sS.inc(a)
                a = s.activation(AFT[:, sq_o + 512:sq_o + 1024], C.pL[:, 512:1024], AF.Square)
                sS.inc(a)
            C.s(fsq)
            tksq = sS.n
            def fnr(v, tksq=tksq):
                sS.wait(v, tksq)
                sV.wait(v)
                g3 = AFT[:, sq_o:sq_o + 1024].rearrange("p (h d) -> p h d", h=16)
                a = v.tensor_reduce(AFT[:, nr_o:nr_o + 16], g3, mybir.AxisListType.X, ALU.add)
                sV.inc(a)
            C.v(fnr)
            tkr = sV.n
            def fns(s, tkr=tkr):
                sV.wait(s, tkr)
                a = s.activation(AFT[:, nr_o + 16:nr_o + 32], AFT[:, nr_o:nr_o + 16], AF.Sqrt)
                sS.inc(a)
            C.s(fns)
            tkn1 = sS.n
            def fne(v, tkn1=tkn1):
                sS.wait(v, tkn1)
                eps = bass.AP(C.cstf.ap().tensor, C.cstf.ap().offset + CF_CONST + 1,
                              [C.cstf.ap().ap[0], [0, 16]])
                a = v.tensor_tensor(AFT[:, nr_o + 32:nr_o + 48], AFT[:, nr_o + 16:nr_o + 32],
                                    eps, ALU.add)
                sV.inc(a)
            C.v(fne)
            tkn2 = sV.n
            def fnl(s, tkn2=tkn2):
                sV.wait(s, tkn2)
                a = s.activation(AFT[:, nr_o + 48:nr_o + 64], AFT[:, nr_o + 32:nr_o + 48], AF.Ln)
                tk = sS.inc(a)
                sS.wait(s, tk)
                a = s.activation(AFT[:, nr_o + 64:nr_o + 80], AFT[:, nr_o + 48:nr_o + 64], AF.Exp, scale=-1.0)
                sS.inc(a)
            C.s(fnl)
            tkn3 = sS.n
            def fni(v, tkn3=tkn3, c=c):
                sS.wait(v, tkn3)
                a = v.tensor_mul(AFT[:, nr_o + 80:nr_o + 96], AFT[:, nr_o + 64:nr_o + 80],
                                 AFT[:, ap_o + 16:ap_o + 32])
                tk = sV.inc(a)
                sV.wait(v, tk)
                lo = bass.AP(AFT.ap().tensor, AFT.ap().offset + nr_o + 80,
                             [AFT.ap().ap[0], [1, 8], [0, 64]])
                a = v.tensor_tensor(kns[:, 0:512], C.pL[:, 0:512], lo, ALU.mult)
                tk = sV.inc(a)
                sV.wait(v, tk)
                hi = bass.AP(AFT.ap().tensor, AFT.ap().offset + nr_o + 88,
                             [AFT.ap().ap[0], [1, 8], [0, 64]])
                a = v.tensor_tensor(kns[:, 512:1024], C.pL[:, 512:1024], hi, ALU.mult)
                sV.inc(a)
            C.v(fni)
            st["knmul"] = sV.n
            tkkm = sV.n
            def ftr(t, tkkm=tkkm, pkv=st["knev"]):
                sV.wait(t, tkkm)
                sS.wait(t, pkv)
                m = None
                for dc in range(8):
                    m = t.transpose(C.pA[:, 128 * dc:128 * dc + 128], kns[:, 128 * dc:128 * dc + 128],
                                    C.cstf[:, CF_ID:CF_ID + 128])
                sT.inc(m)
            C.t(ftr)
            tktr = sT.n
            def fke(s, c=c, tktr=tktr):
                sT.wait(s, tktr)
                a = None
                for dc in range(8):
                    a = s.activation(kn[:, dc * TP1 + 1 + 128 * c: dc * TP1 + 1 + 128 * c + 128],
                                     C.pA[:, 128 * dc:128 * dc + 128], AF.Copy)
                    sS.inc(a)
            C.s(fke)
            st["knev"] = sS.n
    C.fence()
    def f(g):
        C.ld(g, C.knhb[:, :], kn[:, :])
        sDW.wait(g)
        sG.inc(g.nop("rl", nofuse=True))
    C.g(f)
    C.fence()
    AFA.pop()
    AB.pop()
    vn_o = AB.alloc(NC * sv)
    vn = C.AB_t[:, vn_o:vn_o + NC * sv]
    AB.push()
    AFA.push()
    vfa_o = AB.alloc(8192)
    wsv_o = AB.alloc(8192)
    wsv = C.AB_t[:, wsv_o:wsv_o + 8192]
    xb_o = AB.alloc(4096)
    xb = C.AB_t[:, xb_o:xb_o + 4096]
    vma_o = AB.alloc(4096)
    i2b_o = AB.alloc(512)
    xt = AFT[:, AFA.alloc(4096):AFA.stack[-1]]
    vst_o = AFA.alloc(512)
    vstage = AFT[:, vst_o:vst_o + 512]
    vm_o = AFA.alloc(512)
    vsh_o = AFA.alloc(512)
    vsq_o = AFA.alloc(512)
    n16_o = AFA.alloc(1024)

    def f(g):
        a = g.memset(vn[:, :].rearrange("p (k h c) -> p k h c", k=NC, h=16)[:, :, :, 64:65], 1.0)
        sG.inc(a)
    C.g(f)
    wtokV = load_wsec(C, wsv_o, C.wvT, 1024, 1024)
    C.w1_bias(wsv, 1024, 1024, None)
    C.fence()

    st = {"evac": (0, 0), "vno": (0, 0), "chain": (0, 0), "vnev": [(0, 0), (0, 0)],
          "trv": (0, 0)}
    for ti in range(NT):
        tt = NT - 1 - ti
        cur, nxt = tt % 2, (tt + 1) % 2
        def pre_t(t, pc=st["chain"]):
            sS.wait(t, pc)
        lt = C.ln_lite(tt, xt, xb, pre_t=pre_t)
        for dc in range(8):
            def pre(t, pc=st["evac"], pv=st["vno"], lt=lt, wtokV=wtokV):
                sG.wait(t, wtokV)
                sS.wait(t, pc)
                sV.wait(t, tuple(map(max, pv, lt)))
            tok = C.proj_feat_oc(xb, lambda ic, dc=dc: wsv[:, ic * 1024 + dc * 128: ic * 1024 + dc * 128 + 128],
                                 dc, 1024, pre=pre)
            def fse(s, dc=dc, cur=cur, tok=tok):
                sT.wait(s, tok)
                sS.wait(s)
                sV.wait(s)
                a = s.activation(C.AB_t[:, vfa_o + cur * 4096 + 512 * dc: vfa_o + cur * 4096 + 512 * dc + 512],
                                 C.pC[0:128, 0:512], AF.Copy)
                sS.inc(a)
            C.s(fse)
            st["evac"] = sS.n
        vfaTok = sS.n
        for dc in range(8):
            def fvm(v, dc=dc, tt=tt, cur=cur, nxt=nxt, vfaTok=vfaTok):
                sS.wait(v)
                sV.wait(v)
                def bc(colo, base, n):
                    sl = AFT[base:base + 64, colo:colo + 1]
                    return bass.AP(sl.tensor, sl.offset, [[sl.ap[0][0], 64], [0, n]])
                for ph in range(2):
                    b0 = 64 * ph
                    hh = 2 * dc + ph
                    vcur = C.AB_t[b0:b0 + 64, vfa_o + cur * 4096 + 512 * dc: vfa_o + cur * 4096 + 512 * dc + 512]
                    a = v.tensor_tensor(AFT[b0:b0 + 64, vm_o:vm_o + 512], vcur,
                                        bc(ap_o + 32 + hh, b0, 512), ALU.mult)
                    tk = sV.inc(a)
                    sV.wait(v, tk)
                    a = v.tensor_tensor(AFT[b0:b0 + 64, vsh_o:vsh_o + 511],
                                        C.AB_t[b0:b0 + 64, vfa_o + cur * 4096 + 512 * dc + 1: vfa_o + cur * 4096 + 512 * dc + 512],
                                        bc(ap_o + 48 + hh, b0, 511), ALU.mult)
                    tk = sV.inc(a)
                    sV.wait(v, tk)
                    if tt < NT - 1:
                        tail = C.AB_t[b0:b0 + 64, vfa_o + nxt * 4096 + 512 * dc: vfa_o + nxt * 4096 + 512 * dc + 1]
                    else:
                        tail = C.cstf[b0:b0 + 64, CF_CONST + 5:CF_CONST + 6]
                    a = v.tensor_tensor(AFT[b0:b0 + 64, vsh_o + 511:vsh_o + 512], tail,
                                        bc(ap_o + 48 + hh, b0, 1), ALU.mult)
                    tk = sV.inc(a)
                    sV.wait(v, tk)
                    a = v.tensor_add(AFT[b0:b0 + 64, vm_o:vm_o + 512], AFT[b0:b0 + 64, vm_o:vm_o + 512],
                                     AFT[b0:b0 + 64, vsh_o:vsh_o + 512])
                    sV.inc(a)
            C.v(fvm)
            tkvm = sV.n
            def fsqv(s, dc=dc, cur=cur, tkvm=tkvm):
                sV.wait(s, tkvm)
                sT.wait(s)
                a = s.activation(AFT[:, vsq_o:vsq_o + 512], AFT[:, vm_o:vm_o + 512], AF.Square)
                sS.inc(a)
                a = s.activation(C.AB_t[:, vma_o + 512 * dc: vma_o + 512 * dc + 512],
                                 AFT[:, vm_o:vm_o + 512], AF.Copy)
                sS.inc(a)
            C.s(fsqv)
            tksqv = sS.n
            def fssq(t, dc=dc, tksqv=tksqv, lt=lt):
                sS.wait(t, tksqv)
                sV.wait(t, lt)
                m = t.matmul(C.pB[0:16, 0:512], C.cstf[:, CF_BD + 16 * dc: CF_BD + 16 * dc + 16],
                             AFT[:, vsq_o:vsq_o + 512], start=(dc == 0), stop=(dc == 7))
                sT.inc(m)
            C.t(fssq)
        ssqTok = sT.n
        def fn1(s, ssqTok=ssqTok):
            sT.wait(s, ssqTok)
            a = s.activation(AFT[0:16, n16_o:n16_o + 512], C.pB[0:16, 0:512], AF.Sqrt)
            sS.inc(a)
        C.s(fn1)
        chain0 = sS.n
        st["chain"] = chain0
        def fn2(v, chain0=chain0):
            sS.wait(v, chain0)
            eps = bass.AP(C.cstf.ap().tensor, C.cstf.ap().offset + CF_CONST + 1,
                          [[C.cstf.ap().ap[0][0], 16], [0, 512]])
            a = v.tensor_tensor(AFT[0:16, n16_o + 512:n16_o + 1024], AFT[0:16, n16_o:n16_o + 512],
                                eps, ALU.add)
            sV.inc(a)
        C.v(fn2)
        tkn2 = sV.n
        def fn3(s, tkn2=tkn2):
            sV.wait(s, tkn2)
            a = s.activation(AFT[0:16, n16_o:n16_o + 512], AFT[0:16, n16_o + 512:n16_o + 1024], AF.Ln)
            tk = sS.inc(a)
            sS.wait(s, tk)
            a = s.activation(AFT[0:16, n16_o + 512:n16_o + 1024], AFT[0:16, n16_o:n16_o + 512],
                             AF.Exp, scale=-1.0)
            sS.inc(a)
        C.s(fn3)
        tkn3 = sS.n
        def fn4(v, tkn3=tkn3):
            sS.wait(v, tkn3)
            vs = bass.AP(AFT.ap().tensor, AFT.ap().offset + v16_o,
                         [[AFT.ap().ap[0][0], 16], [0, 512]])
            a = v.tensor_tensor(AFT[0:16, n16_o:n16_o + 512], AFT[0:16, n16_o + 512:n16_o + 1024],
                                vs, ALU.mult)
            sV.inc(a)
        C.v(fn4)
        tkn4 = sV.n
        def fn5(s, tkn4=tkn4):
            sV.wait(s, tkn4)
            a = s.activation(C.AB_t[0:16, i2b_o:i2b_o + 512], AFT[0:16, n16_o:n16_o + 512], AF.Copy)
            sS.inc(a)
        C.s(fn5)
        i2bTok = sS.n
        for dc in range(8):
            def fbc(t, dc=dc, i2bTok=i2bTok, pv=st["vno"]):
                sS.wait(t, i2bTok)
                sV.wait(t, pv)
                m = t.matmul(C.pC[0:128, 0:512], C.cstb[0:16, CB_BDT + 128 * dc: CB_BDT + 128 * dc + 128],
                             C.AB_t[0:16, i2b_o:i2b_o + 512], start=True, stop=True)
                sT.inc(m)
            C.t(fbc)
            tkbc = sT.n
            def fvn(v, dc=dc, tkbc=tkbc, ptr=st["trv"]):
                sT.wait(v, tkbc)
                sT.wait(v, ptr)
                a = v.tensor_mul(vstage[:, 0:512], C.AB_t[:, vma_o + 512 * dc: vma_o + 512 * dc + 512],
                                 C.pC[0:128, 0:512])
                sV.inc(a)
            C.v(fvn)
            st["vno"] = sV.n
            tkvn = sV.n
            def ftrv(t, dc=dc, tkvn=tkvn, pev=st["vnev"][dc % 2]):
                sV.wait(t, tkvn)
                sV.wait(t, pev)
                m = None
                for blk in range(4):
                    m = t.transpose(C.pA[:, 512 * (dc % 2) + 128 * blk: 512 * (dc % 2) + 128 * blk + 128],
                                    vstage[:, 128 * blk:128 * blk + 128],
                                    C.cstf[:, CF_ID:CF_ID + 128])
                sT.inc(m)
            C.t(ftrv)
            st["trv"] = sT.n
            tktrv = sT.n
            def fvev(v, dc=dc, tt=tt, tktrv=tktrv):
                sT.wait(v, tktrv)
                a = None
                for blk in range(4):
                    kt = 4 * tt + blk
                    dst = vn[:, kt * sv + 65 * 2 * dc: kt * sv + 65 * 2 * dc + 130]
                    dst = dst.rearrange("p (h c) -> p h c", c=65)[:, :, 0:64]
                    src = C.pA[:, 512 * (dc % 2) + 128 * blk: 512 * (dc % 2) + 128 * blk + 128]
                    a = v.tensor_copy(dst, src.rearrange("p (h c) -> p h c", c=64))
                    sV.inc(a)
            C.v(fvev)
            st["vnev"][dc % 2] = sV.n
    AFA.pop()
    AB.pop()
    C.fence()
    AB.push()
    AFA.push()
    knr_o = AB.alloc(8 * TP1)
    knr = C.AB_t[:, knr_o:knr_o + 8 * TP1]
    ob_o = AB.alloc(4096)
    obuf = C.AB_t[:, ob_o:ob_o + 4096]
    odd_o = AB.alloc(512)
    odds = C.AB_t[:, odd_o:odd_o + 512]
    wp_o = AB.alloc(2048)
    wp = C.AB_t[:, wp_o:wp_o + 2048]
    meg_o = AFA.alloc(C.MEGW)
    mega = AFT[:, meg_o:meg_o + C.MEGW]
    inv_o = AFA.alloc(512)
    invs = AFT[:, inv_o:inv_o + 512]
    xo_o = AFA.alloc(1024)
    stf_o = AFA.alloc(1024)

    def f(g):
        C.ld(g, knr[:, :], C.knhb[:, :])
        C.ld(g, C.w2[0:1, 3072:3072 + 1024], C.mrows[1:2, 0:1024])
        C.ld(g, C.w2[1:2, 3072:3072 + 1024], C.mrows[1:2, 0:1024])
    C.g(f)
    C.emit_mega_mask(mega, C.tbl_ma, qshift=0)
    C.fence()

    epst = {"tk2": (0, 0), "tk3": (0, 0), "tk4": (0, 0), "odd": (0, 0)}

    def epilogue(slot, h, av, tok_av):
        blk = h // 2
        if slot == 0:
            def ffix(v, av=av, tok_av=tok_av):
                C.sT2.wait(v, tok_av)
                a = v.tensor_add(av[64:65, 0:1], av[64:65, 0:1], C.cstf[64:65, CF_ONES:CF_ONES + 1])
                sV.inc(a)
            C.v(ffix)
            fixTok = sV.n
        else:
            fixTok = None
        def fs(s, tok_av=tok_av, p2=epst["tk2"], fixTok=fixTok):
            C.sT2.wait(s, tok_av)
            if fixTok is not None:
                sV.wait(s, fixTok)
            sT.wait(s, p2)
            a = s.activation(C.rows[64:65, 0:512], av[64:65, :], AF.Ln)
            sS.inc(a)
        C.s(fs)
        tk1 = sS.n
        def ft(t, tk1=tk1, p3=epst["tk3"]):
            sS.wait(t, tk1)
            sS.wait(t, p3)
            m = t.matmul(C.pB[0:64, 0:512], C.cstf[64:65, CF_ONESB:CF_ONESB + 64],
                         C.rows[64:65, 0:512], start=True, stop=True)
            sT.inc(m)
        C.t(ft)
        tk2 = sT.n
        epst["tk2"] = tk2
        def fs2(s, tk2=tk2, p4=epst["tk4"]):
            sT.wait(s, tk2)
            sV.wait(s, p4)
            a = s.activation(invs[0:64, :], C.pB[0:64, 0:512], AF.Exp, scale=-1.0)
            sS.inc(a)
        C.s(fs2)
        tk3 = sS.n
        epst["tk3"] = tk3
        def fv(v, h=h, blk=blk, tk3=tk3, po=epst["odd"]):
            sS.wait(v, tk3)
            if h % 2 == 0:
                dst = obuf[0:64, blk * 512:(blk + 1) * 512]
            else:
                C.sY.wait(v, po)
                dst = odds[0:64, 0:512]
            c2 = v.tensor_mul(dst, av[0:64, :], invs[0:64, :])
            sV.inc(c2)
        C.v(fv)
        tk4 = sV.n
        epst["tk4"] = tk4
        if h % 2 == 1:
            def fy(y, blk=blk, tk4=tk4):
                sV.wait(y, tk4)
                C.ld(y, obuf[64:128, blk * 512:(blk + 1) * 512], odds[0:64, 0:512], sDS)
                sDS.wait(y)
                C.sY.inc(y.nop("rl", nofuse=True))
            C.y(fy)
            epst["odd"] = C.sY.n
        return (sV, sV.n)

    def k_ap(slot, kt, h):
        dc = h // 2
        return knr[64 * (h % 2):64 * (h % 2) + 64, dc * TP1 + 1 + kt * 128: dc * TP1 + 1 + kt * 128 + 128]

    def q_ap(slot, h):
        dc = h // 2
        return knr[64 * (h % 2):64 * (h % 2) + 64, dc * TP1 + 1 + slot * 512: dc * TP1 + 1 + slot * 512 + 512]

    def vn_ap(kt, h):
        base = kt * sv + 65 * h
        return vn[:, base:base + 65]

    def bias_ap(slot, kt, h):
        cnt = 4 * (slot + 1)
        if kt >= cnt - 4:
            c0 = C.mega_c0(slot, kt)
            return mega[:, c0:c0 + 512]
        return None

    cfg = dict(H=16, counts=lambda s: 4 * (s + 1), k_ap=k_ap, q_ap=q_ap,
               vn_ap=vn_ap, bias_ap=bias_ap, phantom=None, eshift=-20.0,
               epilogue=epilogue)

    for slot in range(NT):
        C.emit_attention_slot(cfg, slot)
        C.out_proj_residual(slot, obuf, wp, wp_o, C.wcT, 3072,
                            AFT[:, xo_o:xo_o + 1024], AFT[:, stf_o:stf_o + 1024], last, nwp=2)
        C.fence()

    AFA.pop()
    AB.pop()
    AFA.pop()
    AB.pop()


KB.sub_attn_a = sub_attn_a


# version 11 (final)
# speedup vs baseline: 1.1497x; 1.0118x over previous
import math
import numpy as np
import ml_dtypes
import concourse.bass as bass
import concourse.mybir as mybir
from concourse.bass_utils import run_bass_kernel_spmd
from contextlib import ExitStack

F32 = mybir.dt.float32
BF16 = mybir.dt.bfloat16
AF = mybir.ActivationFunctionType
ALU = mybir.AluOpType

NEG = -1.0e30
KEY_SCALE_MAX = math.log(2 ** 16 - 1)
CONSTS = (1e-5, 1e-10, 10.0, KEY_SCALE_MAX, 1.0, 0.0, -1.0, 0.125, -40.0, -20.0)


class Sem:

    def __init__(self, nc, ctx, name, dma=False):
        self.hA = ctx.enter_context(nc.semaphore(name + "a"))
        self.hB = ctx.enter_context(nc.semaphore(name + "b"))
        self.va = 0
        self.vb = 0
        self.ic = 0
        self.dma = dma

    @property
    def n(self):
        return (self.va, self.vb)

    def inc(self, instr, k=1):
        if self.ic % 2 == 0:
            instr.then_inc(self.hA, k)
            self.va += k
        else:
            instr.then_inc(self.hB, k)
            self.vb += k
        self.ic += 1
        return (self.va, self.vb)

    def tok_after(self, snap, j, k=1):
        va, vb, ic = snap
        for i in range(j):
            if (ic + i) % 2 == 0:
                va += k
            else:
                vb += k
        return (va, vb)

    @property
    def snap(self):
        return (self.va, self.vb, self.ic)

    def wait(self, eng, tok=None):
        va, vb = (self.va, self.vb) if (tok is None or self.dma) else tok
        if va:
            eng.wait_ge(self.hA, va)
        if vb:
            eng.wait_ge(self.hB, vb)


class Arena:
    def __init__(self, tensor, width):
        self.t = tensor
        self.w = width
        self.stack = [0]

    def alloc(self, cols):
        o = self.stack[-1]
        assert o + cols <= self.w, f"arena overflow {o}+{cols}>{self.w}"
        self.stack[-1] = o + cols
        return o

    def push(self):
        self.stack.append(self.stack[-1])

    def pop(self):
        self.stack.pop()


def hope_tables(T, base=10000, d_model=1024):
    dim = d_model // 2
    pos = np.arange(dim, dtype=np.float64)
    freqs_np = 1.0 / base ** (pos / dim)
    active = int(np.sum(freqs_np * 2 * math.pi * T >= 1.0))
    active = min(active, dim - d_model // 4)
    freqs = freqs_np[:active].astype(np.float32)
    d = np.arange(0, T, dtype=np.float32)
    theta = (d[:, None] * freqs[None, :]).astype(np.float32)
    pv = (np.cos(theta, dtype=np.float32) + np.sin(theta, dtype=np.float32)).astype(np.float32)
    nh = active // 32
    per_head = pv.astype(np.float64).reshape(T, nh, 32).sum(axis=2)
    hi = per_head.astype(np.float32)
    lo = (per_head - hi.astype(np.float64)).astype(np.float32)
    return hi.T.copy(), lo.T.copy()


def shift2d(row):
    L = row.shape[0]
    padded = np.concatenate([np.zeros(127, row.dtype), row])
    s = np.lib.stride_tricks.sliding_window_view(padded, L)
    return np.ascontiguousarray(s[::-1])


def build_tables(T):
    LPAD = T + 128
    L = LPAD + T
    d = np.arange(-LPAD, T)
    mask_a = np.where(d >= 1, 0.0, NEG).astype(np.float32)
    mask_f = np.where(d >= 0, 0.0, NEG).astype(np.float32)
    hi8, lo8 = hope_tables(T)
    hope_hi = np.zeros((8, L), dtype=np.float32)
    hope_lo = np.zeros((8, L), dtype=np.float32)
    hope_hi[:, :LPAD] = NEG
    hope_hi[:, LPAD:] = hi8
    hope_lo[:, LPAD:] = lo8
    ma2 = shift2d(mask_a)
    mf2 = shift2d(mask_f)
    hh2 = np.concatenate([shift2d(hope_hi[h]) for h in range(8)], axis=0)
    hl2 = np.concatenate([shift2d(hope_lo[h]) for h in range(8)], axis=0)
    return ma2, mf2, hh2, hl2


def mk_constants():
    c = np.zeros((128, 192), dtype=np.float32)
    c[:, 0] = 1.0
    c[:, 1] = np.arange(1, 129, dtype=np.float32)
    for h in range(8):
        c[h, 16 + 8 * h:16 + 8 * h + 8] = 1.0
    for h in range(8):
        c[64, 80 + 8 * h + h] = 1.0
    return c


def mk_selectors():
    s = np.zeros((64, 32 * 128), dtype=np.float32)
    for e in range(8):
        for j in range(4):
            m = e * 4 + j
            s[(2 * j) * 8 + e, 128 * m:128 * m + 64] = 1.0
            s[(2 * j + 1) * 8 + e, 128 * m + 64:128 * m + 128] = 1.0
    return s


class KB:
    def __init__(self, T, subs):
        self.T = T
        self.subs = subs
        self.NT = T // 512
        self.NC = T // 128
        self.TP1 = T + 1
        self.LPAD = T + 128
        self.TBL = self.LPAD + T
        self.MEGW = T + 384
        self.nc = bass.Bass("TRN2", target_bir_lowering=False)

    def build(self):
        nc = self.nc
        T = self.T

        def din(name, shape, dtype=F32):
            return nc.dram_tensor(name, shape, dtype, kind="ExternalInput").ap()

        self.x_in = din("x_feat", [1024, T])
        self.wkT = din("wkT", [1024, 1024])
        self.wvT = din("wvT", [1024, 1024])
        self.wcT = din("wcT", [128, 8 * 1024])
        self.aprm = din("aprm", [1, 64])
        self.wqkvT = din("wqkvT", [1024, 3072])
        self.bqkv = din("bqkv", [1, 3072])
        self.wpT = din("wpT", [128, 8 * 1024])
        self.bproj = din("bproj", [1, 1024])
        self.wqf = din("wqf", [1024, 512])
        self.wkf = din("wkf", [1024, 512])
        self.wgf = din("wgf", [1024, 64])
        self.wve = din("wve", [8, 1024, 512])
        self.woe = din("woe", [32, 128, 1024])
        self.tbl_ma = din("tbl_ma", [128, self.TBL])
        self.tbl_mf = din("tbl_mf", [128, self.TBL])
        self.tbl_hh = din("tbl_hh", [8 * 128, self.TBL])
        self.tbl_hl = din("tbl_hl", [8 * 128, self.TBL])
        self.cstb_in = din("cstb_in", [128, 2048])
        self.tblu = din("tblu", [128, 256])
        self.mrows = din("mrows", [2, 4096])
        self.cstf_in = din("cstf_in", [128, 512])
        self.selc_in = din("selc_in", [64, 32 * 128])

        self.y_out = nc.dram_tensor("y", [1024, T], F32, kind="ExternalOutput").ap()
        self.xres = nc.dram_tensor("xres", [1024, T], F32, kind="Internal").ap()
        self.knhb = nc.dram_tensor("knhb", [128, 8 * self.TP1], BF16, kind="Internal").ap()
        self.qhb = nc.dram_tensor("qhb", [1024, T], BF16, kind="Internal").ap()
        self.aghb = nc.dram_tensor("aghb", [2, T], F32, kind="Internal").ap()

        ctx = ExitStack()
        with ctx:
            self._alloc(ctx)
            self._emit()
        return nc

    def _alloc(self, ctx):
        nc = self.nc
        self.pt = ctx.enter_context(nc.sbuf_tensor("pt", [128, 4 * 512], BF16))
        self.rows = ctx.enter_context(nc.sbuf_tensor("rows", [128, 2048], F32))
        self.cstf = ctx.enter_context(nc.sbuf_tensor("cstf", [128, 512], F32))
        self.cstb = ctx.enter_context(nc.sbuf_tensor("cstb", [128, 2048], BF16))
        self.w2 = ctx.enter_context(nc.sbuf_tensor("w2", [2, 4096], BF16))
        self.g2 = ctx.enter_context(nc.sbuf_tensor("g2", [2, 512], BF16))
        self.AB_t = ctx.enter_context(nc.sbuf_tensor("AB", [128, 76000], BF16))
        self.AF_t = ctx.enter_context(nc.sbuf_tensor("AFa", [128, 8200], F32))
        self.AB = Arena(self.AB_t, 76000)
        self.AFA = Arena(self.AF_t, 8200)

        self.pL = ctx.enter_context(nc.psum_tensor("pL", [128, 2048], F32))
        self.pA = ctx.enter_context(nc.psum_tensor("pA", [128, 1024], F32))
        self.pB = ctx.enter_context(nc.psum_tensor("pB", [128, 512], F32))
        self.pC = ctx.enter_context(nc.psum_tensor("pC", [128, 512], F32))

        self.sDW = Sem(nc, ctx, "sDW", dma=True)
        self.sDA = Sem(nc, ctx, "sDA", dma=True)
        self.sDX = Sem(nc, ctx, "sDX", dma=True)
        self.sDS = Sem(nc, ctx, "sDS", dma=True)
        self.sT = Sem(nc, ctx, "sT")
        self.sS = Sem(nc, ctx, "sS")
        self.sV = Sem(nc, ctx, "sV")
        self.sG = Sem(nc, ctx, "sG")
        self.sT2 = Sem(nc, ctx, "sT2")
        self.sS2 = Sem(nc, ctx, "sS2")
        self.sY = Sem(nc, ctx, "sY")

        self.block = ctx.enter_context(nc.Block())
        for i, val in enumerate(CONSTS):
            nc.const_aps.register(F32, val, self.cstf[:, 220 + i:221 + i])

    def t(self, f): self.block.tensor(f)
    def s(self, f): self.block.scalar(f)
    def v(self, f): self.block.vector(f)
    def g(self, f): self.block.gpsimd(f)
    def y(self, f): self.block.sync(f)

    def ld(self, eng, dst, src, sem=None):
        sem = sem or self.sDW
        sem.inc(eng.dma_start(out=dst, in_=src), 16)

    def fence(self):
        sT, sS, sV, sG, sY = self.sT, self.sS, self.sV, self.sG, self.sY
        sDS, sDX, sDW = self.sDS, self.sDX, self.sDW
        def ft(t):
            sT.wait(t)
            sT.inc(t.nop("fence", nofuse=True))
        def fs(s):
            sS.wait(s)
            sS.inc(s.nop("fence", nofuse=True))
        def fv(v):
            sV.wait(v)
            sV.inc(v.nop("fence", nofuse=True))
        def fg(g):
            sDW.wait(g)
            sG.wait(g)
            sG.inc(g.nop("fence", nofuse=True))
        def fy(y):
            sDS.wait(y)
            sDX.wait(y)
            sY.wait(y)
            sY.inc(y.nop("fence", nofuse=True))
        self.t(ft); self.s(fs); self.v(fv); self.g(fg); self.y(fy)
        def wt(t): sS.wait(t); sV.wait(t); sG.wait(t); sY.wait(t)
        def ws(s): sT.wait(s); sV.wait(s); sG.wait(s); sY.wait(s)
        def wv(v): sT.wait(v); sS.wait(v); sG.wait(v); sY.wait(v)
        def wg(g): sT.wait(g); sS.wait(g); sV.wait(g); sY.wait(g)
        def wy(y): sT.wait(y); sS.wait(y); sV.wait(y); sG.wait(y)
        self.t(wt); self.s(ws); self.v(wv); self.g(wg); self.y(wy)

    def _emit(self):
        C = self
        sDW, sDX, sDS = C.sDW, C.sDX, C.sDS
        sV = C.sV

        def f(g):
            C.ld(g, C.cstf[:, :], C.cstf_in[:, :])
            C.ld(g, C.cstb[:, :], C.cstb_in[:, :])
            C.ld(g, C.xres[:, :], C.x_in[:, :])
        C.g(f)
        def f(g):
            C.ld(g, C.g2[1:2, 0:512], C.mrows[0:1, 0:512])
        C.g(f)
        C.fence()

        nsub = len(C.subs)
        for i, sub in enumerate(C.subs):
            last = (i == nsub - 1)
            if sub == "attn_a":
                C.sub_attn_a(last)
            elif sub in ("ffn1", "ffn2"):
                C.sub_ffn(last)
            elif sub == "attn_b":
                C.sub_attn_b(last)
            C.fence()

        def f(y_):
            sDS.wait(y_)
        C.y(f)
        def f(g):
            sDS.wait(g)
        C.g(f)


def emit_attention_slot(C, cfg, slot):
    sT, sS, sV, sT2, sS2 = C.sT, C.sS, C.sV, C.sT2, C.sS2
    st = cfg.setdefault("_state", {"epi": {0: None, 1: None}, "hctr": 0})
    cnt = cfg["counts"](slot)
    order = cfg.get("head_order") or list(range(cfg["H"]))
    for h in order:
        if cfg.get("pre_head"):
            cfg["pre_head"](slot, h)
        hp = st["hctr"] % 2
        st["hctr"] += 1
        av = C.pA[:, hp * 512:(hp + 1) * 512]
        qap = cfg["q_ap"](slot, h)
        snT2 = sT2.snap
        snS2 = sS2.snap
        snT = sT.snap
        prev_exp_all = sS2.n
        prev_av_all = sT2.n
        bprev = st["epi"][hp]
        ph = cfg["phantom"](slot) if cfg.get("phantom") else None
        biases = [cfg["bias_ap"](slot, kt, h) for kt in range(cnt)]

        def ft(t, slot=slot, h=h, cnt=cnt, av=av, qap=qap,
               snS2=snS2, prev_exp_all=prev_exp_all, bprev=bprev, ph=ph):
            sS2.wait(t, prev_exp_all)
            if bprev is not None:
                bprev[0].wait(t, bprev[1])
            for kt in range(cnt):
                r = kt % 4
                if kt >= 1:
                    sS2.wait(t, sS2.tok_after(snS2, kt))
                    m = t.matmul(av[0:65, :], cfg["vn_ap"](kt - 1, h),
                                 C.pt[:, ((kt - 1) % 4) * 512:((kt - 1) % 4) * 512 + 512],
                                 start=(kt - 1 == 0), stop=False)
                    sT2.inc(m)
                m = t.matmul(C.pL[:, r * 512:(r + 1) * 512],
                             cfg["k_ap"](slot, kt, h), qap,
                             start=True, stop=True)
                sT.inc(m)
            sS2.wait(t, sS2.tok_after(snS2, cnt))
            m = t.matmul(av[0:65, :], cfg["vn_ap"](cnt - 1, h),
                         C.pt[:, ((cnt - 1) % 4) * 512:((cnt - 1) % 4) * 512 + 512],
                         start=(cnt - 1 == 0), stop=(ph is None))
            if ph is not None:
                m = t.matmul(av[0:65, :], ph[0], ph[1], start=False, stop=True)
            sT2.inc(m)
        C.t(ft)
        final_av_tok = sT2.n

        vtoks = [None] * cnt
        for kt in range(cnt):
            if biases[kt] is None:
                continue
            def fb(v, kt=kt, r=kt % 4, bias=biases[kt], snT=snT):
                sT.wait(v, sT.tok_after(snT, kt + 1))
                c = v.tensor_add(C.pL[:, r * 512:(r + 1) * 512],
                                 C.pL[:, r * 512:(r + 1) * 512], bias)
                sV.inc(c)
            C.v(fb)
            vtoks[kt] = sV.n

        def fs(s, cnt=cnt, snT=snT, snT2=snT2, prev_av_all=prev_av_all, vtoks=vtoks):
            for kt in range(cnt):
                r = kt % 4
                if kt < 4:
                    sT2.wait(s, prev_av_all)
                else:
                    sT2.wait(s, sT2.tok_after(snT2, kt - 3))
                sT.wait(s, sT.tok_after(snT, kt + 1))
                if vtoks[kt] is not None:
                    sV.wait(s, vtoks[kt])
                a = s.activation(C.pt[:, r * 512:(r + 1) * 512],
                                 C.pL[:, r * 512:(r + 1) * 512], AF.Exp,
                                 bias=cfg.get("eshift", 0.0))
                sS2.inc(a)
        C.s(fs)

        st["epi"][hp] = cfg["epilogue"](slot, h, av, final_av_tok)


def emit_mega_mask(C, mega_ap, tbl, qshift, W=None):
    W = W or C.MEGW
    off = C.LPAD + qshift - 384
    def f(g):
        C.sT.wait(g)
        C.sV.wait(g)
        C.sDW.inc(g.dma_start(out=mega_ap[:, 0:W], in_=tbl[0:128, off:off + W]), 16)
        C.sDW.wait(g)
        C.sG.inc(g.nop("rl", nofuse=True))
    C.g(f)
    return C.sG.n


def mega_c0(C, slot, kt):
    return slot * 512 - 128 * kt + 384


KB.emit_attention_slot = emit_attention_slot
KB.emit_mega_mask = emit_mega_mask
KB.mega_c0 = mega_c0


CB_ONES = 0
CB_P1 = 1
CB_E8 = 16
CB_OH = 80
CB_ONESB = 144
CB_P1ROW = 208
CB_ID = 336
CB_DEC = 464
CB_TRI = 592
CB_BDT = 720
CB_BD = 1744
CB_ONESR = 1880
CF_ONES = 0
CF_E8 = 16
CF_ONESB = 80
CF_CONST = 220
CF_ID = 256
CF_BD = 384


def mk_cstb():
    c = np.zeros((128, 2048), dtype=np.float32)
    c[:, CB_ONES] = 1.0
    c[:, CB_P1] = np.arange(1, 129, dtype=np.float32)
    for h in range(8):
        c[h, CB_E8 + 8 * h: CB_E8 + 8 * h + 8] = 1.0
    for h in range(8):
        c[64, CB_OH + 8 * h + h] = 1.0
    c[:, CB_ONESB:CB_ONESB + 64] = 1.0
    c[0, CB_P1ROW:CB_P1ROW + 128] = np.arange(1, 129, dtype=np.float32)
    c[:, CB_ID:CB_ID + 128] = np.eye(128, dtype=np.float32)
    p = np.arange(128)[:, None]
    j = np.arange(128)[None, :]
    c[:, CB_DEC:CB_DEC + 128] = np.maximum(p - j, 0).astype(np.float32)
    c[:, CB_TRI:CB_TRI + 128] = (p >= j).astype(np.float32)
    for dc in range(8):
        blk = np.zeros((128, 16), dtype=np.float32)
        for pp in range(128):
            blk[pp, 2 * dc + pp // 64] = 1.0
        c[:, CB_BD + 16 * dc: CB_BD + 16 * dc + 16] = blk
        bt = np.zeros((128, 128), dtype=np.float32)
        for pp in range(128):
            bt[2 * dc + pp // 64, pp] = 1.0
        c[:, CB_BDT + 128 * dc: CB_BDT + 128 * dc + 128] = bt
    c[:, CB_ONESR:CB_ONESR + 128] = 1.0
    return c


def mk_cstf():
    c = np.zeros((128, 512), dtype=np.float32)
    c[:, CF_ONES] = 1.0
    for h in range(8):
        c[h, CF_E8 + 8 * h: CF_E8 + 8 * h + 8] = 1.0
    c[:, CF_ONESB:CF_ONESB + 128] = 1.0
    c[:, CF_ID:CF_ID + 128] = np.eye(128, dtype=np.float32)
    for i, val in enumerate(CONSTS):
        c[:, CF_CONST + i] = val
    for dc in range(8):
        for pp in range(128):
            c[pp, CF_BD + 16 * dc + 2 * dc + pp // 64] = 1.0
    return c


def ln_full(C, tt, xt, xb, store=True):
    sDX, sDS, sT, sS, sV = C.sDX, C.sDS, C.sT, C.sS, C.sV
    R = C.rows
    t0 = tt * 512
    def f(y):
        sV.wait(y)
        sS.wait(y)
        for dc in range(8):
            C.ld(y, xt[:, dc * 512:(dc + 1) * 512],
                 C.xres[dc * 128:(dc + 1) * 128, t0:t0 + 512], sDX)
        sDX.wait(y)
        C.sY.inc(y.nop("rl", nofuse=True))
    C.y(f)
    tkx = C.sY.n
    def f(v):
        C.sY.wait(v, tkx)
        sT.wait(v)
        c = None
        for dc in range(8):
            c = v.tensor_copy(xb[:, dc * 512:(dc + 1) * 512], xt[:, dc * 512:(dc + 1) * 512])
        sV.inc(c)
    C.v(f)
    tkb = sV.n
    C.AFA.push()
    o = C.AFA.alloc(1024)
    sq_sl = [C.AF_t[:, o:o + 512], C.AF_t[:, o + 512:o + 1024]]
    snT = sT.snap
    def f(s):
        sV.wait(s, tkb)
        sT.wait(s, (snT[0], snT[1]))
        for dc in range(8):
            if dc >= 2:
                sT.wait(s, sT.tok_after(snT, dc - 1))
            a = s.activation(sq_sl[dc % 2][:, :], xt[:, dc * 512:(dc + 1) * 512], AF.Square)
            sS.inc(a)
    snS = sS.snap
    C.s(f)
    def f(t):
        for dc in range(8):
            sS.wait(t, sS.tok_after(snS, dc + 1))
            t.matmul(C.pB[0:1, 0:512], C.cstb[:, CB_ONES:CB_ONES + 1],
                     xb[:, dc * 512:(dc + 1) * 512], start=(dc == 0), stop=(dc == 7))
            m2 = t.matmul(C.pL[0:1, 0:512], C.cstf[:, CF_ONES:CF_ONES + 1],
                          sq_sl[dc % 2][:, :], start=(dc == 0), stop=(dc == 7))
            sT.inc(m2)
    C.t(f)
    tksum = sT.n
    def f(s):
        sT.wait(s, tksum)
        C.sY.wait(s)
        s.activation(R[0:1, 0:512], C.pB[0:1, 0:512], AF.Copy, scale=1.0 / 1024.0)
        a = s.activation(R[0:1, 512:1024], C.pL[0:1, 0:512], AF.Copy, scale=1.0 / 1024.0)
        tk = sS.inc(a)
        sS.wait(s, tk)
        a = s.activation(R[0:1, 1024:1536], R[0:1, 0:512], AF.Square)
        sS.inc(a)
    C.s(f)
    tk1 = sS.n
    def f(v):
        sS.wait(v, tk1)
        c = v.tensor_sub(R[0:1, 512:1024], R[0:1, 512:1024], R[0:1, 1024:1536])
        sV.inc(c)
    C.v(f)
    tk2 = sV.n
    def f(s):
        sV.wait(s, tk2)
        a = s.activation(R[0:1, 1024:1536], R[0:1, 512:1024], AF.Ln, bias=1e-5)
        tk = sS.inc(a)
        sS.wait(s, tk)
        a = s.activation(R[0:1, 1536:2048], R[0:1, 1024:1536], AF.Exp, scale=-0.5)
        sS.inc(a)
    C.s(f)
    tk3 = sS.n
    def f(v):
        sS.wait(v, tk3)
        v.tensor_mul(C.g2[0:1, 0:512], R[0:1, 0:512], R[0:1, 1536:2048])
        c = v.tensor_mul(R[0:1, 1024:1536], R[0:1, 0:512], R[0:1, 1536:2048])
        sV.inc(c)
    C.v(f)
    tk4 = sV.n
    if store:
        def f(y, tt=tt):
            sV.wait(y, tk4)
            C.ld(y, C.aghb[0:1, tt * 512:(tt + 1) * 512], R[0:1, 1536:2048], sDS)
            C.ld(y, C.aghb[1:2, tt * 512:(tt + 1) * 512], R[0:1, 1024:1536], sDS)
            sDS.wait(y)
            C.sY.inc(y.nop("rl", nofuse=True))
        C.y(f)
    def f(t):
        sV.wait(t, tk4)
        m = t.matmul(C.pB[0:128, 0:512], C.cstf[0:1, CF_ONESB:CF_ONESB + 128], R[0:1, 1536:2048],
                     start=True, stop=True)
        sT.inc(m)
    C.t(f)
    tk5 = sT.n
    def f(v):
        sT.wait(v, tk5)
        c = None
        for dc in range(8):
            c = v.tensor_mul(xb[:, dc * 512:(dc + 1) * 512],
                             xt[:, dc * 512:(dc + 1) * 512], C.pB[0:128, 0:512])
        sV.inc(c)
    C.v(f)
    C.AFA.pop()
    return sV.n


def ln_lite(C, tt, xt, xb, pre_t=None):
    sDX, sT, sV = C.sDX, C.sT, C.sV
    R = C.rows
    t0 = tt * 512
    def f(y):
        sV.wait(y)
        C.sS.wait(y)
        sT.wait(y)
        C.ld(y, R[0:1, 1536:2048], C.aghb[0:1, t0:t0 + 512], sDX)
        C.ld(y, R[0:1, 0:512], C.aghb[1:2, t0:t0 + 512], sDX)
        for dc in range(8):
            C.ld(y, xt[:, dc * 512:(dc + 1) * 512],
                 C.xres[dc * 128:(dc + 1) * 128, t0:t0 + 512], sDX)
        sDX.wait(y)
        C.sY.inc(y.nop("rl", nofuse=True))
    C.y(f)
    tkx = C.sY.n
    def f(t):
        C.sY.wait(t, tkx)
        if pre_t is not None:
            pre_t(t)
        m = t.matmul(C.pB[0:128, 0:512], C.cstf[0:1, CF_ONESB:CF_ONESB + 128],
                     R[0:1, 1536:2048], start=True, stop=True)
        sT.inc(m)
    C.t(f)
    tk5 = sT.n
    def f(v):
        sT.wait(v, tk5)
        c = None
        for dc in range(8):
            c = v.tensor_mul(xb[:, dc * 512:(dc + 1) * 512],
                             xt[:, dc * 512:(dc + 1) * 512], C.pB[0:128, 0:512])
        c2 = v.tensor_copy(C.g2[0:1, 0:512], R[0:1, 0:512])
        sV.inc(c2)
    C.v(f)
    return sV.n


def w1_bias(C, wsb, dout, w2off, bias_src=None, wsb_cols0=0, wait_tok=None):
    sT, sS, sG, sDW = C.sT, C.sS, C.sG, C.sDW
    def f(g):
        if bias_src is not None:
            C.ld(g, C.w2[1:2, w2off:w2off + dout], bias_src)
        else:
            C.ld(g, C.w2[1:2, w2off:w2off + dout], C.mrows[1:2, 0:dout])
        sDW.wait(g)
        sG.inc(g.nop("rl", nofuse=True))
    C.g(f)
    wtok = sG.n
    for oc in range((dout + 511) // 512):
        o0 = oc * 512
        ow = min(512, dout - o0)
        def f(t, o0=o0, ow=ow, snv=C.sV.n, sns=C.sS.n):
            sG.wait(t, wtok)
            C.sV.wait(t, snv)
            C.sS.wait(t, sns)
            m = None
            for ic in range(8):
                m = t.matmul(C.pB[0:1, 0:ow], C.cstb[:, CB_ONES:CB_ONES + 1],
                             wsb[:, wsb_cols0 + ic * dout + o0: wsb_cols0 + ic * dout + o0 + ow],
                             start=(ic == 0), stop=(ic == 7))
            sT.inc(m)
        C.t(f)
        tk = sT.n
        def f(s, o0=o0, ow=ow, tk=tk):
            sT.wait(s, tk)
            a = s.activation(C.w2[0:1, w2off + o0: w2off + o0 + ow], C.pB[0:1, 0:ow],
                             AF.Copy, scale=-1.0)
            sS.inc(a)
        C.s(f)
        tk2 = sS.n
        def f(t, tk2=tk2):
            sS.wait(t, tk2)
        C.t(f)
    return sS.n, sG.n, sDW.n


def proj_feat_oc(C, xb, wslice_fn, oc, w2off, pre=None):
    sT = C.sT
    def f(t):
        if pre is not None:
            pre(t)
        t.matmul(C.pC[0:128, 0:512], C.w2[:, w2off + oc * 128: w2off + (oc + 1) * 128],
                 C.g2[:, 0:512], start=True, stop=False)
        m = None
        for ic in range(8):
            m = t.matmul(C.pC[0:128, 0:512], wslice_fn(ic),
                         xb[:, ic * 512:(ic + 1) * 512], start=False, stop=(ic == 7))
        sT.inc(m)
    C.t(f)
    return sT.n


def proj_tok_tc(C, xb, wslice_fn, tc, ow, w2off, o0=0, pre=None):
    sT = C.sT
    def f(t):
        if pre is not None:
            pre(t)
        t.matmul(C.pC[0:128, 0:ow], C.g2[0:2, tc * 128:(tc + 1) * 128],
                 C.w2[:, w2off + o0: w2off + o0 + ow], start=True, stop=False)
        m = None
        for ic in range(8):
            m = t.matmul(C.pC[0:128, 0:ow],
                         xb[:, ic * 512 + tc * 128: ic * 512 + tc * 128 + 128],
                         wslice_fn(ic), start=False, stop=(ic == 7))
        sT.inc(m)
    C.t(f)
    return sT.n


KB.ln_full = ln_full
KB.ln_lite = ln_lite
KB.w1_bias = w1_bias
KB.proj_feat_oc = proj_feat_oc
KB.proj_tok_tc = proj_tok_tc

def load_wsec(C, dst_cols, src2d, dout, pitch, rows0=0, nic=8):
    t = C.AB_t
    def f(g):
        C.sT.wait(g)
        src = bass.AP(src2d.tensor, src2d.offset + rows0,
                      [[pitch, 128], [128 * pitch, nic], [1, dout]])
        C.sDW.inc(g.dma_start(out=t[:, dst_cols:dst_cols + nic * dout].rearrange("p (i o) -> p i o", i=nic),
                    in_=src), 16)
        C.sDW.wait(g)
        C.sG.inc(g.nop("rl", nofuse=True))
    C.g(f)
    return C.sG.n


def sub_attn_b(C, last):
    T, NT, NC, TP1 = C.T, C.NT, C.NC, C.TP1
    sv = 65 * 16
    AB, AFA = C.AB, C.AFA
    sT, sS, sV, sG, sDW, sDX, sDS = C.sT, C.sS, C.sV, C.sG, C.sDW, C.sDX, C.sDS

    AB.push()
    kn_o = AB.alloc(8 * TP1)
    kn = C.AB_t[:, kn_o:kn_o + 8 * TP1]

    def f(g):
        a = g.memset(kn[:, :].rearrange("p (c t) -> p c t", c=8)[:, :, 0:1], 0.0)
        sG.inc(a)
    C.g(f)
    C.fence()

    for sec, o0, w2off in (("q", 0, 0), ("k", 1024, 1024)):
        AB.push()
        AFA.push()
        xt = C.AF_t[:, AFA.alloc(4096):AFA.stack[-1]]
        xb_o = AB.alloc(4096)
        xb = C.AB_t[:, xb_o:xb_o + 4096]
        ws_o = AB.alloc(8192)
        ws = C.AB_t[:, ws_o:ws_o + 8192]
        wtok = load_wsec(C, ws_o, C.wqkvT, 1024, 3072, rows0=o0)
        C.w1_bias(ws, 1024, w2off, C.bqkv[0:1, o0:o0 + 1024], wait_tok=wtok)
        stg_o = AB.alloc(1024)
        stg = C.AB_t[:, stg_o:stg_o + 1024]
        stok = [(0, 0), (0, 0)]
        for tt in range(NT):
            if sec == "q":
                lt = C.ln_full(tt, xt, xb, store=True)
            else:
                lt = C.ln_lite(tt, xt, xb)
            prev_cp = [sS.n]
            for oc in range(8):
                def pre(t, pc=prev_cp[0], lt=lt):
                    sS.wait(t, pc)
                    sV.wait(t, lt)
                tok = C.proj_feat_oc(xb, lambda ic, oc=oc: ws[:, ic * 1024 + oc * 128: ic * 1024 + (oc + 1) * 128],
                                     oc, w2off, pre=pre)
                if sec == "q":
                    sl = oc % 2
                    def fse(s, oc=oc, sl=sl, tok=tok):
                        sT.wait(s, tok)
                        C.sY.wait(s, stok[sl])
                        a = s.activation(stg[:, sl * 512:(sl + 1) * 512], C.pC[0:128, 0:512], AF.Copy, scale=0.125)
                        sS.inc(a)
                    C.s(fse)
                    prev_cp[0] = sS.n
                    cp = sS.n
                    def fy(y, oc=oc, sl=sl, tt=tt, cp=cp):
                        sS.wait(y, cp)
                        C.ld(y, C.qhb[oc * 128:(oc + 1) * 128, tt * 512:(tt + 1) * 512],
                             stg[:, sl * 512:(sl + 1) * 512], sDS)
                        sDS.wait(y)
                        C.sY.inc(y.nop("rl", nofuse=True))
                    C.y(fy)
                    stok[sl] = C.sY.n
                else:
                    def fse(s, oc=oc, tt=tt, tok=tok):
                        sT.wait(s, tok)
                        a = s.activation(kn[:, oc * TP1 + tt * 512 + 1: oc * TP1 + tt * 512 + 513],
                                         C.pC[0:128, 0:512], AF.Copy)
                        sS.inc(a)
                    C.s(fse)
                    prev_cp[0] = sS.n
        AFA.pop()
        AB.pop()
        C.fence()

    vn_o = AB.alloc(NC * sv)
    vn = C.AB_t[:, vn_o:vn_o + NC * sv]
    def f(g):
        b = g.memset(vn[:, :].rearrange("p (k h c) -> p k h c", k=NC, h=16)[:, :, :, 64:65], 1.0)
        sG.inc(b)
    C.g(f)
    AB.push()
    AFA.push()
    xt = C.AF_t[:, AFA.alloc(4096):AFA.stack[-1]]
    xb_o = AB.alloc(4096)
    xb = C.AB_t[:, xb_o:xb_o + 4096]
    ws_o = AB.alloc(4096)
    o0 = 2048
    w2off = 2048
    wsu = [(0, 0), (0, 0)]
    for tt in range(NT):
        lt = C.ln_lite(tt, xt, xb)
        for sl in range(2):
            wsb = ws_o
            def fg(g, sl=sl, pu=tuple(map(max, wsu[0], wsu[1]))):
                sT.wait(g, pu)
                src = bass.AP(C.wqkvT.tensor, C.wqkvT.offset + o0 + sl * 512,
                              [[3072, 128], [128 * 3072, 8], [1, 512]])
                sDW.inc(g.dma_start(out=C.AB_t[:, ws_o:ws_o + 4096].rearrange("p (i o) -> p i o", i=8),
                            in_=src), 16)
                sDW.wait(g)
                sG.inc(g.nop("rl", nofuse=True))
            C.g(fg)
            wtok = sG.n
            if tt == 0:
                C.w1_bias(C.AB_t[:, wsb:wsb + 4096], 512, w2off + sl * 512,
                          C.bqkv[0:1, o0 + sl * 512:o0 + sl * 512 + 512], wait_tok=wtok)
            prev_ev = [sV.n]
            for tc in range(4):
                def pre(t, pv=prev_ev[0], wtok=wtok, lt=lt):
                    sG.wait(t, wtok)
                    sV.wait(t, tuple(map(max, pv, lt)))
                tok = C.proj_tok_tc(xb, lambda ic, wsb=wsb: C.AB_t[:, wsb + ic * 512:wsb + (ic + 1) * 512],
                                    tc, 512, w2off, o0=sl * 512, pre=pre)
                kt = tt * 4 + tc
                def fv(v, kt=kt, sl=sl, tok=tok, pe=prev_ev[0]):
                    sT.wait(v, tok)
                    sV.wait(v, pe)
                    dst = vn[:, kt * sv + 65 * 8 * sl: kt * sv + 65 * 8 * sl + 520]
                    dst = dst.rearrange("p (h c) -> p h c", c=65)[:, :, 0:64]
                    c = v.tensor_copy(dst, C.pC[0:128, 0:512].rearrange("p (h c) -> p h c", c=64))
                    sV.inc(c)
                C.v(fv)
                prev_ev[0] = sV.n
            wsu[sl] = sT.n
    AFA.pop()
    AB.pop()
    C.fence()

    AB.push()
    AFA.push()
    qs_o = AB.alloc(4096)
    qs = C.AB_t[:, qs_o:qs_o + 4096]
    ob_o = AB.alloc(4096)
    obuf = C.AB_t[:, ob_o:ob_o + 4096]
    odd_o = AB.alloc(512)
    odds = C.AB_t[:, odd_o:odd_o + 512]
    wp_o = AB.alloc(1024)
    wp = C.AB_t[:, wp_o:wp_o + 1024]
    meg_o = AFA.alloc(C.MEGW)
    mega = C.AF_t[:, meg_o:meg_o + C.MEGW]
    inv_o = AFA.alloc(512)
    invs = C.AF_t[:, inv_o:inv_o + 512]
    xo_o = AFA.alloc(1024)
    stf_o = AFA.alloc(1024)

    def f(g):
        C.ld(g, C.w2[0:1, 3072:3072 + 1024], C.mrows[1:2, 0:1024])
        C.ld(g, C.w2[1:2, 3072:3072 + 1024], C.bproj[0:1, :])
    C.g(f)
    C.fence()

    head_order = list(range(8, 16)) + list(range(0, 8))
    state = {"cur": None}

    def pre_head(slot, h):
        want = "mask" if h >= 8 else ("hope", h)
        if state["cur"] == want:
            return
        state["cur"] = want
        W = min(C.MEGW, 512 * (slot + 1) + 384)
        if want == "mask":
            tk = C.emit_mega_mask(mega, C.tbl_mf, qshift=0, W=W)
        else:
            off = C.LPAD - 384
            def fg(g, h=h, W=W):
                sT.wait(g)
                sV.wait(g)
                sDW.inc(g.dma_start(out=mega[:, 0:W],
                                    in_=C.tbl_hh[h * 128:(h + 1) * 128, off:off + W]), 16)
                sDW.wait(g)
                half = (W + 1) // 2
                for c0 in (0, half):
                    cw = min(half, W - c0)
                    sDW.inc(g.dma_start(out=mega[:, c0:c0 + cw],
                                        in_=C.tbl_hl[h * 128:(h + 1) * 128, off + c0:off + c0 + cw],
                                        accum_op=ALU.add), 16)
                    sDW.wait(g)
                sG.inc(g.nop("rl", nofuse=True))
            C.g(fg)
            tk = sG.n
        def ft(t, tk=tk):
            sG.wait(t, tk)
        C.t(ft)

    def q_load(slot):
        def f(y):
            sT.wait(y)
            for dc in range(8):
                C.ld(y, qs[:, dc * 512:(dc + 1) * 512],
                     C.qhb[dc * 128:(dc + 1) * 128, slot * 512:(slot + 1) * 512], sDX)
            sDX.wait(y)
            C.sY.inc(y.nop("rl", nofuse=True))
        C.y(f)
        tk = C.sY.n
        def ft(t, tk=tk):
            C.sY.wait(t, tk)
        C.t(ft)

    epst = {"tk2": (0, 0), "tk3": (0, 0), "tk4": (0, 0), "odd": (0, 0)}

    def epilogue(slot, h, av, tok_av):
        blk = h // 2
        def fs(s, tok_av=tok_av, p2=epst["tk2"]):
            C.sT2.wait(s, tok_av)
            sT.wait(s, p2)
            a = s.activation(C.rows[64:65, 0:512], av[64:65, :], AF.Ln)
            sS.inc(a)
        C.s(fs)
        tk1 = sS.n
        def ft(t, tk1=tk1, p3=epst["tk3"]):
            sS.wait(t, tk1)
            sS.wait(t, p3)
            m = t.matmul(C.pB[0:64, 0:512], C.cstf[64:65, CF_ONESB:CF_ONESB + 64],
                         C.rows[64:65, 0:512], start=True, stop=True)
            sT.inc(m)
        C.t(ft)
        tk2 = sT.n
        epst["tk2"] = tk2
        def fs2(s, tk2=tk2, p4=epst["tk4"]):
            sT.wait(s, tk2)
            sV.wait(s, p4)
            a = s.activation(invs[0:64, :], C.pB[0:64, 0:512], AF.Exp, scale=-1.0)
            sS.inc(a)
        C.s(fs2)
        tk3 = sS.n
        epst["tk3"] = tk3
        def fv(v, h=h, blk=blk, tk3=tk3, po=epst["odd"]):
            sS.wait(v, tk3)
            if h % 2 == 0:
                dst = obuf[0:64, blk * 512:(blk + 1) * 512]
            else:
                C.sY.wait(v, po)
                dst = odds[0:64, 0:512]
            c = v.tensor_mul(dst, av[0:64, :], invs[0:64, :])
            sV.inc(c)
        C.v(fv)
        tk4 = sV.n
        epst["tk4"] = tk4
        if h % 2 == 1:
            def fy(y, blk=blk, tk4=tk4):
                sV.wait(y, tk4)
                C.ld(y, obuf[64:128, blk * 512:(blk + 1) * 512], odds[0:64, 0:512], sDS)
                sDS.wait(y)
                C.sY.inc(y.nop("rl", nofuse=True))
            C.y(fy)
            epst["odd"] = C.sY.n
        return (sV, sV.n)

    def k_ap(slot, kt, h):
        dc = h // 2
        return kn[64 * (h % 2):64 * (h % 2) + 64, dc * TP1 + 1 + kt * 128: dc * TP1 + 1 + kt * 128 + 128]

    def q_ap(slot, h):
        dc = h // 2
        return qs[64 * (h % 2):64 * (h % 2) + 64, dc * 512:(dc + 1) * 512]

    def vn_ap(kt, h):
        base = kt * sv + 65 * h
        return vn[:, base:base + 65]

    def bias_ap(slot, kt, h):
        c0 = C.mega_c0(slot, kt)
        if h < 8:
            return mega[:, c0:c0 + 512]
        cnt = 4 * (slot + 1)
        if kt >= cnt - 4:
            return mega[:, c0:c0 + 512]
        return None

    cfg = dict(H=16, counts=lambda s: 4 * (s + 1), k_ap=k_ap, q_ap=q_ap,
               vn_ap=vn_ap, bias_ap=bias_ap, phantom=None, eshift=-40.0,
               epilogue=epilogue, pre_head=pre_head, head_order=head_order)

    for slot in range(NT):
        state["cur"] = None
        q_load(slot)
        C.emit_attention_slot(cfg, slot)
        C.out_proj_residual(slot, obuf, wp, wp_o, C.wpT, 3072,
                            C.AF_t[:, xo_o:xo_o + 1024], C.AF_t[:, stf_o:stf_o + 1024], last)
        C.fence()

    AFA.pop()
    AB.pop()
    AB.pop()


def out_proj_residual(C, slot, obuf, wp, wp_o, wsrc, w2off, xo, stf, last, nwp=1):
    sT, sS, sV, sDW, sDX, sDS = C.sT, C.sS, C.sV, C.sDW, C.sDX, C.sDS
    t0 = slot * 512
    prev = {"ev": sV.n, "st": [(0, 0), (0, 0)], "mm": (0, 0), "xv": [(0, 0), (0, 0)],
            "wu": [(0, 0), (0, 0)]}
    for oc in range(8):
        sl = oc % 2
        wsl = oc % nwp
        def fg(g, oc=oc, wsl=wsl, pu=prev["wu"][wsl] if nwp == 2 else prev["mm"]):
            sT.wait(g, pu)
            src = bass.AP(wsrc.tensor, wsrc.offset + oc * 128,
                          [[8192, 128], [1024, 8], [1, 128]])
            sDW.inc(g.dma_start(out=wp[:, wsl * 1024:wsl * 1024 + 1024].rearrange("p (j o) -> p j o", j=8),
                        in_=src), 16)
            sDW.wait(g)
            C.sG.inc(g.nop("rl", nofuse=True))
        C.g(fg)
        wtok = C.sG.n
        def fy(y, oc=oc, sl=sl):
            sV.wait(y, prev["xv"][sl])
            C.ld(y, xo[:, sl * 512:sl * 512 + 512], C.xres[oc * 128:(oc + 1) * 128, t0:t0 + 512], sDX)
            sDX.wait(y)
            C.sY.inc(y.nop("rl", nofuse=True))
        C.y(fy)
        xtok = C.sY.n
        def ft(t, oc=oc, wtok=wtok, wsl=wsl):
            C.sG.wait(t, wtok)
            sV.wait(t, prev["ev"])
            if oc == 0:
                C.sY.wait(t)
            t.matmul(C.pC[0:128, 0:512], C.w2[:, w2off + oc * 128: w2off + (oc + 1) * 128],
                     C.g2[:, 0:512], start=True, stop=False)
            m = None
            for j in range(8):
                m = t.matmul(C.pC[0:128, 0:512], wp[:, wsl * 1024 + j * 128:wsl * 1024 + (j + 1) * 128],
                             obuf[:, j * 512:(j + 1) * 512], start=False, stop=(j == 7))
            sT.inc(m)
        C.t(ft)
        mtok = sT.n
        prev["mm"] = mtok
        prev["wu"][wsl] = mtok
        def fv(v, sl=sl, mtok=mtok, xtok=xtok, pst=prev["st"][sl]):
            sT.wait(v, mtok)
            C.sY.wait(v, xtok)
            C.sY.wait(v, pst)
            c = v.tensor_add(stf[:, sl * 512:sl * 512 + 512], xo[:, sl * 512:sl * 512 + 512],
                             C.pC[0:128, 0:512])
            sV.inc(c)
        C.v(fv)
        prev["ev"] = sV.n
        prev["xv"][sl] = sV.n
        vtok = sV.n
        def fy2(y, oc=oc, sl=sl, vtok=vtok):
            sV.wait(y, vtok)
            dst = (C.y_out if last else C.xres)
            C.ld(y, dst[oc * 128:(oc + 1) * 128, t0:t0 + 512], stf[:, sl * 512:sl * 512 + 512], sDS)
            sDS.wait(y)
            C.sY.inc(y.nop("rl", nofuse=True))
        C.y(fy2)
        prev["st"][sl] = C.sY.n


KB.sub_attn_b = sub_attn_b
KB.out_proj_residual = out_proj_residual
KB.load_wsec = load_wsec

def prep_inputs(ins, T=4096):
    f32 = np.float32
    d = {}
    d["wkT"] = np.ascontiguousarray(ins["L0_attn_a_W_k"].T.astype(f32))
    d["wvT"] = np.ascontiguousarray(ins["L0_attn_a_W_v"].T.astype(f32))
    wc = ins["L0_attn_a_W_c"].astype(f32)
    wc_re = np.zeros((128, 8 * 1024), dtype=f32)
    for j in range(8):
        for p in range(128):
            wc_re[p, j * 1024:(j + 1) * 1024] = wc[:, 64 * (2 * j + p // 64) + (p % 64)]
    d["wcT"] = wc_re
    d["aprm"] = np.concatenate([
        ins["L0_attn_a_leaky_key_beta"].ravel(), ins["L0_attn_a_key_scale"].ravel(),
        ins["L0_attn_a_v_coef"].ravel(), ins["L0_attn_a_val_scale"].ravel()]
    ).astype(f32).reshape(1, 64)
    d["wqkvT"] = np.ascontiguousarray(ins["L0_attn_b_w_qkv"].T.astype(f32))
    d["bqkv"] = ins["L0_attn_b_b_qkv"].astype(f32).reshape(1, 3072)
    wp = ins["L0_attn_b_w_proj"].astype(f32)
    wp_re = np.zeros((128, 8 * 1024), dtype=f32)
    for j in range(8):
        for p in range(128):
            wp_re[p, j * 1024:(j + 1) * 1024] = wp[:, 64 * (2 * j + p // 64) + (p % 64)]
    d["wpT"] = wp_re
    d["bproj"] = ins["L0_attn_b_b_proj"].astype(f32).reshape(1, 1024)
    d["wqf"] = np.ascontiguousarray(ins["L0_ffn_w_q"].astype(f32))
    d["wkf"] = np.ascontiguousarray(ins["L0_ffn_w_k"].astype(f32))
    d["wgf"] = np.ascontiguousarray(ins["L0_ffn_w_gate"].astype(f32))
    ve = ins["L0_ffn_value_experts"].astype(f32)
    d["wve"] = np.ascontiguousarray(np.transpose(ve, (1, 2, 0, 3)).reshape(8, 1024, 512))
    oe = ins["L0_ffn_output_experts"].astype(f32)
    woe = np.zeros((32, 128, 1024), dtype=f32)
    for e in range(8):
        for j in range(4):
            m = e * 4 + j
            woe[m, 0:64, :] = oe[e, 2 * j]
            woe[m, 64:128, :] = oe[e, 2 * j + 1]
    d["woe"] = woe
    ma, mf, hh, hl = build_tables(T)
    d["tbl_ma"] = ma
    d["tbl_mf"] = mf
    d["tbl_hh"] = hh
    d["tbl_hl"] = hl
    d["cstb_in"] = mk_cstb()
    p = np.arange(128)[:, None]
    j = np.arange(128)[None, :]
    tu = np.zeros((128, 256), dtype=f32)
    tu[:, 0:128] = np.maximum(j - p, 0).astype(f32)
    tu[:, 128:256] = (j >= p).astype(f32)
    d["tblu"] = tu
    mr = np.zeros((2, 4096), dtype=np.float32); mr[0, :] = 1.0
    d["mrows"] = mr
    d["cstf_in"] = mk_cstf()
    d["selc_in"] = mk_selectors()
    return d


_SUBS_ALL = ("attn_a", "ffn1", "attn_b", "ffn2")


def run_device(ins, T=4096, subs=_SUBS_ALL, n_b=None):
    x = np.asarray(ins["L0_attn_a_x"], dtype=np.float32)
    B = x.shape[0]
    n_b = n_b or B
    shared = prep_inputs(ins, T)
    kb = KB(T, subs)
    nc = kb.build()
    in_maps = []
    for c in range(8):
        b = c % n_b
        m = dict(shared)
        m["x_feat"] = np.ascontiguousarray(x[b].T)
        in_maps.append(m)
    res = run_bass_kernel_spmd(nc, in_maps, core_ids=list(range(8)))
    outs = []
    for b in range(B):
        outs.append(np.ascontiguousarray(res.results[b]["y"].T))
    return np.stack(outs, axis=0)


def kernel(**inputs):
    return run_device(inputs, T=4096, subs=_SUBS_ALL)

def sub_ffn(C, last):
    T, NT, NC, TP1 = C.T, C.NT, C.NC, C.TP1
    sv = 65 * 8
    AB, AFA = C.AB, C.AFA
    sT, sS, sV, sG, sDW, sDX, sDS = C.sT, C.sS, C.sV, C.sG, C.sDW, C.sDX, C.sDS

    AB.push()
    knf_o = AB.alloc(4 * TP1)
    knf = C.AB_t[:, knf_o:knf_o + 4 * TP1]
    vnf_o = AB.alloc(NC * sv)
    vnf = C.AB_t[:, vnf_o:vnf_o + NC * sv]
    gmT_o = AB.alloc(NT * 512)
    def f(g):
        g.memset(knf[:, :].rearrange("p (c t) -> p c t", c=4)[:, :, 0:1], 0.0)
        a = g.memset(vnf[:, :].rearrange("p (k h c) -> p k h c", k=NC, h=8)[:, :, :, 64:65], 1.0)
        sG.inc(a)
    C.g(f)
    C.fence()

    AB.push()
    AFA.push()
    xt = C.AF_t[:, AFA.alloc(4096):AFA.stack[-1]]
    gt_o = AFA.alloc(640)
    mt_o = AFA.alloc(512)
    tmp_o = AFA.alloc(512)
    xb_o = AB.alloc(4096)
    xb = C.AB_t[:, xb_o:xb_o + 4096]
    wq_o = AB.alloc(4096)
    wk_o = AB.alloc(4096)
    wg_o = AB.alloc(512)
    wv_o = AB.alloc(8192)
    stg_o = AB.alloc(1024)
    stg = C.AB_t[:, stg_o:stg_o + 1024]

    load_wsec(C, wq_o, C.wqf, 512, 512)
    load_wsec(C, wk_o, C.wkf, 512, 512)
    load_wsec(C, wg_o, C.wgf, 64, 64)
    C.w1_bias(C.AB_t[:, wq_o:wq_o + 4096], 512, 0, None)
    C.w1_bias(C.AB_t[:, wk_o:wk_o + 4096], 512, 512, None)
    C.w1_bias(C.AB_t[:, wg_o:wg_o + 512], 64, 1024, None)

    wvu = [(0, 0), (0, 0)]
    for tt in range(NT):
        lt = C.ln_full(tt, xt, xb, store=False)
        prev_cp = [sS.n]
        for oc in range(4):
            def pre(t, pc=prev_cp[0], lt=lt):
                sS.wait(t, pc)
                sV.wait(t, lt)
            tok = C.proj_feat_oc(xb, lambda ic, oc=oc: C.AB_t[:, wq_o + ic * 512 + oc * 128: wq_o + ic * 512 + (oc + 1) * 128],
                                 oc, 0, pre=pre)
            sl = oc % 2
            def fse(s, sl=sl, tok=tok):
                sT.wait(s, tok)
                C.sY.wait(s)
                a = s.activation(stg[:, sl * 512:(sl + 1) * 512], C.pC[0:128, 0:512], AF.Copy, scale=0.125)
                sS.inc(a)
            C.s(fse)
            prev_cp[0] = sS.n
            cp = sS.n
            def fy(y, oc=oc, sl=sl, tt=tt, cp=cp):
                sS.wait(y, cp)
                C.ld(y, C.qhb[oc * 128:(oc + 1) * 128, tt * 512:(tt + 1) * 512],
                     stg[:, sl * 512:(sl + 1) * 512], sDS)
                sDS.wait(y)
                C.sY.inc(y.nop("rl", nofuse=True))
            C.y(fy)
        for oc in range(4):
            def pre(t, pc=prev_cp[0]):
                sS.wait(t, pc)
            tok = C.proj_feat_oc(xb, lambda ic, oc=oc: C.AB_t[:, wk_o + ic * 512 + oc * 128: wk_o + ic * 512 + (oc + 1) * 128],
                                 oc, 512, pre=pre)
            def fse(s, oc=oc, tt=tt, tok=tok):
                sT.wait(s, tok)
                a = s.activation(knf[:, oc * TP1 + tt * 512 + 1: oc * TP1 + tt * 512 + 513],
                                 C.pC[0:128, 0:512], AF.Copy)
                sS.inc(a)
            C.s(fse)
            prev_cp[0] = sS.n
        gm_toks = {}
        for tc in range(4):
            def pre(t, pc=prev_cp[0]):
                sS.wait(t, pc)
            tok = C.proj_tok_tc(xb, lambda ic: C.AB_t[:, wg_o + ic * 64: wg_o + (ic + 1) * 64],
                                tc, 64, 1024, pre=pre)
            gbase = gt_o + tc * 160
            def fse(s, tok=tok, gbase=gbase):
                sT.wait(s, tok)
                a = s.activation(C.AF_t[:, gbase:gbase + 64], C.pC[0:128, 0:64], AF.Sigmoid)
                sS.inc(a)
            C.s(fse)
            prev_cp[0] = sS.n
            stok = sS.n
            def fv(v, gbase=gbase, stok=stok):
                sS.wait(v, stok)
                sV.wait(v)
                g3 = C.AF_t[:, gbase:gbase + 64].rearrange("p (h e) -> p h e", h=8)
                a = v.tensor_reduce(C.AF_t[:, gbase + 128:gbase + 136], g3,
                                    mybir.AxisListType.X, ALU.max)
                tk = sV.inc(a)
                sV.wait(v, tk)
                mb = bass.AP(C.AF_t.ap().tensor, C.AF_t.ap().offset + gbase + 128,
                             [C.AF_t.ap().ap[0], [1, 8], [0, 8]])
                a = v.tensor_tensor(C.AF_t[:, gbase + 64:gbase + 128],
                                    C.AF_t[:, gbase:gbase + 64], mb, ALU.is_equal)
                tk = sV.inc(a)
                sV.wait(v, tk)
                c = v.tensor_mul(C.AF_t[:, gbase + 64:gbase + 128],
                                 C.AF_t[:, gbase + 64:gbase + 128], C.AF_t[:, gbase:gbase + 64])
                gm_toks[gbase] = sV.inc(c)
            C.v(fv)
        def fgf(t, pc=prev_cp[0], lt=lt):
            sS.wait(t, pc)
            sV.wait(t, lt)
            t.matmul(C.pC[0:64, 0:512], C.w2[:, 1024:1088], C.g2[:, 0:512], start=True, stop=False)
            m = None
            for ic in range(8):
                m = t.matmul(C.pC[0:64, 0:512], C.AB_t[:, wg_o + ic * 64: wg_o + (ic + 1) * 64],
                             xb[:, ic * 512:(ic + 1) * 512], start=False, stop=(ic == 7))
            sT.inc(m)
        C.t(fgf)
        tkgf = sT.n
        def fse(s, tkgf=tkgf):
            sT.wait(s, tkgf)
            a = s.activation(C.AF_t[0:64, tmp_o:tmp_o + 512], C.pC[0:64, 0:512], AF.Sigmoid)
            sS.inc(a)
        C.s(fse)
        tk_gf = sS.n
        prev_cp[0] = sS.n
        def ftr(t, tk_gf=tk_gf, gm_toks=gm_toks):
            for tc in range(4):
                sV.wait(t, gm_toks[gt_o + tc * 160])
            sS.wait(t, tk_gf)
            m = None
            for tc in range(4):
                m = t.transpose(C.pC[0:8, 128 * tc:128 * tc + 128],
                                C.AF_t[:, gt_o + tc * 160 + 128: gt_o + tc * 160 + 136],
                                C.cstf[:, CF_ID:CF_ID + 128])
            sT.inc(m)
        C.t(ftr)
        tktr = sT.n
        def fmt(v, tktr=tktr):
            sT.wait(v, tktr)
            c = v.tensor_copy(C.AF_t[0:8, mt_o:mt_o + 512], C.pC[0:8, 0:512])
            sV.inc(c)
        C.v(fmt)
        tkmt = sV.n
        def fmb(t, tkmt=tkmt):
            sV.wait(t, tkmt)
            m = t.matmul(C.pC[0:64, 0:512], C.cstf[0:8, CF_E8:CF_E8 + 64],
                         C.AF_t[0:8, mt_o:mt_o + 512], start=True, stop=True)
            sT.inc(m)
        C.t(fmb)
        tkmb = sT.n
        def fgm(v, tkmb=tkmb, tt=tt):
            sT.wait(v, tkmb)
            a = v.tensor_tensor(C.AF_t[0:64, mt_o:mt_o + 512],
                                C.AF_t[0:64, tmp_o:tmp_o + 512], C.pC[0:64, 0:512], ALU.is_equal)
            tk = sV.inc(a)
            sV.wait(v, tk)
            c = v.tensor_mul(C.AB_t[0:64, gmT_o + tt * 512: gmT_o + (tt + 1) * 512],
                             C.AF_t[0:64, mt_o:mt_o + 512], C.AF_t[0:64, tmp_o:tmp_o + 512])
            sV.inc(c)
        C.v(fgm)
        for hh in range(8):
            sl = hh % 2
            wvb = wv_o + sl * 4096
            def fg(g, hh=hh, pu=wvu[sl]):
                sT.wait(g, pu)
                src = bass.AP(C.wve.tensor, C.wve.offset + hh * 1024 * 512,
                              [[512, 128], [128 * 512, 8], [1, 512]])
                sDW.inc(g.dma_start(out=C.AB_t[:, wv_o + (hh % 2) * 4096:wv_o + (hh % 2) * 4096 + 4096].rearrange("p (i o) -> p i o", i=8),
                        in_=src), 16)
                sDW.wait(g)
                sG.inc(g.nop("rl", nofuse=True))
            C.g(fg)
            C.w1_bias(C.AB_t[:, wvb:wvb + 4096], 512, 2048 + sl * 512, None)
            for tc in range(4):
                def pre(t, pc=prev_cp[0]):
                    sS.wait(t, pc)
                    sV.wait(t)
                tok = C.proj_tok_tc(xb, lambda ic, wvb=wvb: C.AB_t[:, wvb + ic * 512:wvb + (ic + 1) * 512],
                                    tc, 512, 2048 + sl * 512, pre=pre)
                kt = tt * 4 + tc
                gbase = gt_o + tc * 160
                def fcv(v, hh=hh, kt=kt, tok=tok, gbase=gbase):
                    sT.wait(v, tok)
                    sV.wait(v)
                    t3 = C.AF_t[:, tmp_o:tmp_o + 512]
                    gmb = bass.AP(C.AF_t.ap().tensor, C.AF_t.ap().offset + gbase + 64 + 8 * hh,
                                  [C.AF_t.ap().ap[0], [1, 8], [0, 64]])
                    a = v.tensor_tensor(t3, C.pC[0:128, 0:512], gmb, ALU.mult)
                    tk = sV.inc(a)
                    sV.wait(v, tk)
                    a = v.tensor_add(t3[:, 0:256], t3[:, 0:256], t3[:, 256:512])
                    tk = sV.inc(a)
                    sV.wait(v, tk)
                    a = v.tensor_add(t3[:, 0:128], t3[:, 0:128], t3[:, 128:256])
                    tk = sV.inc(a)
                    sV.wait(v, tk)
                    c = v.tensor_add(vnf[:, kt * sv + 65 * hh: kt * sv + 65 * hh + 64],
                                     t3[:, 0:64], t3[:, 64:128])
                    sV.inc(c)
                C.v(fcv)
            wvu[sl] = sT.n
    AFA.pop()
    AB.pop()
    C.fence()

    AB.push()
    AFA.push()
    qs_o = AB.alloc(2048)
    qs = C.AB_t[:, qs_o:qs_o + 2048]
    ob_o = AB.alloc(2048)
    obuf = C.AB_t[:, ob_o:ob_o + 2048]
    odd_o = AB.alloc(512)
    odds = C.AB_t[:, odd_o:odd_o + 512]
    selc_o = AB.alloc(4096)
    selc = C.AB_t[:, selc_o:selc_o + 4096]
    ub_o = AB.alloc(512)
    ub = C.AB_t[:, ub_o:ub_o + 512]
    oe_o = AB.alloc(32 * 512)
    woe_o = AB.alloc(8192)
    meg_o = AFA.alloc(C.MEGW)
    mega = C.AF_t[:, meg_o:meg_o + C.MEGW]
    srow_o = AFA.alloc(512)
    srow = C.AF_t[:, srow_o:srow_o + 512]
    xo_o = AFA.alloc(1024)
    stf_o = AFA.alloc(1024)

    def f(g):
        C.ld(g, C.AB_t[0:64, selc_o:selc_o + 4096], C.selc_in[:, :])
    C.g(f)
    C.emit_mega_mask(mega, C.tbl_mf, qshift=0)
    C.fence()

    epst = {"gmm": (0, 0)}

    def epilogue(slot, h, av, tok_av):
        blk = h // 2
        def fs(s, h=h, blk=blk, tok_av=tok_av, pg=epst["gmm"]):
            C.sT2.wait(s, tok_av)
            sT.wait(s, pg)
            if h % 2 == 0:
                dst = obuf[0:64, blk * 512:(blk + 1) * 512]
            else:
                C.sY.wait(s)
                dst = odds[0:64, 0:512]
            s.activation(dst, av[0:64, :], AF.Copy)
            a = s.activation(srow[64:65, 0:512], av[64:65, :], AF.Copy)
            sS.inc(a)
        C.s(fs)
        tk1 = sS.n
        def ft(t, h=h, tk1=tk1):
            sS.wait(t, tk1)
            m = t.matmul(C.pB[0:8, 0:512], C.cstf[64:65, CF_ID + 64 - h:CF_ID + 72 - h],
                         srow[64:65, 0:512], start=(h == 0), stop=(h == 7))
            sT.inc(m)
        C.t(ft)
        epst["gmm"] = sT.n
        if h % 2 == 1:
            def fy(y, blk=blk, tk1=tk1):
                sS.wait(y, tk1)
                C.ld(y, obuf[64:128, blk * 512:(blk + 1) * 512], odds[0:64, 0:512], sDS)
                sDS.wait(y)
                C.sY.inc(y.nop("rl", nofuse=True))
            C.y(fy)
        return (sS, tk1)

    def q_load(slot):
        def f(y):
            sT.wait(y)
            for dc in range(4):
                C.ld(y, qs[:, dc * 512:(dc + 1) * 512],
                     C.qhb[dc * 128:(dc + 1) * 128, slot * 512:(slot + 1) * 512], sDX)
            sDX.wait(y)
            C.sY.inc(y.nop("rl", nofuse=True))
        C.y(f)
        tkq = C.sY.n
        def ft(t, tkq=tkq):
            C.sY.wait(t, tkq)
        C.t(ft)

    def k_ap(slot, kt, h):
        dc = h // 2
        return knf[64 * (h % 2):64 * (h % 2) + 64, dc * TP1 + 1 + kt * 128: dc * TP1 + 1 + kt * 128 + 128]

    def q_ap(slot, h):
        dc = h // 2
        return qs[64 * (h % 2):64 * (h % 2) + 64, dc * 512:(dc + 1) * 512]

    def vn_ap(kt, h):
        base = kt * sv + 65 * h
        return vnf[:, base:base + 65]

    def bias_ap(slot, kt, h):
        cnt = 4 * (slot + 1)
        if kt >= cnt - 4:
            c0 = C.mega_c0(slot, kt)
            return mega[:, c0:c0 + 512]
        return None

    cfg = dict(H=8, counts=lambda s: 4 * (s + 1), k_ap=k_ap, q_ap=q_ap,
               vn_ap=vn_ap, bias_ap=bias_ap, phantom=None, eshift=0.0,
               epilogue=epilogue)

    wou = [(0, 0), (0, 0)]
    for slot in range(NT):
        q_load(slot)
        C.emit_attention_slot(cfg, slot)
        def fr8(s):
            sT.wait(s)
            a = s.activation(C.rows[0:8, 0:512], C.pB[0:8, 0:512], AF.Ln)
            tk = sS.inc(a)
            sS.wait(s, tk)
            a = s.activation(C.rows[0:8, 512:1024], C.rows[0:8, 0:512], AF.Exp, scale=-1.0)
            sS.inc(a)
        C.s(fr8)
        tkr8 = sS.n
        def frb(t, tkr8=tkr8):
            sS.wait(t, tkr8)
            sV.wait(t)
            m = t.matmul(C.pC[0:64, 0:512], C.cstf[0:8, CF_E8:CF_E8 + 64],
                         C.rows[0:8, 512:1024], start=True, stop=True)
            sT.inc(m)
        C.t(frb)
        tkrb = sT.n
        def fu(v, slot=slot, tkrb=tkrb):
            sT.wait(v, tkrb)
            c = v.tensor_mul(ub[0:64, 0:512], C.AB_t[0:64, gmT_o + slot * 512: gmT_o + (slot + 1) * 512],
                             C.pC[0:64, 0:512])
            sV.inc(c)
        C.v(fu)
        tku = sV.n
        prev_oe = [tku, tku]
        for mm_i in range(32):
            bk = C.pC if mm_i % 2 == 0 else C.pB
            def fub(t, mm_i=mm_i, bk=bk, po=prev_oe[mm_i % 2]):
                sV.wait(t, po)
                m = t.matmul(bk[0:128, 0:512], C.AB_t[0:64, selc_o + 128 * mm_i: selc_o + 128 * mm_i + 128],
                             ub[0:64, 0:512], start=True, stop=True)
                sT.inc(m)
            C.t(fub)
            tub = sT.n
            jj = mm_i % 4
            def foe(v, mm_i=mm_i, jj=jj, bk=bk, tub=tub):
                sT.wait(v, tub)
                if mm_i == 0:
                    C.sY.wait(v)
                c = v.tensor_mul(C.AB_t[:, oe_o + mm_i * 512: oe_o + (mm_i + 1) * 512],
                                 obuf[:, jj * 512:(jj + 1) * 512], bk[0:128, 0:512])
                sV.inc(c)
            C.v(foe)
            prev_oe[mm_i % 2] = sV.n
        tkoe = sV.n
        t0 = slot * 512
        prevv = {"mm": (0, 0), "ev": tkoe, "st": [(0, 0), (0, 0)], "xv": [(0, 0), (0, 0)],
                 "wu": [wou[0], wou[1]]}
        for oc in range(8):
            sl = oc % 2
            wob = woe_o + sl * 4096
            def fgw(g, oc=oc, pu=prevv["wu"][sl]):
                sT.wait(g, pu)
                src = bass.AP(C.woe.tensor, C.woe.offset + oc * 128,
                              [[1024, 128], [128 * 1024, 32], [1, 128]])
                sDW.inc(g.dma_start(out=C.AB_t[:, woe_o + (oc % 2) * 4096:woe_o + (oc % 2) * 4096 + 4096].rearrange("p (m o) -> p m o", m=32),
                        in_=src), 16)
                sDW.wait(g)
                sG.inc(g.nop("rl", nofuse=True))
            C.g(fgw)
            wtok = sG.n
            def fy(y, oc=oc, sl=sl):
                sV.wait(y, prevv["xv"][sl])
                C.ld(y, C.AF_t[:, xo_o + sl * 512: xo_o + sl * 512 + 512],
                     C.xres[oc * 128:(oc + 1) * 128, t0:t0 + 512], sDX)
                sDX.wait(y)
                C.sY.inc(y.nop("rl", nofuse=True))
            C.y(fy)
            xtok = C.sY.n
            def ft(t, oc=oc, wtok=wtok, wob=wob):
                sG.wait(t, wtok)
                sV.wait(t, prevv["ev"])
                m = None
                for mm_i in range(32):
                    m = t.matmul(C.pC[0:128, 0:512], C.AB_t[:, wob + mm_i * 128: wob + (mm_i + 1) * 128],
                                 C.AB_t[:, oe_o + mm_i * 512: oe_o + (mm_i + 1) * 512],
                                 start=(mm_i == 0), stop=(mm_i == 31))
                sT.inc(m)
            C.t(ft)
            prevv["mm"] = sT.n
            prevv["wu"][sl] = sT.n
            mtok = sT.n
            def fv(v, sl=sl, mtok=mtok, pst=prevv["st"][sl], xtok=xtok):
                sT.wait(v, mtok)
                C.sY.wait(v, xtok)
                C.sY.wait(v, pst)
                c = v.tensor_add(C.AF_t[:, stf_o + sl * 512: stf_o + sl * 512 + 512],
                                 C.AF_t[:, xo_o + sl * 512: xo_o + sl * 512 + 512], C.pC[0:128, 0:512])
                sV.inc(c)
            C.v(fv)
            prevv["ev"] = sV.n
            prevv["xv"][sl] = sV.n
            vtok = sV.n
            def fy2(y, oc=oc, sl=sl, vtok=vtok):
                sV.wait(y, vtok)
                dst = (C.y_out if last else C.xres)
                C.ld(y, dst[oc * 128:(oc + 1) * 128, t0:t0 + 512],
                     C.AF_t[:, stf_o + sl * 512: stf_o + sl * 512 + 512], sDS)
                sDS.wait(y)
                C.sY.inc(y.nop("rl", nofuse=True))
            C.y(fy2)
            prevv["st"][sl] = C.sY.n
        wou[0], wou[1] = prevv["wu"]
        C.fence()

    AFA.pop()
    AB.pop()
    AB.pop()


KB.sub_ffn = sub_ffn

def sub_attn_a(C, last):
    T, NT, NC, TP1 = C.T, C.NT, C.NC, C.TP1
    sv = 65 * 16
    AB, AFA = C.AB, C.AFA
    sT, sS, sV, sG, sY = C.sT, C.sS, C.sV, C.sG, C.sY
    sDW, sDX, sDS = C.sDW, C.sDX, C.sDS
    AFT = C.AF_t

    AB.push()
    AFA.push()
    pr_o = AFA.alloc(64)
    dr_o = AFA.alloc(64)
    sc_o = AFA.alloc(64)
    ap_o = AFA.alloc(64)
    v16_o = AFA.alloc(1)

    def f(g):
        C.ld(g, AFT[0:1, pr_o:pr_o + 64], C.aprm[0:1, :])
        sDW.wait(g)
        sG.inc(g.nop("rl", nofuse=True))
    C.g(f)
    tok_pr = sG.n

    def f(s, tok_pr=tok_pr):
        sG.wait(s, tok_pr)
        a = s.activation(AFT[0:1, sc_o:sc_o + 16], AFT[0:1, pr_o:pr_o + 16], AF.Abs)
        tk = sS.inc(a)
        sS.wait(s, tk)
        a = s.activation(AFT[0:1, dr_o:dr_o + 16], AFT[0:1, sc_o:sc_o + 16], AF.Copy, scale=-10.0)
        sS.inc(a)
        a = s.activation(AFT[0:1, sc_o + 16:sc_o + 32], AFT[0:1, pr_o + 16:pr_o + 32], AF.Copy, scale=10.0)
        sS.inc(a)
        a = s.activation(AFT[0:1, dr_o + 32:dr_o + 48], AFT[0:1, pr_o + 32:pr_o + 48], AF.Copy)
        sS.inc(a)
        a = s.activation(AFT[0:1, sc_o + 48:sc_o + 64], AFT[0:1, pr_o + 48:pr_o + 64], AF.Exp, scale=10.0)
        sS.inc(a)
    C.s(f)
    tk_s1 = sS.n

    def f(v, tk_s1=tk_s1):
        sS.wait(v, tk_s1)
        ksm = bass.AP(C.cstf.ap().tensor, C.cstf.ap().offset + CF_CONST + 3,
                      [[C.cstf.ap().ap[0][0], 1], [0, 16]])
        a = v.tensor_tensor(AFT[0:1, sc_o + 32:sc_o + 48], AFT[0:1, sc_o + 16:sc_o + 32],
                            ksm, ALU.min)
        sV.inc(a)
        one = bass.AP(C.cstf.ap().tensor, C.cstf.ap().offset + CF_CONST + 4,
                      [[C.cstf.ap().ap[0][0], 1], [0, 16]])
        a = v.tensor_tensor(AFT[0:1, dr_o + 48:dr_o + 64], one,
                            AFT[0:1, pr_o + 32:pr_o + 48], ALU.subtract)
        sV.inc(a)
    C.v(f)
    tk_v1 = sV.n

    def f(s, tk_v1=tk_v1):
        sV.wait(s, tk_v1)
        a = s.activation(AFT[0:1, dr_o + 16:dr_o + 32], AFT[0:1, sc_o + 32:sc_o + 48], AF.Exp)
        sS.inc(a)
    C.s(f)
    tk_s2 = sS.n

    def fmm(t, tk_s2=tk_s2, tk_v1=tk_v1):
        sS.wait(t, tk_s2)
        sV.wait(t, tk_v1)
        m = t.matmul(C.pB[0:128, 0:64], C.cstf[0:1, CF_ONESB:CF_ONESB + 128],
                     AFT[0:1, dr_o:dr_o + 64], start=True, stop=True)
        sT.inc(m)
        m = t.matmul(C.pB[0:16, 64:65], AFT[0:1, sc_o + 48:sc_o + 64],
                     C.cstf[0:1, CF_CONST + 4:CF_CONST + 5], start=True, stop=True)
        sT.inc(m)
    C.t(fmm)
    tkmm = sT.n

    def fev(s, tkmm=tkmm):
        sT.wait(s, tkmm)
        a = s.activation(AFT[:, ap_o:ap_o + 64], C.pB[0:128, 0:64], AF.Copy)
        sS.inc(a)
        a = s.activation(AFT[0:16, v16_o:v16_o + 1], C.pB[0:16, 64:65], AF.Copy)
        sS.inc(a)
    C.s(fev)
    tok_ap = sS.n

    AB.push()
    AFA.push()
    du_o = AB.alloc(128)
    ut_o = AB.alloc(128)
    dcb_o = AB.alloc(2048)
    car_o = AB.alloc(1024)
    kn_o = AB.alloc(8 * TP1)
    kn = C.AB_t[:, kn_o:kn_o + 8 * TP1]
    wsk_o = AB.alloc(8192)
    wsk = C.AB_t[:, wsk_o:wsk_o + 8192]
    xb_o = AB.alloc(4096)
    xb = C.AB_t[:, xb_o:xb_o + 4096]
    kt_o = AB.alloc(1024)
    ktok = C.AB_t[:, kt_o:kt_o + 1024]
    cr2_o = AB.alloc(1024)
    xt = AFT[:, AFA.alloc(4096):AFA.stack[-1]]
    kns_o = AFA.alloc(1024)
    kns = AFT[:, kns_o:kns_o + 1024]
    lt1_o = AFA.alloc(128)
    lt2_o = AFA.alloc(128)
    sq_o = AFA.alloc(1024)
    nr_o = AFA.alloc(96)

    def f(g):
        a = g.memset(kn[:, :].rearrange("p (c t) -> p c t", c=8)[:, :, 0:1], 0.0)
        sG.inc(a)
    C.g(f)

    def f(g):
        C.ld(g, C.AB_t[:, du_o:du_o + 128], C.tblu[:, 0:128])
        C.ld(g, C.AB_t[:, ut_o:ut_o + 128], C.tblu[:, 128:256])
        sDW.wait(g)
        sG.inc(g.nop("rl", nofuse=True))
    C.g(f)
    tk4 = sG.n

    for h in range(16):
        def fv(v, h=h, tk4=tk4):
            sG.wait(v, tk4)
            sS.wait(v, tok_ap)
            sV.wait(v)
            sS.wait(v)
            nb = bass.AP(AFT.ap().tensor, AFT.ap().offset + ap_o + h,
                         [AFT.ap().ap[0], [0, 128]])
            a = v.tensor_tensor(AFT[:, lt1_o:lt1_o + 128], C.AB_t[:, du_o:du_o + 128],
                                nb, ALU.mult)
            sV.inc(a)
        C.v(fv)
        tkm = sV.n
        def fs(s, h=h, tkm=tkm):
            sV.wait(s, tkm)
            a = s.activation(AFT[:, lt2_o:lt2_o + 128], AFT[:, lt1_o:lt1_o + 128], AF.Exp)
            sS.inc(a)
        C.s(fs)
        tke = sS.n
        def fv2(v, h=h, tke=tke):
            sS.wait(v, tke)
            a = v.tensor_mul(C.pt[:, 128 * h:128 * h + 128], AFT[:, lt2_o:lt2_o + 128],
                             C.AB_t[:, ut_o:ut_o + 128])
            tk = sV.inc(a)
            sV.wait(v, tk)
            nb0 = bass.AP(AFT.ap().tensor, AFT.ap().offset + ap_o + h,
                          [[AFT.ap().ap[0][0], 1], [0, 128]])
            a = v.tensor_tensor(AFT[0:1, lt1_o:lt1_o + 128], C.cstb[0:1, CB_P1ROW:CB_P1ROW + 128],
                                nb0, ALU.mult)
            sV.inc(a)
        C.v(fv2)
        tkd = sV.n
        def fs2(s, h=h, tkd=tkd):
            sV.wait(s, tkd)
            a = s.activation(C.AB_t[0:1, dcb_o + 128 * h:dcb_o + 128 * h + 128],
                             AFT[0:1, lt1_o:lt1_o + 128], AF.Exp)
            sS.inc(a)
        C.s(fs2)

    wtokK = load_wsec(C, wsk_o, C.wkT, 1024, 1024)
    C.w1_bias(wsk, 1024, 0, None)
    C.fence()

    st = {"evac": (0, 0), "knmul": (0, 0), "scan": (0, 0), "knev": (0, 0),
          "carev": (0, 0)}
    for tt in range(NT):
        lt = C.ln_full(tt, xt, xb, store=True)
        for tc in range(4):
            c = 4 * tt + tc
            for half in range(2):
                def pre(t, pc=st["evac"], lt=lt, wtokK=wtokK):
                    sG.wait(t, wtokK)
                    sS.wait(t, pc)
                    sV.wait(t, lt)
                tok = C.proj_tok_tc(xb, lambda ic, half=half: wsk[:, ic * 1024 + half * 512: ic * 1024 + half * 512 + 512],
                                    tc, 512, 0, o0=half * 512, pre=pre)
                def fse(s, half=half, tok=tok, sc=st["scan"]):
                    sT.wait(s, tok)
                    sT.wait(s, sc)
                    a = s.activation(ktok[:, half * 512:(half + 1) * 512], C.pC[0:128, 0:512], AF.Copy)
                    sS.inc(a)
                C.s(fse)
                st["evac"] = sS.n

            ktokTok = sS.n
            def ftsc(t, c=c, ktokTok=ktokTok, pk=st["knmul"]):
                sS.wait(t, ktokTok)
                sV.wait(t, pk)
                m = None
                for h in range(16):
                    m = t.matmul(C.pL[:, 64 * h:64 * h + 64], C.pt[:, 128 * h:128 * h + 128],
                                 ktok[:, 64 * h:64 * h + 64], start=True, stop=(c == 0))
                    if c > 0:
                        m = t.matmul(C.pL[:, 64 * h:64 * h + 64],
                                     C.AB_t[0:1, dcb_o + 128 * h:dcb_o + 128 * h + 128],
                                     C.AB_t[0:1, car_o + 64 * h:car_o + 64 * h + 64],
                                     start=False, stop=True)
                sT.inc(m)
            C.t(ftsc)
            st["scan"] = sT.n
            scanTok = sT.n
            if c < NC - 1:
                def fcr(s, scanTok=scanTok, pce=st["carev"]):
                    sT.wait(s, scanTok)
                    sS.wait(s, pce)
                    s.activation(C.AB_t[64:128, cr2_o:cr2_o + 512], C.pL[64:128, 0:512], AF.Copy)
                    a = s.activation(C.AB_t[64:128, cr2_o + 512:cr2_o + 1024],
                                     C.pL[64:128, 512:1024], AF.Copy)
                    tk = sS.inc(a)
                    sS.wait(s, tk)
                    C.sDA.inc(s.dma_start(out=C.AB_t[0:1, car_o:car_o + 1024],
                                          in_=C.AB_t[127:128, cr2_o:cr2_o + 1024]), 16)
                    C.sDA.wait(s)
                    sS.inc(s.nop("rl", nofuse=True))
                C.s(fcr)
                st["carev"] = sS.n
            def fsq(s, scanTok=scanTok):
                sT.wait(s, scanTok)
                a = s.activation(AFT[:, sq_o:sq_o + 512], C.pL[:, 0:512], AF.Square)
                sS.inc(a)
                a = s.activation(AFT[:, sq_o + 512:sq_o + 1024], C.pL[:, 512:1024], AF.Square)
                sS.inc(a)
            C.s(fsq)
            tksq = sS.n
            def fnr(v, tksq=tksq):
                sS.wait(v, tksq)
                sV.wait(v)
                g3 = AFT[:, sq_o:sq_o + 1024].rearrange("p (h d) -> p h d", h=16)
                a = v.tensor_reduce(AFT[:, nr_o:nr_o + 16], g3, mybir.AxisListType.X, ALU.add)
                sV.inc(a)
            C.v(fnr)
            tkr = sV.n
            def fns(s, tkr=tkr):
                sV.wait(s, tkr)
                a = s.activation(AFT[:, nr_o + 16:nr_o + 32], AFT[:, nr_o:nr_o + 16], AF.Sqrt)
                sS.inc(a)
            C.s(fns)
            tkn1 = sS.n
            def fne(v, tkn1=tkn1):
                sS.wait(v, tkn1)
                eps = bass.AP(C.cstf.ap().tensor, C.cstf.ap().offset + CF_CONST + 1,
                              [C.cstf.ap().ap[0], [0, 16]])
                a = v.tensor_tensor(AFT[:, nr_o + 32:nr_o + 48], AFT[:, nr_o + 16:nr_o + 32],
                                    eps, ALU.add)
                sV.inc(a)
            C.v(fne)
            tkn2 = sV.n
            def fnl(s, tkn2=tkn2):
                sV.wait(s, tkn2)
                a = s.activation(AFT[:, nr_o + 48:nr_o + 64], AFT[:, nr_o + 32:nr_o + 48], AF.Ln)
                tk = sS.inc(a)
                sS.wait(s, tk)
                a = s.activation(AFT[:, nr_o + 64:nr_o + 80], AFT[:, nr_o + 48:nr_o + 64], AF.Exp, scale=-1.0)
                sS.inc(a)
            C.s(fnl)
            tkn3 = sS.n
            def fni(v, tkn3=tkn3, c=c):
                sS.wait(v, tkn3)
                a = v.tensor_mul(AFT[:, nr_o + 80:nr_o + 96], AFT[:, nr_o + 64:nr_o + 80],
                                 AFT[:, ap_o + 16:ap_o + 32])
                tk = sV.inc(a)
                sV.wait(v, tk)
                lo = bass.AP(AFT.ap().tensor, AFT.ap().offset + nr_o + 80,
                             [AFT.ap().ap[0], [1, 8], [0, 64]])
                a = v.tensor_tensor(kns[:, 0:512], C.pL[:, 0:512], lo, ALU.mult)
                tk = sV.inc(a)
                sV.wait(v, tk)
                hi = bass.AP(AFT.ap().tensor, AFT.ap().offset + nr_o + 88,
                             [AFT.ap().ap[0], [1, 8], [0, 64]])
                a = v.tensor_tensor(kns[:, 512:1024], C.pL[:, 512:1024], hi, ALU.mult)
                sV.inc(a)
            C.v(fni)
            st["knmul"] = sV.n
            tkkm = sV.n
            def ftr(t, tkkm=tkkm, pkv=st["knev"]):
                sV.wait(t, tkkm)
                sS.wait(t, pkv)
                m = None
                for dc in range(8):
                    m = t.transpose(C.pA[:, 128 * dc:128 * dc + 128], kns[:, 128 * dc:128 * dc + 128],
                                    C.cstf[:, CF_ID:CF_ID + 128])
                sT.inc(m)
            C.t(ftr)
            tktr = sT.n
            def fke(s, c=c, tktr=tktr):
                sT.wait(s, tktr)
                a = None
                for dc in range(8):
                    a = s.activation(kn[:, dc * TP1 + 1 + 128 * c: dc * TP1 + 1 + 128 * c + 128],
                                     C.pA[:, 128 * dc:128 * dc + 128], AF.Copy)
                    sS.inc(a)
            C.s(fke)
            st["knev"] = sS.n
    C.fence()
    def f(g):
        C.ld(g, C.knhb[:, :], kn[:, :])
        sDW.wait(g)
        sG.inc(g.nop("rl", nofuse=True))
    C.g(f)
    C.fence()
    AFA.pop()
    AB.pop()
    vn_o = AB.alloc(NC * sv)
    vn = C.AB_t[:, vn_o:vn_o + NC * sv]
    AB.push()
    AFA.push()
    vfa_o = AB.alloc(8192)
    wsv_o = AB.alloc(8192)
    wsv = C.AB_t[:, wsv_o:wsv_o + 8192]
    xb_o = AB.alloc(4096)
    xb = C.AB_t[:, xb_o:xb_o + 4096]
    vma_o = AB.alloc(4096)
    i2b_o = AB.alloc(512)
    xt = AFT[:, AFA.alloc(4096):AFA.stack[-1]]
    vst_o = AFA.alloc(512)
    vstage = AFT[:, vst_o:vst_o + 512]
    vm_o = AFA.alloc(512)
    vsh_o = AFA.alloc(512)
    vsq_o = AFA.alloc(512)
    n16_o = AFA.alloc(1024)

    def f(g):
        a = g.memset(vn[:, :].rearrange("p (k h c) -> p k h c", k=NC, h=16)[:, :, :, 64:65], 1.0)
        sG.inc(a)
    C.g(f)
    wtokV = load_wsec(C, wsv_o, C.wvT, 1024, 1024)
    C.w1_bias(wsv, 1024, 1024, None)
    C.fence()

    st = {"evac": (0, 0), "vno": (0, 0), "chain": (0, 0), "vnev": [(0, 0), (0, 0)],
          "trv": (0, 0)}
    for ti in range(NT):
        tt = NT - 1 - ti
        cur, nxt = tt % 2, (tt + 1) % 2
        def pre_t(t, pc=st["chain"]):
            sS.wait(t, pc)
        lt = C.ln_lite(tt, xt, xb, pre_t=pre_t)
        for dc in range(8):
            def pre(t, pc=st["evac"], pv=st["vno"], lt=lt, wtokV=wtokV):
                sG.wait(t, wtokV)
                sS.wait(t, pc)
                sV.wait(t, tuple(map(max, pv, lt)))
            tok = C.proj_feat_oc(xb, lambda ic, dc=dc: wsv[:, ic * 1024 + dc * 128: ic * 1024 + dc * 128 + 128],
                                 dc, 1024, pre=pre)
            def fse(s, dc=dc, cur=cur, tok=tok):
                sT.wait(s, tok)
                sS.wait(s)
                sV.wait(s)
                a = s.activation(C.AB_t[:, vfa_o + cur * 4096 + 512 * dc: vfa_o + cur * 4096 + 512 * dc + 512],
                                 C.pC[0:128, 0:512], AF.Copy)
                sS.inc(a)
            C.s(fse)
            st["evac"] = sS.n
        vfaTok = sS.n
        for dc in range(8):
            def fvm(v, dc=dc, tt=tt, cur=cur, nxt=nxt, vfaTok=vfaTok):
                sS.wait(v)
                sV.wait(v)
                def bc(colo, base, n):
                    sl = AFT[base:base + 64, colo:colo + 1]
                    return bass.AP(sl.tensor, sl.offset, [[sl.ap[0][0], 64], [0, n]])
                for ph in range(2):
                    b0 = 64 * ph
                    hh = 2 * dc + ph
                    vcur = C.AB_t[b0:b0 + 64, vfa_o + cur * 4096 + 512 * dc: vfa_o + cur * 4096 + 512 * dc + 512]
                    a = v.tensor_tensor(AFT[b0:b0 + 64, vm_o:vm_o + 512], vcur,
                                        bc(ap_o + 32 + hh, b0, 512), ALU.mult)
                    tk = sV.inc(a)
                    sV.wait(v, tk)
                    a = v.tensor_tensor(AFT[b0:b0 + 64, vsh_o:vsh_o + 511],
                                        C.AB_t[b0:b0 + 64, vfa_o + cur * 4096 + 512 * dc + 1: vfa_o + cur * 4096 + 512 * dc + 512],
                                        bc(ap_o + 48 + hh, b0, 511), ALU.mult)
                    tk = sV.inc(a)
                    sV.wait(v, tk)
                    if tt < NT - 1:
                        tail = C.AB_t[b0:b0 + 64, vfa_o + nxt * 4096 + 512 * dc: vfa_o + nxt * 4096 + 512 * dc + 1]
                    else:
                        tail = C.cstf[b0:b0 + 64, CF_CONST + 5:CF_CONST + 6]
                    a = v.tensor_tensor(AFT[b0:b0 + 64, vsh_o + 511:vsh_o + 512], tail,
                                        bc(ap_o + 48 + hh, b0, 1), ALU.mult)
                    tk = sV.inc(a)
                    sV.wait(v, tk)
                    a = v.tensor_add(AFT[b0:b0 + 64, vm_o:vm_o + 512], AFT[b0:b0 + 64, vm_o:vm_o + 512],
                                     AFT[b0:b0 + 64, vsh_o:vsh_o + 512])
                    sV.inc(a)
            C.v(fvm)
            tkvm = sV.n
            def fsqv(s, dc=dc, cur=cur, tkvm=tkvm):
                sV.wait(s, tkvm)
                sT.wait(s)
                a = s.activation(AFT[:, vsq_o:vsq_o + 512], AFT[:, vm_o:vm_o + 512], AF.Square)
                sS.inc(a)
                a = s.activation(C.AB_t[:, vma_o + 512 * dc: vma_o + 512 * dc + 512],
                                 AFT[:, vm_o:vm_o + 512], AF.Copy)
                sS.inc(a)
            C.s(fsqv)
            tksqv = sS.n
            def fssq(t, dc=dc, tksqv=tksqv, lt=lt):
                sS.wait(t, tksqv)
                sV.wait(t, lt)
                m = t.matmul(C.pB[0:16, 0:512], C.cstf[:, CF_BD + 16 * dc: CF_BD + 16 * dc + 16],
                             AFT[:, vsq_o:vsq_o + 512], start=(dc == 0), stop=(dc == 7))
                sT.inc(m)
            C.t(fssq)
        ssqTok = sT.n
        def fn1(s, ssqTok=ssqTok):
            sT.wait(s, ssqTok)
            a = s.activation(AFT[0:16, n16_o:n16_o + 512], C.pB[0:16, 0:512], AF.Sqrt)
            sS.inc(a)
        C.s(fn1)
        chain0 = sS.n
        st["chain"] = chain0
        def fn2(v, chain0=chain0):
            sS.wait(v, chain0)
            eps = bass.AP(C.cstf.ap().tensor, C.cstf.ap().offset + CF_CONST + 1,
                          [[C.cstf.ap().ap[0][0], 16], [0, 512]])
            a = v.tensor_tensor(AFT[0:16, n16_o + 512:n16_o + 1024], AFT[0:16, n16_o:n16_o + 512],
                                eps, ALU.add)
            sV.inc(a)
        C.v(fn2)
        tkn2 = sV.n
        def fn3(s, tkn2=tkn2):
            sV.wait(s, tkn2)
            a = s.activation(AFT[0:16, n16_o:n16_o + 512], AFT[0:16, n16_o + 512:n16_o + 1024], AF.Ln)
            tk = sS.inc(a)
            sS.wait(s, tk)
            a = s.activation(AFT[0:16, n16_o + 512:n16_o + 1024], AFT[0:16, n16_o:n16_o + 512],
                             AF.Exp, scale=-1.0)
            sS.inc(a)
        C.s(fn3)
        tkn3 = sS.n
        def fn4(v, tkn3=tkn3):
            sS.wait(v, tkn3)
            vs = bass.AP(AFT.ap().tensor, AFT.ap().offset + v16_o,
                         [[AFT.ap().ap[0][0], 16], [0, 512]])
            a = v.tensor_tensor(AFT[0:16, n16_o:n16_o + 512], AFT[0:16, n16_o + 512:n16_o + 1024],
                                vs, ALU.mult)
            sV.inc(a)
        C.v(fn4)
        tkn4 = sV.n
        def fn5(s, tkn4=tkn4):
            sV.wait(s, tkn4)
            a = s.activation(C.AB_t[0:16, i2b_o:i2b_o + 512], AFT[0:16, n16_o:n16_o + 512], AF.Copy)
            sS.inc(a)
        C.s(fn5)
        i2bTok = sS.n
        for dc in range(8):
            def fbc(t, dc=dc, i2bTok=i2bTok, pv=st["vno"]):
                sS.wait(t, i2bTok)
                sV.wait(t, pv)
                m = t.matmul(C.pC[0:128, 0:512], C.cstb[0:16, CB_BDT + 128 * dc: CB_BDT + 128 * dc + 128],
                             C.AB_t[0:16, i2b_o:i2b_o + 512], start=True, stop=True)
                sT.inc(m)
            C.t(fbc)
            tkbc = sT.n
            def fvn(v, dc=dc, tkbc=tkbc, ptr=st["trv"]):
                sT.wait(v, tkbc)
                sT.wait(v, ptr)
                a = v.tensor_mul(vstage[:, 0:512], C.AB_t[:, vma_o + 512 * dc: vma_o + 512 * dc + 512],
                                 C.pC[0:128, 0:512])
                sV.inc(a)
            C.v(fvn)
            st["vno"] = sV.n
            tkvn = sV.n
            def ftrv(t, dc=dc, tkvn=tkvn, pev=st["vnev"][dc % 2]):
                sV.wait(t, tkvn)
                sV.wait(t, pev)
                m = None
                for blk in range(4):
                    m = t.transpose(C.pA[:, 512 * (dc % 2) + 128 * blk: 512 * (dc % 2) + 128 * blk + 128],
                                    vstage[:, 128 * blk:128 * blk + 128],
                                    C.cstf[:, CF_ID:CF_ID + 128])
                sT.inc(m)
            C.t(ftrv)
            st["trv"] = sT.n
            tktrv = sT.n
            def fvev(v, dc=dc, tt=tt, tktrv=tktrv):
                sT.wait(v, tktrv)
                a = None
                for blk in range(4):
                    kt = 4 * tt + blk
                    dst = vn[:, kt * sv + 65 * 2 * dc: kt * sv + 65 * 2 * dc + 130]
                    dst = dst.rearrange("p (h c) -> p h c", c=65)[:, :, 0:64]
                    src = C.pA[:, 512 * (dc % 2) + 128 * blk: 512 * (dc % 2) + 128 * blk + 128]
                    a = v.tensor_copy(dst, src.rearrange("p (h c) -> p h c", c=64))
                    sV.inc(a)
            C.v(fvev)
            st["vnev"][dc % 2] = sV.n
    AFA.pop()
    AB.pop()
    C.fence()
    AB.push()
    AFA.push()
    knr_o = AB.alloc(8 * TP1)
    knr = C.AB_t[:, knr_o:knr_o + 8 * TP1]
    ob_o = AB.alloc(4096)
    obuf = C.AB_t[:, ob_o:ob_o + 4096]
    odd_o = AB.alloc(512)
    odds = C.AB_t[:, odd_o:odd_o + 512]
    wp_o = AB.alloc(2048)
    wp = C.AB_t[:, wp_o:wp_o + 2048]
    meg_o = AFA.alloc(C.MEGW)
    mega = AFT[:, meg_o:meg_o + C.MEGW]
    inv_o = AFA.alloc(512)
    invs = AFT[:, inv_o:inv_o + 512]
    xo_o = AFA.alloc(1024)
    stf_o = AFA.alloc(1024)

    def f(g):
        C.ld(g, knr[:, :], C.knhb[:, :])
        C.ld(g, C.w2[0:1, 3072:3072 + 1024], C.mrows[1:2, 0:1024])
        C.ld(g, C.w2[1:2, 3072:3072 + 1024], C.mrows[1:2, 0:1024])
    C.g(f)
    C.emit_mega_mask(mega, C.tbl_ma, qshift=0)
    C.fence()

    epst = {"tk2": (0, 0), "tk3": (0, 0), "tk4": (0, 0), "odd": (0, 0)}

    def epilogue(slot, h, av, tok_av):
        blk = h // 2
        if slot == 0:
            def ffix(v, av=av, tok_av=tok_av):
                C.sT2.wait(v, tok_av)
                a = v.tensor_add(av[64:65, 0:1], av[64:65, 0:1], C.cstf[64:65, CF_ONES:CF_ONES + 1])
                sV.inc(a)
            C.v(ffix)
            fixTok = sV.n
        else:
            fixTok = None
        def fs(s, tok_av=tok_av, p2=epst["tk2"], fixTok=fixTok):
            C.sT2.wait(s, tok_av)
            if fixTok is not None:
                sV.wait(s, fixTok)
            sT.wait(s, p2)
            a = s.activation(C.rows[64:65, 0:512], av[64:65, :], AF.Ln)
            sS.inc(a)
        C.s(fs)
        tk1 = sS.n
        def ft(t, tk1=tk1, p3=epst["tk3"]):
            sS.wait(t, tk1)
            sS.wait(t, p3)
            m = t.matmul(C.pB[0:64, 0:512], C.cstf[64:65, CF_ONESB:CF_ONESB + 64],
                         C.rows[64:65, 0:512], start=True, stop=True)
            sT.inc(m)
        C.t(ft)
        tk2 = sT.n
        epst["tk2"] = tk2
        def fs2(s, tk2=tk2, p4=epst["tk4"]):
            sT.wait(s, tk2)
            sV.wait(s, p4)
            a = s.activation(invs[0:64, :], C.pB[0:64, 0:512], AF.Exp, scale=-1.0)
            sS.inc(a)
        C.s(fs2)
        tk3 = sS.n
        epst["tk3"] = tk3
        def fv(v, h=h, blk=blk, tk3=tk3, po=epst["odd"]):
            sS.wait(v, tk3)
            if h % 2 == 0:
                dst = obuf[0:64, blk * 512:(blk + 1) * 512]
            else:
                C.sY.wait(v, po)
                dst = odds[0:64, 0:512]
            c2 = v.tensor_mul(dst, av[0:64, :], invs[0:64, :])
            sV.inc(c2)
        C.v(fv)
        tk4 = sV.n
        epst["tk4"] = tk4
        if h % 2 == 1:
            def fy(y, blk=blk, tk4=tk4):
                sV.wait(y, tk4)
                C.ld(y, obuf[64:128, blk * 512:(blk + 1) * 512], odds[0:64, 0:512], sDS)
                sDS.wait(y)
                C.sY.inc(y.nop("rl", nofuse=True))
            C.y(fy)
            epst["odd"] = C.sY.n
        return (sV, sV.n)

    def k_ap(slot, kt, h):
        dc = h // 2
        return knr[64 * (h % 2):64 * (h % 2) + 64, dc * TP1 + 1 + kt * 128: dc * TP1 + 1 + kt * 128 + 128]

    def q_ap(slot, h):
        dc = h // 2
        return knr[64 * (h % 2):64 * (h % 2) + 64, dc * TP1 + 1 + slot * 512: dc * TP1 + 1 + slot * 512 + 512]

    def vn_ap(kt, h):
        base = kt * sv + 65 * h
        return vn[:, base:base + 65]

    def bias_ap(slot, kt, h):
        cnt = 4 * (slot + 1)
        if kt >= cnt - 4:
            c0 = C.mega_c0(slot, kt)
            return mega[:, c0:c0 + 512]
        return None

    cfg = dict(H=16, counts=lambda s: 4 * (s + 1), k_ap=k_ap, q_ap=q_ap,
               vn_ap=vn_ap, bias_ap=bias_ap, phantom=None, eshift=-20.0,
               epilogue=epilogue)

    for slot in range(NT):
        C.emit_attention_slot(cfg, slot)
        C.out_proj_residual(slot, obuf, wp, wp_o, C.wcT, 3072,
                            AFT[:, xo_o:xo_o + 1024], AFT[:, stf_o:stf_o + 1024], last, nwp=2)
        C.fence()

    AFA.pop()
    AB.pop()
    AFA.pop()
    AB.pop()


KB.sub_attn_a = sub_attn_a
